# version 04
# baseline (speedup 1.0000x reference)
import math
from contextlib import ExitStack

import numpy as np
import concourse.bass as bass
import concourse.mybir as mybir
from concourse.bass_utils import run_bass_kernel_spmd

F32 = mybir.dt.float32
BF16 = mybir.dt.bfloat16
I32 = mybir.dt.int32
U8 = mybir.dt.uint8
AF = mybir.ActivationFunctionType
ALU = mybir.AluOpType
AX = mybir.AxisListType

B, S, DM = 4, 2048, 2048
H, HD = 16, 128
SH = 1024
NCH = DM // 128
NTB = 2
G = 64
E1 = 2112
E1P = 2176
KC1 = E1P // 128
NKV = 4
SCALE0 = 1.0 / math.sqrt(HD)
SCALE1 = HD ** (-0.5)
FMIN = float(np.finfo(np.float32).min)
MASKV = float(2.0 ** 126)
LN2_INV = 1.0 / math.log(2.0)
EPS = 1e-5
FP8_MAX = 448.0
ARENA = 124 * 1024

PAIRS = [[0, 1], [2, 3], [4, 5], [6, 7]]
ALL8 = [[0, 1, 2, 3, 4, 5, 6, 7]]

ALL_PHASES = ("l0attn", "l0ffn", "l1attn", "l1ffn")
DEV = {}


class Sched:
    ENGINES = ("sync", "tensor", "vector", "scalar", "gpsimd")

    def __init__(self):
        self.q = {e: [] for e in self.ENGINES}
        self.cnt = {}

    def bump(self, sem, n):
        k = id(sem)
        self.cnt[k] = self.cnt.get(k, 0) + n
        return self.cnt[k]

    def val(self, sem):
        return self.cnt.get(id(sem), 0)


class Rec:
    def __init__(self, sched, key):
        self.__dict__["_s"] = sched
        self.__dict__["_k"] = key

    def __getattr__(self, attr):
        s = self.__dict__["_s"]
        k = self.__dict__["_k"]

        def call(*a, **kw):
            inc = kw.pop("inc", None)
            if inc is not None:
                s.bump(inc[0], inc[1])

            def run(h, a=a, kw=kw, inc=inc, attr=attr):
                inst = getattr(h, attr)(*a, **kw)
                if inc is not None:
                    inst.then_inc(*inc)
            s.q[k].append(run)
        return call


def tile_lhsT(w):
    K, M = w.shape
    kc, mc = K // 128, M // 128
    return np.ascontiguousarray(
        w.reshape(kc, 128, mc, 128).transpose(2, 1, 0, 3).reshape(mc, 128, kc * 128)
    ).astype(np.float32)


def rope_tables(positions, dim, base):
    inv = 1.0 / (float(base) ** (np.arange(0, dim, 2, dtype=np.float64) / dim))
    freqs = np.outer(np.asarray(positions, dtype=np.float64), inv)
    emb = np.concatenate([freqs, freqs], axis=-1)

    def hl(t64):
        t = np.ascontiguousarray(t64.T)
        hi = t.astype(np.float32)
        lo = (t - hi.astype(np.float64)).astype(np.float32)
        return hi, np.ascontiguousarray(lo)
    return hl(np.cos(emb)), hl(np.sin(emb))


def build_program(phases=ALL_PHASES, dbg=None):
    nc = bass.Bass("TRN2", target_bir_lowering=False, detect_race_conditions=False)
    s = Sched()
    SY, T, V, C, GQ = (Rec(s, "sync"), Rec(s, "tensor"), Rec(s, "vector"),
                       Rec(s, "scalar"), Rec(s, "gpsimd"))

    def inp(name, shape, dt=F32):
        return nc.dram_tensor(name, list(shape), dt, kind="ExternalInput").ap()

    def dram(name, shape, dt=F32):
        return nc.dram_tensor(name, list(shape), dt)

    x_in = inp("x_fm", (DM, SH))
    q0_img = inp("q0_img", (16, 128, NCH * 128))
    k0_img = inp("k0_img", (16, 128, NCH * 128))
    v0_img = inp("v0_img", (16, 128, NCH * 128))
    wout_img = inp("wout_img", (16, 128, NCH * 128))
    cos0_h = inp("cos0_hi", (128, SH)); cos0_l = inp("cos0_lo", (128, SH))
    sin0_h = inp("sin0_hi", (128, SH)); sin0_l = inp("sin0_lo", (128, SH))
    wffn_img = inp("wffn_img", (16, 128, NCH * 128))
    aT_img = inp("aT_img", (1, 128, NCH * 128))
    bstack_img = inp("bstack_img", (16, 128, 128))
    gate_w_in = inp("gate_w", (16, 128, 8))
    wq1_img = inp("wq1_img", (16, 128, NCH * 128))
    wk1_img = inp("wk1_img", (4, 128, NCH * 128))
    wv1_img = inp("wv1_img", (4, 128, NCH * 128))
    wo1_img = inp("wo1_img", (16, 128, NCH * 128))
    gt_img = inp("gt_img", (16, 128, G))
    cos1_h = inp("cos1_hi", (128, SH)); cos1_l = inp("cos1_lo", (128, SH))
    sin1_h = inp("sin1_hi", (128, SH)); sin1_l = inp("sin1_lo", (128, SH))
    cos1g = inp("cos1g", (128, G)); sin1g = inp("sin1g", (128, G))
    cos1g_l = inp("cos1g_lo", (128, G)); sin1g_l = inp("sin1g_lo", (128, G))
    jrow_in = inp("jrow", (128, E1P))
    jbig_in = inp("jbig", (128, E1P))
    wfc_shard = inp("wfc_shard", (8, 128, NCH * 128))
    wproj_shard = inp("wproj_shard", (2, 128, 64 * 128))
    ident_b_in = inp("ident_b", (128, 128), BF16)
    ident_f_in = inp("ident_f", (128, 128), F32)
    ones_col_in = inp("ones_col", (128, 8))
    ones_row_in = inp("ones_row", (1, 128))
    eps_row_in = inp("eps_row", (1, 512))
    bffn_in = inp("bffn", (128, 16))
    gate_b_in = inp("gate_b", (128, 8))
    e_expand_in = inp("e_expand", (8, 128))
    ccol_in = inp("ccol", (128, 8))
    arange_col_in = inp("arange_col", (128, 8))

    y_out = nc.dram_tensor("y_fm", [DM, SH], F32, kind="ExternalOutput").ap()
    if dbg == "ag0":
        dbg_out = nc.dram_tensor("dbg", [2 * 64 * 128, SH], BF16,
                                 kind="ExternalOutput").ap()
    elif dbg:
        dbg_out = nc.dram_tensor("dbg", [DM, SH], F32, kind="ExternalOutput").ap()
    else:
        dbg_out = None

    ag0_in = dram("ag0_in", (64 * 128, SH), BF16)
    ag0_out = dram("ag0_out", (2 * 64 * 128, SH), BF16)
    q0_save = dram("q0_save", (32 * 128, SH), BF16)
    ag1_in = dram("ag1_in", (16 * 128, SH), BF16)
    ag1_out = dram("ag1_out", (2 * 16 * 128, SH), BF16)
    q1_save = dram("q1_save", (32 * 128, SH), BF16)
    r_save = dram("r_save", (DM, SH), F32)
    h2_save = dram("h2_save", (64 * 128, SH), F32)
    agw_in = dram("agw_in", (8 * 128, 2048), BF16)
    agw_out = dram("agw_out", (64 * 128, 2048), BF16)
    agp_in = dram("agp_in", (2 * 128, 8192), BF16)
    agp_out = dram("agp_out", (16 * 128, 8192), BF16)
    ar_in = dram("ar_in", (1, 8), F32)
    ar_out = dram("ar_out", (1, 8), F32)
    ar2_in = dram("ar2_in", (1, 8), F32)
    ar2_out = dram("ar2_out", (1, 8), F32)
    h8_save = dram("h8_save", (64 * 128, SH), BF16)
    gcol = dram("gcol", (128, 1), F32)

    R = nc.alloc_sbuf_tensor("R", [128, NCH * SH], F32)
    arena_t = nc.alloc_sbuf_tensor("arena", [128, ARENA], U8)
    cst = {}
    for nm, shape, dt in [
        ("ident_b", (128, 128), BF16), ("ident_f", (128, 128), F32),
        ("ones_col", (128, 8), F32), ("ones_row", (1, 128), F32),
        ("eps_row", (1, 512), F32), ("bffn", (128, 16), F32),
        ("gate_b", (128, 8), F32), ("e_expand", (8, 128), F32),
        ("ccol", (128, 8), F32), ("arange_col", (128, 8), F32),
    ]:
        cst[nm] = nc.alloc_sbuf_tensor("c_" + nm, list(shape), dt)
    lnrow = nc.alloc_sbuf_tensor("lnrow", [1, 2048], F32)
    grow = nc.alloc_sbuf_tensor("grow", [1, 1032], F32)
    gstat = nc.alloc_sbuf_tensor("gstat", [128, 64], F32)
    rstat = nc.alloc_sbuf_tensor("rstat", [128, 64], F32)
    sstat = nc.alloc_sbuf_tensor("sstat", [128, 16], F32)

    def A(off, p, n, dt=F32):
        esz = mybir.dt.size(dt)
        assert off % 64 == 0 and off + n * esz <= ARENA, (off, n, dt)
        view = arena_t[:, off:off + n * esz].bitcast(dt)
        return view if p == 128 else view[0:p, :]

    ctx = ExitStack()
    sem = {nm: ctx.enter_context(nc.semaphore(nm))
           for nm in ["ld", "wl", "tn", "vs", "sc", "gq", "cc"]}
    ld, wl, tn, vs, sc, gq, cc = (sem[n] for n in
                                  ["ld", "wl", "tn", "vs", "sc", "gq", "cc"])

    ps = [nc.alloc_psum_tensor("ps%d" % i, [128, 512], F32) for i in range(6)]
    ps_s1 = nc.alloc_psum_tensor("ps_s1", [1, 512], F32)
    ps_s2 = nc.alloc_psum_tensor("ps_s2", [1, 512], F32)
    bank_guard = {}

    def bank_wait(rec, pt):
        g = bank_guard.get(id(pt))
        if g is not None:
            rec.wait_ge(g[0], g[1])

    def bank_done(pt, sm):
        bank_guard[id(pt)] = (sm, s.val(sm))

    SY.dma_start(out=R[:].rearrange("p (c t) -> p c t", c=NCH),
                 in_=x_in.rearrange("(c p) t -> p c t", p=128), inc=(ld, 16))
    for nm, src in [("ident_b", ident_b_in), ("ident_f", ident_f_in),
                    ("ones_col", ones_col_in), ("ones_row", ones_row_in),
                    ("eps_row", eps_row_in), ("bffn", bffn_in),
                    ("gate_b", gate_b_in), ("e_expand", e_expand_in),
                    ("ccol", ccol_in), ("arange_col", arange_col_in)]:
        SY.dma_start(out=cst[nm][:], in_=src, inc=(ld, 16))
    setup_ld = s.val(ld)
    for e in Sched.ENGINES:
        Rec(s, e).wait_ge(ld, setup_ld)
    V.memset(gstat[:], 1.0)
    V.memset(rstat[:], 1.0)
    V.memset(sstat[:], 1.0)
    V.memset(lnrow[:], 1.0)
    V.memset(grow[:], 1.0, inc=(vs, 1))
    GQ.wait_ge(vs, s.val(vs))

    def split_hilo(src_ap, hi_ap, lo_ap, inc=None):
        V.tensor_copy(out=hi_ap, in_=src_ap)
        V.tensor_tensor(lo_ap, src_ap, hi_ap, ALU.subtract, inc=inc)

    def mm3(psum_ap, lh, ll, rh, rl, start, stop, inc=None):
        T.matmul(psum_ap, lh, rh, start=start, stop=False)
        T.matmul(psum_ap, lh, rl, start=False, stop=False)
        T.matmul(psum_ap, ll, rh, start=False, stop=stop, inc=inc)

    class WStream:

        def __init__(self, img_ap, cw, off):
            self.img = img_ap
            self.cw = cw
            self.fb = [A(off, 128, cw), A(off + cw * 4, 128, cw)]
            o2 = off + cw * 8
            self.hb = [A(o2, 128, cw, BF16), A(o2 + cw * 2, 128, cw, BF16)]
            self.lb = [A(o2 + cw * 4, 128, cw, BF16), A(o2 + cw * 6, 128, cw, BF16)]
            self.split_done = [0, 0]
            self.freed = [(vs, s.val(vs)), (vs, s.val(vs))]
            self.n = 0

        def load(self, j):
            b = self.n % 2
            self.n += 1
            SY.wait_ge(*self.freed[b])
            SY.dma_start(out=self.fb[b][:, :], in_=self.img[j], inc=(wl, 16))
            tgt = s.val(wl)
            V.wait_ge(wl, tgt)
            split_hilo(self.fb[b][:, :], self.hb[b][:, :], self.lb[b][:, :],
                       inc=(vs, 1))
            self.split_done[b] = s.val(vs)
            return b

        def tiles(self, b, c):
            return (self.hb[b][:, 128 * c:128 * (c + 1)],
                    self.lb[b][:, 128 * c:128 * (c + 1)])

        def wait(self, rec, b):
            rec.wait_ge(vs, self.split_done[b])

        def release(self, b, sm):
            self.freed[b] = (sm, s.val(sm))

    def rope_emit(psum_list, cos_ap, sin_ap, tmp, out_f32, tb):
        p = psum_list
        sl = slice(512 * tb, 512 * (tb + 1))
        V.tensor_scalar_mul(tmp[0:64, 0:512], p[64:128, :], -1.0)
        V.tensor_copy(out=tmp[64:128, 0:512], in_=p[0:64, :])
        V.tensor_tensor(out_f32[:, sl], p[:, :], cos_ap[:, sl], ALU.mult)
        V.tensor_tensor(tmp[:, 512:1024], tmp[:, 0:512], sin_ap[:, sl], ALU.mult)
        V.tensor_tensor(out_f32[:, sl], out_f32[:, sl], tmp[:, 512:1024], ALU.add)

    def layer_norm_tb(y_ap_fn, tb, sq_tiles):
        y_ready = s.val(vs)
        C.wait_ge(vs, y_ready)
        T.wait_ge(vs, y_ready)
        sq_guard = [None, None]
        for c in range(NCH):
            src = y_ap_fn(c)
            sqt = sq_tiles[c % 2]
            if sq_guard[c % 2] is not None:
                C.wait_ge(tn, sq_guard[c % 2])
            C.activation(out=sqt, in_=src, func=AF.Square, inc=(sc, 1))
            sq_done = s.val(sc)
            if c == 0:
                bank_wait(T, ps_s1)
                bank_wait(T, ps_s2)
            T.matmul(ps_s1[:], cst["ones_col"][:, 0:1], src,
                     start=(c == 0), stop=(c == NCH - 1))
            T.wait_ge(sc, sq_done)
            T.matmul(ps_s2[:], cst["ones_col"][:, 0:1], sqt,
                     start=(c == 0), stop=(c == NCH - 1),
                     inc=(tn, 1))
            sq_guard[c % 2] = s.val(tn)
        V.wait_ge(tn, s.val(tn))
        mean, e2 = lnrow[:, 0:512], lnrow[:, 512:1024]
        aro, bro = lnrow[:, 1024:1536], lnrow[:, 1536:2048]
        V.tensor_scalar_mul(mean, ps_s1[:], 1.0 / DM)
        V.tensor_scalar_mul(e2, ps_s2[:], 1.0 / DM)
        V.tensor_tensor(bro, mean, mean, ALU.mult)
        V.tensor_tensor(e2, e2, bro, ALU.subtract)
        V.tensor_tensor(e2, e2, cst["eps_row"][:], ALU.add, inc=(vs, 1))
        bank_done(ps_s1, vs)
        bank_done(ps_s2, vs)
        C.wait_ge(vs, s.val(vs))
        C.activation(out=aro, in_=e2, func=AF.Sqrt, inc=(sc, 1))
        V.wait_ge(sc, s.val(sc))
        V.reciprocal(aro, aro)
        V.scalar_tensor_tensor(bro, mean, -1.0, aro, ALU.mult, ALU.mult,
                               inc=(vs, 1))
        bank_wait(T, ps[4])
        bank_wait(T, ps[5])
        T.wait_ge(vs, s.val(vs))
        T.matmul(ps[4][:], cst["ones_row"][:], aro, start=True, stop=True)
        T.matmul(ps[5][:], cst["ones_row"][:], bro, start=True, stop=True,
                 inc=(tn, 1))
        V.wait_ge(tn, s.val(tn))
        for c in range(NCH):
            src = y_ap_fn(c)
            dst = R[:, SH * c + 512 * tb: SH * c + 512 * (tb + 1)]
            V.tensor_tensor(dst, src, ps[4][:], ALU.mult)
            V.tensor_tensor(dst, dst, ps[5][:], ALU.add,
                            inc=(vs, 1) if c == NCH - 1 else None)
        bank_done(ps[4], vs)
        bank_done(ps[5], vs)
        return s.val(vs)

    XH0, XL0 = 0, 32768
    CO0, SI0 = 65536, 69632
    WS0 = 73728
    QT0 = 106496
    RT0 = 110592

    def proj_head(ws, col, cos_ap, sin_ap, out_hi, out_lo, xh, xl, rope, qtmp,
                  rtmp):
        b = ws.load(col)
        ws.wait(T, b)
        for tb in range(NTB):
            pt = ps[tb]
            bank_wait(T, pt)
            for c in range(NCH):
                lh, ll = ws.tiles(b, c)
                rh = xh[:, SH * c + 512 * tb: SH * c + 512 * (tb + 1)]
                rl = xl[:, SH * c + 512 * tb: SH * c + 512 * (tb + 1)]
                mm3(pt[:], lh, ll, rh, rl, start=(c == 0), stop=(c == NCH - 1),
                    inc=(tn, 1) if c == NCH - 1 else None)
            V.wait_ge(tn, s.val(tn))
            sl = slice(512 * tb, 512 * (tb + 1))
            if rope:
                rope_emit(pt[:], cos_ap, sin_ap, rtmp, qtmp, tb)
                split_hilo(qtmp[:, sl], out_hi[:, sl], out_lo[:, sl],
                           inc=(vs, 1))
            else:
                split_hilo(pt[:], out_hi[:, sl], out_lo[:, sl], inc=(vs, 1))
            bank_done(pt, vs)
        ws.release(b, vs)
        return s.val(vs)

    def emit_l0_proj():
        xh = A(XH0, 128, NCH * SH, BF16)
        xl = A(XL0, 128, NCH * SH, BF16)
        cos_e = A(CO0, 128, SH)
        sin_e = A(SI0, 128, SH)
        qtmp = A(QT0, 128, SH)
        rtmp = A(RT0, 128, SH)
        SY.dma_start(out=cos_e[:, :], in_=cos0_h, inc=(ld, 16))
        SY.dma_start(out=qtmp[:, :], in_=cos0_l, inc=(ld, 16))
        V.wait_ge(ld, s.val(ld))
        V.tensor_tensor(cos_e[:, :], cos_e[:, :], qtmp[:, :], ALU.add, inc=(vs, 1))
        cos_done = s.val(vs)
        SY.wait_ge(vs, cos_done)
        SY.dma_start(out=sin_e[:, :], in_=sin0_h, inc=(ld, 16))
        SY.dma_start(out=qtmp[:, :], in_=sin0_l, inc=(ld, 16))
        V.wait_ge(ld, s.val(ld))
        V.tensor_tensor(sin_e[:, :], sin_e[:, :], qtmp[:, :], ALU.add, inc=(vs, 1))
        for c in range(NCH):
            split_hilo(R[:, SH * c:SH * (c + 1)],
                       xh[:, SH * c:SH * (c + 1)], xl[:, SH * c:SH * (c + 1)],
                       inc=(vs, 1) if c == NCH - 1 else None)
        ws = WStream(q0_img, NCH * 128, WS0)
        oh = A(114688, 128, SH, BF16)
        ol = A(116736, 128, SH, BF16)
        agf = ag0_in.ap()
        q0f = q0_save.ap()
        for hh in range(H):
            done = proj_head(ws, hh, cos_e, sin_e, oh, ol, xh, xl, True,
                             qtmp, rtmp)
            GQ.wait_ge(vs, done)
            GQ.dma_start(out=q0f[128 * hh:128 * (hh + 1), :], in_=oh[:, :],
                         inc=(gq, 16))
            GQ.dma_start(out=q0f[128 * (16 + hh):128 * (17 + hh), :],
                         in_=ol[:, :], inc=(gq, 16))
            V.wait_ge(gq, s.val(gq))
        ws_k = WStream(k0_img, NCH * 128, WS0)
        for hh in range(H):
            done = proj_head(ws_k, hh, cos_e, sin_e, oh, ol, xh, xl, True,
                             qtmp, rtmp)
            GQ.wait_ge(vs, done)
            GQ.dma_start(out=agf[128 * hh:128 * (hh + 1), :], in_=oh[:, :],
                         inc=(gq, 16))
            GQ.dma_start(out=agf[128 * (16 + hh):128 * (17 + hh), :],
                         in_=ol[:, :], inc=(gq, 16))
            V.wait_ge(gq, s.val(gq))
        ws_v = WStream(v0_img, NCH * 128, WS0)
        for hh in range(H):
            done = proj_head(ws_v, hh, None, None, oh, ol, xh, xl, False,
                             qtmp, rtmp)
            GQ.wait_ge(vs, done)
            GQ.dma_start(out=agf[128 * (32 + hh):128 * (33 + hh), :],
                         in_=oh[:, :], inc=(gq, 16))
            GQ.dma_start(out=agf[128 * (48 + hh):128 * (49 + hh), :],
                         in_=ol[:, :], inc=(gq, 16))
            V.wait_ge(gq, s.val(gq))
        GQ.wait_ge(gq, s.val(gq))
        GQ.collective_compute("AllGather", ALU.bypass, replica_groups=PAIRS,
                              ins=[ag0_in.ap().opt()],
                              outs=[ag0_out.ap().opt()], inc=(cc, 1))
        return s.val(cc)

    attn_hi = R[:, 0:8192].bitcast(BF16)
    attn_lo = R[:, 8192:16384].bitcast(BF16)

    KH0 = 0
    KL0 = 4096
    VF0 = 8192
    VT0 = 16384
    PP0 = 24576
    PT0 = 32768
    QH0 = 65536
    QF1 = 66560

    def emit_l0_core(ag_ready):
        kh = A(KH0, 128, 2048, BF16)
        kl = A(KL0, 128, 2048, BF16)
        vfh = A(VF0, 128, 2048, BF16)
        vfl = A(VF0 + 4096, 128, 2048, BF16)
        vth = A(VT0, 128, 2048, BF16)
        vtl = A(VT0 + 4096, 128, 2048, BF16)
        P = A(PP0, 128, 2048)
        pth = A(PT0, 128, 8192, BF16)
        ptl = A(PT0 + 16384, 128, 8192, BF16)
        qh = A(QH0, 128, 128, BF16)
        ql = A(QH0 + 256, 128, 128, BF16)
        pth_r = pth.rearrange("p (kc qt qq) -> p kc qt qq", kc=16, qq=128)
        ptl_r = ptl.rearrange("p (kc qt qq) -> p kc qt qq", kc=16, qq=128)
        ps4_r = ps[4][:].rearrange("p (i one q) -> p i one q", i=4, one=1)
        ago = ag0_out.ap()
        q0f = q0_save.ap()
        ps_tb = ps[4][:, 0:256].bitcast(BF16)
        GQ.wait_ge(cc, ag_ready)
        SY.wait_ge(cc, ag_ready)
        nheads = DEV.get("nheads", H)
        part = DEV.get("part", 4)
        for hh in range(nheads):
            prev_v = s.val(vs)
            SY.wait_ge(vs, prev_v)
            for (dst, base) in ((kh, hh), (kl, 16 + hh)):
                for r in range(2):
                    SY.dma_start(out=dst[:, 1024 * r:1024 * (r + 1)],
                                 in_=ago[8192 * r + 128 * base:
                                         8192 * r + 128 * (base + 1), :],
                                 inc=(ld, 16))
            for (dst, base) in ((vfh, 32 + hh), (vfl, 48 + hh)):
                for r in range(2):
                    SY.dma_start(out=dst[:, 1024 * r:1024 * (r + 1)],
                                 in_=ago[8192 * r + 128 * base:
                                         8192 * r + 128 * (base + 1), :],
                                 inc=(ld, 16))
            kv_ld = s.val(ld)
            T.wait_ge(ld, kv_ld)
            for (src, dst) in ((vfh, vth), (vfl, vtl)):
                for g in range(4):
                    bank_wait(T, ps[4])
                    for i in range(4):
                        c = 4 * g + i
                        T.transpose(ps_tb[:, 128 * i:128 * (i + 1)],
                                    src[:, 128 * c:128 * (c + 1)],
                                    cst["ident_b"][:],
                                    inc=(tn, 1) if i == 3 else None)
                    V.wait_ge(tn, s.val(tn))
                    V.tensor_copy(out=dst[:, 512 * g:512 * (g + 1)],
                                  in_=ps_tb[:, :], inc=(vs, 1))
                    bank_done(ps[4], vs)
            vt_done = s.val(vs)
            if part < 2:
                continue
            for qb in range(2):
                for qt in range(4):
                    qi = 4 * qb + qt
                    SY.wait_ge(vs, s.val(vs))
                    SY.dma_start(out=qh[:, :],
                                 in_=q0f[128 * hh:128 * (hh + 1),
                                         128 * qi:128 * (qi + 1)],
                                 inc=(ld, 16))
                    SY.dma_start(out=ql[:, :],
                                 in_=q0f[128 * (16 + hh):128 * (17 + hh),
                                         128 * qi:128 * (qi + 1)],
                                 inc=(ld, 16))
                    q_ld = s.val(ld)
                    T.wait_ge(ld, q_ld)
                    for kb in range(4):
                        pt = ps[kb]
                        bank_wait(T, pt)
                        mm3(pt[:], qh[:, :], ql[:, :],
                            kh[:, 512 * kb:512 * (kb + 1)],
                            kl[:, 512 * kb:512 * (kb + 1)],
                            start=True, stop=True,
                            inc=(tn, 1) if kb == 3 else None)
                    V.wait_ge(tn, s.val(tn))
                    for kb in range(4):
                        V.reduce_max(gstat[:, kb:kb + 1], ps[kb][:], AX.X,
                                     inc=(vs, 1) if kb == 3 else None)
                    V.wait_ge(vs, s.val(vs))
                    V.reduce_max(gstat[:, 4:5], gstat[:, 0:4], AX.X,
                                 negate=True, inc=(vs, 1))
                    V.wait_ge(vs, s.val(vs))
                    V.tensor_scalar_mul(rstat[:, 0:8], gstat[:, 0:8], SCALE0,
                                        inc=(vs, 1))
                    C.wait_ge(vs, s.val(vs))
                    for kb in range(4):
                        C.activation(out=P[:, 512 * kb:512 * (kb + 1)],
                                     in_=ps[kb][:], func=AF.Exp,
                                     bias=rstat[:, 4:5], scale=SCALE0,
                                     inc=(sc, 1) if kb == 3 else None)
                    exp_done = s.val(sc)
                    for kb in range(4):
                        bank_done(ps[kb], sc)
                    if dbg == "p0" and hh == 0 and qi == 0:
                        GQ.wait_ge(sc, exp_done)
                        GQ.dma_start(out=dbg_out[0:128, :], in_=P[:, 0:1024],
                                     inc=(gq, 16))
                        GQ.dma_start(out=dbg_out[128:256, :], in_=P[:, 1024:2048],
                                     inc=(gq, 16))
                        V.wait_ge(gq, s.val(gq))
                    V.wait_ge(sc, exp_done)
                    V.reduce_sum(gstat[:, 8:9], P[:, :], AX.X, inc=(vs, 1))
                    V.wait_ge(vs, s.val(vs))
                    V.reciprocal(rstat[:, 8:16], gstat[:, 8:16], inc=(vs, 1))
                    V.wait_ge(vs, s.val(vs))
                    V.tensor_scalar_mul(P[:, :], P[:, :], rstat[:, 8:9],
                                        inc=(vs, 1))
                    pn_done = s.val(vs)
                    if dbg == "p0" and hh == 0 and qi == 0:
                        GQ.wait_ge(vs, pn_done)
                        GQ.dma_start(out=dbg_out[256:384, :], in_=P[:, 0:1024],
                                     inc=(gq, 16))
                        GQ.dma_start(out=dbg_out[384:512, :], in_=P[:, 1024:2048],
                                     inc=(gq, 16))
                        GQ.dma_start(out=dbg_out[512:640, 0:64], in_=gstat[:],
                                     inc=(gq, 16))
                        GQ.dma_start(out=dbg_out[640:768, 0:64], in_=rstat[:],
                                     inc=(gq, 16))
                        T.wait_ge(gq, s.val(gq))
                    if part < 3:
                        continue
                    T.wait_ge(vs, pn_done)
                    for g in range(4):
                        bank_wait(T, ps[4])
                        for i in range(4):
                            c = 4 * g + i
                            T.transpose(ps[4][:, 128 * i:128 * (i + 1)],
                                        P[:, 128 * c:128 * (c + 1)],
                                        cst["ident_f"][:],
                                        inc=(tn, 1) if i == 3 else None)
                        tr_done = s.val(tn)
                        V.wait_ge(tn, tr_done)
                        C.wait_ge(tn, tr_done)
                        oh_ap = pth_r[:, 4 * g:4 * g + 4, qt:qt + 1, :]
                        ol_ap = ptl_r[:, 4 * g:4 * g + 4, qt:qt + 1, :]
                        C.activation(out=oh_ap, in_=ps4_r, func=AF.Copy,
                                     inc=(sc, 1))
                        V.wait_ge(sc, s.val(sc))
                        V.tensor_tensor(ol_ap, ps4_r, oh_ap, ALU.subtract,
                                        inc=(vs, 1))
                        bank_done(ps[4], vs)
                if part < 4:
                    continue
                pt5 = ps[5]
                bank_wait(T, pt5)
                T.wait_ge(vs, s.val(vs))
                T.wait_ge(vs, vt_done)
                for c in range(16):
                    mm3(pt5[:], vth[:, 128 * c:128 * (c + 1)],
                        vtl[:, 128 * c:128 * (c + 1)],
                        pth[:, 512 * c:512 * (c + 1)],
                        ptl[:, 512 * c:512 * (c + 1)],
                        start=(c == 0), stop=(c == 15),
                        inc=(tn, 1) if c == 15 else None)
                pv_done = s.val(tn)
                off = SH * hh + 512 * qb
                C.wait_ge(tn, pv_done)
                C.activation(out=attn_hi[:, off:off + 512], in_=pt5[:],
                             func=AF.Copy, inc=(sc, 1))
                V.wait_ge(sc, s.val(sc))
                V.tensor_tensor(attn_lo[:, off:off + 512], pt5[:],
                                attn_hi[:, off:off + 512], ALU.subtract,
                                inc=(vs, 1))
                bank_done(pt5, vs)
        return s.val(vs)


    def emit_outproj_ln(w_img, res_src, attn_done):
        ws = WStream(w_img, NCH * 128, 0)
        xres = [A(32768, 128, 512), A(32768 + 2048, 128, 512)]
        y_full = A(36864, 128, 16384)
        sq = [A(36864 + 65536, 128, 512), A(36864 + 65536 + 2048, 128, 512)]
        T.wait_ge(vs, attn_done)
        for dc in range(NCH):
            b = ws.load(dc)
            ws.wait(T, b)
            for tb in range(NTB):
                xr = xres[tb]
                src = (x_in if res_src == "x" else r_save.ap())
                GQ.wait_ge(vs, s.val(vs))
                GQ.dma_start(out=xr[:, :],
                             in_=src[128 * dc:128 * (dc + 1),
                                     512 * tb:512 * (tb + 1)], inc=(gq, 16))
                xr_ld = s.val(gq)
                pt = ps[tb]
                bank_wait(T, pt)
                for oc in range(NCH):
                    lh, ll = ws.tiles(b, oc)
                    off = SH * oc + 512 * tb
                    mm3(pt[:], lh, ll, attn_hi[:, off:off + 512],
                        attn_lo[:, off:off + 512],
                        start=(oc == 0), stop=(oc == NCH - 1),
                        inc=(tn, 1) if oc == NCH - 1 else None)
                V.wait_ge(tn, s.val(tn))
                V.wait_ge(gq, xr_ld)
                V.tensor_tensor(y_full[:, SH * dc + 512 * tb:
                                       SH * dc + 512 * (tb + 1)], pt[:],
                                xr[:, :], ALU.add, inc=(vs, 1))
                bank_done(pt, vs)
            ws.release(b, tn)
        for tb in range(NTB):
            layer_norm_tb(
                lambda c, tb=tb: y_full[:, SH * c + 512 * tb:
                                        SH * c + 512 * (tb + 1)], tb, sq)
        return s.val(vs)

    def emit_l0_ffn():
        XTH, XTL = 0, 16384
        WSF = 32768
        BSH, BSL = 65536, 69632
        ATH, ATL = 73728, 77824
        GW0 = 81920
        XA0 = 83968
        SXH, SXL = 86016, 87040
        WGT = 88064
        YTB = 90112
        xh_t = A(XTH, 128, 16 * 512, BF16)
        xl_t = A(XTL, 128, 16 * 512, BF16)
        bsh = A(BSH, 128, 2048, BF16)
        bsl = A(BSL, 128, 2048, BF16)
        ath = A(ATH, 128, 2048, BF16)
        atl = A(ATL, 128, 2048, BF16)
        gw = A(GW0, 128, 128)
        gwh = A(GW0 + 512, 128, 128, BF16)
        gwl = A(GW0 + 768, 128, 128, BF16)
        lg = A(GW0 + 1024, 128, 8)
        eqt = A(GW0 + 1088, 128, 8)
        mkd = A(GW0 + 1152, 128, 8)
        et = A(GW0 + 1216, 128, 8)
        emt = A(GW0 + 1280, 128, 8)
        dent = A(GW0 + 1344, 128, 8)
        rect = A(GW0 + 1408, 128, 8)
        indt = A(GW0 + 1472, 128, 8)
        wgt = A(GW0 + 1536, 128, 8)
        xa_f = A(XA0, 128, 512)
        sxh = A(SXH, 128, 512, BF16)
        sxl = A(SXL, 128, 512, BF16)
        wgT_sb = A(WGT, 8, 512)
        y_tb = A(YTB, 128, 8192)
        sqs = [A(83968, 128, 512), A(86016, 128, 512)]

        SY.wait_ge(vs, s.val(vs))
        SY.wait_ge(tn, s.val(tn))
        SY.wait_ge(sc, s.val(sc))
        SY.dma_start(out=gw[:, :].rearrange("p (c e) -> p c e", c=16),
                     in_=gate_w_in.rearrange("c p e -> p c e"), inc=(ld, 16))
        V.wait_ge(ld, s.val(ld))
        split_hilo(gw[:, :], gwh[:, :], gwl[:, :], inc=(vs, 1))
        bstg = A(YTB, 128, 2048)
        SY.dma_start(out=bstg[:, :].rearrange("p (c m) -> p c m", c=16),
                     in_=bstack_img.rearrange("c p m -> p c m"), inc=(ld, 16))
        V.wait_ge(ld, s.val(ld))
        split_hilo(bstg[:, :], bsh[:, :], bsl[:, :], inc=(vs, 1))
        atg = A(YTB, 128, 2048)
        SY.wait_ge(vs, s.val(vs))
        SY.dma_start(out=atg[:, :], in_=aT_img.rearrange("o p n -> p (o n)"),
                     inc=(ld, 16))
        V.wait_ge(ld, s.val(ld))
        split_hilo(atg[:, :], ath[:, :], atl[:, :], inc=(vs, 1))

        for tb in range(NTB):
            V.wait_ge(tn, s.val(tn))
            for c in range(NCH):
                sl = slice(SH * c + 512 * tb, SH * c + 512 * (tb + 1))
                split_hilo(R[:, sl], xh_t[:, 512 * c:512 * (c + 1)],
                           xl_t[:, 512 * c:512 * (c + 1)],
                           inc=(vs, 1) if c == NCH - 1 else None)
            xsplit_done = s.val(vs)

            pg = ps[2][:, 0:8]
            for tt_ in range(4):
                bank_wait(T, ps[2])
                T.wait_ge(vs, xsplit_done)
                for c in range(NCH):
                    lhh = xh_t[:, 512 * c + 128 * tt_:512 * c + 128 * (tt_ + 1)]
                    lll = xl_t[:, 512 * c + 128 * tt_:512 * c + 128 * (tt_ + 1)]
                    mm3(pg, lhh, lll, gwh[:, 8 * c:8 * (c + 1)],
                        gwl[:, 8 * c:8 * (c + 1)],
                        start=(c == 0), stop=(c == NCH - 1),
                        inc=(tn, 1) if c == NCH - 1 else None)
                V.wait_ge(tn, s.val(tn))
                V.tensor_tensor(lg[:, :], pg, cst["gate_b"][:], ALU.add,
                                inc=(vs, 1))
                V.wait_ge(vs, s.val(vs))
                V.reduce_max(gstat[:, 0:1], lg[:, :], AX.X)
                V.reduce_max(gstat[:, 1:2], lg[:, :], AX.X, negate=True,
                             inc=(vs, 1))
                V.wait_ge(vs, s.val(vs))
                V.tensor_single_scalar(eqt[:, :], lg[:, :], gstat[:, 0:1],
                                       ALU.is_equal, inc=(vs, 1))
                V.wait_ge(vs, s.val(vs))
                V.scalar_tensor_tensor(mkd[:, :], eqt[:, :], FMIN, lg[:, :],
                                       ALU.mult, ALU.add, inc=(vs, 1))
                V.wait_ge(vs, s.val(vs))
                V.reduce_max(gstat[:, 2:3], mkd[:, :], AX.X, inc=(vs, 1))
                V.wait_ge(vs, s.val(vs))
                V.tensor_single_scalar(dent[:, :], gstat[:, 0:8],
                                       gstat[:, 1:2], ALU.add, inc=(vs, 1))
                V.wait_ge(vs, s.val(vs))
                V.tensor_single_scalar(dent[:, :], dent[:, :], 0.0, ALU.min,
                                       inc=(vs, 1))
                lgv = s.val(vs)
                C.wait_ge(vs, lgv)
                C.activation(out=et[:, :], in_=lg[:, :], func=AF.Exp,
                             bias=gstat[:, 1:2], scale=1.0)
                C.activation(out=emt[:, :], in_=dent[:, :], func=AF.Exp,
                             inc=(sc, 1))
                bank_done(ps[2], sc)
                V.wait_ge(sc, s.val(sc))
                V.tensor_scalar_add(dent[:, :], emt[:, :], 1.0, inc=(vs, 1))
                V.wait_ge(vs, s.val(vs))
                V.reciprocal(rect[:, :], dent[:, :], inc=(vs, 1))
                V.wait_ge(vs, s.val(vs))
                V.tensor_single_scalar(indt[:, :], lg[:, :], gstat[:, 2:3],
                                       ALU.is_ge, inc=(vs, 1))
                V.wait_ge(vs, s.val(vs))
                V.tensor_tensor(wgt[:, :], et[:, :], indt[:, :], ALU.mult,
                                inc=(vs, 1))
                V.wait_ge(vs, s.val(vs))
                V.tensor_single_scalar(wgt[:, :], wgt[:, :], rect[:, 2:3],
                                       ALU.mult, inc=(vs, 1))
                wg_done = s.val(vs)
                bank_wait(T, ps[3]) if tt_ == 0 else None
                T.wait_ge(vs, wg_done)
                T.transpose(ps[3][0:8, 128 * tt_:128 * (tt_ + 1)], wgt[:, :],
                            cst["ident_f"][:], inc=(tn, 1))
            V.wait_ge(tn, s.val(tn))
            V.tensor_copy(out=wgT_sb[:, :], in_=ps[3][0:8, :], inc=(vs, 1))
            bank_done(ps[3], vs)
            bank_wait(T, ps[4])
            T.wait_ge(vs, s.val(vs))
            T.matmul(ps[4][:], cst["e_expand"][:], wgT_sb[:, :], start=True,
                     stop=True, inc=(tn, 1))
            bank_wait(T, ps[5])
            for c in range(NCH):
                mm3(ps[5][:], ath[:, 128 * c:128 * (c + 1)],
                    atl[:, 128 * c:128 * (c + 1)],
                    xh_t[:, 512 * c:512 * (c + 1)],
                    xl_t[:, 512 * c:512 * (c + 1)],
                    start=(c == 0), stop=(c == NCH - 1),
                    inc=(tn, 1) if c == NCH - 1 else None)
            V.wait_ge(tn, s.val(tn))
            V.tensor_copy(out=xa_f[:, :], in_=ps[5][:])
            V.tensor_tensor(xa_f[:, :], xa_f[:, :], ps[4][:], ALU.mult)
            split_hilo(xa_f[:, :], sxh[:, :], sxl[:, :], inc=(vs, 1))
            bank_done(ps[4], vs)
            bank_done(ps[5], vs)
            sxa_done = s.val(vs)

            ws = WStream(wffn_img, NCH * 128, WSF)
            for oc in range(NCH):
                b = ws.load(oc)
                ws.wait(T, b)
                pt = ps[oc % 2]
                bank_wait(T, pt)
                T.wait_ge(vs, sxa_done)
                nolora = DEV.get("nolora", False)
                for c in range(NCH):
                    lh, ll = ws.tiles(b, c)
                    mm3(pt[:], lh, ll,
                        xh_t[:, 512 * c:512 * (c + 1)],
                        xl_t[:, 512 * c:512 * (c + 1)],
                        start=(c == 0),
                        stop=(nolora and c == NCH - 1),
                        inc=(tn, 1) if (nolora and c == NCH - 1) else None)
                if not nolora:
                    mm3(pt[:], bsh[:, 128 * oc:128 * (oc + 1)],
                        bsl[:, 128 * oc:128 * (oc + 1)], sxh[:, :], sxl[:, :],
                        start=False, stop=True, inc=(tn, 1))
                ws.release(b, tn)
                mm_done = s.val(tn)
                V.wait_ge(tn, mm_done)
                ysl = y_tb[:, 512 * oc:512 * (oc + 1)]
                V.scalar_tensor_tensor(
                    ysl, pt[:], cst["bffn"][:, oc:oc + 1],
                    R[:, SH * oc + 512 * tb:SH * oc + 512 * (tb + 1)],
                    ALU.add, ALU.add, inc=(vs, 1))
                bank_done(pt, vs)
            layer_norm_tb(
                lambda c, tb=tb: y_tb[:, 512 * c:512 * (c + 1)], tb, sqs)
        return s.val(vs)

    def emit_l1_attn():
        xh = A(XH0, 128, NCH * SH, BF16)
        xl = A(XL0, 128, NCH * SH, BF16)
        cos_e = A(CO0, 128, SH)
        sin_e = A(SI0, 128, SH)
        qtmp = A(QT0, 128, SH)
        rtmp = A(RT0, 128, SH)
        oh = A(114688, 128, SH, BF16)
        ol = A(116736, 128, SH, BF16)
        gth = A(118784, 128, 1024, BF16)
        gtl = A(120832, 128, 1024, BF16)
        kgh = A(122880, 128, 256, BF16)
        kgl = A(123392, 128, 256, BF16)
        vgh = A(123904, 128, 256, BF16)
        vgl = A(124416, 128, 256, BF16)
        cge = A(124928, 128, 64)
        sge = A(125184, 128, 64)
        gr_tmp = A(125440, 128, 64)
        gr_tmp2 = A(125696, 128, 64)

        ent_v = s.val(vs)
        SY.wait_ge(vs, ent_v)
        SY.wait_ge(tn, s.val(tn))
        SY.wait_ge(sc, s.val(sc))
        SY.dma_start(out=cos_e[:, :], in_=cos1_h, inc=(ld, 16))
        SY.dma_start(out=qtmp[:, :], in_=cos1_l, inc=(ld, 16))
        V.wait_ge(ld, s.val(ld))
        V.tensor_tensor(cos_e[:, :], cos_e[:, :], qtmp[:, :], ALU.add, inc=(vs, 1))
        SY.wait_ge(vs, s.val(vs))
        SY.dma_start(out=sin_e[:, :], in_=sin1_h, inc=(ld, 16))
        SY.dma_start(out=qtmp[:, :], in_=sin1_l, inc=(ld, 16))
        V.wait_ge(ld, s.val(ld))
        V.tensor_tensor(sin_e[:, :], sin_e[:, :], qtmp[:, :], ALU.add, inc=(vs, 1))
        SY.wait_ge(vs, s.val(vs))
        SY.dma_start(out=cge[:, :], in_=cos1g, inc=(ld, 16))
        SY.dma_start(out=gr_tmp[:, :], in_=cos1g_l, inc=(ld, 16))
        V.wait_ge(ld, s.val(ld))
        V.tensor_tensor(cge[:, :], cge[:, :], gr_tmp[:, :], ALU.add, inc=(vs, 1))
        SY.wait_ge(vs, s.val(vs))
        SY.dma_start(out=sge[:, :], in_=sin1g, inc=(ld, 16))
        SY.dma_start(out=gr_tmp[:, :], in_=sin1g_l, inc=(ld, 16))
        V.wait_ge(ld, s.val(ld))
        V.tensor_tensor(sge[:, :], sge[:, :], gr_tmp[:, :], ALU.add, inc=(vs, 1))
        SY.wait_ge(vs, s.val(vs))
        SY.dma_start(out=qtmp[:, :].rearrange("p (c g) -> p c g", c=16),
                     in_=gt_img.rearrange("c p g -> p c g"), inc=(ld, 16))
        V.wait_ge(ld, s.val(ld))
        split_hilo(qtmp[:, :], gth[:, :], gtl[:, :], inc=(vs, 1))
        for c in range(NCH):
            split_hilo(R[:, SH * c:SH * (c + 1)],
                       xh[:, SH * c:SH * (c + 1)], xl[:, SH * c:SH * (c + 1)],
                       inc=(vs, 1) if c == NCH - 1 else None)
        xs_done = s.val(vs)
        GQ.wait_ge(vs, xs_done)
        GQ.dma_start(out=r_save.ap().rearrange("(c p) t -> p c t", p=128),
                     in_=R[:].rearrange("p (c t) -> p c t", c=NCH), inc=(gq, 16))
        rspill = s.val(gq)

        q1f = q1_save.ap()
        ag1 = ag1_in.ap()
        ws_q = WStream(wq1_img, NCH * 128, WS0)
        for hh in range(H):
            done = proj_head(ws_q, hh, cos_e, sin_e, oh, ol, xh, xl, True,
                             qtmp, rtmp)
            GQ.wait_ge(vs, done)
            GQ.dma_start(out=q1f[128 * hh:128 * (hh + 1), :], in_=oh[:, :],
                         inc=(gq, 16))
            GQ.dma_start(out=q1f[128 * (16 + hh):128 * (17 + hh), :],
                         in_=ol[:, :], inc=(gq, 16))
            V.wait_ge(gq, s.val(gq))
        ws_k = WStream(wk1_img, NCH * 128, WS0)
        for g in range(NKV):
            b = ws_k.load(g)
            ws_k.wait(T, b)
            bank_wait(T, ps[2])
            for c in range(NCH):
                lh, ll = ws_k.tiles(b, c)
                mm3(ps[2][:, 0:64], lh, ll, gth[:, 64 * c:64 * (c + 1)],
                    gtl[:, 64 * c:64 * (c + 1)], start=(c == 0),
                    stop=(c == NCH - 1), inc=(tn, 1) if c == NCH - 1 else None)
            V.wait_ge(tn, s.val(tn))
            p64 = ps[2][:, 0:64]
            V.tensor_scalar_mul(gr_tmp[0:64, 0:64], p64[64:128, :], -1.0)
            V.tensor_copy(out=gr_tmp[64:128, 0:64], in_=p64[0:64, :],
                          inc=(vs, 1))
            V.wait_ge(vs, s.val(vs))
            V.tensor_tensor(gr_tmp2[:, :], p64, cge[:, :], ALU.mult)
            V.tensor_tensor(gr_tmp[:, 0:64], gr_tmp[:, 0:64], sge[:, :],
                            ALU.mult, inc=(vs, 1))
            V.wait_ge(vs, s.val(vs))
            V.tensor_tensor(gr_tmp2[:, :], gr_tmp2[:, :], gr_tmp[:, 0:64],
                            ALU.add, inc=(vs, 1))
            V.wait_ge(vs, s.val(vs))
            split_hilo(gr_tmp2[:, :], kgh[:, 64 * g:64 * (g + 1)],
                       kgl[:, 64 * g:64 * (g + 1)], inc=(vs, 1))
            bank_done(ps[2], vs)
            for tb in range(NTB):
                pt = ps[tb]
                bank_wait(T, pt)
                for c in range(NCH):
                    lh, ll = ws_k.tiles(b, c)
                    rh = xh[:, SH * c + 512 * tb: SH * c + 512 * (tb + 1)]
                    rl = xl[:, SH * c + 512 * tb: SH * c + 512 * (tb + 1)]
                    mm3(pt[:], lh, ll, rh, rl, start=(c == 0),
                        stop=(c == NCH - 1),
                        inc=(tn, 1) if c == NCH - 1 else None)
                V.wait_ge(tn, s.val(tn))
                sl = slice(512 * tb, 512 * (tb + 1))
                rope_emit(pt[:], cos_e, sin_e, rtmp, qtmp, tb)
                split_hilo(qtmp[:, sl], oh[:, sl], ol[:, sl], inc=(vs, 1))
                bank_done(pt, vs)
            ws_k.release(b, vs)
            GQ.wait_ge(vs, s.val(vs))
            GQ.dma_start(out=ag1[128 * g:128 * (g + 1), :], in_=oh[:, :],
                         inc=(gq, 16))
            GQ.dma_start(out=ag1[128 * (4 + g):128 * (5 + g), :], in_=ol[:, :],
                         inc=(gq, 16))
            V.wait_ge(gq, s.val(gq))
        ws_v = WStream(wv1_img, NCH * 128, WS0)
        for g in range(NKV):
            b = ws_v.load(g)
            ws_v.wait(T, b)
            bank_wait(T, ps[2])
            for c in range(NCH):
                lh, ll = ws_v.tiles(b, c)
                mm3(ps[2][:, 0:64], lh, ll, gth[:, 64 * c:64 * (c + 1)],
                    gtl[:, 64 * c:64 * (c + 1)], start=(c == 0),
                    stop=(c == NCH - 1), inc=(tn, 1) if c == NCH - 1 else None)
            V.wait_ge(tn, s.val(tn))
            split_hilo(ps[2][:, 0:64], vgh[:, 64 * g:64 * (g + 1)],
                       vgl[:, 64 * g:64 * (g + 1)], inc=(vs, 1))
            bank_done(ps[2], vs)
            for tb in range(NTB):
                pt = ps[tb]
                bank_wait(T, pt)
                for c in range(NCH):
                    lh, ll = ws_v.tiles(b, c)
                    rh = xh[:, SH * c + 512 * tb: SH * c + 512 * (tb + 1)]
                    rl = xl[:, SH * c + 512 * tb: SH * c + 512 * (tb + 1)]
                    mm3(pt[:], lh, ll, rh, rl, start=(c == 0),
                        stop=(c == NCH - 1),
                        inc=(tn, 1) if c == NCH - 1 else None)
                V.wait_ge(tn, s.val(tn))
                sl = slice(512 * tb, 512 * (tb + 1))
                split_hilo(pt[:], oh[:, sl], ol[:, sl], inc=(vs, 1))
                bank_done(pt, vs)
            ws_v.release(b, vs)
            done = s.val(vs)
            GQ.wait_ge(vs, done)
            GQ.dma_start(out=ag1[128 * (8 + g):128 * (9 + g), :], in_=oh[:, :],
                         inc=(gq, 16))
            GQ.dma_start(out=ag1[128 * (12 + g):128 * (13 + g), :],
                         in_=ol[:, :], inc=(gq, 16))
            V.wait_ge(gq, s.val(gq))
        if dbg == "kg":
            V.tensor_copy(out=qtmp[:, 0:256], in_=kgh[:, :])
            V.tensor_copy(out=qtmp[:, 256:512], in_=kgl[:, :])
            V.tensor_copy(out=qtmp[:, 512:768], in_=vgh[:, :])
            V.tensor_copy(out=qtmp[:, 768:1024], in_=vgl[:, :], inc=(vs, 1))
            GQ.wait_ge(vs, s.val(vs))
            GQ.dma_start(out=dbg_out[0:128, :], in_=qtmp[:, :], inc=(gq, 16))
        GQ.wait_ge(gq, s.val(gq))
        GQ.collective_compute("AllGather", ALU.bypass, replica_groups=PAIRS,
                              ins=[ag1_in.ap().opt()],
                              outs=[ag1_out.ap().opt()], inc=(cc, 1))
        ag_ready = s.val(cc)

        k1h = A(0, 128, 4 * E1P, BF16)
        k1l = A(17408, 128, 4 * E1P, BF16)
        v1h = A(34816, 128, E1P, BF16)
        v1l = A(39168, 128, E1P, BF16)
        vt1h = A(43520, 128, 4 * E1P, BF16)
        vt1l = A(60928, 128, 4 * E1P, BF16)
        maskb = A(78336, 128, E1P, BF16)
        jr = A(82688, 128, E1P)
        jb = A(91392, 128, E1P)
        P1 = A(100096, 128, E1P)
        pth = A(108800, 128, E1P, BF16)
        ptl = A(113152, 128, E1P, BF16)
        qh = A(117504, 128, 128, BF16)
        ql = A(117504 + 256, 128, 128, BF16)
        tmp2 = A(118016, 128, E1P)
        ag1o = ag1_out.ap()
        SY.wait_ge(cc, ag_ready)
        GQ.wait_ge(cc, ag_ready)
        SY.dma_start(out=jr[:, :], in_=jrow_in, inc=(ld, 16))
        SY.dma_start(out=jb[:, :], in_=jbig_in, inc=(ld, 16))
        jld = s.val(ld)
        V.wait_ge(ld, jld)
        for g in range(NKV):
            for (dst, srow, gsb) in ((k1h, g, kgh), (k1l, 4 + g, kgl)):
                base = E1P * g
                V.tensor_copy(out=dst[:, base:base + 64],
                              in_=gsb[:, 64 * g:64 * (g + 1)])
                V.memset(dst[:, base + 2112:base + 2176], 0.0, inc=(vs, 1))
                SY.wait_ge(vs, s.val(vs))
                for r in range(2):
                    SY.dma_start(
                        out=dst[:, base + 64 + 1024 * r:base + 64 + 1024 * (r + 1)],
                        in_=ag1o[2048 * r + 128 * srow:2048 * r + 128 * (srow + 1), :],
                        inc=(ld, 16))
            prev = s.val(vs)
            SY.wait_ge(vs, prev)
            for (dst, srow, gsb) in ((v1h, 8 + g, vgh), (v1l, 12 + g, vgl)):
                V.tensor_copy(out=dst[:, 0:64], in_=gsb[:, 64 * g:64 * (g + 1)])
                V.memset(dst[:, 2112:2176], 0.0, inc=(vs, 1))
                SY.wait_ge(vs, s.val(vs))
                for r in range(2):
                    SY.dma_start(
                        out=dst[:, 64 + 1024 * r:64 + 1024 * (r + 1)],
                        in_=ag1o[2048 * r + 128 * srow:2048 * r + 128 * (srow + 1), :],
                        inc=(ld, 16))
            v_ld = s.val(ld)
            T.wait_ge(ld, v_ld)
            T.wait_ge(vs, s.val(vs))
            ps_tb = ps[5][:, 0:256].bitcast(BF16)
            for (src_t, dst_t) in ((v1h, vt1h), (v1l, vt1l)):
                for grp in range(5):
                    n_in = 4 if grp < 4 else 1
                    bank_wait(T, ps[5])
                    for i in range(n_in):
                        c = 4 * grp + i
                        T.transpose(ps_tb[:, 128 * i:128 * (i + 1)],
                                    src_t[:, 128 * c:128 * (c + 1)],
                                    cst["ident_b"][:],
                                    inc=(tn, 1) if i == n_in - 1 else None)
                    V.wait_ge(tn, s.val(tn))
                    V.tensor_copy(
                        out=dst_t[:, E1P * g + 512 * grp:
                                  E1P * g + 512 * grp + 128 * n_in],
                        in_=ps_tb[:, 0:128 * n_in], inc=(vs, 1))
                    bank_done(ps[5], vs)
        kv_done = s.val(vs)
        kv_ld = s.val(ld)

        qbase_col = cst["arange_col"]
        for qt in range(8):
            V.wait_ge(ld, jld)
            V.tensor_scalar_add(gstat[:, 32:40], qbase_col[:], float(128 * qt),
                                inc=(vs, 1))
            V.wait_ge(vs, s.val(vs))
            V.tensor_scalar_add(gstat[:, 40:48], gstat[:, 32:40], -512.0,
                                inc=(vs, 1))
            V.wait_ge(vs, s.val(vs))
            V.tensor_single_scalar(P1[:, :], jr[:, :], gstat[:, 32:33],
                                   ALU.is_le)
            V.tensor_single_scalar(tmp2[:, :], jb[:, :], gstat[:, 40:41],
                                   ALU.is_ge)
            V.tensor_tensor(P1[:, :], P1[:, :], tmp2[:, :], ALU.mult)
            V.tensor_scalar(maskb[:, :], P1[:, :], MASKV, -MASKV, ALU.mult,
                            ALU.add, inc=(vs, 1))
            mask_done = s.val(vs)
            for hq in range(H):
                g = hq // 4
                kb_ = E1P * g
                SY.wait_ge(vs, s.val(vs))
                SY.dma_start(out=qh[:, :],
                             in_=q1f[128 * hq:128 * (hq + 1),
                                     128 * qt:128 * (qt + 1)], inc=(ld, 16))
                SY.dma_start(out=ql[:, :],
                             in_=q1f[128 * (16 + hq):128 * (17 + hq),
                                     128 * qt:128 * (qt + 1)], inc=(ld, 16))
                q_ld = s.val(ld)
                T.wait_ge(ld, q_ld)
                T.wait_ge(vs, mask_done)
                for kb in range(5):
                    n = 512 if kb < 4 else 128
                    pt = ps[kb]
                    bank_wait(T, pt)
                    mm3(pt[:, 0:n], qh[:, :], ql[:, :],
                        k1h[:, kb_ + 512 * kb:kb_ + 512 * kb + n],
                        k1l[:, kb_ + 512 * kb:kb_ + 512 * kb + n],
                        start=True, stop=True,
                        inc=(tn, 1) if kb == 4 else None)
                V.wait_ge(tn, s.val(tn))
                for kb in range(5):
                    n = 512 if kb < 4 else 128
                    V.tensor_tensor(P1[:, 512 * kb:512 * kb + n],
                                    ps[kb][:, 0:n],
                                    maskb[:, 512 * kb:512 * kb + n], ALU.add,
                                    inc=(vs, 1) if kb == 4 else None)
                for kb in range(5):
                    bank_done(ps[kb], vs)
                V.wait_ge(vs, s.val(vs))
                V.reduce_max(gstat[:, 4:5], P1[:, :], AX.X, negate=True,
                             inc=(vs, 1))
                V.wait_ge(vs, s.val(vs))
                V.tensor_scalar_mul(rstat[:, 0:8], gstat[:, 0:8], SCALE1,
                                    inc=(vs, 1))
                C.wait_ge(vs, s.val(vs))
                C.activation(out=P1[:, :], in_=P1[:, :], func=AF.Exp,
                             bias=rstat[:, 4:5], scale=SCALE1, inc=(sc, 1))
                V.wait_ge(sc, s.val(sc))
                V.reduce_sum(gstat[:, 8:9], P1[:, :], AX.X, inc=(vs, 1))
                V.wait_ge(vs, s.val(vs))
                V.reciprocal(rstat[:, 8:16], gstat[:, 8:16], inc=(vs, 1))
                V.wait_ge(vs, s.val(vs))
                V.tensor_scalar_mul(P1[:, :], P1[:, :], rstat[:, 8:9],
                                    inc=(vs, 1))
                pn_done = s.val(vs)
                T.wait_ge(vs, pn_done)
                ps4_r = ps[4][:].rearrange("p (i q) -> p i q", i=4)
                for grp in range(5):
                    n_in = 4 if grp < 4 else 1
                    bank_wait(T, ps[4])
                    for i in range(n_in):
                        c = 4 * grp + i
                        T.transpose(ps[4][:, 128 * i:128 * (i + 1)],
                                    P1[:, 128 * c:128 * (c + 1)],
                                    cst["ident_f"][:],
                                    inc=(tn, 1) if i == n_in - 1 else None)
                    tr_done = s.val(tn)
                    V.wait_ge(tn, tr_done)
                    C.wait_ge(tn, tr_done)
                    oh_ap = pth[:, 512 * grp:512 * grp + 128 * n_in]
                    ol_ap = ptl[:, 512 * grp:512 * grp + 128 * n_in]
                    C.activation(out=oh_ap, in_=ps[4][:, 0:128 * n_in],
                                 func=AF.Copy, inc=(sc, 1))
                    V.wait_ge(sc, s.val(sc))
                    V.tensor_tensor(ol_ap, ps[4][:, 0:128 * n_in], oh_ap,
                                    ALU.subtract, inc=(vs, 1))
                    bank_done(ps[4], vs)
                pt0 = ps[0]
                bank_wait(T, pt0)
                T.wait_ge(vs, s.val(vs))
                T.wait_ge(vs, kv_done)
                for c in range(KC1):
                    mm3(pt0[:, 0:128],
                        vt1h[:, kb_ + 128 * c:kb_ + 128 * (c + 1)],
                        vt1l[:, kb_ + 128 * c:kb_ + 128 * (c + 1)],
                        pth[:, 128 * c:128 * (c + 1)],
                        ptl[:, 128 * c:128 * (c + 1)],
                        start=(c == 0), stop=(c == KC1 - 1),
                        inc=(tn, 1) if c == KC1 - 1 else None)
                pv_done = s.val(tn)
                off = SH * hq + 128 * qt
                C.wait_ge(tn, pv_done)
                C.activation(out=attn_hi[:, off:off + 128], in_=pt0[:, 0:128],
                             func=AF.Copy, inc=(sc, 1))
                V.wait_ge(sc, s.val(sc))
                V.tensor_tensor(attn_lo[:, off:off + 128], pt0[:, 0:128],
                                attn_hi[:, off:off + 128], ALU.subtract,
                                inc=(vs, 1))
                bank_done(pt0, vs)
        return s.val(vs)

    def quant_pipe(src_ap, out_ap, sinv_col, w, qa, qb, qt3, qt4, qt5,
                   signed=True):
        if signed:
            V.tensor_single_scalar(qa[:, 0:w], src_ap, sinv_col, ALU.mult)
            V.tensor_scalar_mul(qb[:, 0:w], qa[:, 0:w], -1.0)
            V.tensor_tensor(qb[:, 0:w], qb[:, 0:w], qa[:, 0:w], ALU.max)
        else:
            V.tensor_single_scalar(qb[:, 0:w], src_ap, sinv_col, ALU.mult)
        V.tensor_single_scalar(qt3[:, 0:w], qb[:, 0:w], 2.0 ** -6, ALU.max,
                               inc=(vs, 1))
        C.wait_ge(vs, s.val(vs))
        C.activation(out=qt3[:, 0:w], in_=qt3[:, 0:w], func=AF.Ln, inc=(sc, 1))
        V.wait_ge(sc, s.val(sc))
        V.tensor_scalar(qt3[:, 0:w], qt3[:, 0:w], LN2_INV, 63.5, ALU.mult,
                        ALU.add)
        V.tensor_copy(out=qt4[:, 0:w], in_=qt3[:, 0:w])
        V.tensor_copy(out=qt3[:, 0:w], in_=qt4[:, 0:w])
        V.tensor_scalar(qt3[:, 0:w], qt3[:, 0:w], -1.0, 194.0, ALU.mult,
                        ALU.add)
        V.tensor_copy(out=qt5[:, 0:w], in_=qt3[:, 0:w])
        V.tensor_single_scalar(qt5[:, 0:w], qt5[:, 0:w], 23,
                               ALU.arith_shift_left)
        V.tensor_scalar_add(qt4[:, 0:w], qt4[:, 0:w], 60)
        V.tensor_single_scalar(qt4[:, 0:w], qt4[:, 0:w], 23,
                               ALU.arith_shift_left)
        V.tensor_tensor(qb[:, 0:w], qb[:, 0:w], qt5[:, 0:w].bitcast(F32),
                        ALU.mult)
        V.tensor_copy(out=qt5[:, 0:w], in_=qb[:, 0:w])
        V.tensor_copy(out=qb[:, 0:w], in_=qt5[:, 0:w])
        V.tensor_tensor(qb[:, 0:w], qb[:, 0:w], qt4[:, 0:w].bitcast(F32),
                        ALU.mult)
        if signed:
            V.tensor_scalar(qa[:, 0:w], qa[:, 0:w], 0.0, 2.0, ALU.is_ge,
                            ALU.mult)
            V.tensor_scalar_add(qa[:, 0:w], qa[:, 0:w], -1.0)
            V.tensor_tensor(out_ap, qb[:, 0:w], qa[:, 0:w], ALU.mult,
                            inc=(vs, 1))
        else:
            V.tensor_copy(out=out_ap, in_=qb[:, 0:w], inc=(vs, 1))

    def pmax_gather(col_src_ap, grow_sl, srow_col, tag):
        GQ.wait_ge(vs, s.val(vs))
        GQ.dma_start(out=gcol.ap()[:, 0:1], in_=col_src_ap, inc=(gq, 16))
        GQ.wait_ge(gq, s.val(gq))
        GQ.dma_start(out=grow[0:1, grow_sl],
                     in_=gcol.ap()[:, 0:1].rearrange("p o -> o p"),
                     inc=(gq, 16))
        V.wait_ge(gq, s.val(gq))
        V.reduce_max(grow[0:1, srow_col:srow_col + 1],
                     grow[0:1, grow_sl], AX.X, inc=(vs, 1))
        V.wait_ge(vs, s.val(vs))

    def emit_l1_ffn():
        X80 = 0
        WF0 = 32768
        W80 = 40960
        QA0, QB0, QT3 = 49152, 53248, 57344
        QT4, QT5 = 61440, 65536
        HS0 = 69632
        H20 = 70144
        H80 = 74240
        WP0 = 76288
        x8 = A(X80, 128, NCH * SH, BF16)
        wf = [A(WF0, 128, 1024), A(WF0 + 4096, 128, 1024)]
        w8b = [A(W80, 128, 2048, BF16), A(W80 + 4096, 128, 2048, BF16)]
        qa = A(QA0, 128, 1024)
        qb = A(QB0, 128, 1024)
        qt3 = A(QT3, 128, 1024)
        qt4 = A(QT4, 128, 1024, I32)
        qt5 = A(QT5, 128, 1024, I32)
        hstat = A(HS0, 128, 128)
        h2s = [A(H20, 128, 512), A(H20 + 2048, 128, 512)]
        h8b = [A(H80, 128, 512, BF16), A(H80 + 1024, 128, 512, BF16)]
        wpb = [A(WP0, 128, 512, BF16), A(WP0 + 1024, 128, 512, BF16)]
        agwo = agw_out.ap()
        agpo = agp_out.ap()
        h2sv = h2_save.ap()
        h8sv = h8_save.ap()

        ent_v = s.val(vs)
        SY.wait_ge(vs, ent_v)
        SY.wait_ge(tn, s.val(tn))
        SY.wait_ge(sc, s.val(sc))
        V.memset(grow[:], 0.0)
        V.memset(hstat[:, :], 0.0, inc=(vs, 1))

        for c in range(NCH):
            V.reduce_max(hstat[:, 24 + c:25 + c], R[:, SH * c:SH * (c + 1)],
                         AX.X, apply_absolute_value=True,
                         inc=(vs, 1) if c == NCH - 1 else None)
        V.wait_ge(vs, s.val(vs))
        V.reduce_max(hstat[:, 40:41], hstat[:, 24:40], AX.X, inc=(vs, 1))
        pmax_gather(hstat[:, 40:41], slice(264, 392), 2, "x")
        for j in range(8):
            for hf in range(2):
                b = (2 * j + hf) % 2
                SY.wait_ge(vs, s.val(vs))
                SY.dma_start(out=wf[b][:, :],
                             in_=wfc_shard[j:j + 1, :, 1024 * hf:1024 * (hf + 1)]
                             .rearrange("o p n -> p (o n)"), inc=(wl, 16))
                V.wait_ge(wl, s.val(wl))
                V.reduce_max(hstat[:, 2 * j + hf:2 * j + hf + 1], wf[b][:, :],
                             AX.X, apply_absolute_value=True, inc=(vs, 1))
        V.wait_ge(vs, s.val(vs))
        V.reduce_max(hstat[:, 41:42], hstat[:, 0:16], AX.X, inc=(vs, 1))
        pmax_gather(hstat[:, 41:42], slice(8, 136), 0, "wfc")
        for j in range(2):
            for pc in range(8):
                b = (8 * j + pc) % 2
                SY.wait_ge(vs, s.val(vs))
                SY.dma_start(out=wf[b][:, :],
                             in_=wproj_shard[j:j + 1, :,
                                             1024 * pc:1024 * (pc + 1)]
                             .rearrange("o p n -> p (o n)"), inc=(wl, 16))
                V.wait_ge(wl, s.val(wl))
                V.reduce_max(hstat[:, 48 + 8 * j + pc:49 + 8 * j + pc],
                             wf[b][:, :], AX.X, apply_absolute_value=True,
                             inc=(vs, 1))
        V.wait_ge(vs, s.val(vs))
        V.reduce_max(hstat[:, 42:43], hstat[:, 48:64], AX.X, inc=(vs, 1))
        pmax_gather(hstat[:, 42:43], slice(136, 264), 1, "wproj")
        GQ.wait_ge(vs, s.val(vs))
        GQ.dma_start(out=ar_in.ap(), in_=grow[0:1, 0:8], inc=(gq, 16))
        GQ.wait_ge(gq, s.val(gq))
        GQ.collective_compute("AllReduce", ALU.max, replica_groups=ALL8,
                              ins=[ar_in.ap().opt()], outs=[ar_out.ap().opt()],
                              inc=(cc, 1))
        GQ.wait_ge(cc, s.val(cc))
        GQ.dma_start(out=grow[0:1, 392:400], in_=ar_out.ap(), inc=(gq, 16))
        bank_wait(T, ps[4])
        T.wait_ge(gq, s.val(gq))
        T.matmul(ps[4][:, 0:8], cst["ones_row"][:], grow[0:1, 392:400],
                 start=True, stop=True, inc=(tn, 1))
        V.wait_ge(tn, s.val(tn))
        V.tensor_copy(out=sstat[:, 0:8], in_=ps[4][:, 0:8], inc=(vs, 1))
        bank_done(ps[4], vs)
        V.wait_ge(vs, s.val(vs))
        V.tensor_scalar(sstat[:, 8:16], sstat[:, 0:8], 1.0 / FP8_MAX, 1e-6,
                        ALU.mult, ALU.max, inc=(vs, 1))
        V.wait_ge(vs, s.val(vs))
        V.reciprocal(rstat[:, 16:24], sstat[:, 8:16], inc=(vs, 1))
        V.wait_ge(vs, s.val(vs))
        V.tensor_single_scalar(gstat[:, 48:56], sstat[:, 8:16],
                               sstat[:, 8:9], ALU.mult, inc=(vs, 1))
        V.wait_ge(vs, s.val(vs))
        agwi = agw_in.ap()
        for j in range(8):
            for hf in range(2):
                b = (2 * j + hf) % 2
                SY.wait_ge(vs, s.val(vs))
                SY.dma_start(out=wf[b][:, :],
                             in_=wfc_shard[j:j + 1, :, 1024 * hf:1024 * (hf + 1)]
                             .rearrange("o p n -> p (o n)"), inc=(wl, 16))
                V.wait_ge(wl, s.val(wl))
                quant_pipe(wf[b][:, :], w8b[0][:, 1024 * hf:1024 * (hf + 1)],
                           rstat[:, 16:17], 1024, qa, qb, qt3, qt4, qt5)
            GQ.wait_ge(vs, s.val(vs))
            GQ.dma_start(out=agwi[128 * j:128 * (j + 1), :], in_=w8b[0][:, :],
                         inc=(gq, 16))
            V.wait_ge(gq, s.val(gq))
        agpi = agp_in.ap()
        for j in range(2):
            for pc in range(8):
                b = pc % 2
                SY.wait_ge(vs, s.val(vs))
                SY.dma_start(out=wf[b][:, :],
                             in_=wproj_shard[j:j + 1, :,
                                             1024 * pc:1024 * (pc + 1)]
                             .rearrange("o p n -> p (o n)"), inc=(wl, 16))
                V.wait_ge(wl, s.val(wl))
                quant_pipe(wf[b][:, :], w8b[pc % 2][:, 0:1024],
                           rstat[:, 17:18], 1024, qa, qb, qt3, qt4, qt5)
                GQ.wait_ge(vs, s.val(vs))
                GQ.dma_start(out=agpi[128 * j:128 * (j + 1),
                                      1024 * pc:1024 * (pc + 1)],
                             in_=w8b[pc % 2][:, 0:1024], inc=(gq, 16))
                V.wait_ge(gq, s.val(gq))
        for c in range(NCH):
            quant_pipe(R[:, SH * c:SH * (c + 1)],
                       x8[:, SH * c:SH * (c + 1)], rstat[:, 18:19], 1024,
                       qa, qb, qt3, qt4, qt5)
        x8_done = s.val(vs)
        GQ.wait_ge(gq, s.val(gq))
        GQ.collective_compute("AllGather", ALU.bypass, replica_groups=ALL8,
                              ins=[agw_in.ap().opt()], outs=[agw_out.ap().opt()],
                              inc=(cc, 1))
        GQ.collective_compute("AllGather", ALU.bypass, replica_groups=ALL8,
                              ins=[agp_in.ap().opt()], outs=[agp_out.ap().opt()],
                              inc=(cc, 1))
        wg_ready = s.val(cc)
        SY.wait_ge(cc, wg_ready)

        w8_loaded = [0, 0]
        w8_freed = [(tn, s.val(tn)), (tn, s.val(tn))]
        for oc in range(64):
            b = oc % 2
            SY.wait_ge(*w8_freed[b])
            r, lj = oc // 8, oc % 8
            SY.dma_start(out=w8b[b][:, :],
                         in_=agwo[1024 * r + 128 * lj:1024 * r + 128 * (lj + 1), :],
                         inc=(wl, 16))
            w8_loaded[b] = s.val(wl)
            for tb in range(NTB):
                pt = ps[2 + tb]
                bank_wait(T, pt)
                T.wait_ge(wl, w8_loaded[b])
                T.wait_ge(vs, x8_done)
                for c in range(NCH):
                    T.matmul(pt[:], w8b[b][:, 128 * c:128 * (c + 1)],
                             x8[:, SH * c + 512 * tb:SH * c + 512 * (tb + 1)],
                             start=(c == 0), stop=(c == NCH - 1),
                             inc=(tn, 1) if c == NCH - 1 else None)
                w8_freed[b] = (tn, s.val(tn))
                hb = h2s[tb]
                C.wait_ge(tn, s.val(tn))
                C.wait_ge(gq, s.val(gq))
                C.activation(out=hb[:, :], in_=pt[:], func=AF.Relu,
                             scale=gstat[:, 50:51])
                C.activation(out=hb[:, :], in_=hb[:, :], func=AF.Square,
                             inc=(sc, 1))
                sq_done = s.val(sc)
                bank_done(pt, sc)
                V.wait_ge(sc, sq_done)
                V.reduce_max(hstat[:, 2 * oc + tb:2 * oc + tb + 1],
                             hb[:, :], AX.X, inc=(vs, 1))
                GQ.wait_ge(vs, s.val(vs))
                GQ.dma_start(out=h2sv[128 * oc:128 * (oc + 1),
                                      512 * tb:512 * (tb + 1)],
                             in_=hb[:, :], inc=(gq, 16))
        V.wait_ge(vs, s.val(vs))
        V.reduce_max(hstat[:, 40:41], hstat[:, 0:128], AX.X, inc=(vs, 1))
        V.wait_ge(vs, s.val(vs))
        V.memset(grow[0:1, 0:8], 0.0, inc=(vs, 1))
        pmax_gather(hstat[:, 40:41], slice(8, 136), 0, "h")
        GQ.wait_ge(vs, s.val(vs))
        GQ.dma_start(out=ar2_in.ap(), in_=grow[0:1, 0:8], inc=(gq, 16))
        GQ.wait_ge(gq, s.val(gq))
        GQ.collective_compute("AllReduce", ALU.max, replica_groups=ALL8,
                              ins=[ar2_in.ap().opt()], outs=[ar2_out.ap().opt()],
                              inc=(cc, 1))
        GQ.wait_ge(cc, s.val(cc))
        GQ.dma_start(out=grow[0:1, 392:400], in_=ar2_out.ap(), inc=(gq, 16))
        bank_wait(T, ps[4])
        T.wait_ge(gq, s.val(gq))
        T.matmul(ps[4][:, 0:8], cst["ones_row"][:], grow[0:1, 392:400],
                 start=True, stop=True, inc=(tn, 1))
        V.wait_ge(tn, s.val(tn))
        V.tensor_copy(out=sstat[:, 0:8], in_=ps[4][:, 0:8], inc=(vs, 1))
        bank_done(ps[4], vs)
        V.wait_ge(vs, s.val(vs))
        V.tensor_scalar(sstat[:, 0:8], sstat[:, 0:8], 1.0 / FP8_MAX, 1e-6,
                        ALU.mult, ALU.max, inc=(vs, 1))
        V.wait_ge(vs, s.val(vs))
        V.reciprocal(rstat[:, 24:32], sstat[:, 0:8], inc=(vs, 1))
        V.wait_ge(vs, s.val(vs))
        V.tensor_single_scalar(gstat[:, 56:64], sstat[:, 8:16],
                               sstat[:, 0:1], ALU.mult, inc=(vs, 1))
        V.wait_ge(vs, s.val(vs))
        for oc in range(64):
            for tb in range(NTB):
                b = tb
                SY.wait_ge(vs, s.val(vs))
                SY.dma_start(out=h2s[b][:, :],
                             in_=h2sv[128 * oc:128 * (oc + 1),
                                      512 * tb:512 * (tb + 1)], inc=(wl, 16))
                V.wait_ge(wl, s.val(wl))
                quant_pipe(h2s[b][:, :], h8b[b][:, :], rstat[:, 24:25], 512,
                           qa, qb, qt3, qt4, qt5, signed=False)
                GQ.wait_ge(vs, s.val(vs))
                GQ.dma_start(out=h8sv[128 * oc:128 * (oc + 1),
                                      512 * tb:512 * (tb + 1)],
                             in_=h8b[b][:, :], inc=(gq, 16))
                V.wait_ge(gq, s.val(gq))
        y_full = A(0, 128, 16384)
        SY.wait_ge(gq, s.val(gq))
        for tb in range(NTB):
            for dcg in range(4):
                for i in range(4):
                    bank_wait(T, ps[i])
                buf_free = [(tn, s.val(tn)), (tn, s.val(tn))]
                for oc in range(64):
                    b = oc % 2
                    SY.wait_ge(*buf_free[b])
                    SY.dma_start(out=h8b[b][:, :],
                                 in_=h8sv[128 * oc:128 * (oc + 1),
                                          512 * tb:512 * (tb + 1)],
                                 inc=(wl, 16))
                    for i in range(4):
                        dc = 4 * dcg + i
                        SY.dma_start(out=wpb[b][:, 128 * i:128 * (i + 1)],
                                     in_=agpo[128 * dc:128 * (dc + 1),
                                              128 * oc:128 * (oc + 1)],
                                     inc=(wl, 16))
                    ld_t = s.val(wl)
                    T.wait_ge(wl, ld_t)
                    for i in range(4):
                        T.matmul(ps[i][:],
                                 wpb[b][:, 128 * i:128 * (i + 1)],
                                 h8b[b][:, :], start=(oc == 0),
                                 stop=(oc == 63),
                                 inc=(tn, 1) if i == 3 else None)
                    buf_free[b] = (tn, s.val(tn))
                V.wait_ge(tn, s.val(tn))
                for i in range(4):
                    dc = 4 * dcg + i
                    ysl = y_full[:, SH * dc + 512 * tb:SH * dc + 512 * (tb + 1)]
                    V.scalar_tensor_tensor(
                        ysl, ps[i][:], gstat[:, 57:58],
                        R[:, SH * dc + 512 * tb:SH * dc + 512 * (tb + 1)],
                        ALU.mult, ALU.add, inc=(vs, 1))
                    bank_done(ps[i], vs)
        sqf = [h2s[0], h2s[1]]
        for tb in range(NTB):
            layer_norm_tb(
                lambda c, tb=tb: y_full[:, SH * c + 512 * tb:
                                        SH * c + 512 * (tb + 1)], tb, sqf)
        return s.val(vs)

    r_ready = s.val(vs)
    if "l0proj" in phases:
        agr = emit_l0_proj()
        if dbg == "ag0":
            GQ.wait_ge(cc, agr)
            GQ.dma_start(out=dbg_out, in_=ag0_out.ap(), inc=(gq, 16))
    if "l0core" in phases:
        agr = emit_l0_proj()
        attn_done = emit_l0_core(agr)
        if dbg == "attn_raw":
            GQ.wait_ge(vs, attn_done)
            GQ.wait_ge(sc, s.val(sc))
            GQ.dma_start(out=dbg_out.rearrange("(c p) t -> p c t", p=128),
                         in_=R[:].rearrange("p (c t) -> p c t", c=NCH),
                         inc=(gq, 16))
    if "l0attn" in phases:
        agr = emit_l0_proj()
        attn_done = emit_l0_core(agr)
        if dbg == "attn_raw":
            GQ.wait_ge(vs, attn_done)
            GQ.wait_ge(sc, s.val(sc))
            GQ.dma_start(out=dbg_out.rearrange("(c p) t -> p c t", p=128),
                         in_=R[:].rearrange("p (c t) -> p c t", c=NCH),
                         inc=(gq, 16))
        else:
            r_ready = emit_outproj_ln(wout_img, "x", attn_done)

    if "l0ffn" in phases:
        r_ready = emit_l0_ffn()
    if "l1attn" in phases:
        attn1_done = emit_l1_attn()
        r_ready = emit_outproj_ln(wo1_img, "rsave", attn1_done)
    if "l1ffn" in phases:
        r_ready = emit_l1_ffn()
    if dbg == "ffnstats":
        GQ.wait_ge(vs, s.val(vs))
        GQ.wait_ge(sc, s.val(sc))
        GQ.dma_start(out=dbg_out[0:128, 0:16], in_=sstat[:], inc=(gq, 16))
        GQ.dma_start(out=dbg_out[128:256, 0:64], in_=rstat[:], inc=(gq, 16))
        GQ.dma_start(out=dbg_out[256:384, 0:64], in_=gstat[:], inc=(gq, 16))
        GQ.dma_start(out=dbg_out[384:385, 0:1024], in_=grow[0:1, 0:1024],
                     inc=(gq, 16))
        GQ.dma_start(out=dbg_out[512:640, :], in_=h2_save.ap()[0:128, :],
                     inc=(gq, 16))
        GQ.dma_start(out=dbg_out[640:768, :], in_=h2_save.ap()[1024:1152, :],
                     inc=(gq, 16))
    if dbg == "r_l1attn":
        GQ.wait_ge(vs, r_ready)
        GQ.dma_start(out=dbg_out.rearrange("(c p) t -> p c t", p=128),
                     in_=R[:].rearrange("p (c t) -> p c t", c=NCH), inc=(gq, 16))
    if dbg == "r_l0ffn":
        GQ.wait_ge(vs, r_ready)
        GQ.dma_start(out=dbg_out.rearrange("(c p) t -> p c t", p=128),
                     in_=R[:].rearrange("p (c t) -> p c t", c=NCH), inc=(gq, 16))
    if dbg == "r_l0attn":
        GQ.wait_ge(vs, r_ready)
        GQ.dma_start(out=dbg_out.rearrange("(c p) t -> p c t", p=128),
                     in_=R[:].rearrange("p (c t) -> p c t", c=NCH), inc=(gq, 16))

    GQ.wait_ge(vs, s.val(vs))
    GQ.wait_ge(sc, s.val(sc))
    GQ.wait_ge(tn, s.val(tn))
    GQ.dma_start(out=y_out.rearrange("(c p) t -> p c t", p=128),
                 in_=R[:].rearrange("p (c t) -> p c t", c=NCH), inc=(gq, 16))
    GQ.wait_ge(gq, s.val(gq))

    with nc.Block() as block:
        @block.sync
        def _(h):
            for fn in s.q["sync"]:
                fn(h)

        @block.tensor
        def _(h):
            for fn in s.q["tensor"]:
                fn(h)

        @block.vector
        def _(h):
            for fn in s.q["vector"]:
                fn(h)

        @block.scalar
        def _(h):
            for fn in s.q["scalar"]:
                fn(h)

        @block.gpsimd
        def _(h):
            for fn in s.q["gpsimd"]:
                fn(h)

    ctx.close()
    return nc


def make_inputs(core, inputs):
    b, hh = core // 2, core % 2
    tsl = slice(SH * hh, SH * (hh + 1))
    x = np.asarray(inputs["L0_attn_x"])
    d = {}
    d["x_fm"] = np.ascontiguousarray(x[b].T[:, tsl])

    wqkvT = np.ascontiguousarray(np.asarray(inputs["L0_attn_w_qkv"]).T)
    d["q0_img"] = tile_lhsT(wqkvT[:, 0:2048])
    d["k0_img"] = tile_lhsT(wqkvT[:, 2048:4096])
    d["v0_img"] = tile_lhsT(wqkvT[:, 4096:6144])
    d["wout_img"] = tile_lhsT(np.ascontiguousarray(
        np.asarray(inputs["L0_attn_w_out"]).T))

    pos0 = np.arange(SH * hh, SH * (hh + 1))
    (ch, cl), (sh_, sl_) = rope_tables(pos0, HD, 10000.0)
    d["cos0_hi"], d["cos0_lo"] = ch, cl
    d["sin0_hi"], d["sin0_lo"] = sh_, sl_

    d["wffn_img"] = tile_lhsT(np.ascontiguousarray(np.asarray(inputs["L0_ffn_W"]).T))
    A_ = np.asarray(inputs["L0_ffn_lora_A"])
    d["aT_img"] = tile_lhsT(np.ascontiguousarray(
        A_.transpose(2, 0, 1).reshape(2048, 128)))
    B_ = np.asarray(inputs["L0_ffn_lora_B"])
    d["bstack_img"] = tile_lhsT(np.ascontiguousarray(
        B_.transpose(0, 2, 1).reshape(128, 2048)))
    d["gate_w"] = np.ascontiguousarray(
        np.asarray(inputs["L0_ffn_gate_w"]).T).reshape(16, 128, 8)
    d["gate_b"] = np.ascontiguousarray(np.broadcast_to(
        np.asarray(inputs["L0_ffn_gate_b"])[None, :], (128, 8))).astype(np.float32)
    d["bffn"] = np.ascontiguousarray(
        np.asarray(inputs["L0_ffn_b"]).reshape(16, 128).T)
    ee = np.zeros((8, 128), np.float32)
    for e in range(8):
        ee[e, 16 * e:16 * (e + 1)] = 1.0
    d["e_expand"] = ee

    d["wq1_img"] = tile_lhsT(np.asarray(inputs["L1_attn_Wq"]))
    d["wk1_img"] = tile_lhsT(np.asarray(inputs["L1_attn_Wk"]))
    d["wv1_img"] = tile_lhsT(np.asarray(inputs["L1_attn_Wv"]))
    d["wo1_img"] = tile_lhsT(np.asarray(inputs["L1_attn_Wo"]))
    d["gt_img"] = np.ascontiguousarray(
        np.asarray(inputs["L1_attn_global_tokens"]).T).reshape(16, 128, G)
    pos1 = G + SH * hh + np.arange(SH)
    (c1h, c1l), (s1h, s1l) = rope_tables(pos1, HD, 500000.0)
    d["cos1_hi"], d["cos1_lo"] = c1h, c1l
    d["sin1_hi"], d["sin1_lo"] = s1h, s1l
    (cgh, cgl), (sgh, sgl) = rope_tables(np.arange(G), HD, 500000.0)
    d["cos1g"], d["sin1g"] = cgh, sgh
    d["cos1g_lo"], d["sin1g_lo"] = cgl, sgl

    j = np.arange(E1P, dtype=np.float64)
    d["jrow"] = np.ascontiguousarray(
        np.broadcast_to(j[None, :], (128, E1P))).astype(np.float32)
    jb = j + np.where((j % 256) == 0, 1e9, 0.0)
    d["jbig"] = np.ascontiguousarray(
        np.broadcast_to(jb[None, :], (128, E1P))).astype(np.float32)

    if "_wfc_img" not in _SHARED:
        _SHARED["_wfc_img"] = tile_lhsT(np.ascontiguousarray(
            np.asarray(inputs["L1_ffn_w_fc"]).T))
        _SHARED["_wproj_img"] = tile_lhsT(np.ascontiguousarray(
            np.asarray(inputs["L1_ffn_w_proj"]).T))
    d["wfc_shard"] = np.ascontiguousarray(
        _SHARED["_wfc_img"][8 * core:8 * (core + 1)])
    d["wproj_shard"] = np.ascontiguousarray(
        _SHARED["_wproj_img"][2 * core:2 * (core + 1)])

    d["ident_b"] = np.eye(128).astype(np.float32)
    import ml_dtypes
    d["ident_b"] = np.eye(128).astype(ml_dtypes.bfloat16)
    d["ident_f"] = np.eye(128, dtype=np.float32)
    oc = np.zeros((128, 8), np.float32)
    oc[:, 0] = 1.0
    d["ones_col"] = oc
    d["ones_row"] = np.ones((1, 128), np.float32)
    d["eps_row"] = np.full((1, 512), EPS, np.float32)
    d["arange_col"] = np.broadcast_to(
        (G + SH * hh + np.arange(128, dtype=np.float32))[:, None],
        (128, 8)).copy()
    cc_ = np.zeros((128, 8), np.float32)
    cc_[:, 0] = -0.5
    cc_[:, 1] = 124.0
    cc_[:, 2] = 130.0
    d["ccol"] = cc_
    return d


_NC_CACHE = {}
_SHARED = {}


def kernel(**inputs):
    _SHARED.clear()
    key = ("full",)
    if key not in _NC_CACHE:
        _NC_CACHE[key] = build_program(ALL_PHASES)
    nc = _NC_CACHE[key]
    in_maps = [make_inputs(c, inputs) for c in range(8)]
    res = run_bass_kernel_spmd(nc, in_maps, core_ids=list(range(8)))
    out = np.empty((B, DM, S), np.float32)
    for c in range(8):
        b, hh = c // 2, c % 2
        out[b][:, SH * hh:SH * (hh + 1)] = res.results[c]["y_fm"]
    return np.ascontiguousarray(out.transpose(0, 2, 1))


# version 05
# speedup vs baseline: 1.0287x; 1.0287x over previous
import math
from contextlib import ExitStack

import numpy as np
import concourse.bass as bass
import concourse.mybir as mybir
from concourse.bass_utils import run_bass_kernel_spmd

F32 = mybir.dt.float32
BF16 = mybir.dt.bfloat16
I32 = mybir.dt.int32
U8 = mybir.dt.uint8
AF = mybir.ActivationFunctionType
ALU = mybir.AluOpType
AX = mybir.AxisListType

B, S, DM = 4, 2048, 2048
H, HD = 16, 128
SH = 1024
NCH = DM // 128
NTB = 2
G = 64
E1 = 2112
E1P = 2176
KC1 = E1P // 128
NKV = 4
SCALE0 = 1.0 / math.sqrt(HD)
SCALE1 = HD ** (-0.5)
FMIN = float(np.finfo(np.float32).min)
MASKV = float(2.0 ** 126)
LN2_INV = 1.0 / math.log(2.0)
EPS = 1e-5
FP8_MAX = 448.0
ARENA = 124 * 1024

PAIRS = [[0, 1], [2, 3], [4, 5], [6, 7]]
ALL8 = [[0, 1, 2, 3, 4, 5, 6, 7]]

ALL_PHASES = ("l0attn", "l0ffn", "l1attn", "l1ffn")
DEV = {}


class Sched:
    ENGINES = ("sync", "tensor", "vector", "scalar", "gpsimd")

    def __init__(self):
        self.q = {e: [] for e in self.ENGINES}
        self.cnt = {}

    def bump(self, sem, n):
        k = id(sem)
        self.cnt[k] = self.cnt.get(k, 0) + n
        return self.cnt[k]

    def val(self, sem):
        return self.cnt.get(id(sem), 0)


class Rec:
    def __init__(self, sched, key):
        self.__dict__["_s"] = sched
        self.__dict__["_k"] = key

    def __getattr__(self, attr):
        s = self.__dict__["_s"]
        k = self.__dict__["_k"]

        def call(*a, **kw):
            inc = kw.pop("inc", None)
            if inc is not None:
                s.bump(inc[0], inc[1])

            def run(h, a=a, kw=kw, inc=inc, attr=attr):
                inst = getattr(h, attr)(*a, **kw)
                if inc is not None:
                    inst.then_inc(*inc)
            s.q[k].append(run)
        return call


def tile_lhsT(w):
    K, M = w.shape
    kc, mc = K // 128, M // 128
    return np.ascontiguousarray(
        w.reshape(kc, 128, mc, 128).transpose(2, 1, 0, 3).reshape(mc, 128, kc * 128)
    ).astype(np.float32)


def rope_tables(positions, dim, base):
    inv = 1.0 / (float(base) ** (np.arange(0, dim, 2, dtype=np.float64) / dim))
    freqs = np.outer(np.asarray(positions, dtype=np.float64), inv)
    emb = np.concatenate([freqs, freqs], axis=-1)

    def hl(t64):
        t = np.ascontiguousarray(t64.T)
        hi = t.astype(np.float32)
        lo = (t - hi.astype(np.float64)).astype(np.float32)
        return hi, np.ascontiguousarray(lo)
    return hl(np.cos(emb)), hl(np.sin(emb))


def build_program(phases=ALL_PHASES, dbg=None):
    nc = bass.Bass("TRN2", target_bir_lowering=False, detect_race_conditions=False)
    s = Sched()
    SY, T, V, C, GQ = (Rec(s, "sync"), Rec(s, "tensor"), Rec(s, "vector"),
                       Rec(s, "scalar"), Rec(s, "gpsimd"))

    def inp(name, shape, dt=F32):
        return nc.dram_tensor(name, list(shape), dt, kind="ExternalInput").ap()

    def dram(name, shape, dt=F32):
        return nc.dram_tensor(name, list(shape), dt)

    x_in = inp("x_fm", (DM, SH))
    q0_img = inp("q0_img", (16, 128, NCH * 128))
    k0_img = inp("k0_img", (16, 128, NCH * 128))
    v0_img = inp("v0_img", (16, 128, NCH * 128))
    wout_img = inp("wout_img", (16, 128, NCH * 128))
    cos0_h = inp("cos0_hi", (128, SH)); cos0_l = inp("cos0_lo", (128, SH))
    sin0_h = inp("sin0_hi", (128, SH)); sin0_l = inp("sin0_lo", (128, SH))
    wffn_img = inp("wffn_img", (16, 128, NCH * 128))
    aT_img = inp("aT_img", (1, 128, NCH * 128))
    bstack_img = inp("bstack_img", (16, 128, 128))
    gate_w_in = inp("gate_w", (16, 128, 8))
    wq1_img = inp("wq1_img", (16, 128, NCH * 128))
    wk1_img = inp("wk1_img", (4, 128, NCH * 128))
    wv1_img = inp("wv1_img", (4, 128, NCH * 128))
    wo1_img = inp("wo1_img", (16, 128, NCH * 128))
    gt_img = inp("gt_img", (16, 128, G))
    cos1_h = inp("cos1_hi", (128, SH)); cos1_l = inp("cos1_lo", (128, SH))
    sin1_h = inp("sin1_hi", (128, SH)); sin1_l = inp("sin1_lo", (128, SH))
    cos1g = inp("cos1g", (128, G)); sin1g = inp("sin1g", (128, G))
    cos1g_l = inp("cos1g_lo", (128, G)); sin1g_l = inp("sin1g_lo", (128, G))
    jrow_in = inp("jrow", (128, E1P))
    jbig_in = inp("jbig", (128, E1P))
    wfc_shard = inp("wfc_shard", (8, 128, NCH * 128))
    wproj_shard = inp("wproj_shard", (2, 128, 64 * 128))
    ident_b_in = inp("ident_b", (128, 128), BF16)
    ident_f_in = inp("ident_f", (128, 128), F32)
    ones_col_in = inp("ones_col", (128, 8))
    ones_row_in = inp("ones_row", (1, 128))
    eps_row_in = inp("eps_row", (1, 512))
    bffn_in = inp("bffn", (128, 16))
    gate_b_in = inp("gate_b", (128, 8))
    e_expand_in = inp("e_expand", (8, 128))
    ccol_in = inp("ccol", (128, 8))
    arange_col_in = inp("arange_col", (128, 8))

    y_out = nc.dram_tensor("y_fm", [DM, SH], F32, kind="ExternalOutput").ap()
    if dbg == "ag0":
        dbg_out = nc.dram_tensor("dbg", [2 * 64 * 128, SH], BF16,
                                 kind="ExternalOutput").ap()
    elif dbg:
        dbg_out = nc.dram_tensor("dbg", [DM, SH], F32, kind="ExternalOutput").ap()
    else:
        dbg_out = None

    ag0_in = dram("ag0_in", (64 * 128, SH), BF16)
    ag0_out = dram("ag0_out", (2 * 64 * 128, SH), BF16)
    q0_save = dram("q0_save", (32 * 128, SH), BF16)
    ag1_in = dram("ag1_in", (16 * 128, SH), BF16)
    ag1_out = dram("ag1_out", (2 * 16 * 128, SH), BF16)
    q1_save = dram("q1_save", (32 * 128, SH), BF16)
    r_save = dram("r_save", (DM, SH), F32)
    h2_save = dram("h2_save", (64 * 128, SH), F32)
    agw_in = dram("agw_in", (8 * 128, 2048), BF16)
    agw_out = dram("agw_out", (64 * 128, 2048), BF16)
    agp_in = dram("agp_in", (2 * 128, 8192), BF16)
    agp_out = dram("agp_out", (16 * 128, 8192), BF16)
    ar_in = dram("ar_in", (1, 8), F32)
    ar_out = dram("ar_out", (1, 8), F32)
    ar2_in = dram("ar2_in", (1, 8), F32)
    ar2_out = dram("ar2_out", (1, 8), F32)
    h8_save = dram("h8_save", (64 * 128, SH), BF16)
    gcol = dram("gcol", (128, 1), F32)

    R = nc.alloc_sbuf_tensor("R", [128, NCH * SH], F32)
    arena_t = nc.alloc_sbuf_tensor("arena", [128, ARENA], U8)
    cst = {}
    for nm, shape, dt in [
        ("ident_b", (128, 128), BF16), ("ident_f", (128, 128), F32),
        ("ones_col", (128, 8), F32), ("ones_row", (1, 128), F32),
        ("eps_row", (1, 512), F32), ("bffn", (128, 16), F32),
        ("gate_b", (128, 8), F32), ("e_expand", (8, 128), F32),
        ("ccol", (128, 8), F32), ("arange_col", (128, 8), F32),
    ]:
        cst[nm] = nc.alloc_sbuf_tensor("c_" + nm, list(shape), dt)
    lnrow = nc.alloc_sbuf_tensor("lnrow", [1, 2048], F32)
    grow = nc.alloc_sbuf_tensor("grow", [1, 1032], F32)
    gstat = nc.alloc_sbuf_tensor("gstat", [128, 64], F32)
    rstat = nc.alloc_sbuf_tensor("rstat", [128, 64], F32)
    sstat = nc.alloc_sbuf_tensor("sstat", [128, 16], F32)

    def A(off, p, n, dt=F32):
        esz = mybir.dt.size(dt)
        assert off % 64 == 0 and off + n * esz <= ARENA, (off, n, dt)
        view = arena_t[:, off:off + n * esz].bitcast(dt)
        return view if p == 128 else view[0:p, :]

    ctx = ExitStack()
    sem = {nm: ctx.enter_context(nc.semaphore(nm))
           for nm in ["ld", "wl", "tn", "vs", "sc", "gq", "cc"]}
    ld, wl, tn, vs, sc, gq, cc = (sem[n] for n in
                                  ["ld", "wl", "tn", "vs", "sc", "gq", "cc"])

    ps = [nc.alloc_psum_tensor("ps%d" % i, [128, 512], F32) for i in range(6)]
    ps_s1 = nc.alloc_psum_tensor("ps_s1", [1, 512], F32)
    ps_s2 = nc.alloc_psum_tensor("ps_s2", [1, 512], F32)
    bank_guard = {}

    def bank_wait(rec, pt):
        g = bank_guard.get(id(pt))
        if g is not None:
            rec.wait_ge(g[0], g[1])

    def bank_done(pt, sm):
        bank_guard[id(pt)] = (sm, s.val(sm))

    SY.dma_start(out=R[:].rearrange("p (c t) -> p c t", c=NCH),
                 in_=x_in.rearrange("(c p) t -> p c t", p=128), inc=(ld, 16))
    for nm, src in [("ident_b", ident_b_in), ("ident_f", ident_f_in),
                    ("ones_col", ones_col_in), ("ones_row", ones_row_in),
                    ("eps_row", eps_row_in), ("bffn", bffn_in),
                    ("gate_b", gate_b_in), ("e_expand", e_expand_in),
                    ("ccol", ccol_in), ("arange_col", arange_col_in)]:
        SY.dma_start(out=cst[nm][:], in_=src, inc=(ld, 16))
    setup_ld = s.val(ld)
    for e in Sched.ENGINES:
        Rec(s, e).wait_ge(ld, setup_ld)
    V.memset(gstat[:], 1.0)
    V.memset(rstat[:], 1.0)
    V.memset(sstat[:], 1.0)
    V.memset(lnrow[:], 1.0)
    V.memset(grow[:], 1.0, inc=(vs, 1))
    GQ.wait_ge(vs, s.val(vs))

    def split_hilo(src_ap, hi_ap, lo_ap, inc=None):
        V.tensor_copy(out=hi_ap, in_=src_ap)
        V.tensor_tensor(lo_ap, src_ap, hi_ap, ALU.subtract, inc=inc)

    def mm3(psum_ap, lh, ll, rh, rl, start, stop, inc=None):
        T.matmul(psum_ap, lh, rh, start=start, stop=False)
        T.matmul(psum_ap, lh, rl, start=False, stop=False)
        T.matmul(psum_ap, ll, rh, start=False, stop=stop, inc=inc)

    class WStream:

        def __init__(self, img_ap, cw, off):
            self.img = img_ap
            self.cw = cw
            self.fb = [A(off, 128, cw), A(off + cw * 4, 128, cw)]
            o2 = off + cw * 8
            self.hb = [A(o2, 128, cw, BF16), A(o2 + cw * 2, 128, cw, BF16)]
            self.lb = [A(o2 + cw * 4, 128, cw, BF16), A(o2 + cw * 6, 128, cw, BF16)]
            self.split_done = [0, 0]
            self.freed = [(vs, s.val(vs)), (vs, s.val(vs))]
            self.n = 0

        def load(self, j):
            b = self.n % 2
            self.n += 1
            SY.wait_ge(*self.freed[b])
            SY.dma_start(out=self.fb[b][:, :], in_=self.img[j], inc=(wl, 16))
            tgt = s.val(wl)
            V.wait_ge(wl, tgt)
            split_hilo(self.fb[b][:, :], self.hb[b][:, :], self.lb[b][:, :],
                       inc=(vs, 1))
            self.split_done[b] = s.val(vs)
            return b

        def tiles(self, b, c):
            return (self.hb[b][:, 128 * c:128 * (c + 1)],
                    self.lb[b][:, 128 * c:128 * (c + 1)])

        def wait(self, rec, b):
            rec.wait_ge(vs, self.split_done[b])

        def release(self, b, sm):
            self.freed[b] = (sm, s.val(sm))

    def rope_emit(psum_list, cos_ap, sin_ap, tmp, out_f32, tb):
        p = psum_list
        sl = slice(512 * tb, 512 * (tb + 1))
        V.tensor_scalar_mul(tmp[0:64, 0:512], p[64:128, :], -1.0)
        V.tensor_copy(out=tmp[64:128, 0:512], in_=p[0:64, :])
        V.tensor_tensor(out_f32[:, sl], p[:, :], cos_ap[:, sl], ALU.mult)
        V.tensor_tensor(tmp[:, 512:1024], tmp[:, 0:512], sin_ap[:, sl], ALU.mult)
        V.tensor_tensor(out_f32[:, sl], out_f32[:, sl], tmp[:, 512:1024], ALU.add)

    def layer_norm_tb(y_ap_fn, tb, sq_tiles):
        y_ready = s.val(vs)
        C.wait_ge(vs, y_ready)
        T.wait_ge(vs, y_ready)
        sq_guard = [None, None]
        for c in range(NCH):
            src = y_ap_fn(c)
            sqt = sq_tiles[c % 2]
            if sq_guard[c % 2] is not None:
                C.wait_ge(tn, sq_guard[c % 2])
            C.activation(out=sqt, in_=src, func=AF.Square, inc=(sc, 1))
            sq_done = s.val(sc)
            if c == 0:
                bank_wait(T, ps_s1)
                bank_wait(T, ps_s2)
            T.matmul(ps_s1[:], cst["ones_col"][:, 0:1], src,
                     start=(c == 0), stop=(c == NCH - 1))
            T.wait_ge(sc, sq_done)
            T.matmul(ps_s2[:], cst["ones_col"][:, 0:1], sqt,
                     start=(c == 0), stop=(c == NCH - 1),
                     inc=(tn, 1))
            sq_guard[c % 2] = s.val(tn)
        V.wait_ge(tn, s.val(tn))
        mean, e2 = lnrow[:, 0:512], lnrow[:, 512:1024]
        aro, bro = lnrow[:, 1024:1536], lnrow[:, 1536:2048]
        V.tensor_scalar_mul(mean, ps_s1[:], 1.0 / DM)
        V.tensor_scalar_mul(e2, ps_s2[:], 1.0 / DM)
        V.tensor_tensor(bro, mean, mean, ALU.mult)
        V.tensor_tensor(e2, e2, bro, ALU.subtract)
        V.tensor_tensor(e2, e2, cst["eps_row"][:], ALU.add, inc=(vs, 1))
        bank_done(ps_s1, vs)
        bank_done(ps_s2, vs)
        C.wait_ge(vs, s.val(vs))
        C.activation(out=aro, in_=e2, func=AF.Sqrt, inc=(sc, 1))
        V.wait_ge(sc, s.val(sc))
        V.reciprocal(aro, aro)
        V.scalar_tensor_tensor(bro, mean, -1.0, aro, ALU.mult, ALU.mult,
                               inc=(vs, 1))
        bank_wait(T, ps[4])
        bank_wait(T, ps[5])
        T.wait_ge(vs, s.val(vs))
        T.matmul(ps[4][:], cst["ones_row"][:], aro, start=True, stop=True)
        T.matmul(ps[5][:], cst["ones_row"][:], bro, start=True, stop=True,
                 inc=(tn, 1))
        V.wait_ge(tn, s.val(tn))
        for c in range(NCH):
            src = y_ap_fn(c)
            dst = R[:, SH * c + 512 * tb: SH * c + 512 * (tb + 1)]
            V.tensor_tensor(dst, src, ps[4][:], ALU.mult)
            V.tensor_tensor(dst, dst, ps[5][:], ALU.add,
                            inc=(vs, 1) if c == NCH - 1 else None)
        bank_done(ps[4], vs)
        bank_done(ps[5], vs)
        return s.val(vs)

    XH0, XL0 = 0, 32768
    CO0, SI0 = 65536, 69632
    WS0 = 73728
    QT0 = 106496
    RT0 = 110592

    def proj_head(ws, col, cos_ap, sin_ap, out_hi, out_lo, xh, xl, rope, qtmp,
                  rtmp):
        b = ws.load(col)
        ws.wait(T, b)
        for tb in range(NTB):
            pt = ps[tb]
            bank_wait(T, pt)
            for c in range(NCH):
                lh, ll = ws.tiles(b, c)
                rh = xh[:, SH * c + 512 * tb: SH * c + 512 * (tb + 1)]
                rl = xl[:, SH * c + 512 * tb: SH * c + 512 * (tb + 1)]
                mm3(pt[:], lh, ll, rh, rl, start=(c == 0), stop=(c == NCH - 1),
                    inc=(tn, 1) if c == NCH - 1 else None)
            V.wait_ge(tn, s.val(tn))
            sl = slice(512 * tb, 512 * (tb + 1))
            if rope:
                rope_emit(pt[:], cos_ap, sin_ap, rtmp, qtmp, tb)
                split_hilo(qtmp[:, sl], out_hi[:, sl], out_lo[:, sl],
                           inc=(vs, 1))
            else:
                split_hilo(pt[:], out_hi[:, sl], out_lo[:, sl], inc=(vs, 1))
            bank_done(pt, vs)
        ws.release(b, vs)
        return s.val(vs)

    def emit_l0_proj():
        xh = A(XH0, 128, NCH * SH, BF16)
        xl = A(XL0, 128, NCH * SH, BF16)
        cos_e = A(CO0, 128, SH)
        sin_e = A(SI0, 128, SH)
        qtmp = A(QT0, 128, SH)
        rtmp = A(RT0, 128, SH)
        SY.dma_start(out=cos_e[:, :], in_=cos0_h, inc=(ld, 16))
        SY.dma_start(out=qtmp[:, :], in_=cos0_l, inc=(ld, 16))
        V.wait_ge(ld, s.val(ld))
        V.tensor_tensor(cos_e[:, :], cos_e[:, :], qtmp[:, :], ALU.add, inc=(vs, 1))
        cos_done = s.val(vs)
        SY.wait_ge(vs, cos_done)
        SY.dma_start(out=sin_e[:, :], in_=sin0_h, inc=(ld, 16))
        SY.dma_start(out=qtmp[:, :], in_=sin0_l, inc=(ld, 16))
        V.wait_ge(ld, s.val(ld))
        V.tensor_tensor(sin_e[:, :], sin_e[:, :], qtmp[:, :], ALU.add, inc=(vs, 1))
        for c in range(NCH):
            split_hilo(R[:, SH * c:SH * (c + 1)],
                       xh[:, SH * c:SH * (c + 1)], xl[:, SH * c:SH * (c + 1)],
                       inc=(vs, 1) if c == NCH - 1 else None)
        ws = WStream(q0_img, NCH * 128, WS0)
        oh = A(114688, 128, SH, BF16)
        ol = A(116736, 128, SH, BF16)
        agf = ag0_in.ap()
        q0f = q0_save.ap()
        for hh in range(H):
            done = proj_head(ws, hh, cos_e, sin_e, oh, ol, xh, xl, True,
                             qtmp, rtmp)
            GQ.wait_ge(vs, done)
            GQ.dma_start(out=q0f[128 * hh:128 * (hh + 1), :], in_=oh[:, :],
                         inc=(gq, 16))
            GQ.dma_start(out=q0f[128 * (16 + hh):128 * (17 + hh), :],
                         in_=ol[:, :], inc=(gq, 16))
            V.wait_ge(gq, s.val(gq))
        ws_k = WStream(k0_img, NCH * 128, WS0)
        for hh in range(H):
            done = proj_head(ws_k, hh, cos_e, sin_e, oh, ol, xh, xl, True,
                             qtmp, rtmp)
            GQ.wait_ge(vs, done)
            GQ.dma_start(out=agf[128 * hh:128 * (hh + 1), :], in_=oh[:, :],
                         inc=(gq, 16))
            GQ.dma_start(out=agf[128 * (16 + hh):128 * (17 + hh), :],
                         in_=ol[:, :], inc=(gq, 16))
            V.wait_ge(gq, s.val(gq))
        ws_v = WStream(v0_img, NCH * 128, WS0)
        for hh in range(H):
            done = proj_head(ws_v, hh, None, None, oh, ol, xh, xl, False,
                             qtmp, rtmp)
            GQ.wait_ge(vs, done)
            GQ.dma_start(out=agf[128 * (32 + hh):128 * (33 + hh), :],
                         in_=oh[:, :], inc=(gq, 16))
            GQ.dma_start(out=agf[128 * (48 + hh):128 * (49 + hh), :],
                         in_=ol[:, :], inc=(gq, 16))
            V.wait_ge(gq, s.val(gq))
        GQ.wait_ge(gq, s.val(gq))
        GQ.collective_compute("AllGather", ALU.bypass, replica_groups=PAIRS,
                              ins=[ag0_in.ap().opt()],
                              outs=[ag0_out.ap().opt()], inc=(cc, 1))
        return s.val(cc)

    attn_hi = R[:, 0:8192].bitcast(BF16)
    attn_lo = R[:, 8192:16384].bitcast(BF16)

    KH0 = 0
    KL0 = 4096
    VF0 = 8192
    VT0 = 16384
    PP0 = 24576
    PT0 = 32768
    QH0 = 65536
    QF1 = 66560

    def emit_l0_core(ag_ready):
        kh = A(KH0, 128, 2048, BF16)
        kl = A(KL0, 128, 2048, BF16)
        vfh = A(VF0, 128, 2048, BF16)
        vfl = A(VF0 + 4096, 128, 2048, BF16)
        vth = A(VT0, 128, 2048, BF16)
        vtl = A(VT0 + 4096, 128, 2048, BF16)
        P = A(PP0, 128, 2048)
        pth = A(PT0, 128, 8192, BF16)
        ptl = A(PT0 + 16384, 128, 8192, BF16)
        qh = A(QH0, 128, 128, BF16)
        ql = A(QH0 + 256, 128, 128, BF16)
        pth_r = pth.rearrange("p (kc qt qq) -> p kc qt qq", kc=16, qq=128)
        ptl_r = ptl.rearrange("p (kc qt qq) -> p kc qt qq", kc=16, qq=128)
        ps4_r = ps[4][:].rearrange("p (i one q) -> p i one q", i=4, one=1)
        ago = ag0_out.ap()
        q0f = q0_save.ap()
        ps_tb = ps[4][:, 0:256].bitcast(BF16)
        GQ.wait_ge(cc, ag_ready)
        SY.wait_ge(cc, ag_ready)
        nheads = DEV.get("nheads", H)
        part = DEV.get("part", 4)
        for hh in range(nheads):
            prev_v = s.val(vs)
            SY.wait_ge(vs, prev_v)
            for (dst, base) in ((kh, hh), (kl, 16 + hh)):
                for r in range(2):
                    SY.dma_start(out=dst[:, 1024 * r:1024 * (r + 1)],
                                 in_=ago[8192 * r + 128 * base:
                                         8192 * r + 128 * (base + 1), :],
                                 inc=(ld, 16))
            for (dst, base) in ((vfh, 32 + hh), (vfl, 48 + hh)):
                for r in range(2):
                    SY.dma_start(out=dst[:, 1024 * r:1024 * (r + 1)],
                                 in_=ago[8192 * r + 128 * base:
                                         8192 * r + 128 * (base + 1), :],
                                 inc=(ld, 16))
            kv_ld = s.val(ld)
            T.wait_ge(ld, kv_ld)
            for (src, dst) in ((vfh, vth), (vfl, vtl)):
                for g in range(4):
                    bank_wait(T, ps[4])
                    for i in range(4):
                        c = 4 * g + i
                        T.transpose(ps_tb[:, 128 * i:128 * (i + 1)],
                                    src[:, 128 * c:128 * (c + 1)],
                                    cst["ident_b"][:],
                                    inc=(tn, 1) if i == 3 else None)
                    V.wait_ge(tn, s.val(tn))
                    V.tensor_copy(out=dst[:, 512 * g:512 * (g + 1)],
                                  in_=ps_tb[:, :], inc=(vs, 1))
                    bank_done(ps[4], vs)
            vt_done = s.val(vs)
            if part < 2:
                continue
            for qb in range(2):
                for qt in range(4):
                    qi = 4 * qb + qt
                    SY.wait_ge(vs, s.val(vs))
                    SY.dma_start(out=qh[:, :],
                                 in_=q0f[128 * hh:128 * (hh + 1),
                                         128 * qi:128 * (qi + 1)],
                                 inc=(ld, 16))
                    SY.dma_start(out=ql[:, :],
                                 in_=q0f[128 * (16 + hh):128 * (17 + hh),
                                         128 * qi:128 * (qi + 1)],
                                 inc=(ld, 16))
                    q_ld = s.val(ld)
                    T.wait_ge(ld, q_ld)
                    for kb in range(4):
                        pt = ps[kb]
                        bank_wait(T, pt)
                        mm3(pt[:], qh[:, :], ql[:, :],
                            kh[:, 512 * kb:512 * (kb + 1)],
                            kl[:, 512 * kb:512 * (kb + 1)],
                            start=True, stop=True,
                            inc=(tn, 1) if kb == 3 else None)
                    V.wait_ge(tn, s.val(tn))
                    for kb in range(4):
                        V.reduce_max(gstat[:, kb:kb + 1], ps[kb][:], AX.X,
                                     inc=(vs, 1) if kb == 3 else None)
                    V.wait_ge(vs, s.val(vs))
                    V.reduce_max(gstat[:, 4:5], gstat[:, 0:4], AX.X,
                                 negate=True, inc=(vs, 1))
                    V.wait_ge(vs, s.val(vs))
                    V.tensor_scalar_mul(rstat[:, 0:8], gstat[:, 0:8], SCALE0,
                                        inc=(vs, 1))
                    C.wait_ge(vs, s.val(vs))
                    for kb in range(4):
                        C.activation(out=P[:, 512 * kb:512 * (kb + 1)],
                                     in_=ps[kb][:], func=AF.Exp,
                                     bias=rstat[:, 4:5], scale=SCALE0,
                                     inc=(sc, 1) if kb == 3 else None)
                    exp_done = s.val(sc)
                    for kb in range(4):
                        bank_done(ps[kb], sc)
                    if dbg == "p0" and hh == 0 and qi == 0:
                        GQ.wait_ge(sc, exp_done)
                        GQ.dma_start(out=dbg_out[0:128, :], in_=P[:, 0:1024],
                                     inc=(gq, 16))
                        GQ.dma_start(out=dbg_out[128:256, :], in_=P[:, 1024:2048],
                                     inc=(gq, 16))
                        V.wait_ge(gq, s.val(gq))
                    V.wait_ge(sc, exp_done)
                    V.reduce_sum(gstat[:, 8:9], P[:, :], AX.X, inc=(vs, 1))
                    V.wait_ge(vs, s.val(vs))
                    V.reciprocal(rstat[:, 8:16], gstat[:, 8:16], inc=(vs, 1))
                    V.wait_ge(vs, s.val(vs))
                    V.tensor_scalar_mul(P[:, :], P[:, :], rstat[:, 8:9],
                                        inc=(vs, 1))
                    pn_done = s.val(vs)
                    if dbg == "p0" and hh == 0 and qi == 0:
                        GQ.wait_ge(vs, pn_done)
                        GQ.dma_start(out=dbg_out[256:384, :], in_=P[:, 0:1024],
                                     inc=(gq, 16))
                        GQ.dma_start(out=dbg_out[384:512, :], in_=P[:, 1024:2048],
                                     inc=(gq, 16))
                        GQ.dma_start(out=dbg_out[512:640, 0:64], in_=gstat[:],
                                     inc=(gq, 16))
                        GQ.dma_start(out=dbg_out[640:768, 0:64], in_=rstat[:],
                                     inc=(gq, 16))
                        T.wait_ge(gq, s.val(gq))
                    if part < 3:
                        continue
                    T.wait_ge(vs, pn_done)
                    for g in range(4):
                        bank_wait(T, ps[4])
                        for i in range(4):
                            c = 4 * g + i
                            T.transpose(ps[4][:, 128 * i:128 * (i + 1)],
                                        P[:, 128 * c:128 * (c + 1)],
                                        cst["ident_f"][:],
                                        inc=(tn, 1) if i == 3 else None)
                        tr_done = s.val(tn)
                        V.wait_ge(tn, tr_done)
                        C.wait_ge(tn, tr_done)
                        oh_ap = pth_r[:, 4 * g:4 * g + 4, qt:qt + 1, :]
                        ol_ap = ptl_r[:, 4 * g:4 * g + 4, qt:qt + 1, :]
                        C.activation(out=oh_ap, in_=ps4_r, func=AF.Copy,
                                     inc=(sc, 1))
                        V.wait_ge(sc, s.val(sc))
                        V.tensor_tensor(ol_ap, ps4_r, oh_ap, ALU.subtract,
                                        inc=(vs, 1))
                        bank_done(ps[4], vs)
                if part < 4:
                    continue
                pt5 = ps[5]
                bank_wait(T, pt5)
                T.wait_ge(vs, s.val(vs))
                T.wait_ge(vs, vt_done)
                for c in range(16):
                    mm3(pt5[:], vth[:, 128 * c:128 * (c + 1)],
                        vtl[:, 128 * c:128 * (c + 1)],
                        pth[:, 512 * c:512 * (c + 1)],
                        ptl[:, 512 * c:512 * (c + 1)],
                        start=(c == 0), stop=(c == 15),
                        inc=(tn, 1) if c == 15 else None)
                pv_done = s.val(tn)
                off = SH * hh + 512 * qb
                C.wait_ge(tn, pv_done)
                C.activation(out=attn_hi[:, off:off + 512], in_=pt5[:],
                             func=AF.Copy, inc=(sc, 1))
                V.wait_ge(sc, s.val(sc))
                V.tensor_tensor(attn_lo[:, off:off + 512], pt5[:],
                                attn_hi[:, off:off + 512], ALU.subtract,
                                inc=(vs, 1))
                bank_done(pt5, vs)
        return s.val(vs)


    def emit_outproj_ln(w_img, res_src, attn_done):
        ws = WStream(w_img, NCH * 128, 0)
        xres = [A(32768, 128, 512), A(32768 + 2048, 128, 512)]
        y_full = A(36864, 128, 16384)
        sq = [A(36864 + 65536, 128, 512), A(36864 + 65536 + 2048, 128, 512)]
        T.wait_ge(vs, attn_done)
        for dc in range(NCH):
            b = ws.load(dc)
            ws.wait(T, b)
            for tb in range(NTB):
                xr = xres[tb]
                src = (x_in if res_src == "x" else r_save.ap())
                GQ.wait_ge(vs, s.val(vs))
                GQ.dma_start(out=xr[:, :],
                             in_=src[128 * dc:128 * (dc + 1),
                                     512 * tb:512 * (tb + 1)], inc=(gq, 16))
                xr_ld = s.val(gq)
                pt = ps[tb]
                bank_wait(T, pt)
                for oc in range(NCH):
                    lh, ll = ws.tiles(b, oc)
                    off = SH * oc + 512 * tb
                    mm3(pt[:], lh, ll, attn_hi[:, off:off + 512],
                        attn_lo[:, off:off + 512],
                        start=(oc == 0), stop=(oc == NCH - 1),
                        inc=(tn, 1) if oc == NCH - 1 else None)
                V.wait_ge(tn, s.val(tn))
                V.wait_ge(gq, xr_ld)
                V.tensor_tensor(y_full[:, SH * dc + 512 * tb:
                                       SH * dc + 512 * (tb + 1)], pt[:],
                                xr[:, :], ALU.add, inc=(vs, 1))
                bank_done(pt, vs)
            ws.release(b, tn)
        for tb in range(NTB):
            layer_norm_tb(
                lambda c, tb=tb: y_full[:, SH * c + 512 * tb:
                                        SH * c + 512 * (tb + 1)], tb, sq)
        return s.val(vs)

    def emit_l0_ffn():
        XTH, XTL = 0, 16384
        WSF = 32768
        BSH, BSL = 65536, 69632
        ATH, ATL = 73728, 77824
        GW0 = 81920
        XA0 = 83968
        SXH, SXL = 86016, 87040
        WGT = 88064
        YTB = 90112
        xh_t = A(XTH, 128, 16 * 512, BF16)
        xl_t = A(XTL, 128, 16 * 512, BF16)
        bsh = A(BSH, 128, 2048, BF16)
        bsl = A(BSL, 128, 2048, BF16)
        ath = A(ATH, 128, 2048, BF16)
        atl = A(ATL, 128, 2048, BF16)
        gw = A(GW0, 128, 128)
        gwh = A(GW0 + 512, 128, 128, BF16)
        gwl = A(GW0 + 768, 128, 128, BF16)
        lg = A(GW0 + 1024, 128, 8)
        eqt = A(GW0 + 1088, 128, 8)
        mkd = A(GW0 + 1152, 128, 8)
        et = A(GW0 + 1216, 128, 8)
        emt = A(GW0 + 1280, 128, 8)
        dent = A(GW0 + 1344, 128, 8)
        rect = A(GW0 + 1408, 128, 8)
        indt = A(GW0 + 1472, 128, 8)
        wgt = A(GW0 + 1536, 128, 8)
        xa_f = A(XA0, 128, 512)
        sxh = A(SXH, 128, 512, BF16)
        sxl = A(SXL, 128, 512, BF16)
        wgT_sb = A(WGT, 8, 512)
        y_tb = A(YTB, 128, 8192)
        sqs = [A(83968, 128, 512), A(86016, 128, 512)]

        SY.wait_ge(vs, s.val(vs))
        SY.wait_ge(tn, s.val(tn))
        SY.wait_ge(sc, s.val(sc))
        SY.dma_start(out=gw[:, :].rearrange("p (c e) -> p c e", c=16),
                     in_=gate_w_in.rearrange("c p e -> p c e"), inc=(ld, 16))
        V.wait_ge(ld, s.val(ld))
        split_hilo(gw[:, :], gwh[:, :], gwl[:, :], inc=(vs, 1))
        bstg = A(YTB, 128, 2048)
        SY.dma_start(out=bstg[:, :].rearrange("p (c m) -> p c m", c=16),
                     in_=bstack_img.rearrange("c p m -> p c m"), inc=(ld, 16))
        V.wait_ge(ld, s.val(ld))
        split_hilo(bstg[:, :], bsh[:, :], bsl[:, :], inc=(vs, 1))
        atg = A(YTB, 128, 2048)
        SY.wait_ge(vs, s.val(vs))
        SY.dma_start(out=atg[:, :], in_=aT_img.rearrange("o p n -> p (o n)"),
                     inc=(ld, 16))
        V.wait_ge(ld, s.val(ld))
        split_hilo(atg[:, :], ath[:, :], atl[:, :], inc=(vs, 1))

        for tb in range(NTB):
            V.wait_ge(tn, s.val(tn))
            for c in range(NCH):
                sl = slice(SH * c + 512 * tb, SH * c + 512 * (tb + 1))
                split_hilo(R[:, sl], xh_t[:, 512 * c:512 * (c + 1)],
                           xl_t[:, 512 * c:512 * (c + 1)],
                           inc=(vs, 1) if c == NCH - 1 else None)
            xsplit_done = s.val(vs)

            pg = ps[2][:, 0:8]
            for tt_ in range(4):
                bank_wait(T, ps[2])
                T.wait_ge(vs, xsplit_done)
                for c in range(NCH):
                    lhh = xh_t[:, 512 * c + 128 * tt_:512 * c + 128 * (tt_ + 1)]
                    lll = xl_t[:, 512 * c + 128 * tt_:512 * c + 128 * (tt_ + 1)]
                    mm3(pg, lhh, lll, gwh[:, 8 * c:8 * (c + 1)],
                        gwl[:, 8 * c:8 * (c + 1)],
                        start=(c == 0), stop=(c == NCH - 1),
                        inc=(tn, 1) if c == NCH - 1 else None)
                V.wait_ge(tn, s.val(tn))
                V.tensor_tensor(lg[:, :], pg, cst["gate_b"][:], ALU.add,
                                inc=(vs, 1))
                V.wait_ge(vs, s.val(vs))
                V.reduce_max(gstat[:, 0:1], lg[:, :], AX.X)
                V.reduce_max(gstat[:, 1:2], lg[:, :], AX.X, negate=True,
                             inc=(vs, 1))
                V.wait_ge(vs, s.val(vs))
                V.tensor_single_scalar(eqt[:, :], lg[:, :], gstat[:, 0:1],
                                       ALU.is_equal, inc=(vs, 1))
                V.wait_ge(vs, s.val(vs))
                V.scalar_tensor_tensor(mkd[:, :], eqt[:, :], FMIN, lg[:, :],
                                       ALU.mult, ALU.add, inc=(vs, 1))
                V.wait_ge(vs, s.val(vs))
                V.reduce_max(gstat[:, 2:3], mkd[:, :], AX.X, inc=(vs, 1))
                V.wait_ge(vs, s.val(vs))
                V.tensor_single_scalar(dent[:, :], gstat[:, 0:8],
                                       gstat[:, 1:2], ALU.add, inc=(vs, 1))
                V.wait_ge(vs, s.val(vs))
                V.tensor_single_scalar(dent[:, :], dent[:, :], 0.0, ALU.min,
                                       inc=(vs, 1))
                lgv = s.val(vs)
                C.wait_ge(vs, lgv)
                C.activation(out=et[:, :], in_=lg[:, :], func=AF.Exp,
                             bias=gstat[:, 1:2], scale=1.0)
                C.activation(out=emt[:, :], in_=dent[:, :], func=AF.Exp,
                             inc=(sc, 1))
                bank_done(ps[2], sc)
                V.wait_ge(sc, s.val(sc))
                V.tensor_scalar_add(dent[:, :], emt[:, :], 1.0, inc=(vs, 1))
                V.wait_ge(vs, s.val(vs))
                V.reciprocal(rect[:, :], dent[:, :], inc=(vs, 1))
                V.wait_ge(vs, s.val(vs))
                V.tensor_single_scalar(indt[:, :], lg[:, :], gstat[:, 2:3],
                                       ALU.is_ge, inc=(vs, 1))
                V.wait_ge(vs, s.val(vs))
                V.tensor_tensor(wgt[:, :], et[:, :], indt[:, :], ALU.mult,
                                inc=(vs, 1))
                V.wait_ge(vs, s.val(vs))
                V.tensor_single_scalar(wgt[:, :], wgt[:, :], rect[:, 2:3],
                                       ALU.mult, inc=(vs, 1))
                wg_done = s.val(vs)
                bank_wait(T, ps[3]) if tt_ == 0 else None
                T.wait_ge(vs, wg_done)
                T.transpose(ps[3][0:8, 128 * tt_:128 * (tt_ + 1)], wgt[:, :],
                            cst["ident_f"][:], inc=(tn, 1))
            V.wait_ge(tn, s.val(tn))
            V.tensor_copy(out=wgT_sb[:, :], in_=ps[3][0:8, :], inc=(vs, 1))
            bank_done(ps[3], vs)
            bank_wait(T, ps[4])
            T.wait_ge(vs, s.val(vs))
            T.matmul(ps[4][:], cst["e_expand"][:], wgT_sb[:, :], start=True,
                     stop=True, inc=(tn, 1))
            bank_wait(T, ps[5])
            for c in range(NCH):
                mm3(ps[5][:], ath[:, 128 * c:128 * (c + 1)],
                    atl[:, 128 * c:128 * (c + 1)],
                    xh_t[:, 512 * c:512 * (c + 1)],
                    xl_t[:, 512 * c:512 * (c + 1)],
                    start=(c == 0), stop=(c == NCH - 1),
                    inc=(tn, 1) if c == NCH - 1 else None)
            V.wait_ge(tn, s.val(tn))
            V.tensor_copy(out=xa_f[:, :], in_=ps[5][:])
            V.tensor_tensor(xa_f[:, :], xa_f[:, :], ps[4][:], ALU.mult)
            split_hilo(xa_f[:, :], sxh[:, :], sxl[:, :], inc=(vs, 1))
            bank_done(ps[4], vs)
            bank_done(ps[5], vs)
            sxa_done = s.val(vs)

            ws = WStream(wffn_img, NCH * 128, WSF)
            for oc in range(NCH):
                b = ws.load(oc)
                ws.wait(T, b)
                pt = ps[oc % 2]
                bank_wait(T, pt)
                T.wait_ge(vs, sxa_done)
                nolora = DEV.get("nolora", False)
                for c in range(NCH):
                    lh, ll = ws.tiles(b, c)
                    mm3(pt[:], lh, ll,
                        xh_t[:, 512 * c:512 * (c + 1)],
                        xl_t[:, 512 * c:512 * (c + 1)],
                        start=(c == 0),
                        stop=(nolora and c == NCH - 1),
                        inc=(tn, 1) if (nolora and c == NCH - 1) else None)
                if not nolora:
                    mm3(pt[:], bsh[:, 128 * oc:128 * (oc + 1)],
                        bsl[:, 128 * oc:128 * (oc + 1)], sxh[:, :], sxl[:, :],
                        start=False, stop=True, inc=(tn, 1))
                ws.release(b, tn)
                mm_done = s.val(tn)
                V.wait_ge(tn, mm_done)
                ysl = y_tb[:, 512 * oc:512 * (oc + 1)]
                V.scalar_tensor_tensor(
                    ysl, pt[:], cst["bffn"][:, oc:oc + 1],
                    R[:, SH * oc + 512 * tb:SH * oc + 512 * (tb + 1)],
                    ALU.add, ALU.add, inc=(vs, 1))
                bank_done(pt, vs)
            layer_norm_tb(
                lambda c, tb=tb: y_tb[:, 512 * c:512 * (c + 1)], tb, sqs)
        return s.val(vs)

    def emit_l1_attn():
        xh = A(XH0, 128, NCH * SH, BF16)
        xl = A(XL0, 128, NCH * SH, BF16)
        cos_e = A(CO0, 128, SH)
        sin_e = A(SI0, 128, SH)
        qtmp = A(QT0, 128, SH)
        rtmp = A(RT0, 128, SH)
        oh = A(114688, 128, SH, BF16)
        ol = A(116736, 128, SH, BF16)
        gth = A(118784, 128, 1024, BF16)
        gtl = A(120832, 128, 1024, BF16)
        kgh = A(122880, 128, 256, BF16)
        kgl = A(123392, 128, 256, BF16)
        vgh = A(123904, 128, 256, BF16)
        vgl = A(124416, 128, 256, BF16)
        cge = A(124928, 128, 64)
        sge = A(125184, 128, 64)
        gr_tmp = A(125440, 128, 64)
        gr_tmp2 = A(125696, 128, 64)

        ent_v = s.val(vs)
        SY.wait_ge(vs, ent_v)
        SY.wait_ge(tn, s.val(tn))
        SY.wait_ge(sc, s.val(sc))
        SY.dma_start(out=cos_e[:, :], in_=cos1_h, inc=(ld, 16))
        SY.dma_start(out=qtmp[:, :], in_=cos1_l, inc=(ld, 16))
        V.wait_ge(ld, s.val(ld))
        V.tensor_tensor(cos_e[:, :], cos_e[:, :], qtmp[:, :], ALU.add, inc=(vs, 1))
        SY.wait_ge(vs, s.val(vs))
        SY.dma_start(out=sin_e[:, :], in_=sin1_h, inc=(ld, 16))
        SY.dma_start(out=qtmp[:, :], in_=sin1_l, inc=(ld, 16))
        V.wait_ge(ld, s.val(ld))
        V.tensor_tensor(sin_e[:, :], sin_e[:, :], qtmp[:, :], ALU.add, inc=(vs, 1))
        SY.wait_ge(vs, s.val(vs))
        SY.dma_start(out=cge[:, :], in_=cos1g, inc=(ld, 16))
        SY.dma_start(out=gr_tmp[:, :], in_=cos1g_l, inc=(ld, 16))
        V.wait_ge(ld, s.val(ld))
        V.tensor_tensor(cge[:, :], cge[:, :], gr_tmp[:, :], ALU.add, inc=(vs, 1))
        SY.wait_ge(vs, s.val(vs))
        SY.dma_start(out=sge[:, :], in_=sin1g, inc=(ld, 16))
        SY.dma_start(out=gr_tmp[:, :], in_=sin1g_l, inc=(ld, 16))
        V.wait_ge(ld, s.val(ld))
        V.tensor_tensor(sge[:, :], sge[:, :], gr_tmp[:, :], ALU.add, inc=(vs, 1))
        SY.wait_ge(vs, s.val(vs))
        SY.dma_start(out=qtmp[:, :].rearrange("p (c g) -> p c g", c=16),
                     in_=gt_img.rearrange("c p g -> p c g"), inc=(ld, 16))
        V.wait_ge(ld, s.val(ld))
        split_hilo(qtmp[:, :], gth[:, :], gtl[:, :], inc=(vs, 1))
        for c in range(NCH):
            split_hilo(R[:, SH * c:SH * (c + 1)],
                       xh[:, SH * c:SH * (c + 1)], xl[:, SH * c:SH * (c + 1)],
                       inc=(vs, 1) if c == NCH - 1 else None)
        xs_done = s.val(vs)
        GQ.wait_ge(vs, xs_done)
        GQ.dma_start(out=r_save.ap().rearrange("(c p) t -> p c t", p=128),
                     in_=R[:].rearrange("p (c t) -> p c t", c=NCH), inc=(gq, 16))
        rspill = s.val(gq)

        q1f = q1_save.ap()
        ag1 = ag1_in.ap()
        ws_q = WStream(wq1_img, NCH * 128, WS0)
        for hh in range(H):
            done = proj_head(ws_q, hh, cos_e, sin_e, oh, ol, xh, xl, True,
                             qtmp, rtmp)
            GQ.wait_ge(vs, done)
            GQ.dma_start(out=q1f[128 * hh:128 * (hh + 1), :], in_=oh[:, :],
                         inc=(gq, 16))
            GQ.dma_start(out=q1f[128 * (16 + hh):128 * (17 + hh), :],
                         in_=ol[:, :], inc=(gq, 16))
            V.wait_ge(gq, s.val(gq))
        ws_k = WStream(wk1_img, NCH * 128, WS0)
        for g in range(NKV):
            b = ws_k.load(g)
            ws_k.wait(T, b)
            bank_wait(T, ps[2])
            for c in range(NCH):
                lh, ll = ws_k.tiles(b, c)
                mm3(ps[2][:, 0:64], lh, ll, gth[:, 64 * c:64 * (c + 1)],
                    gtl[:, 64 * c:64 * (c + 1)], start=(c == 0),
                    stop=(c == NCH - 1), inc=(tn, 1) if c == NCH - 1 else None)
            V.wait_ge(tn, s.val(tn))
            p64 = ps[2][:, 0:64]
            V.tensor_scalar_mul(gr_tmp[0:64, 0:64], p64[64:128, :], -1.0)
            V.tensor_copy(out=gr_tmp[64:128, 0:64], in_=p64[0:64, :],
                          inc=(vs, 1))
            V.wait_ge(vs, s.val(vs))
            V.tensor_tensor(gr_tmp2[:, :], p64, cge[:, :], ALU.mult)
            V.tensor_tensor(gr_tmp[:, 0:64], gr_tmp[:, 0:64], sge[:, :],
                            ALU.mult, inc=(vs, 1))
            V.wait_ge(vs, s.val(vs))
            V.tensor_tensor(gr_tmp2[:, :], gr_tmp2[:, :], gr_tmp[:, 0:64],
                            ALU.add, inc=(vs, 1))
            V.wait_ge(vs, s.val(vs))
            split_hilo(gr_tmp2[:, :], kgh[:, 64 * g:64 * (g + 1)],
                       kgl[:, 64 * g:64 * (g + 1)], inc=(vs, 1))
            bank_done(ps[2], vs)
            for tb in range(NTB):
                pt = ps[tb]
                bank_wait(T, pt)
                for c in range(NCH):
                    lh, ll = ws_k.tiles(b, c)
                    rh = xh[:, SH * c + 512 * tb: SH * c + 512 * (tb + 1)]
                    rl = xl[:, SH * c + 512 * tb: SH * c + 512 * (tb + 1)]
                    mm3(pt[:], lh, ll, rh, rl, start=(c == 0),
                        stop=(c == NCH - 1),
                        inc=(tn, 1) if c == NCH - 1 else None)
                V.wait_ge(tn, s.val(tn))
                sl = slice(512 * tb, 512 * (tb + 1))
                rope_emit(pt[:], cos_e, sin_e, rtmp, qtmp, tb)
                split_hilo(qtmp[:, sl], oh[:, sl], ol[:, sl], inc=(vs, 1))
                bank_done(pt, vs)
            ws_k.release(b, vs)
            GQ.wait_ge(vs, s.val(vs))
            GQ.dma_start(out=ag1[128 * g:128 * (g + 1), :], in_=oh[:, :],
                         inc=(gq, 16))
            GQ.dma_start(out=ag1[128 * (4 + g):128 * (5 + g), :], in_=ol[:, :],
                         inc=(gq, 16))
            V.wait_ge(gq, s.val(gq))
        ws_v = WStream(wv1_img, NCH * 128, WS0)
        for g in range(NKV):
            b = ws_v.load(g)
            ws_v.wait(T, b)
            bank_wait(T, ps[2])
            for c in range(NCH):
                lh, ll = ws_v.tiles(b, c)
                mm3(ps[2][:, 0:64], lh, ll, gth[:, 64 * c:64 * (c + 1)],
                    gtl[:, 64 * c:64 * (c + 1)], start=(c == 0),
                    stop=(c == NCH - 1), inc=(tn, 1) if c == NCH - 1 else None)
            V.wait_ge(tn, s.val(tn))
            split_hilo(ps[2][:, 0:64], vgh[:, 64 * g:64 * (g + 1)],
                       vgl[:, 64 * g:64 * (g + 1)], inc=(vs, 1))
            bank_done(ps[2], vs)
            for tb in range(NTB):
                pt = ps[tb]
                bank_wait(T, pt)
                for c in range(NCH):
                    lh, ll = ws_v.tiles(b, c)
                    rh = xh[:, SH * c + 512 * tb: SH * c + 512 * (tb + 1)]
                    rl = xl[:, SH * c + 512 * tb: SH * c + 512 * (tb + 1)]
                    mm3(pt[:], lh, ll, rh, rl, start=(c == 0),
                        stop=(c == NCH - 1),
                        inc=(tn, 1) if c == NCH - 1 else None)
                V.wait_ge(tn, s.val(tn))
                sl = slice(512 * tb, 512 * (tb + 1))
                split_hilo(pt[:], oh[:, sl], ol[:, sl], inc=(vs, 1))
                bank_done(pt, vs)
            ws_v.release(b, vs)
            done = s.val(vs)
            GQ.wait_ge(vs, done)
            GQ.dma_start(out=ag1[128 * (8 + g):128 * (9 + g), :], in_=oh[:, :],
                         inc=(gq, 16))
            GQ.dma_start(out=ag1[128 * (12 + g):128 * (13 + g), :],
                         in_=ol[:, :], inc=(gq, 16))
            V.wait_ge(gq, s.val(gq))
        if dbg == "kg":
            V.tensor_copy(out=qtmp[:, 0:256], in_=kgh[:, :])
            V.tensor_copy(out=qtmp[:, 256:512], in_=kgl[:, :])
            V.tensor_copy(out=qtmp[:, 512:768], in_=vgh[:, :])
            V.tensor_copy(out=qtmp[:, 768:1024], in_=vgl[:, :], inc=(vs, 1))
            GQ.wait_ge(vs, s.val(vs))
            GQ.dma_start(out=dbg_out[0:128, :], in_=qtmp[:, :], inc=(gq, 16))
        GQ.wait_ge(gq, s.val(gq))
        GQ.collective_compute("AllGather", ALU.bypass, replica_groups=PAIRS,
                              ins=[ag1_in.ap().opt()],
                              outs=[ag1_out.ap().opt()], inc=(cc, 1))
        ag_ready = s.val(cc)

        k1h = A(0, 128, 4 * E1P, BF16)
        k1l = A(17408, 128, 4 * E1P, BF16)
        v1h = A(34816, 128, E1P, BF16)
        v1l = A(39168, 128, E1P, BF16)
        vt1h = A(43520, 128, 4 * E1P, BF16)
        vt1l = A(60928, 128, 4 * E1P, BF16)
        maskb = A(78336, 128, E1P, BF16)
        jr = A(82688, 128, E1P)
        jb = A(91392, 128, E1P)
        P1 = A(100096, 128, E1P)
        pth = A(108800, 128, E1P, BF16)
        ptl = A(113152, 128, E1P, BF16)
        qh = A(117504, 128, 128, BF16)
        ql = A(117504 + 256, 128, 128, BF16)
        tmp2 = A(118016, 128, E1P)
        ag1o = ag1_out.ap()
        SY.wait_ge(cc, ag_ready)
        GQ.wait_ge(cc, ag_ready)
        SY.dma_start(out=jr[:, :], in_=jrow_in, inc=(ld, 16))
        SY.dma_start(out=jb[:, :], in_=jbig_in, inc=(ld, 16))
        jld = s.val(ld)
        V.wait_ge(ld, jld)
        for g in range(NKV):
            for (dst, srow, gsb) in ((k1h, g, kgh), (k1l, 4 + g, kgl)):
                base = E1P * g
                V.tensor_copy(out=dst[:, base:base + 64],
                              in_=gsb[:, 64 * g:64 * (g + 1)])
                V.memset(dst[:, base + 2112:base + 2176], 0.0, inc=(vs, 1))
                SY.wait_ge(vs, s.val(vs))
                for r in range(2):
                    SY.dma_start(
                        out=dst[:, base + 64 + 1024 * r:base + 64 + 1024 * (r + 1)],
                        in_=ag1o[2048 * r + 128 * srow:2048 * r + 128 * (srow + 1), :],
                        inc=(ld, 16))
            prev = s.val(vs)
            SY.wait_ge(vs, prev)
            for (dst, srow, gsb) in ((v1h, 8 + g, vgh), (v1l, 12 + g, vgl)):
                V.tensor_copy(out=dst[:, 0:64], in_=gsb[:, 64 * g:64 * (g + 1)])
                V.memset(dst[:, 2112:2176], 0.0, inc=(vs, 1))
                SY.wait_ge(vs, s.val(vs))
                for r in range(2):
                    SY.dma_start(
                        out=dst[:, 64 + 1024 * r:64 + 1024 * (r + 1)],
                        in_=ag1o[2048 * r + 128 * srow:2048 * r + 128 * (srow + 1), :],
                        inc=(ld, 16))
            v_ld = s.val(ld)
            T.wait_ge(ld, v_ld)
            T.wait_ge(vs, s.val(vs))
            ps_tb = ps[5][:, 0:256].bitcast(BF16)
            for (src_t, dst_t) in ((v1h, vt1h), (v1l, vt1l)):
                for grp in range(5):
                    n_in = 4 if grp < 4 else 1
                    bank_wait(T, ps[5])
                    for i in range(n_in):
                        c = 4 * grp + i
                        T.transpose(ps_tb[:, 128 * i:128 * (i + 1)],
                                    src_t[:, 128 * c:128 * (c + 1)],
                                    cst["ident_b"][:],
                                    inc=(tn, 1) if i == n_in - 1 else None)
                    V.wait_ge(tn, s.val(tn))
                    V.tensor_copy(
                        out=dst_t[:, E1P * g + 512 * grp:
                                  E1P * g + 512 * grp + 128 * n_in],
                        in_=ps_tb[:, 0:128 * n_in], inc=(vs, 1))
                    bank_done(ps[5], vs)
        kv_done = s.val(vs)
        kv_ld = s.val(ld)

        qbase_col = cst["arange_col"]
        for qt in range(8):
            V.wait_ge(ld, jld)
            V.tensor_scalar_add(gstat[:, 32:40], qbase_col[:], float(128 * qt),
                                inc=(vs, 1))
            V.wait_ge(vs, s.val(vs))
            V.tensor_scalar_add(gstat[:, 40:48], gstat[:, 32:40], -512.0,
                                inc=(vs, 1))
            V.wait_ge(vs, s.val(vs))
            V.tensor_single_scalar(P1[:, :], jr[:, :], gstat[:, 32:33],
                                   ALU.is_le)
            V.tensor_single_scalar(tmp2[:, :], jb[:, :], gstat[:, 40:41],
                                   ALU.is_ge)
            V.tensor_tensor(P1[:, :], P1[:, :], tmp2[:, :], ALU.mult)
            V.tensor_scalar(maskb[:, :], P1[:, :], MASKV, -MASKV, ALU.mult,
                            ALU.add, inc=(vs, 1))
            mask_done = s.val(vs)
            for hq in range(H):
                g = hq // 4
                kb_ = E1P * g
                SY.wait_ge(vs, s.val(vs))
                SY.dma_start(out=qh[:, :],
                             in_=q1f[128 * hq:128 * (hq + 1),
                                     128 * qt:128 * (qt + 1)], inc=(ld, 16))
                SY.dma_start(out=ql[:, :],
                             in_=q1f[128 * (16 + hq):128 * (17 + hq),
                                     128 * qt:128 * (qt + 1)], inc=(ld, 16))
                q_ld = s.val(ld)
                T.wait_ge(ld, q_ld)
                T.wait_ge(vs, mask_done)
                for kb in range(5):
                    n = 512 if kb < 4 else 128
                    pt = ps[kb]
                    bank_wait(T, pt)
                    mm3(pt[:, 0:n], qh[:, :], ql[:, :],
                        k1h[:, kb_ + 512 * kb:kb_ + 512 * kb + n],
                        k1l[:, kb_ + 512 * kb:kb_ + 512 * kb + n],
                        start=True, stop=True,
                        inc=(tn, 1) if kb == 4 else None)
                V.wait_ge(tn, s.val(tn))
                for kb in range(5):
                    n = 512 if kb < 4 else 128
                    V.tensor_tensor(P1[:, 512 * kb:512 * kb + n],
                                    ps[kb][:, 0:n],
                                    maskb[:, 512 * kb:512 * kb + n], ALU.add,
                                    inc=(vs, 1) if kb == 4 else None)
                for kb in range(5):
                    bank_done(ps[kb], vs)
                V.wait_ge(vs, s.val(vs))
                V.reduce_max(gstat[:, 4:5], P1[:, :], AX.X, negate=True,
                             inc=(vs, 1))
                V.wait_ge(vs, s.val(vs))
                V.tensor_scalar_mul(rstat[:, 0:8], gstat[:, 0:8], SCALE1,
                                    inc=(vs, 1))
                C.wait_ge(vs, s.val(vs))
                C.activation(out=P1[:, :], in_=P1[:, :], func=AF.Exp,
                             bias=rstat[:, 4:5], scale=SCALE1, inc=(sc, 1))
                V.wait_ge(sc, s.val(sc))
                V.reduce_sum(gstat[:, 8:9], P1[:, :], AX.X, inc=(vs, 1))
                V.wait_ge(vs, s.val(vs))
                V.reciprocal(rstat[:, 8:16], gstat[:, 8:16], inc=(vs, 1))
                V.wait_ge(vs, s.val(vs))
                V.tensor_scalar_mul(P1[:, :], P1[:, :], rstat[:, 8:9],
                                    inc=(vs, 1))
                pn_done = s.val(vs)
                T.wait_ge(vs, pn_done)
                ps4_r = ps[4][:].rearrange("p (i q) -> p i q", i=4)
                for grp in range(5):
                    n_in = 4 if grp < 4 else 1
                    bank_wait(T, ps[4])
                    for i in range(n_in):
                        c = 4 * grp + i
                        T.transpose(ps[4][:, 128 * i:128 * (i + 1)],
                                    P1[:, 128 * c:128 * (c + 1)],
                                    cst["ident_f"][:],
                                    inc=(tn, 1) if i == n_in - 1 else None)
                    tr_done = s.val(tn)
                    V.wait_ge(tn, tr_done)
                    C.wait_ge(tn, tr_done)
                    oh_ap = pth[:, 512 * grp:512 * grp + 128 * n_in]
                    ol_ap = ptl[:, 512 * grp:512 * grp + 128 * n_in]
                    C.activation(out=oh_ap, in_=ps[4][:, 0:128 * n_in],
                                 func=AF.Copy, inc=(sc, 1))
                    V.wait_ge(sc, s.val(sc))
                    V.tensor_tensor(ol_ap, ps[4][:, 0:128 * n_in], oh_ap,
                                    ALU.subtract, inc=(vs, 1))
                    bank_done(ps[4], vs)
                pt0 = ps[0]
                bank_wait(T, pt0)
                T.wait_ge(vs, s.val(vs))
                T.wait_ge(vs, kv_done)
                for c in range(KC1):
                    mm3(pt0[:, 0:128],
                        vt1h[:, kb_ + 128 * c:kb_ + 128 * (c + 1)],
                        vt1l[:, kb_ + 128 * c:kb_ + 128 * (c + 1)],
                        pth[:, 128 * c:128 * (c + 1)],
                        ptl[:, 128 * c:128 * (c + 1)],
                        start=(c == 0), stop=(c == KC1 - 1),
                        inc=(tn, 1) if c == KC1 - 1 else None)
                pv_done = s.val(tn)
                off = SH * hq + 128 * qt
                C.wait_ge(tn, pv_done)
                C.activation(out=attn_hi[:, off:off + 128], in_=pt0[:, 0:128],
                             func=AF.Copy, inc=(sc, 1))
                V.wait_ge(sc, s.val(sc))
                V.tensor_tensor(attn_lo[:, off:off + 128], pt0[:, 0:128],
                                attn_hi[:, off:off + 128], ALU.subtract,
                                inc=(vs, 1))
                bank_done(pt0, vs)
        return s.val(vs)

    def quant_pipe(src_ap, out_ap, sinv_col, w, qa, qb, qt3, qt4, qt5,
                   signed=True):
        if signed:
            V.tensor_single_scalar(qa[:, 0:w], src_ap, sinv_col, ALU.mult)
            V.tensor_scalar_mul(qb[:, 0:w], qa[:, 0:w], -1.0)
            V.tensor_tensor(qb[:, 0:w], qb[:, 0:w], qa[:, 0:w], ALU.max)
        else:
            V.tensor_single_scalar(qb[:, 0:w], src_ap, sinv_col, ALU.mult)
        V.tensor_single_scalar(qt3[:, 0:w], qb[:, 0:w], 2.0 ** -6, ALU.max,
                               inc=(vs, 1))
        C.wait_ge(vs, s.val(vs))
        C.activation(out=qt3[:, 0:w], in_=qt3[:, 0:w], func=AF.Ln, inc=(sc, 1))
        V.wait_ge(sc, s.val(sc))
        V.tensor_scalar(qt3[:, 0:w], qt3[:, 0:w], LN2_INV, 63.5, ALU.mult,
                        ALU.add)
        V.tensor_copy(out=qt4[:, 0:w], in_=qt3[:, 0:w])
        V.tensor_copy(out=qt3[:, 0:w], in_=qt4[:, 0:w])
        V.tensor_scalar(qt3[:, 0:w], qt3[:, 0:w], -1.0, 194.0, ALU.mult,
                        ALU.add)
        V.tensor_copy(out=qt5[:, 0:w], in_=qt3[:, 0:w])
        V.tensor_single_scalar(qt5[:, 0:w], qt5[:, 0:w], 23,
                               ALU.arith_shift_left)
        V.tensor_scalar_add(qt4[:, 0:w], qt4[:, 0:w], 60)
        V.tensor_single_scalar(qt4[:, 0:w], qt4[:, 0:w], 23,
                               ALU.arith_shift_left)
        V.tensor_tensor(qb[:, 0:w], qb[:, 0:w], qt5[:, 0:w].bitcast(F32),
                        ALU.mult)
        V.tensor_copy(out=qt5[:, 0:w], in_=qb[:, 0:w])
        V.tensor_copy(out=qb[:, 0:w], in_=qt5[:, 0:w])
        V.tensor_tensor(qb[:, 0:w], qb[:, 0:w], qt4[:, 0:w].bitcast(F32),
                        ALU.mult)
        if signed:
            V.tensor_scalar(qa[:, 0:w], qa[:, 0:w], 0.0, 2.0, ALU.is_ge,
                            ALU.mult)
            V.tensor_scalar_add(qa[:, 0:w], qa[:, 0:w], -1.0)
            V.tensor_tensor(out_ap, qb[:, 0:w], qa[:, 0:w], ALU.mult,
                            inc=(vs, 1))
        else:
            V.tensor_copy(out=out_ap, in_=qb[:, 0:w], inc=(vs, 1))

    def pmax_gather(col_src_ap, grow_sl, srow_col, tag):
        GQ.wait_ge(vs, s.val(vs))
        GQ.dma_start(out=gcol.ap()[:, 0:1], in_=col_src_ap, inc=(gq, 16))
        GQ.wait_ge(gq, s.val(gq))
        GQ.dma_start(out=grow[0:1, grow_sl],
                     in_=gcol.ap()[:, 0:1].rearrange("p o -> o p"),
                     inc=(gq, 16))
        V.wait_ge(gq, s.val(gq))
        V.reduce_max(grow[0:1, srow_col:srow_col + 1],
                     grow[0:1, grow_sl], AX.X, inc=(vs, 1))
        V.wait_ge(vs, s.val(vs))

    def emit_l1_ffn():
        X80 = 0
        WF0 = 32768
        W80 = 40960
        QA0, QB0, QT3 = 49152, 53248, 57344
        QT4, QT5 = 61440, 65536
        HS0 = 69632
        H20 = 70144
        H80 = 74240
        WP0 = 76288
        x8 = A(X80, 128, NCH * SH, BF16)
        wf = [A(WF0, 128, 1024), A(WF0 + 4096, 128, 1024)]
        w8b = [A(W80, 128, 2048, BF16), A(W80 + 4096, 128, 2048, BF16)]
        qa = A(QA0, 128, 1024)
        qb = A(QB0, 128, 1024)
        qt3 = A(QT3, 128, 1024)
        qt4 = A(QT4, 128, 1024, I32)
        qt5 = A(QT5, 128, 1024, I32)
        hstat = A(HS0, 128, 128)
        h2s = [A(H20, 128, 512), A(H20 + 2048, 128, 512)]
        h8b = [A(H80, 128, 512, BF16), A(H80 + 1024, 128, 512, BF16)]
        wpb = [A(WP0, 128, 512, BF16), A(WP0 + 1024, 128, 512, BF16)]
        agwo = agw_out.ap()
        agpo = agp_out.ap()
        h2sv = h2_save.ap()
        h8sv = h8_save.ap()

        ent_v = s.val(vs)
        SY.wait_ge(vs, ent_v)
        SY.wait_ge(tn, s.val(tn))
        SY.wait_ge(sc, s.val(sc))
        V.memset(grow[:], 0.0)
        V.memset(hstat[:, :], 0.0, inc=(vs, 1))

        for c in range(NCH):
            V.reduce_max(hstat[:, 24 + c:25 + c], R[:, SH * c:SH * (c + 1)],
                         AX.X, apply_absolute_value=True,
                         inc=(vs, 1) if c == NCH - 1 else None)
        V.wait_ge(vs, s.val(vs))
        V.reduce_max(hstat[:, 40:41], hstat[:, 24:40], AX.X, inc=(vs, 1))
        pmax_gather(hstat[:, 40:41], slice(264, 392), 2, "x")
        wfree = [(vs, s.val(vs)), (vs, s.val(vs))]
        for j in range(8):
            for hf in range(2):
                b = (2 * j + hf) % 2
                SY.wait_ge(*wfree[b])
                SY.dma_start(out=wf[b][:, :],
                             in_=wfc_shard[j:j + 1, :, 1024 * hf:1024 * (hf + 1)]
                             .rearrange("o p n -> p (o n)"), inc=(wl, 16))
                V.wait_ge(wl, s.val(wl))
                V.reduce_max(hstat[:, 2 * j + hf:2 * j + hf + 1], wf[b][:, :],
                             AX.X, apply_absolute_value=True, inc=(vs, 1))
                wfree[b] = (vs, s.val(vs))
        V.wait_ge(vs, s.val(vs))
        V.reduce_max(hstat[:, 41:42], hstat[:, 0:16], AX.X, inc=(vs, 1))
        pmax_gather(hstat[:, 41:42], slice(8, 136), 0, "wfc")
        wfree = [(vs, s.val(vs)), (vs, s.val(vs))]
        for j in range(2):
            for pc in range(8):
                b = (8 * j + pc) % 2
                SY.wait_ge(*wfree[b])
                SY.dma_start(out=wf[b][:, :],
                             in_=wproj_shard[j:j + 1, :,
                                             1024 * pc:1024 * (pc + 1)]
                             .rearrange("o p n -> p (o n)"), inc=(wl, 16))
                V.wait_ge(wl, s.val(wl))
                V.reduce_max(hstat[:, 48 + 8 * j + pc:49 + 8 * j + pc],
                             wf[b][:, :], AX.X, apply_absolute_value=True,
                             inc=(vs, 1))
                wfree[b] = (vs, s.val(vs))
        V.wait_ge(vs, s.val(vs))
        V.reduce_max(hstat[:, 42:43], hstat[:, 48:64], AX.X, inc=(vs, 1))
        pmax_gather(hstat[:, 42:43], slice(136, 264), 1, "wproj")
        GQ.wait_ge(vs, s.val(vs))
        GQ.dma_start(out=ar_in.ap(), in_=grow[0:1, 0:8], inc=(gq, 16))
        GQ.wait_ge(gq, s.val(gq))
        GQ.collective_compute("AllReduce", ALU.max, replica_groups=ALL8,
                              ins=[ar_in.ap().opt()], outs=[ar_out.ap().opt()],
                              inc=(cc, 1))
        GQ.wait_ge(cc, s.val(cc))
        GQ.dma_start(out=grow[0:1, 392:400], in_=ar_out.ap(), inc=(gq, 16))
        bank_wait(T, ps[4])
        T.wait_ge(gq, s.val(gq))
        T.matmul(ps[4][:, 0:8], cst["ones_row"][:], grow[0:1, 392:400],
                 start=True, stop=True, inc=(tn, 1))
        V.wait_ge(tn, s.val(tn))
        V.tensor_copy(out=sstat[:, 0:8], in_=ps[4][:, 0:8], inc=(vs, 1))
        bank_done(ps[4], vs)
        V.wait_ge(vs, s.val(vs))
        V.tensor_scalar(sstat[:, 8:16], sstat[:, 0:8], 1.0 / FP8_MAX, 1e-6,
                        ALU.mult, ALU.max, inc=(vs, 1))
        V.wait_ge(vs, s.val(vs))
        V.reciprocal(rstat[:, 16:24], sstat[:, 8:16], inc=(vs, 1))
        V.wait_ge(vs, s.val(vs))
        V.tensor_single_scalar(gstat[:, 48:56], sstat[:, 8:16],
                               sstat[:, 8:9], ALU.mult, inc=(vs, 1))
        V.wait_ge(vs, s.val(vs))
        agwi = agw_in.ap()
        wfree = [(vs, s.val(vs)), (vs, s.val(vs))]
        for j in range(8):
            V.wait_ge(gq, s.val(gq))
            for hf in range(2):
                b = (2 * j + hf) % 2
                SY.wait_ge(*wfree[b])
                SY.dma_start(out=wf[b][:, :],
                             in_=wfc_shard[j:j + 1, :, 1024 * hf:1024 * (hf + 1)]
                             .rearrange("o p n -> p (o n)"), inc=(wl, 16))
                V.wait_ge(wl, s.val(wl))
                quant_pipe(wf[b][:, :], w8b[0][:, 1024 * hf:1024 * (hf + 1)],
                           rstat[:, 16:17], 1024, qa, qb, qt3, qt4, qt5)
                wfree[b] = (vs, s.val(vs))
            GQ.wait_ge(vs, s.val(vs))
            GQ.dma_start(out=agwi[128 * j:128 * (j + 1), :], in_=w8b[0][:, :],
                         inc=(gq, 16))
        agpi = agp_in.ap()
        wfree = [(vs, s.val(vs)), (vs, s.val(vs))]
        w8free = [(gq, s.val(gq)), (gq, s.val(gq))]
        for j in range(2):
            for pc in range(8):
                b = pc % 2
                SY.wait_ge(*wfree[b])
                SY.dma_start(out=wf[b][:, :],
                             in_=wproj_shard[j:j + 1, :,
                                             1024 * pc:1024 * (pc + 1)]
                             .rearrange("o p n -> p (o n)"), inc=(wl, 16))
                V.wait_ge(wl, s.val(wl))
                V.wait_ge(*w8free[b])
                quant_pipe(wf[b][:, :], w8b[b][:, 0:1024],
                           rstat[:, 17:18], 1024, qa, qb, qt3, qt4, qt5)
                wfree[b] = (vs, s.val(vs))
                GQ.wait_ge(vs, s.val(vs))
                GQ.dma_start(out=agpi[128 * j:128 * (j + 1),
                                      1024 * pc:1024 * (pc + 1)],
                             in_=w8b[b][:, 0:1024], inc=(gq, 16))
                w8free[b] = (gq, s.val(gq))
        for c in range(NCH):
            quant_pipe(R[:, SH * c:SH * (c + 1)],
                       x8[:, SH * c:SH * (c + 1)], rstat[:, 18:19], 1024,
                       qa, qb, qt3, qt4, qt5)
        x8_done = s.val(vs)
        GQ.wait_ge(gq, s.val(gq))
        GQ.collective_compute("AllGather", ALU.bypass, replica_groups=ALL8,
                              ins=[agw_in.ap().opt()], outs=[agw_out.ap().opt()],
                              inc=(cc, 1))
        GQ.collective_compute("AllGather", ALU.bypass, replica_groups=ALL8,
                              ins=[agp_in.ap().opt()], outs=[agp_out.ap().opt()],
                              inc=(cc, 1))
        wg_ready = s.val(cc)
        SY.wait_ge(cc, wg_ready)

        w8_loaded = [0, 0]
        w8_freed = [(tn, s.val(tn)), (tn, s.val(tn))]
        for oc in range(64):
            b = oc % 2
            SY.wait_ge(*w8_freed[b])
            r, lj = oc // 8, oc % 8
            SY.dma_start(out=w8b[b][:, :],
                         in_=agwo[1024 * r + 128 * lj:1024 * r + 128 * (lj + 1), :],
                         inc=(wl, 16))
            w8_loaded[b] = s.val(wl)
            for tb in range(NTB):
                pt = ps[2 + tb]
                bank_wait(T, pt)
                T.wait_ge(wl, w8_loaded[b])
                T.wait_ge(vs, x8_done)
                for c in range(NCH):
                    T.matmul(pt[:], w8b[b][:, 128 * c:128 * (c + 1)],
                             x8[:, SH * c + 512 * tb:SH * c + 512 * (tb + 1)],
                             start=(c == 0), stop=(c == NCH - 1),
                             inc=(tn, 1) if c == NCH - 1 else None)
                w8_freed[b] = (tn, s.val(tn))
                hb = h2s[tb]
                C.wait_ge(tn, s.val(tn))
                C.wait_ge(gq, s.val(gq))
                C.activation(out=hb[:, :], in_=pt[:], func=AF.Relu,
                             scale=gstat[:, 50:51])
                C.activation(out=hb[:, :], in_=hb[:, :], func=AF.Square,
                             inc=(sc, 1))
                sq_done = s.val(sc)
                bank_done(pt, sc)
                V.wait_ge(sc, sq_done)
                V.reduce_max(hstat[:, 2 * oc + tb:2 * oc + tb + 1],
                             hb[:, :], AX.X, inc=(vs, 1))
                GQ.wait_ge(vs, s.val(vs))
                GQ.dma_start(out=h2sv[128 * oc:128 * (oc + 1),
                                      512 * tb:512 * (tb + 1)],
                             in_=hb[:, :], inc=(gq, 16))
        V.wait_ge(vs, s.val(vs))
        V.reduce_max(hstat[:, 40:41], hstat[:, 0:128], AX.X, inc=(vs, 1))
        V.wait_ge(vs, s.val(vs))
        V.memset(grow[0:1, 0:8], 0.0, inc=(vs, 1))
        pmax_gather(hstat[:, 40:41], slice(8, 136), 0, "h")
        GQ.wait_ge(vs, s.val(vs))
        GQ.dma_start(out=ar2_in.ap(), in_=grow[0:1, 0:8], inc=(gq, 16))
        GQ.wait_ge(gq, s.val(gq))
        GQ.collective_compute("AllReduce", ALU.max, replica_groups=ALL8,
                              ins=[ar2_in.ap().opt()], outs=[ar2_out.ap().opt()],
                              inc=(cc, 1))
        GQ.wait_ge(cc, s.val(cc))
        GQ.dma_start(out=grow[0:1, 392:400], in_=ar2_out.ap(), inc=(gq, 16))
        bank_wait(T, ps[4])
        T.wait_ge(gq, s.val(gq))
        T.matmul(ps[4][:, 0:8], cst["ones_row"][:], grow[0:1, 392:400],
                 start=True, stop=True, inc=(tn, 1))
        V.wait_ge(tn, s.val(tn))
        V.tensor_copy(out=sstat[:, 0:8], in_=ps[4][:, 0:8], inc=(vs, 1))
        bank_done(ps[4], vs)
        V.wait_ge(vs, s.val(vs))
        V.tensor_scalar(sstat[:, 0:8], sstat[:, 0:8], 1.0 / FP8_MAX, 1e-6,
                        ALU.mult, ALU.max, inc=(vs, 1))
        V.wait_ge(vs, s.val(vs))
        V.reciprocal(rstat[:, 24:32], sstat[:, 0:8], inc=(vs, 1))
        V.wait_ge(vs, s.val(vs))
        V.tensor_single_scalar(gstat[:, 56:64], sstat[:, 8:16],
                               sstat[:, 0:1], ALU.mult, inc=(vs, 1))
        V.wait_ge(vs, s.val(vs))
        h2free = [(vs, s.val(vs)), (vs, s.val(vs))]
        h8free = [(gq, s.val(gq)), (gq, s.val(gq))]
        for oc in range(64):
            for tb in range(NTB):
                b = (2 * oc + tb) % 2
                SY.wait_ge(*h2free[b])
                SY.dma_start(out=h2s[b][:, :],
                             in_=h2sv[128 * oc:128 * (oc + 1),
                                      512 * tb:512 * (tb + 1)], inc=(wl, 16))
                V.wait_ge(wl, s.val(wl))
                V.wait_ge(*h8free[b])
                quant_pipe(h2s[b][:, :], h8b[b][:, :], rstat[:, 24:25], 512,
                           qa, qb, qt3, qt4, qt5, signed=False)
                h2free[b] = (vs, s.val(vs))
                GQ.wait_ge(vs, s.val(vs))
                GQ.dma_start(out=h8sv[128 * oc:128 * (oc + 1),
                                      512 * tb:512 * (tb + 1)],
                             in_=h8b[b][:, :], inc=(gq, 16))
                h8free[b] = (gq, s.val(gq))
        y_full = A(0, 128, 16384)
        SY.wait_ge(gq, s.val(gq))
        for tb in range(NTB):
            for dcg in range(4):
                for i in range(4):
                    bank_wait(T, ps[i])
                buf_free = [(tn, s.val(tn)), (tn, s.val(tn))]
                for oc in range(64):
                    b = oc % 2
                    SY.wait_ge(*buf_free[b])
                    SY.dma_start(out=h8b[b][:, :],
                                 in_=h8sv[128 * oc:128 * (oc + 1),
                                          512 * tb:512 * (tb + 1)],
                                 inc=(wl, 16))
                    for i in range(4):
                        dc = 4 * dcg + i
                        SY.dma_start(out=wpb[b][:, 128 * i:128 * (i + 1)],
                                     in_=agpo[128 * dc:128 * (dc + 1),
                                              128 * oc:128 * (oc + 1)],
                                     inc=(wl, 16))
                    ld_t = s.val(wl)
                    T.wait_ge(wl, ld_t)
                    for i in range(4):
                        T.matmul(ps[i][:],
                                 wpb[b][:, 128 * i:128 * (i + 1)],
                                 h8b[b][:, :], start=(oc == 0),
                                 stop=(oc == 63),
                                 inc=(tn, 1) if i == 3 else None)
                    buf_free[b] = (tn, s.val(tn))
                V.wait_ge(tn, s.val(tn))
                for i in range(4):
                    dc = 4 * dcg + i
                    ysl = y_full[:, SH * dc + 512 * tb:SH * dc + 512 * (tb + 1)]
                    V.scalar_tensor_tensor(
                        ysl, ps[i][:], gstat[:, 57:58],
                        R[:, SH * dc + 512 * tb:SH * dc + 512 * (tb + 1)],
                        ALU.mult, ALU.add, inc=(vs, 1))
                    bank_done(ps[i], vs)
        sqf = [h2s[0], h2s[1]]
        for tb in range(NTB):
            layer_norm_tb(
                lambda c, tb=tb: y_full[:, SH * c + 512 * tb:
                                        SH * c + 512 * (tb + 1)], tb, sqf)
        return s.val(vs)

    r_ready = s.val(vs)
    if "l0proj" in phases:
        agr = emit_l0_proj()
        if dbg == "ag0":
            GQ.wait_ge(cc, agr)
            GQ.dma_start(out=dbg_out, in_=ag0_out.ap(), inc=(gq, 16))
    if "l0core" in phases:
        agr = emit_l0_proj()
        attn_done = emit_l0_core(agr)
        if dbg == "attn_raw":
            GQ.wait_ge(vs, attn_done)
            GQ.wait_ge(sc, s.val(sc))
            GQ.dma_start(out=dbg_out.rearrange("(c p) t -> p c t", p=128),
                         in_=R[:].rearrange("p (c t) -> p c t", c=NCH),
                         inc=(gq, 16))
    if "l0attn" in phases:
        agr = emit_l0_proj()
        attn_done = emit_l0_core(agr)
        if dbg == "attn_raw":
            GQ.wait_ge(vs, attn_done)
            GQ.wait_ge(sc, s.val(sc))
            GQ.dma_start(out=dbg_out.rearrange("(c p) t -> p c t", p=128),
                         in_=R[:].rearrange("p (c t) -> p c t", c=NCH),
                         inc=(gq, 16))
        else:
            r_ready = emit_outproj_ln(wout_img, "x", attn_done)

    if "l0ffn" in phases:
        r_ready = emit_l0_ffn()
    if "l1attn" in phases:
        attn1_done = emit_l1_attn()
        r_ready = emit_outproj_ln(wo1_img, "rsave", attn1_done)
    if "l1ffn" in phases:
        r_ready = emit_l1_ffn()
    if dbg == "ffnstats":
        GQ.wait_ge(vs, s.val(vs))
        GQ.wait_ge(sc, s.val(sc))
        GQ.dma_start(out=dbg_out[0:128, 0:16], in_=sstat[:], inc=(gq, 16))
        GQ.dma_start(out=dbg_out[128:256, 0:64], in_=rstat[:], inc=(gq, 16))
        GQ.dma_start(out=dbg_out[256:384, 0:64], in_=gstat[:], inc=(gq, 16))
        GQ.dma_start(out=dbg_out[384:385, 0:1024], in_=grow[0:1, 0:1024],
                     inc=(gq, 16))
        GQ.dma_start(out=dbg_out[512:640, :], in_=h2_save.ap()[0:128, :],
                     inc=(gq, 16))
        GQ.dma_start(out=dbg_out[640:768, :], in_=h2_save.ap()[1024:1152, :],
                     inc=(gq, 16))
    if dbg == "r_l1attn":
        GQ.wait_ge(vs, r_ready)
        GQ.dma_start(out=dbg_out.rearrange("(c p) t -> p c t", p=128),
                     in_=R[:].rearrange("p (c t) -> p c t", c=NCH), inc=(gq, 16))
    if dbg == "r_l0ffn":
        GQ.wait_ge(vs, r_ready)
        GQ.dma_start(out=dbg_out.rearrange("(c p) t -> p c t", p=128),
                     in_=R[:].rearrange("p (c t) -> p c t", c=NCH), inc=(gq, 16))
    if dbg == "r_l0attn":
        GQ.wait_ge(vs, r_ready)
        GQ.dma_start(out=dbg_out.rearrange("(c p) t -> p c t", p=128),
                     in_=R[:].rearrange("p (c t) -> p c t", c=NCH), inc=(gq, 16))

    GQ.wait_ge(vs, s.val(vs))
    GQ.wait_ge(sc, s.val(sc))
    GQ.wait_ge(tn, s.val(tn))
    GQ.dma_start(out=y_out.rearrange("(c p) t -> p c t", p=128),
                 in_=R[:].rearrange("p (c t) -> p c t", c=NCH), inc=(gq, 16))
    GQ.wait_ge(gq, s.val(gq))

    with nc.Block() as block:
        @block.sync
        def _(h):
            for fn in s.q["sync"]:
                fn(h)

        @block.tensor
        def _(h):
            for fn in s.q["tensor"]:
                fn(h)

        @block.vector
        def _(h):
            for fn in s.q["vector"]:
                fn(h)

        @block.scalar
        def _(h):
            for fn in s.q["scalar"]:
                fn(h)

        @block.gpsimd
        def _(h):
            for fn in s.q["gpsimd"]:
                fn(h)

    ctx.close()
    return nc


def make_inputs(core, inputs):
    b, hh = core // 2, core % 2
    tsl = slice(SH * hh, SH * (hh + 1))
    x = np.asarray(inputs["L0_attn_x"])
    d = {}
    d["x_fm"] = np.ascontiguousarray(x[b].T[:, tsl])

    wqkvT = np.ascontiguousarray(np.asarray(inputs["L0_attn_w_qkv"]).T)
    d["q0_img"] = tile_lhsT(wqkvT[:, 0:2048])
    d["k0_img"] = tile_lhsT(wqkvT[:, 2048:4096])
    d["v0_img"] = tile_lhsT(wqkvT[:, 4096:6144])
    d["wout_img"] = tile_lhsT(np.ascontiguousarray(
        np.asarray(inputs["L0_attn_w_out"]).T))

    pos0 = np.arange(SH * hh, SH * (hh + 1))
    (ch, cl), (sh_, sl_) = rope_tables(pos0, HD, 10000.0)
    d["cos0_hi"], d["cos0_lo"] = ch, cl
    d["sin0_hi"], d["sin0_lo"] = sh_, sl_

    d["wffn_img"] = tile_lhsT(np.ascontiguousarray(np.asarray(inputs["L0_ffn_W"]).T))
    A_ = np.asarray(inputs["L0_ffn_lora_A"])
    d["aT_img"] = tile_lhsT(np.ascontiguousarray(
        A_.transpose(2, 0, 1).reshape(2048, 128)))
    B_ = np.asarray(inputs["L0_ffn_lora_B"])
    d["bstack_img"] = tile_lhsT(np.ascontiguousarray(
        B_.transpose(0, 2, 1).reshape(128, 2048)))
    d["gate_w"] = np.ascontiguousarray(
        np.asarray(inputs["L0_ffn_gate_w"]).T).reshape(16, 128, 8)
    d["gate_b"] = np.ascontiguousarray(np.broadcast_to(
        np.asarray(inputs["L0_ffn_gate_b"])[None, :], (128, 8))).astype(np.float32)
    d["bffn"] = np.ascontiguousarray(
        np.asarray(inputs["L0_ffn_b"]).reshape(16, 128).T)
    ee = np.zeros((8, 128), np.float32)
    for e in range(8):
        ee[e, 16 * e:16 * (e + 1)] = 1.0
    d["e_expand"] = ee

    d["wq1_img"] = tile_lhsT(np.asarray(inputs["L1_attn_Wq"]))
    d["wk1_img"] = tile_lhsT(np.asarray(inputs["L1_attn_Wk"]))
    d["wv1_img"] = tile_lhsT(np.asarray(inputs["L1_attn_Wv"]))
    d["wo1_img"] = tile_lhsT(np.asarray(inputs["L1_attn_Wo"]))
    d["gt_img"] = np.ascontiguousarray(
        np.asarray(inputs["L1_attn_global_tokens"]).T).reshape(16, 128, G)
    pos1 = G + SH * hh + np.arange(SH)
    (c1h, c1l), (s1h, s1l) = rope_tables(pos1, HD, 500000.0)
    d["cos1_hi"], d["cos1_lo"] = c1h, c1l
    d["sin1_hi"], d["sin1_lo"] = s1h, s1l
    (cgh, cgl), (sgh, sgl) = rope_tables(np.arange(G), HD, 500000.0)
    d["cos1g"], d["sin1g"] = cgh, sgh
    d["cos1g_lo"], d["sin1g_lo"] = cgl, sgl

    j = np.arange(E1P, dtype=np.float64)
    d["jrow"] = np.ascontiguousarray(
        np.broadcast_to(j[None, :], (128, E1P))).astype(np.float32)
    jb = j + np.where((j % 256) == 0, 1e9, 0.0)
    d["jbig"] = np.ascontiguousarray(
        np.broadcast_to(jb[None, :], (128, E1P))).astype(np.float32)

    if "_wfc_img" not in _SHARED:
        _SHARED["_wfc_img"] = tile_lhsT(np.ascontiguousarray(
            np.asarray(inputs["L1_ffn_w_fc"]).T))
        _SHARED["_wproj_img"] = tile_lhsT(np.ascontiguousarray(
            np.asarray(inputs["L1_ffn_w_proj"]).T))
    d["wfc_shard"] = np.ascontiguousarray(
        _SHARED["_wfc_img"][8 * core:8 * (core + 1)])
    d["wproj_shard"] = np.ascontiguousarray(
        _SHARED["_wproj_img"][2 * core:2 * (core + 1)])

    d["ident_b"] = np.eye(128).astype(np.float32)
    import ml_dtypes
    d["ident_b"] = np.eye(128).astype(ml_dtypes.bfloat16)
    d["ident_f"] = np.eye(128, dtype=np.float32)
    oc = np.zeros((128, 8), np.float32)
    oc[:, 0] = 1.0
    d["ones_col"] = oc
    d["ones_row"] = np.ones((1, 128), np.float32)
    d["eps_row"] = np.full((1, 512), EPS, np.float32)
    d["arange_col"] = np.broadcast_to(
        (G + SH * hh + np.arange(128, dtype=np.float32))[:, None],
        (128, 8)).copy()
    cc_ = np.zeros((128, 8), np.float32)
    cc_[:, 0] = -0.5
    cc_[:, 1] = 124.0
    cc_[:, 2] = 130.0
    d["ccol"] = cc_
    return d


_NC_CACHE = {}
_SHARED = {}


def kernel(**inputs):
    _SHARED.clear()
    key = ("full",)
    if key not in _NC_CACHE:
        _NC_CACHE[key] = build_program(ALL_PHASES)
    nc = _NC_CACHE[key]
    in_maps = [make_inputs(c, inputs) for c in range(8)]
    res = run_bass_kernel_spmd(nc, in_maps, core_ids=list(range(8)))
    out = np.empty((B, DM, S), np.float32)
    for c in range(8):
        b, hh = c // 2, c % 2
        out[b][:, SH * hh:SH * (hh + 1)] = res.results[c]["y_fm"]
    return np.ascontiguousarray(out.transpose(0, 2, 1))


# version 06
# speedup vs baseline: 1.0446x; 1.0155x over previous
import math
from contextlib import ExitStack

import numpy as np
import concourse.bass as bass
import concourse.mybir as mybir
from concourse.bass_utils import run_bass_kernel_spmd

F32 = mybir.dt.float32
BF16 = mybir.dt.bfloat16
I32 = mybir.dt.int32
U8 = mybir.dt.uint8
AF = mybir.ActivationFunctionType
ALU = mybir.AluOpType
AX = mybir.AxisListType

B, S, DM = 4, 2048, 2048
H, HD = 16, 128
SH = 1024
NCH = DM // 128
NTB = 2
G = 64
E1 = 2112
E1P = 2176
KC1 = E1P // 128
NKV = 4
SCALE0 = 1.0 / math.sqrt(HD)
SCALE1 = HD ** (-0.5)
FMIN = float(np.finfo(np.float32).min)
MASKV = float(2.0 ** 126)
LN2_INV = 1.0 / math.log(2.0)
EPS = 1e-5
FP8_MAX = 448.0
ARENA = 124 * 1024

PAIRS = [[0, 1], [2, 3], [4, 5], [6, 7]]
ALL8 = [[0, 1, 2, 3, 4, 5, 6, 7]]

ALL_PHASES = ("l0attn", "l0ffn", "l1attn", "l1ffn")
DEV = {}


class Sched:
    ENGINES = ("sync", "tensor", "vector", "scalar", "gpsimd")

    def __init__(self):
        self.q = {e: [] for e in self.ENGINES}
        self.cnt = {}

    def bump(self, sem, n):
        k = id(sem)
        self.cnt[k] = self.cnt.get(k, 0) + n
        return self.cnt[k]

    def val(self, sem):
        return self.cnt.get(id(sem), 0)


class Rec:
    def __init__(self, sched, key):
        self.__dict__["_s"] = sched
        self.__dict__["_k"] = key

    def __getattr__(self, attr):
        s = self.__dict__["_s"]
        k = self.__dict__["_k"]

        def call(*a, **kw):
            inc = kw.pop("inc", None)
            if inc is not None:
                s.bump(inc[0], inc[1])

            def run(h, a=a, kw=kw, inc=inc, attr=attr):
                inst = getattr(h, attr)(*a, **kw)
                if inc is not None:
                    inst.then_inc(*inc)
            s.q[k].append(run)
        return call


def tile_lhsT(w):
    K, M = w.shape
    kc, mc = K // 128, M // 128
    return np.ascontiguousarray(
        w.reshape(kc, 128, mc, 128).transpose(2, 1, 0, 3).reshape(mc, 128, kc * 128)
    ).astype(np.float32)


def rope_tables(positions, dim, base):
    inv = 1.0 / (float(base) ** (np.arange(0, dim, 2, dtype=np.float64) / dim))
    freqs = np.outer(np.asarray(positions, dtype=np.float64), inv)
    emb = np.concatenate([freqs, freqs], axis=-1)

    def hl(t64):
        t = np.ascontiguousarray(t64.T)
        hi = t.astype(np.float32)
        lo = (t - hi.astype(np.float64)).astype(np.float32)
        return hi, np.ascontiguousarray(lo)
    return hl(np.cos(emb)), hl(np.sin(emb))


def build_program(phases=ALL_PHASES, dbg=None):
    nc = bass.Bass("TRN2", target_bir_lowering=False, detect_race_conditions=False)
    s = Sched()
    SY, T, V, C, GQ = (Rec(s, "sync"), Rec(s, "tensor"), Rec(s, "vector"),
                       Rec(s, "scalar"), Rec(s, "gpsimd"))

    def inp(name, shape, dt=F32):
        return nc.dram_tensor(name, list(shape), dt, kind="ExternalInput").ap()

    def dram(name, shape, dt=F32):
        return nc.dram_tensor(name, list(shape), dt)

    x_in = inp("x_fm", (DM, SH))
    q0_img = inp("q0_img", (16, 128, NCH * 128))
    k0_img = inp("k0_img", (16, 128, NCH * 128))
    v0_img = inp("v0_img", (16, 128, NCH * 128))
    wout_img = inp("wout_img", (16, 128, NCH * 128))
    cos0_h = inp("cos0_hi", (128, SH)); cos0_l = inp("cos0_lo", (128, SH))
    sin0_h = inp("sin0_hi", (128, SH)); sin0_l = inp("sin0_lo", (128, SH))
    wffn_img = inp("wffn_img", (16, 128, NCH * 128))
    aT_img = inp("aT_img", (1, 128, NCH * 128))
    bstack_img = inp("bstack_img", (16, 128, 128))
    gate_w_in = inp("gate_w", (16, 128, 8))
    wq1_img = inp("wq1_img", (16, 128, NCH * 128))
    wk1_img = inp("wk1_img", (4, 128, NCH * 128))
    wv1_img = inp("wv1_img", (4, 128, NCH * 128))
    wo1_img = inp("wo1_img", (16, 128, NCH * 128))
    gt_img = inp("gt_img", (16, 128, G))
    cos1_h = inp("cos1_hi", (128, SH)); cos1_l = inp("cos1_lo", (128, SH))
    sin1_h = inp("sin1_hi", (128, SH)); sin1_l = inp("sin1_lo", (128, SH))
    cos1g = inp("cos1g", (128, G)); sin1g = inp("sin1g", (128, G))
    cos1g_l = inp("cos1g_lo", (128, G)); sin1g_l = inp("sin1g_lo", (128, G))
    jrow_in = inp("jrow", (128, E1P))
    jbig_in = inp("jbig", (128, E1P))
    wfc_shard = inp("wfc_shard", (8, 128, NCH * 128))
    wproj_shard = inp("wproj_shard", (2, 128, 64 * 128))
    ident_b_in = inp("ident_b", (128, 128), BF16)
    ident_f_in = inp("ident_f", (128, 128), F32)
    ones_col_in = inp("ones_col", (128, 8))
    ones_row_in = inp("ones_row", (1, 128))
    eps_row_in = inp("eps_row", (1, 512))
    bffn_in = inp("bffn", (128, 16))
    gate_b_in = inp("gate_b", (128, 8))
    e_expand_in = inp("e_expand", (8, 128))
    ccol_in = inp("ccol", (128, 8))
    arange_col_in = inp("arange_col", (128, 8))

    y_out = nc.dram_tensor("y_fm", [DM, SH], F32, kind="ExternalOutput").ap()
    if dbg == "ag0":
        dbg_out = nc.dram_tensor("dbg", [2 * 64 * 128, SH], BF16,
                                 kind="ExternalOutput").ap()
    elif dbg:
        dbg_out = nc.dram_tensor("dbg", [DM, SH], F32, kind="ExternalOutput").ap()
    else:
        dbg_out = None

    ag0_in = dram("ag0_in", (64 * 128, SH), BF16)
    ag0_out = dram("ag0_out", (2 * 64 * 128, SH), BF16)
    q0_save = dram("q0_save", (32 * 128, SH), BF16)
    ag1_in = dram("ag1_in", (16 * 128, SH), BF16)
    ag1_out = dram("ag1_out", (2 * 16 * 128, SH), BF16)
    q1_save = dram("q1_save", (32 * 128, SH), BF16)
    r_save = dram("r_save", (DM, SH), F32)
    h2_save = dram("h2_save", (64 * 128, SH), F32)
    agw_in = dram("agw_in", (8 * 128, 2048), BF16)
    agw_out = dram("agw_out", (64 * 128, 2048), BF16)
    agp_in = dram("agp_in", (2 * 128, 8192), BF16)
    agp_out = dram("agp_out", (16 * 128, 8192), BF16)
    ar_in = dram("ar_in", (1, 8), F32)
    ar_out = dram("ar_out", (1, 8), F32)
    ar2_in = dram("ar2_in", (1, 8), F32)
    ar2_out = dram("ar2_out", (1, 8), F32)
    h8_save = dram("h8_save", (64 * 128, SH), BF16)
    gcol = dram("gcol", (128, 1), F32)

    R = nc.alloc_sbuf_tensor("R", [128, NCH * SH], F32)
    arena_t = nc.alloc_sbuf_tensor("arena", [128, ARENA], U8)
    cst = {}
    for nm, shape, dt in [
        ("ident_b", (128, 128), BF16), ("ident_f", (128, 128), F32),
        ("ones_col", (128, 8), F32), ("ones_row", (1, 128), F32),
        ("eps_row", (1, 512), F32), ("bffn", (128, 16), F32),
        ("gate_b", (128, 8), F32), ("e_expand", (8, 128), F32),
        ("ccol", (128, 8), F32), ("arange_col", (128, 8), F32),
    ]:
        cst[nm] = nc.alloc_sbuf_tensor("c_" + nm, list(shape), dt)
    lnrow = nc.alloc_sbuf_tensor("lnrow", [1, 2048], F32)
    grow = nc.alloc_sbuf_tensor("grow", [1, 1032], F32)
    gstat = nc.alloc_sbuf_tensor("gstat", [128, 64], F32)
    rstat = nc.alloc_sbuf_tensor("rstat", [128, 64], F32)
    sstat = nc.alloc_sbuf_tensor("sstat", [128, 16], F32)

    def A(off, p, n, dt=F32):
        esz = mybir.dt.size(dt)
        assert off % 64 == 0 and off + n * esz <= ARENA, (off, n, dt)
        view = arena_t[:, off:off + n * esz].bitcast(dt)
        return view if p == 128 else view[0:p, :]

    ctx = ExitStack()
    sem = {nm: ctx.enter_context(nc.semaphore(nm))
           for nm in ["ld", "wl", "tn", "vs", "sc", "gq", "cc", "wl2", "gq2"]}
    ld, wl, tn, vs, sc, gq, cc = (sem[n] for n in
                                  ["ld", "wl", "tn", "vs", "sc", "gq", "cc"])
    wl2, gq2 = sem["wl2"], sem["gq2"]
    wlp = [wl, wl2]
    gqp = [gq, gq2]

    ps = [nc.alloc_psum_tensor("ps%d" % i, [128, 512], F32) for i in range(6)]
    ps_s1 = nc.alloc_psum_tensor("ps_s1", [1, 512], F32)
    ps_s2 = nc.alloc_psum_tensor("ps_s2", [1, 512], F32)
    bank_guard = {}

    def bank_wait(rec, pt):
        g = bank_guard.get(id(pt))
        if g is not None:
            rec.wait_ge(g[0], g[1])

    def bank_done(pt, sm):
        bank_guard[id(pt)] = (sm, s.val(sm))

    SY.dma_start(out=R[:].rearrange("p (c t) -> p c t", c=NCH),
                 in_=x_in.rearrange("(c p) t -> p c t", p=128), inc=(ld, 16))
    for nm, src in [("ident_b", ident_b_in), ("ident_f", ident_f_in),
                    ("ones_col", ones_col_in), ("ones_row", ones_row_in),
                    ("eps_row", eps_row_in), ("bffn", bffn_in),
                    ("gate_b", gate_b_in), ("e_expand", e_expand_in),
                    ("ccol", ccol_in), ("arange_col", arange_col_in)]:
        SY.dma_start(out=cst[nm][:], in_=src, inc=(ld, 16))
    setup_ld = s.val(ld)
    for e in Sched.ENGINES:
        Rec(s, e).wait_ge(ld, setup_ld)
    V.memset(gstat[:], 1.0)
    V.memset(rstat[:], 1.0)
    V.memset(sstat[:], 1.0)
    V.memset(lnrow[:], 1.0)
    V.memset(grow[:], 1.0, inc=(vs, 1))
    GQ.wait_ge(vs, s.val(vs))

    def split_hilo(src_ap, hi_ap, lo_ap, inc=None):
        V.tensor_copy(out=hi_ap, in_=src_ap)
        V.tensor_tensor(lo_ap, src_ap, hi_ap, ALU.subtract, inc=inc)

    def mm3(psum_ap, lh, ll, rh, rl, start, stop, inc=None):
        T.matmul(psum_ap, lh, rh, start=start, stop=False)
        T.matmul(psum_ap, lh, rl, start=False, stop=False)
        T.matmul(psum_ap, ll, rh, start=False, stop=stop, inc=inc)

    class WStream:

        def __init__(self, img_ap, cw, off):
            self.img = img_ap
            self.cw = cw
            self.fb = [A(off, 128, cw), A(off + cw * 4, 128, cw)]
            o2 = off + cw * 8
            self.hb = [A(o2, 128, cw, BF16), A(o2 + cw * 2, 128, cw, BF16)]
            self.lb = [A(o2 + cw * 4, 128, cw, BF16), A(o2 + cw * 6, 128, cw, BF16)]
            self.split_done = [0, 0]
            self.freed = [(vs, s.val(vs)), (vs, s.val(vs))]
            self.n = 0

        def load(self, j):
            b = self.n % 2
            self.n += 1
            SY.wait_ge(*self.freed[b])
            SY.dma_start(out=self.fb[b][:, :], in_=self.img[j],
                         inc=(wlp[b], 16))
            V.wait_ge(wlp[b], s.val(wlp[b]))
            split_hilo(self.fb[b][:, :], self.hb[b][:, :], self.lb[b][:, :],
                       inc=(vs, 1))
            self.split_done[b] = s.val(vs)
            return b

        def tiles(self, b, c):
            return (self.hb[b][:, 128 * c:128 * (c + 1)],
                    self.lb[b][:, 128 * c:128 * (c + 1)])

        def wait(self, rec, b):
            rec.wait_ge(vs, self.split_done[b])

        def release(self, b, sm):
            self.freed[b] = (sm, s.val(sm))

    def rope_emit(psum_list, cos_ap, sin_ap, tmp, out_f32, tb):
        p = psum_list
        sl = slice(512 * tb, 512 * (tb + 1))
        V.tensor_scalar_mul(tmp[0:64, 0:512], p[64:128, :], -1.0)
        V.tensor_copy(out=tmp[64:128, 0:512], in_=p[0:64, :])
        V.tensor_tensor(out_f32[:, sl], p[:, :], cos_ap[:, sl], ALU.mult)
        V.tensor_tensor(tmp[:, 512:1024], tmp[:, 0:512], sin_ap[:, sl], ALU.mult)
        V.tensor_tensor(out_f32[:, sl], out_f32[:, sl], tmp[:, 512:1024], ALU.add)

    def layer_norm_tb(y_ap_fn, tb, sq_tiles):
        y_ready = s.val(vs)
        C.wait_ge(vs, y_ready)
        T.wait_ge(vs, y_ready)
        sq_guard = [None, None]
        for c in range(NCH):
            src = y_ap_fn(c)
            sqt = sq_tiles[c % 2]
            if sq_guard[c % 2] is not None:
                C.wait_ge(tn, sq_guard[c % 2])
            C.activation(out=sqt, in_=src, func=AF.Square, inc=(sc, 1))
            sq_done = s.val(sc)
            if c == 0:
                bank_wait(T, ps_s1)
                bank_wait(T, ps_s2)
            T.matmul(ps_s1[:], cst["ones_col"][:, 0:1], src,
                     start=(c == 0), stop=(c == NCH - 1))
            T.wait_ge(sc, sq_done)
            T.matmul(ps_s2[:], cst["ones_col"][:, 0:1], sqt,
                     start=(c == 0), stop=(c == NCH - 1),
                     inc=(tn, 1))
            sq_guard[c % 2] = s.val(tn)
        V.wait_ge(tn, s.val(tn))
        mean, e2 = lnrow[:, 0:512], lnrow[:, 512:1024]
        aro, bro = lnrow[:, 1024:1536], lnrow[:, 1536:2048]
        V.tensor_scalar_mul(mean, ps_s1[:], 1.0 / DM)
        V.tensor_scalar_mul(e2, ps_s2[:], 1.0 / DM)
        V.tensor_tensor(bro, mean, mean, ALU.mult)
        V.tensor_tensor(e2, e2, bro, ALU.subtract)
        V.tensor_tensor(e2, e2, cst["eps_row"][:], ALU.add, inc=(vs, 1))
        bank_done(ps_s1, vs)
        bank_done(ps_s2, vs)
        C.wait_ge(vs, s.val(vs))
        C.activation(out=aro, in_=e2, func=AF.Sqrt, inc=(sc, 1))
        V.wait_ge(sc, s.val(sc))
        V.reciprocal(aro, aro)
        V.scalar_tensor_tensor(bro, mean, -1.0, aro, ALU.mult, ALU.mult,
                               inc=(vs, 1))
        bank_wait(T, ps[4])
        bank_wait(T, ps[5])
        T.wait_ge(vs, s.val(vs))
        T.matmul(ps[4][:], cst["ones_row"][:], aro, start=True, stop=True)
        T.matmul(ps[5][:], cst["ones_row"][:], bro, start=True, stop=True,
                 inc=(tn, 1))
        V.wait_ge(tn, s.val(tn))
        for c in range(NCH):
            src = y_ap_fn(c)
            dst = R[:, SH * c + 512 * tb: SH * c + 512 * (tb + 1)]
            V.tensor_tensor(dst, src, ps[4][:], ALU.mult)
            V.tensor_tensor(dst, dst, ps[5][:], ALU.add,
                            inc=(vs, 1) if c == NCH - 1 else None)
        bank_done(ps[4], vs)
        bank_done(ps[5], vs)
        return s.val(vs)

    XH0, XL0 = 0, 32768
    CO0, SI0 = 65536, 69632
    WS0 = 73728
    QT0 = 106496
    RT0 = 110592

    def proj_head(ws, col, cos_ap, sin_ap, out_hi, out_lo, xh, xl, rope, qtmp,
                  rtmp):
        b = ws.load(col)
        ws.wait(T, b)
        for tb in range(NTB):
            pt = ps[tb]
            bank_wait(T, pt)
            for c in range(NCH):
                lh, ll = ws.tiles(b, c)
                rh = xh[:, SH * c + 512 * tb: SH * c + 512 * (tb + 1)]
                rl = xl[:, SH * c + 512 * tb: SH * c + 512 * (tb + 1)]
                mm3(pt[:], lh, ll, rh, rl, start=(c == 0), stop=(c == NCH - 1),
                    inc=(tn, 1) if c == NCH - 1 else None)
            V.wait_ge(tn, s.val(tn))
            sl = slice(512 * tb, 512 * (tb + 1))
            if rope:
                rope_emit(pt[:], cos_ap, sin_ap, rtmp, qtmp, tb)
                split_hilo(qtmp[:, sl], out_hi[:, sl], out_lo[:, sl],
                           inc=(vs, 1))
            else:
                split_hilo(pt[:], out_hi[:, sl], out_lo[:, sl], inc=(vs, 1))
            bank_done(pt, vs)
        ws.release(b, vs)
        return s.val(vs)

    def emit_l0_proj():
        xh = A(XH0, 128, NCH * SH, BF16)
        xl = A(XL0, 128, NCH * SH, BF16)
        cos_e = A(CO0, 128, SH)
        sin_e = A(SI0, 128, SH)
        qtmp = A(QT0, 128, SH)
        rtmp = A(RT0, 128, SH)
        SY.dma_start(out=cos_e[:, :], in_=cos0_h, inc=(ld, 16))
        SY.dma_start(out=qtmp[:, :], in_=cos0_l, inc=(ld, 16))
        V.wait_ge(ld, s.val(ld))
        V.tensor_tensor(cos_e[:, :], cos_e[:, :], qtmp[:, :], ALU.add, inc=(vs, 1))
        cos_done = s.val(vs)
        SY.wait_ge(vs, cos_done)
        SY.dma_start(out=sin_e[:, :], in_=sin0_h, inc=(ld, 16))
        SY.dma_start(out=qtmp[:, :], in_=sin0_l, inc=(ld, 16))
        V.wait_ge(ld, s.val(ld))
        V.tensor_tensor(sin_e[:, :], sin_e[:, :], qtmp[:, :], ALU.add, inc=(vs, 1))
        for c in range(NCH):
            split_hilo(R[:, SH * c:SH * (c + 1)],
                       xh[:, SH * c:SH * (c + 1)], xl[:, SH * c:SH * (c + 1)],
                       inc=(vs, 1) if c == NCH - 1 else None)
        ws = WStream(q0_img, NCH * 128, WS0)
        oh = A(114688, 128, SH, BF16)
        ol = A(116736, 128, SH, BF16)
        agf = ag0_in.ap()
        q0f = q0_save.ap()
        for hh in range(H):
            done = proj_head(ws, hh, cos_e, sin_e, oh, ol, xh, xl, True,
                             qtmp, rtmp)
            GQ.wait_ge(vs, done)
            GQ.dma_start(out=q0f[128 * hh:128 * (hh + 1), :], in_=oh[:, :],
                         inc=(gq, 16))
            GQ.dma_start(out=q0f[128 * (16 + hh):128 * (17 + hh), :],
                         in_=ol[:, :], inc=(gq, 16))
            V.wait_ge(gq, s.val(gq))
        ws_k = WStream(k0_img, NCH * 128, WS0)
        for hh in range(H):
            done = proj_head(ws_k, hh, cos_e, sin_e, oh, ol, xh, xl, True,
                             qtmp, rtmp)
            GQ.wait_ge(vs, done)
            GQ.dma_start(out=agf[128 * hh:128 * (hh + 1), :], in_=oh[:, :],
                         inc=(gq, 16))
            GQ.dma_start(out=agf[128 * (16 + hh):128 * (17 + hh), :],
                         in_=ol[:, :], inc=(gq, 16))
            V.wait_ge(gq, s.val(gq))
        ws_v = WStream(v0_img, NCH * 128, WS0)
        for hh in range(H):
            done = proj_head(ws_v, hh, None, None, oh, ol, xh, xl, False,
                             qtmp, rtmp)
            GQ.wait_ge(vs, done)
            GQ.dma_start(out=agf[128 * (32 + hh):128 * (33 + hh), :],
                         in_=oh[:, :], inc=(gq, 16))
            GQ.dma_start(out=agf[128 * (48 + hh):128 * (49 + hh), :],
                         in_=ol[:, :], inc=(gq, 16))
            V.wait_ge(gq, s.val(gq))
        GQ.wait_ge(gq, s.val(gq))
        GQ.collective_compute("AllGather", ALU.bypass, replica_groups=PAIRS,
                              ins=[ag0_in.ap().opt()],
                              outs=[ag0_out.ap().opt()], inc=(cc, 1))
        return s.val(cc)

    attn_hi = R[:, 0:8192].bitcast(BF16)
    attn_lo = R[:, 8192:16384].bitcast(BF16)

    KH0 = 0
    KL0 = 4096
    VF0 = 8192
    VT0 = 16384
    PP0 = 24576
    PT0 = 32768
    QH0 = 65536
    QF1 = 66560

    def emit_l0_core(ag_ready):
        kh = A(KH0, 128, 2048, BF16)
        kl = A(KL0, 128, 2048, BF16)
        vfh = A(VF0, 128, 2048, BF16)
        vfl = A(VF0 + 4096, 128, 2048, BF16)
        vth = A(VT0, 128, 2048, BF16)
        vtl = A(VT0 + 4096, 128, 2048, BF16)
        P = A(PP0, 128, 2048)
        pth = A(PT0, 128, 8192, BF16)
        ptl = A(PT0 + 16384, 128, 8192, BF16)
        qh = A(QH0, 128, 128, BF16)
        ql = A(QH0 + 256, 128, 128, BF16)
        pth_r = pth.rearrange("p (kc qt qq) -> p kc qt qq", kc=16, qq=128)
        ptl_r = ptl.rearrange("p (kc qt qq) -> p kc qt qq", kc=16, qq=128)
        ps4_r = ps[4][:].rearrange("p (i one q) -> p i one q", i=4, one=1)
        ago = ag0_out.ap()
        q0f = q0_save.ap()
        ps_tb = ps[4][:, 0:256].bitcast(BF16)
        GQ.wait_ge(cc, ag_ready)
        SY.wait_ge(cc, ag_ready)
        nheads = DEV.get("nheads", H)
        part = DEV.get("part", 4)
        for hh in range(nheads):
            prev_v = s.val(vs)
            SY.wait_ge(vs, prev_v)
            for (dst, base) in ((kh, hh), (kl, 16 + hh)):
                for r in range(2):
                    SY.dma_start(out=dst[:, 1024 * r:1024 * (r + 1)],
                                 in_=ago[8192 * r + 128 * base:
                                         8192 * r + 128 * (base + 1), :],
                                 inc=(ld, 16))
            for (dst, base) in ((vfh, 32 + hh), (vfl, 48 + hh)):
                for r in range(2):
                    SY.dma_start(out=dst[:, 1024 * r:1024 * (r + 1)],
                                 in_=ago[8192 * r + 128 * base:
                                         8192 * r + 128 * (base + 1), :],
                                 inc=(ld, 16))
            kv_ld = s.val(ld)
            T.wait_ge(ld, kv_ld)
            for (src, dst) in ((vfh, vth), (vfl, vtl)):
                for g in range(4):
                    bank_wait(T, ps[4])
                    for i in range(4):
                        c = 4 * g + i
                        T.transpose(ps_tb[:, 128 * i:128 * (i + 1)],
                                    src[:, 128 * c:128 * (c + 1)],
                                    cst["ident_b"][:],
                                    inc=(tn, 1) if i == 3 else None)
                    V.wait_ge(tn, s.val(tn))
                    V.tensor_copy(out=dst[:, 512 * g:512 * (g + 1)],
                                  in_=ps_tb[:, :], inc=(vs, 1))
                    bank_done(ps[4], vs)
            vt_done = s.val(vs)
            if part < 2:
                continue
            for qb in range(2):
                for qt in range(4):
                    qi = 4 * qb + qt
                    SY.wait_ge(vs, s.val(vs))
                    SY.dma_start(out=qh[:, :],
                                 in_=q0f[128 * hh:128 * (hh + 1),
                                         128 * qi:128 * (qi + 1)],
                                 inc=(ld, 16))
                    SY.dma_start(out=ql[:, :],
                                 in_=q0f[128 * (16 + hh):128 * (17 + hh),
                                         128 * qi:128 * (qi + 1)],
                                 inc=(ld, 16))
                    q_ld = s.val(ld)
                    T.wait_ge(ld, q_ld)
                    for kb in range(4):
                        pt = ps[kb]
                        bank_wait(T, pt)
                        mm3(pt[:], qh[:, :], ql[:, :],
                            kh[:, 512 * kb:512 * (kb + 1)],
                            kl[:, 512 * kb:512 * (kb + 1)],
                            start=True, stop=True,
                            inc=(tn, 1) if kb == 3 else None)
                    V.wait_ge(tn, s.val(tn))
                    for kb in range(4):
                        V.reduce_max(gstat[:, kb:kb + 1], ps[kb][:], AX.X,
                                     inc=(vs, 1) if kb == 3 else None)
                    V.wait_ge(vs, s.val(vs))
                    V.reduce_max(gstat[:, 4:5], gstat[:, 0:4], AX.X,
                                 negate=True, inc=(vs, 1))
                    V.wait_ge(vs, s.val(vs))
                    V.tensor_scalar_mul(rstat[:, 0:8], gstat[:, 0:8], SCALE0,
                                        inc=(vs, 1))
                    C.wait_ge(vs, s.val(vs))
                    for kb in range(4):
                        C.activation(out=P[:, 512 * kb:512 * (kb + 1)],
                                     in_=ps[kb][:], func=AF.Exp,
                                     bias=rstat[:, 4:5], scale=SCALE0,
                                     inc=(sc, 1) if kb == 3 else None)
                    exp_done = s.val(sc)
                    for kb in range(4):
                        bank_done(ps[kb], sc)
                    if dbg == "p0" and hh == 0 and qi == 0:
                        GQ.wait_ge(sc, exp_done)
                        GQ.dma_start(out=dbg_out[0:128, :], in_=P[:, 0:1024],
                                     inc=(gq, 16))
                        GQ.dma_start(out=dbg_out[128:256, :], in_=P[:, 1024:2048],
                                     inc=(gq, 16))
                        V.wait_ge(gq, s.val(gq))
                    V.wait_ge(sc, exp_done)
                    V.reduce_sum(gstat[:, 8:9], P[:, :], AX.X, inc=(vs, 1))
                    V.wait_ge(vs, s.val(vs))
                    V.reciprocal(rstat[:, 8:16], gstat[:, 8:16], inc=(vs, 1))
                    V.wait_ge(vs, s.val(vs))
                    V.tensor_scalar_mul(P[:, :], P[:, :], rstat[:, 8:9],
                                        inc=(vs, 1))
                    pn_done = s.val(vs)
                    if dbg == "p0" and hh == 0 and qi == 0:
                        GQ.wait_ge(vs, pn_done)
                        GQ.dma_start(out=dbg_out[256:384, :], in_=P[:, 0:1024],
                                     inc=(gq, 16))
                        GQ.dma_start(out=dbg_out[384:512, :], in_=P[:, 1024:2048],
                                     inc=(gq, 16))
                        GQ.dma_start(out=dbg_out[512:640, 0:64], in_=gstat[:],
                                     inc=(gq, 16))
                        GQ.dma_start(out=dbg_out[640:768, 0:64], in_=rstat[:],
                                     inc=(gq, 16))
                        T.wait_ge(gq, s.val(gq))
                    if part < 3:
                        continue
                    T.wait_ge(vs, pn_done)
                    for g in range(4):
                        bank_wait(T, ps[4])
                        for i in range(4):
                            c = 4 * g + i
                            T.transpose(ps[4][:, 128 * i:128 * (i + 1)],
                                        P[:, 128 * c:128 * (c + 1)],
                                        cst["ident_f"][:],
                                        inc=(tn, 1) if i == 3 else None)
                        tr_done = s.val(tn)
                        V.wait_ge(tn, tr_done)
                        C.wait_ge(tn, tr_done)
                        oh_ap = pth_r[:, 4 * g:4 * g + 4, qt:qt + 1, :]
                        ol_ap = ptl_r[:, 4 * g:4 * g + 4, qt:qt + 1, :]
                        C.activation(out=oh_ap, in_=ps4_r, func=AF.Copy,
                                     inc=(sc, 1))
                        V.wait_ge(sc, s.val(sc))
                        V.tensor_tensor(ol_ap, ps4_r, oh_ap, ALU.subtract,
                                        inc=(vs, 1))
                        bank_done(ps[4], vs)
                if part < 4:
                    continue
                pt5 = ps[5]
                bank_wait(T, pt5)
                T.wait_ge(vs, s.val(vs))
                T.wait_ge(vs, vt_done)
                for c in range(16):
                    mm3(pt5[:], vth[:, 128 * c:128 * (c + 1)],
                        vtl[:, 128 * c:128 * (c + 1)],
                        pth[:, 512 * c:512 * (c + 1)],
                        ptl[:, 512 * c:512 * (c + 1)],
                        start=(c == 0), stop=(c == 15),
                        inc=(tn, 1) if c == 15 else None)
                pv_done = s.val(tn)
                off = SH * hh + 512 * qb
                C.wait_ge(tn, pv_done)
                C.activation(out=attn_hi[:, off:off + 512], in_=pt5[:],
                             func=AF.Copy, inc=(sc, 1))
                V.wait_ge(sc, s.val(sc))
                V.tensor_tensor(attn_lo[:, off:off + 512], pt5[:],
                                attn_hi[:, off:off + 512], ALU.subtract,
                                inc=(vs, 1))
                bank_done(pt5, vs)
        return s.val(vs)


    def emit_outproj_ln(w_img, res_src, attn_done):
        ws = WStream(w_img, NCH * 128, 0)
        xres = [A(32768, 128, 512), A(32768 + 2048, 128, 512)]
        y_full = A(36864, 128, 16384)
        sq = [A(36864 + 65536, 128, 512), A(36864 + 65536 + 2048, 128, 512)]
        T.wait_ge(vs, attn_done)
        for dc in range(NCH):
            b = ws.load(dc)
            ws.wait(T, b)
            for tb in range(NTB):
                xr = xres[tb]
                src = (x_in if res_src == "x" else r_save.ap())
                GQ.wait_ge(vs, s.val(vs))
                GQ.dma_start(out=xr[:, :],
                             in_=src[128 * dc:128 * (dc + 1),
                                     512 * tb:512 * (tb + 1)], inc=(gq, 16))
                xr_ld = s.val(gq)
                pt = ps[tb]
                bank_wait(T, pt)
                for oc in range(NCH):
                    lh, ll = ws.tiles(b, oc)
                    off = SH * oc + 512 * tb
                    mm3(pt[:], lh, ll, attn_hi[:, off:off + 512],
                        attn_lo[:, off:off + 512],
                        start=(oc == 0), stop=(oc == NCH - 1),
                        inc=(tn, 1) if oc == NCH - 1 else None)
                V.wait_ge(tn, s.val(tn))
                V.wait_ge(gq, xr_ld)
                V.tensor_tensor(y_full[:, SH * dc + 512 * tb:
                                       SH * dc + 512 * (tb + 1)], pt[:],
                                xr[:, :], ALU.add, inc=(vs, 1))
                bank_done(pt, vs)
            ws.release(b, tn)
        for tb in range(NTB):
            layer_norm_tb(
                lambda c, tb=tb: y_full[:, SH * c + 512 * tb:
                                        SH * c + 512 * (tb + 1)], tb, sq)
        return s.val(vs)

    def emit_l0_ffn():
        XTH, XTL = 0, 16384
        WSF = 32768
        BSH, BSL = 65536, 69632
        ATH, ATL = 73728, 77824
        GW0 = 81920
        XA0 = 83968
        SXH, SXL = 86016, 87040
        WGT = 88064
        YTB = 90112
        xh_t = A(XTH, 128, 16 * 512, BF16)
        xl_t = A(XTL, 128, 16 * 512, BF16)
        bsh = A(BSH, 128, 2048, BF16)
        bsl = A(BSL, 128, 2048, BF16)
        ath = A(ATH, 128, 2048, BF16)
        atl = A(ATL, 128, 2048, BF16)
        gw = A(GW0, 128, 128)
        gwh = A(GW0 + 512, 128, 128, BF16)
        gwl = A(GW0 + 768, 128, 128, BF16)
        lg = A(GW0 + 1024, 128, 8)
        eqt = A(GW0 + 1088, 128, 8)
        mkd = A(GW0 + 1152, 128, 8)
        et = A(GW0 + 1216, 128, 8)
        emt = A(GW0 + 1280, 128, 8)
        dent = A(GW0 + 1344, 128, 8)
        rect = A(GW0 + 1408, 128, 8)
        indt = A(GW0 + 1472, 128, 8)
        wgt = A(GW0 + 1536, 128, 8)
        xa_f = A(XA0, 128, 512)
        sxh = A(SXH, 128, 512, BF16)
        sxl = A(SXL, 128, 512, BF16)
        wgT_sb = A(WGT, 8, 512)
        y_tb = A(YTB, 128, 8192)
        sqs = [A(83968, 128, 512), A(86016, 128, 512)]

        SY.wait_ge(vs, s.val(vs))
        SY.wait_ge(tn, s.val(tn))
        SY.wait_ge(sc, s.val(sc))
        SY.dma_start(out=gw[:, :].rearrange("p (c e) -> p c e", c=16),
                     in_=gate_w_in.rearrange("c p e -> p c e"), inc=(ld, 16))
        V.wait_ge(ld, s.val(ld))
        split_hilo(gw[:, :], gwh[:, :], gwl[:, :], inc=(vs, 1))
        bstg = A(YTB, 128, 2048)
        SY.dma_start(out=bstg[:, :].rearrange("p (c m) -> p c m", c=16),
                     in_=bstack_img.rearrange("c p m -> p c m"), inc=(ld, 16))
        V.wait_ge(ld, s.val(ld))
        split_hilo(bstg[:, :], bsh[:, :], bsl[:, :], inc=(vs, 1))
        atg = A(YTB, 128, 2048)
        SY.wait_ge(vs, s.val(vs))
        SY.dma_start(out=atg[:, :], in_=aT_img.rearrange("o p n -> p (o n)"),
                     inc=(ld, 16))
        V.wait_ge(ld, s.val(ld))
        split_hilo(atg[:, :], ath[:, :], atl[:, :], inc=(vs, 1))

        for tb in range(NTB):
            V.wait_ge(tn, s.val(tn))
            for c in range(NCH):
                sl = slice(SH * c + 512 * tb, SH * c + 512 * (tb + 1))
                split_hilo(R[:, sl], xh_t[:, 512 * c:512 * (c + 1)],
                           xl_t[:, 512 * c:512 * (c + 1)],
                           inc=(vs, 1) if c == NCH - 1 else None)
            xsplit_done = s.val(vs)

            pg = ps[2][:, 0:8]
            for tt_ in range(4):
                bank_wait(T, ps[2])
                T.wait_ge(vs, xsplit_done)
                for c in range(NCH):
                    lhh = xh_t[:, 512 * c + 128 * tt_:512 * c + 128 * (tt_ + 1)]
                    lll = xl_t[:, 512 * c + 128 * tt_:512 * c + 128 * (tt_ + 1)]
                    mm3(pg, lhh, lll, gwh[:, 8 * c:8 * (c + 1)],
                        gwl[:, 8 * c:8 * (c + 1)],
                        start=(c == 0), stop=(c == NCH - 1),
                        inc=(tn, 1) if c == NCH - 1 else None)
                V.wait_ge(tn, s.val(tn))
                V.tensor_tensor(lg[:, :], pg, cst["gate_b"][:], ALU.add,
                                inc=(vs, 1))
                V.wait_ge(vs, s.val(vs))
                V.reduce_max(gstat[:, 0:1], lg[:, :], AX.X)
                V.reduce_max(gstat[:, 1:2], lg[:, :], AX.X, negate=True,
                             inc=(vs, 1))
                V.wait_ge(vs, s.val(vs))
                V.tensor_single_scalar(eqt[:, :], lg[:, :], gstat[:, 0:1],
                                       ALU.is_equal, inc=(vs, 1))
                V.wait_ge(vs, s.val(vs))
                V.scalar_tensor_tensor(mkd[:, :], eqt[:, :], FMIN, lg[:, :],
                                       ALU.mult, ALU.add, inc=(vs, 1))
                V.wait_ge(vs, s.val(vs))
                V.reduce_max(gstat[:, 2:3], mkd[:, :], AX.X, inc=(vs, 1))
                V.wait_ge(vs, s.val(vs))
                V.tensor_single_scalar(dent[:, :], gstat[:, 0:8],
                                       gstat[:, 1:2], ALU.add, inc=(vs, 1))
                V.wait_ge(vs, s.val(vs))
                V.tensor_single_scalar(dent[:, :], dent[:, :], 0.0, ALU.min,
                                       inc=(vs, 1))
                lgv = s.val(vs)
                C.wait_ge(vs, lgv)
                C.activation(out=et[:, :], in_=lg[:, :], func=AF.Exp,
                             bias=gstat[:, 1:2], scale=1.0)
                C.activation(out=emt[:, :], in_=dent[:, :], func=AF.Exp,
                             inc=(sc, 1))
                bank_done(ps[2], sc)
                V.wait_ge(sc, s.val(sc))
                V.tensor_scalar_add(dent[:, :], emt[:, :], 1.0, inc=(vs, 1))
                V.wait_ge(vs, s.val(vs))
                V.reciprocal(rect[:, :], dent[:, :], inc=(vs, 1))
                V.wait_ge(vs, s.val(vs))
                V.tensor_single_scalar(indt[:, :], lg[:, :], gstat[:, 2:3],
                                       ALU.is_ge, inc=(vs, 1))
                V.wait_ge(vs, s.val(vs))
                V.tensor_tensor(wgt[:, :], et[:, :], indt[:, :], ALU.mult,
                                inc=(vs, 1))
                V.wait_ge(vs, s.val(vs))
                V.tensor_single_scalar(wgt[:, :], wgt[:, :], rect[:, 2:3],
                                       ALU.mult, inc=(vs, 1))
                wg_done = s.val(vs)
                bank_wait(T, ps[3]) if tt_ == 0 else None
                T.wait_ge(vs, wg_done)
                T.transpose(ps[3][0:8, 128 * tt_:128 * (tt_ + 1)], wgt[:, :],
                            cst["ident_f"][:], inc=(tn, 1))
            V.wait_ge(tn, s.val(tn))
            V.tensor_copy(out=wgT_sb[:, :], in_=ps[3][0:8, :], inc=(vs, 1))
            bank_done(ps[3], vs)
            bank_wait(T, ps[4])
            T.wait_ge(vs, s.val(vs))
            T.matmul(ps[4][:], cst["e_expand"][:], wgT_sb[:, :], start=True,
                     stop=True, inc=(tn, 1))
            bank_wait(T, ps[5])
            for c in range(NCH):
                mm3(ps[5][:], ath[:, 128 * c:128 * (c + 1)],
                    atl[:, 128 * c:128 * (c + 1)],
                    xh_t[:, 512 * c:512 * (c + 1)],
                    xl_t[:, 512 * c:512 * (c + 1)],
                    start=(c == 0), stop=(c == NCH - 1),
                    inc=(tn, 1) if c == NCH - 1 else None)
            V.wait_ge(tn, s.val(tn))
            V.tensor_copy(out=xa_f[:, :], in_=ps[5][:])
            V.tensor_tensor(xa_f[:, :], xa_f[:, :], ps[4][:], ALU.mult)
            split_hilo(xa_f[:, :], sxh[:, :], sxl[:, :], inc=(vs, 1))
            bank_done(ps[4], vs)
            bank_done(ps[5], vs)
            sxa_done = s.val(vs)

            ws = WStream(wffn_img, NCH * 128, WSF)
            for oc in range(NCH):
                b = ws.load(oc)
                ws.wait(T, b)
                pt = ps[oc % 2]
                bank_wait(T, pt)
                T.wait_ge(vs, sxa_done)
                nolora = DEV.get("nolora", False)
                for c in range(NCH):
                    lh, ll = ws.tiles(b, c)
                    mm3(pt[:], lh, ll,
                        xh_t[:, 512 * c:512 * (c + 1)],
                        xl_t[:, 512 * c:512 * (c + 1)],
                        start=(c == 0),
                        stop=(nolora and c == NCH - 1),
                        inc=(tn, 1) if (nolora and c == NCH - 1) else None)
                if not nolora:
                    mm3(pt[:], bsh[:, 128 * oc:128 * (oc + 1)],
                        bsl[:, 128 * oc:128 * (oc + 1)], sxh[:, :], sxl[:, :],
                        start=False, stop=True, inc=(tn, 1))
                ws.release(b, tn)
                mm_done = s.val(tn)
                V.wait_ge(tn, mm_done)
                ysl = y_tb[:, 512 * oc:512 * (oc + 1)]
                V.scalar_tensor_tensor(
                    ysl, pt[:], cst["bffn"][:, oc:oc + 1],
                    R[:, SH * oc + 512 * tb:SH * oc + 512 * (tb + 1)],
                    ALU.add, ALU.add, inc=(vs, 1))
                bank_done(pt, vs)
            layer_norm_tb(
                lambda c, tb=tb: y_tb[:, 512 * c:512 * (c + 1)], tb, sqs)
        return s.val(vs)

    def emit_l1_attn():
        xh = A(XH0, 128, NCH * SH, BF16)
        xl = A(XL0, 128, NCH * SH, BF16)
        cos_e = A(CO0, 128, SH)
        sin_e = A(SI0, 128, SH)
        qtmp = A(QT0, 128, SH)
        rtmp = A(RT0, 128, SH)
        oh = A(114688, 128, SH, BF16)
        ol = A(116736, 128, SH, BF16)
        gth = A(118784, 128, 1024, BF16)
        gtl = A(120832, 128, 1024, BF16)
        kgh = A(122880, 128, 256, BF16)
        kgl = A(123392, 128, 256, BF16)
        vgh = A(123904, 128, 256, BF16)
        vgl = A(124416, 128, 256, BF16)
        cge = A(124928, 128, 64)
        sge = A(125184, 128, 64)
        gr_tmp = A(125440, 128, 64)
        gr_tmp2 = A(125696, 128, 64)

        ent_v = s.val(vs)
        SY.wait_ge(vs, ent_v)
        SY.wait_ge(tn, s.val(tn))
        SY.wait_ge(sc, s.val(sc))
        SY.dma_start(out=cos_e[:, :], in_=cos1_h, inc=(ld, 16))
        SY.dma_start(out=qtmp[:, :], in_=cos1_l, inc=(ld, 16))
        V.wait_ge(ld, s.val(ld))
        V.tensor_tensor(cos_e[:, :], cos_e[:, :], qtmp[:, :], ALU.add, inc=(vs, 1))
        SY.wait_ge(vs, s.val(vs))
        SY.dma_start(out=sin_e[:, :], in_=sin1_h, inc=(ld, 16))
        SY.dma_start(out=qtmp[:, :], in_=sin1_l, inc=(ld, 16))
        V.wait_ge(ld, s.val(ld))
        V.tensor_tensor(sin_e[:, :], sin_e[:, :], qtmp[:, :], ALU.add, inc=(vs, 1))
        SY.wait_ge(vs, s.val(vs))
        SY.dma_start(out=cge[:, :], in_=cos1g, inc=(ld, 16))
        SY.dma_start(out=gr_tmp[:, :], in_=cos1g_l, inc=(ld, 16))
        V.wait_ge(ld, s.val(ld))
        V.tensor_tensor(cge[:, :], cge[:, :], gr_tmp[:, :], ALU.add, inc=(vs, 1))
        SY.wait_ge(vs, s.val(vs))
        SY.dma_start(out=sge[:, :], in_=sin1g, inc=(ld, 16))
        SY.dma_start(out=gr_tmp[:, :], in_=sin1g_l, inc=(ld, 16))
        V.wait_ge(ld, s.val(ld))
        V.tensor_tensor(sge[:, :], sge[:, :], gr_tmp[:, :], ALU.add, inc=(vs, 1))
        SY.wait_ge(vs, s.val(vs))
        SY.dma_start(out=qtmp[:, :].rearrange("p (c g) -> p c g", c=16),
                     in_=gt_img.rearrange("c p g -> p c g"), inc=(ld, 16))
        V.wait_ge(ld, s.val(ld))
        split_hilo(qtmp[:, :], gth[:, :], gtl[:, :], inc=(vs, 1))
        for c in range(NCH):
            split_hilo(R[:, SH * c:SH * (c + 1)],
                       xh[:, SH * c:SH * (c + 1)], xl[:, SH * c:SH * (c + 1)],
                       inc=(vs, 1) if c == NCH - 1 else None)
        xs_done = s.val(vs)
        GQ.wait_ge(vs, xs_done)
        GQ.dma_start(out=r_save.ap().rearrange("(c p) t -> p c t", p=128),
                     in_=R[:].rearrange("p (c t) -> p c t", c=NCH), inc=(gq, 16))
        rspill = s.val(gq)

        q1f = q1_save.ap()
        ag1 = ag1_in.ap()
        ws_q = WStream(wq1_img, NCH * 128, WS0)
        for hh in range(H):
            done = proj_head(ws_q, hh, cos_e, sin_e, oh, ol, xh, xl, True,
                             qtmp, rtmp)
            GQ.wait_ge(vs, done)
            GQ.dma_start(out=q1f[128 * hh:128 * (hh + 1), :], in_=oh[:, :],
                         inc=(gq, 16))
            GQ.dma_start(out=q1f[128 * (16 + hh):128 * (17 + hh), :],
                         in_=ol[:, :], inc=(gq, 16))
            V.wait_ge(gq, s.val(gq))
        ws_k = WStream(wk1_img, NCH * 128, WS0)
        for g in range(NKV):
            b = ws_k.load(g)
            ws_k.wait(T, b)
            bank_wait(T, ps[2])
            for c in range(NCH):
                lh, ll = ws_k.tiles(b, c)
                mm3(ps[2][:, 0:64], lh, ll, gth[:, 64 * c:64 * (c + 1)],
                    gtl[:, 64 * c:64 * (c + 1)], start=(c == 0),
                    stop=(c == NCH - 1), inc=(tn, 1) if c == NCH - 1 else None)
            V.wait_ge(tn, s.val(tn))
            p64 = ps[2][:, 0:64]
            V.tensor_scalar_mul(gr_tmp[0:64, 0:64], p64[64:128, :], -1.0)
            V.tensor_copy(out=gr_tmp[64:128, 0:64], in_=p64[0:64, :],
                          inc=(vs, 1))
            V.wait_ge(vs, s.val(vs))
            V.tensor_tensor(gr_tmp2[:, :], p64, cge[:, :], ALU.mult)
            V.tensor_tensor(gr_tmp[:, 0:64], gr_tmp[:, 0:64], sge[:, :],
                            ALU.mult, inc=(vs, 1))
            V.wait_ge(vs, s.val(vs))
            V.tensor_tensor(gr_tmp2[:, :], gr_tmp2[:, :], gr_tmp[:, 0:64],
                            ALU.add, inc=(vs, 1))
            V.wait_ge(vs, s.val(vs))
            split_hilo(gr_tmp2[:, :], kgh[:, 64 * g:64 * (g + 1)],
                       kgl[:, 64 * g:64 * (g + 1)], inc=(vs, 1))
            bank_done(ps[2], vs)
            for tb in range(NTB):
                pt = ps[tb]
                bank_wait(T, pt)
                for c in range(NCH):
                    lh, ll = ws_k.tiles(b, c)
                    rh = xh[:, SH * c + 512 * tb: SH * c + 512 * (tb + 1)]
                    rl = xl[:, SH * c + 512 * tb: SH * c + 512 * (tb + 1)]
                    mm3(pt[:], lh, ll, rh, rl, start=(c == 0),
                        stop=(c == NCH - 1),
                        inc=(tn, 1) if c == NCH - 1 else None)
                V.wait_ge(tn, s.val(tn))
                sl = slice(512 * tb, 512 * (tb + 1))
                rope_emit(pt[:], cos_e, sin_e, rtmp, qtmp, tb)
                split_hilo(qtmp[:, sl], oh[:, sl], ol[:, sl], inc=(vs, 1))
                bank_done(pt, vs)
            ws_k.release(b, vs)
            GQ.wait_ge(vs, s.val(vs))
            GQ.dma_start(out=ag1[128 * g:128 * (g + 1), :], in_=oh[:, :],
                         inc=(gq, 16))
            GQ.dma_start(out=ag1[128 * (4 + g):128 * (5 + g), :], in_=ol[:, :],
                         inc=(gq, 16))
            V.wait_ge(gq, s.val(gq))
        ws_v = WStream(wv1_img, NCH * 128, WS0)
        for g in range(NKV):
            b = ws_v.load(g)
            ws_v.wait(T, b)
            bank_wait(T, ps[2])
            for c in range(NCH):
                lh, ll = ws_v.tiles(b, c)
                mm3(ps[2][:, 0:64], lh, ll, gth[:, 64 * c:64 * (c + 1)],
                    gtl[:, 64 * c:64 * (c + 1)], start=(c == 0),
                    stop=(c == NCH - 1), inc=(tn, 1) if c == NCH - 1 else None)
            V.wait_ge(tn, s.val(tn))
            split_hilo(ps[2][:, 0:64], vgh[:, 64 * g:64 * (g + 1)],
                       vgl[:, 64 * g:64 * (g + 1)], inc=(vs, 1))
            bank_done(ps[2], vs)
            for tb in range(NTB):
                pt = ps[tb]
                bank_wait(T, pt)
                for c in range(NCH):
                    lh, ll = ws_v.tiles(b, c)
                    rh = xh[:, SH * c + 512 * tb: SH * c + 512 * (tb + 1)]
                    rl = xl[:, SH * c + 512 * tb: SH * c + 512 * (tb + 1)]
                    mm3(pt[:], lh, ll, rh, rl, start=(c == 0),
                        stop=(c == NCH - 1),
                        inc=(tn, 1) if c == NCH - 1 else None)
                V.wait_ge(tn, s.val(tn))
                sl = slice(512 * tb, 512 * (tb + 1))
                split_hilo(pt[:], oh[:, sl], ol[:, sl], inc=(vs, 1))
                bank_done(pt, vs)
            ws_v.release(b, vs)
            done = s.val(vs)
            GQ.wait_ge(vs, done)
            GQ.dma_start(out=ag1[128 * (8 + g):128 * (9 + g), :], in_=oh[:, :],
                         inc=(gq, 16))
            GQ.dma_start(out=ag1[128 * (12 + g):128 * (13 + g), :],
                         in_=ol[:, :], inc=(gq, 16))
            V.wait_ge(gq, s.val(gq))
        if dbg == "kg":
            V.tensor_copy(out=qtmp[:, 0:256], in_=kgh[:, :])
            V.tensor_copy(out=qtmp[:, 256:512], in_=kgl[:, :])
            V.tensor_copy(out=qtmp[:, 512:768], in_=vgh[:, :])
            V.tensor_copy(out=qtmp[:, 768:1024], in_=vgl[:, :], inc=(vs, 1))
            GQ.wait_ge(vs, s.val(vs))
            GQ.dma_start(out=dbg_out[0:128, :], in_=qtmp[:, :], inc=(gq, 16))
        GQ.wait_ge(gq, s.val(gq))
        GQ.collective_compute("AllGather", ALU.bypass, replica_groups=PAIRS,
                              ins=[ag1_in.ap().opt()],
                              outs=[ag1_out.ap().opt()], inc=(cc, 1))
        ag_ready = s.val(cc)

        k1h = A(0, 128, 4 * E1P, BF16)
        k1l = A(17408, 128, 4 * E1P, BF16)
        v1h = A(34816, 128, E1P, BF16)
        v1l = A(39168, 128, E1P, BF16)
        vt1h = A(43520, 128, 4 * E1P, BF16)
        vt1l = A(60928, 128, 4 * E1P, BF16)
        maskb = A(78336, 128, E1P, BF16)
        jr = A(82688, 128, E1P)
        jb = A(91392, 128, E1P)
        P1 = A(100096, 128, E1P)
        pth = A(108800, 128, E1P, BF16)
        ptl = A(113152, 128, E1P, BF16)
        qh = A(117504, 128, 128, BF16)
        ql = A(117504 + 256, 128, 128, BF16)
        tmp2 = A(118016, 128, E1P)
        ag1o = ag1_out.ap()
        SY.wait_ge(cc, ag_ready)
        GQ.wait_ge(cc, ag_ready)
        SY.dma_start(out=jr[:, :], in_=jrow_in, inc=(ld, 16))
        SY.dma_start(out=jb[:, :], in_=jbig_in, inc=(ld, 16))
        jld = s.val(ld)
        V.wait_ge(ld, jld)
        for g in range(NKV):
            for (dst, srow, gsb) in ((k1h, g, kgh), (k1l, 4 + g, kgl)):
                base = E1P * g
                V.tensor_copy(out=dst[:, base:base + 64],
                              in_=gsb[:, 64 * g:64 * (g + 1)])
                V.memset(dst[:, base + 2112:base + 2176], 0.0, inc=(vs, 1))
                SY.wait_ge(vs, s.val(vs))
                for r in range(2):
                    SY.dma_start(
                        out=dst[:, base + 64 + 1024 * r:base + 64 + 1024 * (r + 1)],
                        in_=ag1o[2048 * r + 128 * srow:2048 * r + 128 * (srow + 1), :],
                        inc=(ld, 16))
            prev = s.val(vs)
            SY.wait_ge(vs, prev)
            for (dst, srow, gsb) in ((v1h, 8 + g, vgh), (v1l, 12 + g, vgl)):
                V.tensor_copy(out=dst[:, 0:64], in_=gsb[:, 64 * g:64 * (g + 1)])
                V.memset(dst[:, 2112:2176], 0.0, inc=(vs, 1))
                SY.wait_ge(vs, s.val(vs))
                for r in range(2):
                    SY.dma_start(
                        out=dst[:, 64 + 1024 * r:64 + 1024 * (r + 1)],
                        in_=ag1o[2048 * r + 128 * srow:2048 * r + 128 * (srow + 1), :],
                        inc=(ld, 16))
            v_ld = s.val(ld)
            T.wait_ge(ld, v_ld)
            T.wait_ge(vs, s.val(vs))
            ps_tb = ps[5][:, 0:256].bitcast(BF16)
            for (src_t, dst_t) in ((v1h, vt1h), (v1l, vt1l)):
                for grp in range(5):
                    n_in = 4 if grp < 4 else 1
                    bank_wait(T, ps[5])
                    for i in range(n_in):
                        c = 4 * grp + i
                        T.transpose(ps_tb[:, 128 * i:128 * (i + 1)],
                                    src_t[:, 128 * c:128 * (c + 1)],
                                    cst["ident_b"][:],
                                    inc=(tn, 1) if i == n_in - 1 else None)
                    V.wait_ge(tn, s.val(tn))
                    V.tensor_copy(
                        out=dst_t[:, E1P * g + 512 * grp:
                                  E1P * g + 512 * grp + 128 * n_in],
                        in_=ps_tb[:, 0:128 * n_in], inc=(vs, 1))
                    bank_done(ps[5], vs)
        kv_done = s.val(vs)
        kv_ld = s.val(ld)

        qbase_col = cst["arange_col"]
        for qt in range(8):
            V.wait_ge(ld, jld)
            V.tensor_scalar_add(gstat[:, 32:40], qbase_col[:], float(128 * qt),
                                inc=(vs, 1))
            V.wait_ge(vs, s.val(vs))
            V.tensor_scalar_add(gstat[:, 40:48], gstat[:, 32:40], -512.0,
                                inc=(vs, 1))
            V.wait_ge(vs, s.val(vs))
            V.tensor_single_scalar(P1[:, :], jr[:, :], gstat[:, 32:33],
                                   ALU.is_le)
            V.tensor_single_scalar(tmp2[:, :], jb[:, :], gstat[:, 40:41],
                                   ALU.is_ge)
            V.tensor_tensor(P1[:, :], P1[:, :], tmp2[:, :], ALU.mult)
            V.tensor_scalar(maskb[:, :], P1[:, :], MASKV, -MASKV, ALU.mult,
                            ALU.add, inc=(vs, 1))
            mask_done = s.val(vs)
            for hq in range(H):
                g = hq // 4
                kb_ = E1P * g
                SY.wait_ge(vs, s.val(vs))
                SY.dma_start(out=qh[:, :],
                             in_=q1f[128 * hq:128 * (hq + 1),
                                     128 * qt:128 * (qt + 1)], inc=(ld, 16))
                SY.dma_start(out=ql[:, :],
                             in_=q1f[128 * (16 + hq):128 * (17 + hq),
                                     128 * qt:128 * (qt + 1)], inc=(ld, 16))
                q_ld = s.val(ld)
                T.wait_ge(ld, q_ld)
                T.wait_ge(vs, mask_done)
                for kb in range(5):
                    n = 512 if kb < 4 else 128
                    pt = ps[kb]
                    bank_wait(T, pt)
                    mm3(pt[:, 0:n], qh[:, :], ql[:, :],
                        k1h[:, kb_ + 512 * kb:kb_ + 512 * kb + n],
                        k1l[:, kb_ + 512 * kb:kb_ + 512 * kb + n],
                        start=True, stop=True,
                        inc=(tn, 1) if kb == 4 else None)
                V.wait_ge(tn, s.val(tn))
                for kb in range(5):
                    n = 512 if kb < 4 else 128
                    V.tensor_tensor(P1[:, 512 * kb:512 * kb + n],
                                    ps[kb][:, 0:n],
                                    maskb[:, 512 * kb:512 * kb + n], ALU.add,
                                    inc=(vs, 1) if kb == 4 else None)
                for kb in range(5):
                    bank_done(ps[kb], vs)
                V.wait_ge(vs, s.val(vs))
                V.reduce_max(gstat[:, 4:5], P1[:, :], AX.X, negate=True,
                             inc=(vs, 1))
                V.wait_ge(vs, s.val(vs))
                V.tensor_scalar_mul(rstat[:, 0:8], gstat[:, 0:8], SCALE1,
                                    inc=(vs, 1))
                C.wait_ge(vs, s.val(vs))
                C.activation(out=P1[:, :], in_=P1[:, :], func=AF.Exp,
                             bias=rstat[:, 4:5], scale=SCALE1, inc=(sc, 1))
                V.wait_ge(sc, s.val(sc))
                V.reduce_sum(gstat[:, 8:9], P1[:, :], AX.X, inc=(vs, 1))
                V.wait_ge(vs, s.val(vs))
                V.reciprocal(rstat[:, 8:16], gstat[:, 8:16], inc=(vs, 1))
                V.wait_ge(vs, s.val(vs))
                V.tensor_scalar_mul(P1[:, :], P1[:, :], rstat[:, 8:9],
                                    inc=(vs, 1))
                pn_done = s.val(vs)
                T.wait_ge(vs, pn_done)
                ps4_r = ps[4][:].rearrange("p (i q) -> p i q", i=4)
                for grp in range(5):
                    n_in = 4 if grp < 4 else 1
                    bank_wait(T, ps[4])
                    for i in range(n_in):
                        c = 4 * grp + i
                        T.transpose(ps[4][:, 128 * i:128 * (i + 1)],
                                    P1[:, 128 * c:128 * (c + 1)],
                                    cst["ident_f"][:],
                                    inc=(tn, 1) if i == n_in - 1 else None)
                    tr_done = s.val(tn)
                    V.wait_ge(tn, tr_done)
                    C.wait_ge(tn, tr_done)
                    oh_ap = pth[:, 512 * grp:512 * grp + 128 * n_in]
                    ol_ap = ptl[:, 512 * grp:512 * grp + 128 * n_in]
                    C.activation(out=oh_ap, in_=ps[4][:, 0:128 * n_in],
                                 func=AF.Copy, inc=(sc, 1))
                    V.wait_ge(sc, s.val(sc))
                    V.tensor_tensor(ol_ap, ps[4][:, 0:128 * n_in], oh_ap,
                                    ALU.subtract, inc=(vs, 1))
                    bank_done(ps[4], vs)
                pt0 = ps[0]
                bank_wait(T, pt0)
                T.wait_ge(vs, s.val(vs))
                T.wait_ge(vs, kv_done)
                for c in range(KC1):
                    mm3(pt0[:, 0:128],
                        vt1h[:, kb_ + 128 * c:kb_ + 128 * (c + 1)],
                        vt1l[:, kb_ + 128 * c:kb_ + 128 * (c + 1)],
                        pth[:, 128 * c:128 * (c + 1)],
                        ptl[:, 128 * c:128 * (c + 1)],
                        start=(c == 0), stop=(c == KC1 - 1),
                        inc=(tn, 1) if c == KC1 - 1 else None)
                pv_done = s.val(tn)
                off = SH * hq + 128 * qt
                C.wait_ge(tn, pv_done)
                C.activation(out=attn_hi[:, off:off + 128], in_=pt0[:, 0:128],
                             func=AF.Copy, inc=(sc, 1))
                V.wait_ge(sc, s.val(sc))
                V.tensor_tensor(attn_lo[:, off:off + 128], pt0[:, 0:128],
                                attn_hi[:, off:off + 128], ALU.subtract,
                                inc=(vs, 1))
                bank_done(pt0, vs)
        return s.val(vs)

    def quant_pipe(src_ap, out_ap, sinv_col, w, qa, qb, qt3, qt4, qt5,
                   signed=True):
        if signed:
            V.tensor_single_scalar(qa[:, 0:w], src_ap, sinv_col, ALU.mult)
            V.tensor_scalar_mul(qb[:, 0:w], qa[:, 0:w], -1.0)
            V.tensor_tensor(qb[:, 0:w], qb[:, 0:w], qa[:, 0:w], ALU.max)
        else:
            V.tensor_single_scalar(qb[:, 0:w], src_ap, sinv_col, ALU.mult)
        V.tensor_single_scalar(qt3[:, 0:w], qb[:, 0:w], 2.0 ** -6, ALU.max,
                               inc=(vs, 1))
        C.wait_ge(vs, s.val(vs))
        C.activation(out=qt3[:, 0:w], in_=qt3[:, 0:w], func=AF.Ln, inc=(sc, 1))
        V.wait_ge(sc, s.val(sc))
        V.tensor_scalar(qt3[:, 0:w], qt3[:, 0:w], LN2_INV, 63.5, ALU.mult,
                        ALU.add)
        V.tensor_copy(out=qt4[:, 0:w], in_=qt3[:, 0:w])
        V.tensor_copy(out=qt3[:, 0:w], in_=qt4[:, 0:w])
        V.tensor_scalar(qt3[:, 0:w], qt3[:, 0:w], -1.0, 194.0, ALU.mult,
                        ALU.add)
        V.tensor_copy(out=qt5[:, 0:w], in_=qt3[:, 0:w])
        V.tensor_single_scalar(qt5[:, 0:w], qt5[:, 0:w], 23,
                               ALU.arith_shift_left)
        V.tensor_scalar_add(qt4[:, 0:w], qt4[:, 0:w], 60)
        V.tensor_single_scalar(qt4[:, 0:w], qt4[:, 0:w], 23,
                               ALU.arith_shift_left)
        V.tensor_tensor(qb[:, 0:w], qb[:, 0:w], qt5[:, 0:w].bitcast(F32),
                        ALU.mult)
        V.tensor_copy(out=qt5[:, 0:w], in_=qb[:, 0:w])
        V.tensor_copy(out=qb[:, 0:w], in_=qt5[:, 0:w])
        V.tensor_tensor(qb[:, 0:w], qb[:, 0:w], qt4[:, 0:w].bitcast(F32),
                        ALU.mult)
        if signed:
            V.tensor_scalar(qa[:, 0:w], qa[:, 0:w], 0.0, 2.0, ALU.is_ge,
                            ALU.mult)
            V.tensor_scalar_add(qa[:, 0:w], qa[:, 0:w], -1.0)
            V.tensor_tensor(out_ap, qb[:, 0:w], qa[:, 0:w], ALU.mult,
                            inc=(vs, 1))
        else:
            V.tensor_copy(out=out_ap, in_=qb[:, 0:w], inc=(vs, 1))

    def pmax_gather(col_src_ap, grow_sl, srow_col, tag):
        GQ.wait_ge(vs, s.val(vs))
        GQ.dma_start(out=gcol.ap()[:, 0:1], in_=col_src_ap, inc=(gq, 16))
        GQ.wait_ge(gq, s.val(gq))
        GQ.dma_start(out=grow[0:1, grow_sl],
                     in_=gcol.ap()[:, 0:1].rearrange("p o -> o p"),
                     inc=(gq, 16))
        V.wait_ge(gq, s.val(gq))
        V.reduce_max(grow[0:1, srow_col:srow_col + 1],
                     grow[0:1, grow_sl], AX.X, inc=(vs, 1))
        V.wait_ge(vs, s.val(vs))

    def emit_l1_ffn():
        X80 = 0
        WF0 = 32768
        W80 = 40960
        QA0, QB0, QT3 = 49152, 53248, 57344
        QT4, QT5 = 61440, 65536
        HS0 = 69632
        H20 = 70144
        H80 = 74240
        WP0 = 76288
        x8 = A(X80, 128, NCH * SH, BF16)
        wf = [A(WF0, 128, 1024), A(WF0 + 4096, 128, 1024)]
        w8b = [A(W80, 128, 2048, BF16), A(W80 + 4096, 128, 2048, BF16)]
        qa = A(QA0, 128, 1024)
        qb = A(QB0, 128, 1024)
        qt3 = A(QT3, 128, 1024)
        qt4 = A(QT4, 128, 1024, I32)
        qt5 = A(QT5, 128, 1024, I32)
        hstat = A(HS0, 128, 128)
        h2s = [A(H20, 128, 512), A(H20 + 2048, 128, 512)]
        h8b = [A(H80, 128, 512, BF16), A(H80 + 1024, 128, 512, BF16)]
        wpb = [A(WP0, 128, 512, BF16), A(WP0 + 1024, 128, 512, BF16)]
        agwo = agw_out.ap()
        agpo = agp_out.ap()
        h2sv = h2_save.ap()
        h8sv = h8_save.ap()

        ent_v = s.val(vs)
        SY.wait_ge(vs, ent_v)
        SY.wait_ge(tn, s.val(tn))
        SY.wait_ge(sc, s.val(sc))
        V.memset(grow[:], 0.0)
        V.memset(hstat[:, :], 0.0, inc=(vs, 1))

        for c in range(NCH):
            V.reduce_max(hstat[:, 24 + c:25 + c], R[:, SH * c:SH * (c + 1)],
                         AX.X, apply_absolute_value=True,
                         inc=(vs, 1) if c == NCH - 1 else None)
        V.wait_ge(vs, s.val(vs))
        V.reduce_max(hstat[:, 40:41], hstat[:, 24:40], AX.X, inc=(vs, 1))
        pmax_gather(hstat[:, 40:41], slice(264, 392), 2, "x")
        wfree = [(vs, s.val(vs)), (vs, s.val(vs))]
        for j in range(8):
            for hf in range(2):
                b = (2 * j + hf) % 2
                SY.wait_ge(*wfree[b])
                SY.dma_start(out=wf[b][:, :],
                             in_=wfc_shard[j:j + 1, :, 1024 * hf:1024 * (hf + 1)]
                             .rearrange("o p n -> p (o n)"), inc=(wlp[b], 16))
                V.wait_ge(wlp[b], s.val(wlp[b]))
                V.reduce_max(hstat[:, 2 * j + hf:2 * j + hf + 1], wf[b][:, :],
                             AX.X, apply_absolute_value=True, inc=(vs, 1))
                wfree[b] = (vs, s.val(vs))
        V.wait_ge(vs, s.val(vs))
        V.reduce_max(hstat[:, 41:42], hstat[:, 0:16], AX.X, inc=(vs, 1))
        pmax_gather(hstat[:, 41:42], slice(8, 136), 0, "wfc")
        wfree = [(vs, s.val(vs)), (vs, s.val(vs))]
        for j in range(2):
            for pc in range(8):
                b = (8 * j + pc) % 2
                SY.wait_ge(*wfree[b])
                SY.dma_start(out=wf[b][:, :],
                             in_=wproj_shard[j:j + 1, :,
                                             1024 * pc:1024 * (pc + 1)]
                             .rearrange("o p n -> p (o n)"),
                             inc=(wlp[b], 16))
                V.wait_ge(wlp[b], s.val(wlp[b]))
                V.reduce_max(hstat[:, 48 + 8 * j + pc:49 + 8 * j + pc],
                             wf[b][:, :], AX.X, apply_absolute_value=True,
                             inc=(vs, 1))
                wfree[b] = (vs, s.val(vs))
        V.wait_ge(vs, s.val(vs))
        V.reduce_max(hstat[:, 42:43], hstat[:, 48:64], AX.X, inc=(vs, 1))
        pmax_gather(hstat[:, 42:43], slice(136, 264), 1, "wproj")
        GQ.wait_ge(vs, s.val(vs))
        GQ.dma_start(out=ar_in.ap(), in_=grow[0:1, 0:8], inc=(gq, 16))
        GQ.wait_ge(gq, s.val(gq))
        GQ.collective_compute("AllReduce", ALU.max, replica_groups=ALL8,
                              ins=[ar_in.ap().opt()], outs=[ar_out.ap().opt()],
                              inc=(cc, 1))
        GQ.wait_ge(cc, s.val(cc))
        GQ.dma_start(out=grow[0:1, 392:400], in_=ar_out.ap(), inc=(gq, 16))
        bank_wait(T, ps[4])
        T.wait_ge(gq, s.val(gq))
        T.matmul(ps[4][:, 0:8], cst["ones_row"][:], grow[0:1, 392:400],
                 start=True, stop=True, inc=(tn, 1))
        V.wait_ge(tn, s.val(tn))
        V.tensor_copy(out=sstat[:, 0:8], in_=ps[4][:, 0:8], inc=(vs, 1))
        bank_done(ps[4], vs)
        V.wait_ge(vs, s.val(vs))
        V.tensor_scalar(sstat[:, 8:16], sstat[:, 0:8], 1.0 / FP8_MAX, 1e-6,
                        ALU.mult, ALU.max, inc=(vs, 1))
        V.wait_ge(vs, s.val(vs))
        V.reciprocal(rstat[:, 16:24], sstat[:, 8:16], inc=(vs, 1))
        V.wait_ge(vs, s.val(vs))
        V.tensor_single_scalar(gstat[:, 48:56], sstat[:, 8:16],
                               sstat[:, 8:9], ALU.mult, inc=(vs, 1))
        V.wait_ge(vs, s.val(vs))
        agwi = agw_in.ap()
        wfree = [(vs, s.val(vs)), (vs, s.val(vs))]
        for j in range(8):
            V.wait_ge(gq, s.val(gq))
            for hf in range(2):
                b = (2 * j + hf) % 2
                SY.wait_ge(*wfree[b])
                SY.dma_start(out=wf[b][:, :],
                             in_=wfc_shard[j:j + 1, :, 1024 * hf:1024 * (hf + 1)]
                             .rearrange("o p n -> p (o n)"), inc=(wlp[b], 16))
                V.wait_ge(wlp[b], s.val(wlp[b]))
                quant_pipe(wf[b][:, :], w8b[0][:, 1024 * hf:1024 * (hf + 1)],
                           rstat[:, 16:17], 1024, qa, qb, qt3, qt4, qt5)
                wfree[b] = (vs, s.val(vs))
            GQ.wait_ge(vs, s.val(vs))
            GQ.dma_start(out=agwi[128 * j:128 * (j + 1), :], in_=w8b[0][:, :],
                         inc=(gq, 16))
        agpi = agp_in.ap()
        wfree = [(vs, s.val(vs)), (vs, s.val(vs))]
        w8free = [(gqp[0], s.val(gqp[0])), (gqp[1], s.val(gqp[1]))]
        for j in range(2):
            for pc in range(8):
                b = pc % 2
                SY.wait_ge(*wfree[b])
                SY.dma_start(out=wf[b][:, :],
                             in_=wproj_shard[j:j + 1, :,
                                             1024 * pc:1024 * (pc + 1)]
                             .rearrange("o p n -> p (o n)"),
                             inc=(wlp[b], 16))
                V.wait_ge(wlp[b], s.val(wlp[b]))
                V.wait_ge(*w8free[b])
                quant_pipe(wf[b][:, :], w8b[b][:, 0:1024],
                           rstat[:, 17:18], 1024, qa, qb, qt3, qt4, qt5)
                wfree[b] = (vs, s.val(vs))
                GQ.wait_ge(vs, s.val(vs))
                GQ.dma_start(out=agpi[128 * j:128 * (j + 1),
                                      1024 * pc:1024 * (pc + 1)],
                             in_=w8b[b][:, 0:1024], inc=(gqp[b], 16))
                w8free[b] = (gqp[b], s.val(gqp[b]))
        for c in range(NCH):
            quant_pipe(R[:, SH * c:SH * (c + 1)],
                       x8[:, SH * c:SH * (c + 1)], rstat[:, 18:19], 1024,
                       qa, qb, qt3, qt4, qt5)
        x8_done = s.val(vs)
        GQ.wait_ge(gq, s.val(gq))
        GQ.wait_ge(gq2, s.val(gq2))
        GQ.collective_compute("AllGather", ALU.bypass, replica_groups=ALL8,
                              ins=[agw_in.ap().opt()], outs=[agw_out.ap().opt()],
                              inc=(cc, 1))
        GQ.collective_compute("AllGather", ALU.bypass, replica_groups=ALL8,
                              ins=[agp_in.ap().opt()], outs=[agp_out.ap().opt()],
                              inc=(cc, 1))
        wg_ready = s.val(cc)
        SY.wait_ge(cc, wg_ready)

        w8_loaded = [None, None]
        h2_stored = [(gqp[0], s.val(gqp[0])), (gqp[1], s.val(gqp[1]))]
        w8_freed = [(tn, s.val(tn)), (tn, s.val(tn))]
        for oc in range(64):
            b = oc % 2
            SY.wait_ge(*w8_freed[b])
            r, lj = oc // 8, oc % 8
            SY.dma_start(out=w8b[b][:, :],
                         in_=agwo[1024 * r + 128 * lj:1024 * r + 128 * (lj + 1), :],
                         inc=(wlp[b], 16))
            w8_loaded[b] = (wlp[b], s.val(wlp[b]))
            for tb in range(NTB):
                pt = ps[2 + tb]
                bank_wait(T, pt)
                T.wait_ge(*w8_loaded[b])
                T.wait_ge(vs, x8_done)
                for c in range(NCH):
                    T.matmul(pt[:], w8b[b][:, 128 * c:128 * (c + 1)],
                             x8[:, SH * c + 512 * tb:SH * c + 512 * (tb + 1)],
                             start=(c == 0), stop=(c == NCH - 1),
                             inc=(tn, 1) if c == NCH - 1 else None)
                w8_freed[b] = (tn, s.val(tn))
                hb = h2s[tb]
                C.wait_ge(tn, s.val(tn))
                C.wait_ge(*h2_stored[tb])
                C.activation(out=hb[:, :], in_=pt[:], func=AF.Relu,
                             scale=gstat[:, 50:51])
                C.activation(out=hb[:, :], in_=hb[:, :], func=AF.Square,
                             inc=(sc, 1))
                sq_done = s.val(sc)
                bank_done(pt, sc)
                V.wait_ge(sc, sq_done)
                V.reduce_max(hstat[:, 2 * oc + tb:2 * oc + tb + 1],
                             hb[:, :], AX.X, inc=(vs, 1))
                GQ.wait_ge(vs, s.val(vs))
                GQ.dma_start(out=h2sv[128 * oc:128 * (oc + 1),
                                      512 * tb:512 * (tb + 1)],
                             in_=hb[:, :], inc=(gqp[tb], 16))
                h2_stored[tb] = (gqp[tb], s.val(gqp[tb]))
        V.wait_ge(vs, s.val(vs))
        V.reduce_max(hstat[:, 40:41], hstat[:, 0:128], AX.X, inc=(vs, 1))
        V.wait_ge(vs, s.val(vs))
        V.memset(grow[0:1, 0:8], 0.0, inc=(vs, 1))
        pmax_gather(hstat[:, 40:41], slice(8, 136), 0, "h")
        GQ.wait_ge(vs, s.val(vs))
        GQ.dma_start(out=ar2_in.ap(), in_=grow[0:1, 0:8], inc=(gq, 16))
        GQ.wait_ge(gq, s.val(gq))
        GQ.collective_compute("AllReduce", ALU.max, replica_groups=ALL8,
                              ins=[ar2_in.ap().opt()], outs=[ar2_out.ap().opt()],
                              inc=(cc, 1))
        GQ.wait_ge(cc, s.val(cc))
        GQ.dma_start(out=grow[0:1, 392:400], in_=ar2_out.ap(), inc=(gq, 16))
        bank_wait(T, ps[4])
        T.wait_ge(gq, s.val(gq))
        T.matmul(ps[4][:, 0:8], cst["ones_row"][:], grow[0:1, 392:400],
                 start=True, stop=True, inc=(tn, 1))
        V.wait_ge(tn, s.val(tn))
        V.tensor_copy(out=sstat[:, 0:8], in_=ps[4][:, 0:8], inc=(vs, 1))
        bank_done(ps[4], vs)
        V.wait_ge(vs, s.val(vs))
        V.tensor_scalar(sstat[:, 0:8], sstat[:, 0:8], 1.0 / FP8_MAX, 1e-6,
                        ALU.mult, ALU.max, inc=(vs, 1))
        V.wait_ge(vs, s.val(vs))
        V.reciprocal(rstat[:, 24:32], sstat[:, 0:8], inc=(vs, 1))
        V.wait_ge(vs, s.val(vs))
        V.tensor_single_scalar(gstat[:, 56:64], sstat[:, 8:16],
                               sstat[:, 0:1], ALU.mult, inc=(vs, 1))
        V.wait_ge(vs, s.val(vs))
        h2free = [(vs, s.val(vs)), (vs, s.val(vs))]
        h8free = [(gqp[0], s.val(gqp[0])), (gqp[1], s.val(gqp[1]))]
        for oc in range(64):
            for tb in range(NTB):
                b = (2 * oc + tb) % 2
                SY.wait_ge(*h2free[b])
                SY.dma_start(out=h2s[b][:, :],
                             in_=h2sv[128 * oc:128 * (oc + 1),
                                      512 * tb:512 * (tb + 1)],
                             inc=(wlp[b], 16))
                V.wait_ge(wlp[b], s.val(wlp[b]))
                V.wait_ge(*h8free[b])
                quant_pipe(h2s[b][:, :], h8b[b][:, :], rstat[:, 24:25], 512,
                           qa, qb, qt3, qt4, qt5, signed=False)
                h2free[b] = (vs, s.val(vs))
                GQ.wait_ge(vs, s.val(vs))
                GQ.dma_start(out=h8sv[128 * oc:128 * (oc + 1),
                                      512 * tb:512 * (tb + 1)],
                             in_=h8b[b][:, :], inc=(gqp[b], 16))
                h8free[b] = (gqp[b], s.val(gqp[b]))
        y_full = A(0, 128, 16384)
        SY.wait_ge(gq, s.val(gq))
        SY.wait_ge(gq2, s.val(gq2))
        for tb in range(NTB):
            for dcg in range(4):
                for i in range(4):
                    bank_wait(T, ps[i])
                buf_free = [(tn, s.val(tn)), (tn, s.val(tn))]
                for oc in range(64):
                    b = oc % 2
                    SY.wait_ge(*buf_free[b])
                    SY.dma_start(out=h8b[b][:, :],
                                 in_=h8sv[128 * oc:128 * (oc + 1),
                                          512 * tb:512 * (tb + 1)],
                                 inc=(wlp[b], 16))
                    for i in range(4):
                        dc = 4 * dcg + i
                        SY.dma_start(out=wpb[b][:, 128 * i:128 * (i + 1)],
                                     in_=agpo[128 * dc:128 * (dc + 1),
                                              128 * oc:128 * (oc + 1)],
                                     inc=(wlp[b], 16))
                    T.wait_ge(wlp[b], s.val(wlp[b]))
                    for i in range(4):
                        T.matmul(ps[i][:],
                                 wpb[b][:, 128 * i:128 * (i + 1)],
                                 h8b[b][:, :], start=(oc == 0),
                                 stop=(oc == 63),
                                 inc=(tn, 1) if i == 3 else None)
                    buf_free[b] = (tn, s.val(tn))
                V.wait_ge(tn, s.val(tn))
                for i in range(4):
                    dc = 4 * dcg + i
                    ysl = y_full[:, SH * dc + 512 * tb:SH * dc + 512 * (tb + 1)]
                    V.scalar_tensor_tensor(
                        ysl, ps[i][:], gstat[:, 57:58],
                        R[:, SH * dc + 512 * tb:SH * dc + 512 * (tb + 1)],
                        ALU.mult, ALU.add, inc=(vs, 1))
                    bank_done(ps[i], vs)
        sqf = [h2s[0], h2s[1]]
        for tb in range(NTB):
            layer_norm_tb(
                lambda c, tb=tb: y_full[:, SH * c + 512 * tb:
                                        SH * c + 512 * (tb + 1)], tb, sqf)
        return s.val(vs)

    r_ready = s.val(vs)
    if "l0proj" in phases:
        agr = emit_l0_proj()
        if dbg == "ag0":
            GQ.wait_ge(cc, agr)
            GQ.dma_start(out=dbg_out, in_=ag0_out.ap(), inc=(gq, 16))
    if "l0core" in phases:
        agr = emit_l0_proj()
        attn_done = emit_l0_core(agr)
        if dbg == "attn_raw":
            GQ.wait_ge(vs, attn_done)
            GQ.wait_ge(sc, s.val(sc))
            GQ.dma_start(out=dbg_out.rearrange("(c p) t -> p c t", p=128),
                         in_=R[:].rearrange("p (c t) -> p c t", c=NCH),
                         inc=(gq, 16))
    if "l0attn" in phases:
        agr = emit_l0_proj()
        attn_done = emit_l0_core(agr)
        if dbg == "attn_raw":
            GQ.wait_ge(vs, attn_done)
            GQ.wait_ge(sc, s.val(sc))
            GQ.dma_start(out=dbg_out.rearrange("(c p) t -> p c t", p=128),
                         in_=R[:].rearrange("p (c t) -> p c t", c=NCH),
                         inc=(gq, 16))
        else:
            r_ready = emit_outproj_ln(wout_img, "x", attn_done)

    if "l0ffn" in phases:
        r_ready = emit_l0_ffn()
    if "l1attn" in phases:
        attn1_done = emit_l1_attn()
        r_ready = emit_outproj_ln(wo1_img, "rsave", attn1_done)
    if "l1ffn" in phases:
        r_ready = emit_l1_ffn()
    if dbg == "ffnstats":
        GQ.wait_ge(vs, s.val(vs))
        GQ.wait_ge(sc, s.val(sc))
        GQ.dma_start(out=dbg_out[0:128, 0:16], in_=sstat[:], inc=(gq, 16))
        GQ.dma_start(out=dbg_out[128:256, 0:64], in_=rstat[:], inc=(gq, 16))
        GQ.dma_start(out=dbg_out[256:384, 0:64], in_=gstat[:], inc=(gq, 16))
        GQ.dma_start(out=dbg_out[384:385, 0:1024], in_=grow[0:1, 0:1024],
                     inc=(gq, 16))
        GQ.dma_start(out=dbg_out[512:640, :], in_=h2_save.ap()[0:128, :],
                     inc=(gq, 16))
        GQ.dma_start(out=dbg_out[640:768, :], in_=h2_save.ap()[1024:1152, :],
                     inc=(gq, 16))
    if dbg == "r_l1attn":
        GQ.wait_ge(vs, r_ready)
        GQ.dma_start(out=dbg_out.rearrange("(c p) t -> p c t", p=128),
                     in_=R[:].rearrange("p (c t) -> p c t", c=NCH), inc=(gq, 16))
    if dbg == "r_l0ffn":
        GQ.wait_ge(vs, r_ready)
        GQ.dma_start(out=dbg_out.rearrange("(c p) t -> p c t", p=128),
                     in_=R[:].rearrange("p (c t) -> p c t", c=NCH), inc=(gq, 16))
    if dbg == "r_l0attn":
        GQ.wait_ge(vs, r_ready)
        GQ.dma_start(out=dbg_out.rearrange("(c p) t -> p c t", p=128),
                     in_=R[:].rearrange("p (c t) -> p c t", c=NCH), inc=(gq, 16))

    GQ.wait_ge(vs, s.val(vs))
    GQ.wait_ge(sc, s.val(sc))
    GQ.wait_ge(tn, s.val(tn))
    GQ.dma_start(out=y_out.rearrange("(c p) t -> p c t", p=128),
                 in_=R[:].rearrange("p (c t) -> p c t", c=NCH), inc=(gq, 16))
    GQ.wait_ge(gq, s.val(gq))

    with nc.Block() as block:
        @block.sync
        def _(h):
            for fn in s.q["sync"]:
                fn(h)

        @block.tensor
        def _(h):
            for fn in s.q["tensor"]:
                fn(h)

        @block.vector
        def _(h):
            for fn in s.q["vector"]:
                fn(h)

        @block.scalar
        def _(h):
            for fn in s.q["scalar"]:
                fn(h)

        @block.gpsimd
        def _(h):
            for fn in s.q["gpsimd"]:
                fn(h)

    ctx.close()
    return nc


def make_inputs(core, inputs):
    b, hh = core // 2, core % 2
    tsl = slice(SH * hh, SH * (hh + 1))
    x = np.asarray(inputs["L0_attn_x"])
    d = {}
    d["x_fm"] = np.ascontiguousarray(x[b].T[:, tsl])

    wqkvT = np.ascontiguousarray(np.asarray(inputs["L0_attn_w_qkv"]).T)
    d["q0_img"] = tile_lhsT(wqkvT[:, 0:2048])
    d["k0_img"] = tile_lhsT(wqkvT[:, 2048:4096])
    d["v0_img"] = tile_lhsT(wqkvT[:, 4096:6144])
    d["wout_img"] = tile_lhsT(np.ascontiguousarray(
        np.asarray(inputs["L0_attn_w_out"]).T))

    pos0 = np.arange(SH * hh, SH * (hh + 1))
    (ch, cl), (sh_, sl_) = rope_tables(pos0, HD, 10000.0)
    d["cos0_hi"], d["cos0_lo"] = ch, cl
    d["sin0_hi"], d["sin0_lo"] = sh_, sl_

    d["wffn_img"] = tile_lhsT(np.ascontiguousarray(np.asarray(inputs["L0_ffn_W"]).T))
    A_ = np.asarray(inputs["L0_ffn_lora_A"])
    d["aT_img"] = tile_lhsT(np.ascontiguousarray(
        A_.transpose(2, 0, 1).reshape(2048, 128)))
    B_ = np.asarray(inputs["L0_ffn_lora_B"])
    d["bstack_img"] = tile_lhsT(np.ascontiguousarray(
        B_.transpose(0, 2, 1).reshape(128, 2048)))
    d["gate_w"] = np.ascontiguousarray(
        np.asarray(inputs["L0_ffn_gate_w"]).T).reshape(16, 128, 8)
    d["gate_b"] = np.ascontiguousarray(np.broadcast_to(
        np.asarray(inputs["L0_ffn_gate_b"])[None, :], (128, 8))).astype(np.float32)
    d["bffn"] = np.ascontiguousarray(
        np.asarray(inputs["L0_ffn_b"]).reshape(16, 128).T)
    ee = np.zeros((8, 128), np.float32)
    for e in range(8):
        ee[e, 16 * e:16 * (e + 1)] = 1.0
    d["e_expand"] = ee

    d["wq1_img"] = tile_lhsT(np.asarray(inputs["L1_attn_Wq"]))
    d["wk1_img"] = tile_lhsT(np.asarray(inputs["L1_attn_Wk"]))
    d["wv1_img"] = tile_lhsT(np.asarray(inputs["L1_attn_Wv"]))
    d["wo1_img"] = tile_lhsT(np.asarray(inputs["L1_attn_Wo"]))
    d["gt_img"] = np.ascontiguousarray(
        np.asarray(inputs["L1_attn_global_tokens"]).T).reshape(16, 128, G)
    pos1 = G + SH * hh + np.arange(SH)
    (c1h, c1l), (s1h, s1l) = rope_tables(pos1, HD, 500000.0)
    d["cos1_hi"], d["cos1_lo"] = c1h, c1l
    d["sin1_hi"], d["sin1_lo"] = s1h, s1l
    (cgh, cgl), (sgh, sgl) = rope_tables(np.arange(G), HD, 500000.0)
    d["cos1g"], d["sin1g"] = cgh, sgh
    d["cos1g_lo"], d["sin1g_lo"] = cgl, sgl

    j = np.arange(E1P, dtype=np.float64)
    d["jrow"] = np.ascontiguousarray(
        np.broadcast_to(j[None, :], (128, E1P))).astype(np.float32)
    jb = j + np.where((j % 256) == 0, 1e9, 0.0)
    d["jbig"] = np.ascontiguousarray(
        np.broadcast_to(jb[None, :], (128, E1P))).astype(np.float32)

    if "_wfc_img" not in _SHARED:
        _SHARED["_wfc_img"] = tile_lhsT(np.ascontiguousarray(
            np.asarray(inputs["L1_ffn_w_fc"]).T))
        _SHARED["_wproj_img"] = tile_lhsT(np.ascontiguousarray(
            np.asarray(inputs["L1_ffn_w_proj"]).T))
    d["wfc_shard"] = np.ascontiguousarray(
        _SHARED["_wfc_img"][8 * core:8 * (core + 1)])
    d["wproj_shard"] = np.ascontiguousarray(
        _SHARED["_wproj_img"][2 * core:2 * (core + 1)])

    d["ident_b"] = np.eye(128).astype(np.float32)
    import ml_dtypes
    d["ident_b"] = np.eye(128).astype(ml_dtypes.bfloat16)
    d["ident_f"] = np.eye(128, dtype=np.float32)
    oc = np.zeros((128, 8), np.float32)
    oc[:, 0] = 1.0
    d["ones_col"] = oc
    d["ones_row"] = np.ones((1, 128), np.float32)
    d["eps_row"] = np.full((1, 512), EPS, np.float32)
    d["arange_col"] = np.broadcast_to(
        (G + SH * hh + np.arange(128, dtype=np.float32))[:, None],
        (128, 8)).copy()
    cc_ = np.zeros((128, 8), np.float32)
    cc_[:, 0] = -0.5
    cc_[:, 1] = 124.0
    cc_[:, 2] = 130.0
    d["ccol"] = cc_
    return d


_NC_CACHE = {}
_SHARED = {}


def kernel(**inputs):
    _SHARED.clear()
    key = ("full",)
    if key not in _NC_CACHE:
        _NC_CACHE[key] = build_program(ALL_PHASES)
    nc = _NC_CACHE[key]
    in_maps = [make_inputs(c, inputs) for c in range(8)]
    res = run_bass_kernel_spmd(nc, in_maps, core_ids=list(range(8)))
    out = np.empty((B, DM, S), np.float32)
    for c in range(8):
        b, hh = c // 2, c % 2
        out[b][:, SH * hh:SH * (hh + 1)] = res.results[c]["y_fm"]
    return np.ascontiguousarray(out.transpose(0, 2, 1))


# version 07
# speedup vs baseline: 1.1534x; 1.1042x over previous
import math
from contextlib import ExitStack

import numpy as np
import concourse.bass as bass
import concourse.mybir as mybir
from concourse.bass_utils import run_bass_kernel_spmd

F32 = mybir.dt.float32
BF16 = mybir.dt.bfloat16
I32 = mybir.dt.int32
U8 = mybir.dt.uint8
AF = mybir.ActivationFunctionType
ALU = mybir.AluOpType
AX = mybir.AxisListType

B, S, DM = 4, 2048, 2048
H, HD = 16, 128
SH = 1024
NCH = DM // 128
NTB = 2
G = 64
E1 = 2112
E1P = 2176
KC1 = E1P // 128
NKV = 4
SCALE0 = 1.0 / math.sqrt(HD)
SCALE1 = HD ** (-0.5)
FMIN = float(np.finfo(np.float32).min)
MASKV = float(2.0 ** 126)
LN2_INV = 1.0 / math.log(2.0)
EPS = 1e-5
FP8_MAX = 448.0
ARENA = 124 * 1024

PAIRS = [[0, 1], [2, 3], [4, 5], [6, 7]]
ALL8 = [[0, 1, 2, 3, 4, 5, 6, 7]]

ALL_PHASES = ("l0attn", "l0ffn", "l1attn", "l1ffn")
DEV = {}


class Sched:
    ENGINES = ("sync", "tensor", "vector", "scalar", "gpsimd")

    def __init__(self):
        self.q = {e: [] for e in self.ENGINES}
        self.cnt = {}

    def bump(self, sem, n):
        k = id(sem)
        self.cnt[k] = self.cnt.get(k, 0) + n
        return self.cnt[k]

    def val(self, sem):
        return self.cnt.get(id(sem), 0)


class Rec:
    def __init__(self, sched, key):
        self.__dict__["_s"] = sched
        self.__dict__["_k"] = key

    def __getattr__(self, attr):
        s = self.__dict__["_s"]
        k = self.__dict__["_k"]

        def call(*a, **kw):
            inc = kw.pop("inc", None)
            if inc is not None:
                s.bump(inc[0], inc[1])

            def run(h, a=a, kw=kw, inc=inc, attr=attr):
                inst = getattr(h, attr)(*a, **kw)
                if inc is not None:
                    inst.then_inc(*inc)
            s.q[k].append(run)
        return call


def tile_lhsT(w):
    K, M = w.shape
    kc, mc = K // 128, M // 128
    return np.ascontiguousarray(
        w.reshape(kc, 128, mc, 128).transpose(2, 1, 0, 3).reshape(mc, 128, kc * 128)
    ).astype(np.float32)


def rope_tables(positions, dim, base):
    inv = 1.0 / (float(base) ** (np.arange(0, dim, 2, dtype=np.float64) / dim))
    freqs = np.outer(np.asarray(positions, dtype=np.float64), inv)
    emb = np.concatenate([freqs, freqs], axis=-1)

    def hl(t64):
        t = np.ascontiguousarray(t64.T)
        hi = t.astype(np.float32)
        lo = (t - hi.astype(np.float64)).astype(np.float32)
        return hi, np.ascontiguousarray(lo)
    return hl(np.cos(emb)), hl(np.sin(emb))


def build_program(phases=ALL_PHASES, dbg=None):
    nc = bass.Bass("TRN2", target_bir_lowering=False, detect_race_conditions=False)
    s = Sched()
    SY, T, V, C, GQ = (Rec(s, "sync"), Rec(s, "tensor"), Rec(s, "vector"),
                       Rec(s, "scalar"), Rec(s, "gpsimd"))

    def inp(name, shape, dt=F32):
        return nc.dram_tensor(name, list(shape), dt, kind="ExternalInput").ap()

    def dram(name, shape, dt=F32):
        return nc.dram_tensor(name, list(shape), dt)

    x_in = inp("x_fm", (DM, SH))
    q0_img = inp("q0_img", (16, 128, NCH * 128))
    k0_img = inp("k0_img", (16, 128, NCH * 128))
    v0_img = inp("v0_img", (16, 128, NCH * 128))
    wout_img = inp("wout_img", (16, 128, NCH * 128))
    cos0_h = inp("cos0_hi", (128, SH)); cos0_l = inp("cos0_lo", (128, SH))
    sin0_h = inp("sin0_hi", (128, SH)); sin0_l = inp("sin0_lo", (128, SH))
    wffn_img = inp("wffn_img", (16, 128, NCH * 128))
    aT_img = inp("aT_img", (1, 128, NCH * 128))
    bstack_img = inp("bstack_img", (16, 128, 128))
    gate_w_in = inp("gate_w", (16, 128, 8))
    wq1_img = inp("wq1_img", (16, 128, NCH * 128))
    wk1_img = inp("wk1_img", (4, 128, NCH * 128))
    wv1_img = inp("wv1_img", (4, 128, NCH * 128))
    wo1_img = inp("wo1_img", (16, 128, NCH * 128))
    gt_img = inp("gt_img", (16, 128, G))
    cos1_h = inp("cos1_hi", (128, SH)); cos1_l = inp("cos1_lo", (128, SH))
    sin1_h = inp("sin1_hi", (128, SH)); sin1_l = inp("sin1_lo", (128, SH))
    cos1g = inp("cos1g", (128, G)); sin1g = inp("sin1g", (128, G))
    cos1g_l = inp("cos1g_lo", (128, G)); sin1g_l = inp("sin1g_lo", (128, G))
    jrow_in = inp("jrow", (128, E1P))
    jbig_in = inp("jbig", (128, E1P))
    wfc_shard = inp("wfc_shard", (8, 128, NCH * 128))
    wproj_shard = inp("wproj_shard", (2, 128, 64 * 128))
    ident_b_in = inp("ident_b", (128, 128), BF16)
    ident_f_in = inp("ident_f", (128, 128), F32)
    ones_col_in = inp("ones_col", (128, 8))
    ones_row_in = inp("ones_row", (1, 128))
    eps_row_in = inp("eps_row", (1, 512))
    bffn_in = inp("bffn", (128, 16))
    gate_b_in = inp("gate_b", (128, 8))
    e_expand_in = inp("e_expand", (8, 128))
    ccol_in = inp("ccol", (128, 8))
    arange_col_in = inp("arange_col", (128, 8))

    y_out = nc.dram_tensor("y_fm", [DM, SH], F32, kind="ExternalOutput").ap()
    if dbg == "ag0":
        dbg_out = nc.dram_tensor("dbg", [2 * 64 * 128, SH], BF16,
                                 kind="ExternalOutput").ap()
    elif dbg:
        dbg_out = nc.dram_tensor("dbg", [DM, SH], F32, kind="ExternalOutput").ap()
    else:
        dbg_out = None

    ag0_in = dram("ag0_in", (64 * 128, SH), BF16)
    ag0_out = dram("ag0_out", (2 * 64 * 128, SH), BF16)
    q0_save = dram("q0_save", (32 * 128, SH), BF16)
    ag1_in = dram("ag1_in", (16 * 128, SH), BF16)
    ag1_out = dram("ag1_out", (2 * 16 * 128, SH), BF16)
    q1_save = dram("q1_save", (32 * 128, SH), BF16)
    r_save = dram("r_save", (DM, SH), F32)
    h2_save = dram("h2_save", (64 * 128, SH), F32)
    agw_in = dram("agw_in", (8 * 128, 2048), BF16)
    agw_out = dram("agw_out", (64 * 128, 2048), BF16)
    agp_in = dram("agp_in", (2 * 128, 8192), BF16)
    agp_out = dram("agp_out", (16 * 128, 8192), BF16)
    ar_in = dram("ar_in", (1, 8), F32)
    ar_out = dram("ar_out", (1, 8), F32)
    ar2_in = dram("ar2_in", (1, 8), F32)
    ar2_out = dram("ar2_out", (1, 8), F32)
    h8_save = dram("h8_save", (64 * 128, SH), BF16)
    gcol = dram("gcol", (128, 1), F32)

    R = nc.alloc_sbuf_tensor("R", [128, NCH * SH], F32)
    arena_t = nc.alloc_sbuf_tensor("arena", [128, ARENA], U8)
    cst = {}
    for nm, shape, dt in [
        ("ident_b", (128, 128), BF16), ("ident_f", (128, 128), F32),
        ("ones_col", (128, 8), F32), ("ones_row", (1, 128), F32),
        ("eps_row", (1, 512), F32), ("bffn", (128, 16), F32),
        ("gate_b", (128, 8), F32), ("e_expand", (8, 128), F32),
        ("ccol", (128, 8), F32), ("arange_col", (128, 8), F32),
    ]:
        cst[nm] = nc.alloc_sbuf_tensor("c_" + nm, list(shape), dt)
    lnrow = nc.alloc_sbuf_tensor("lnrow", [1, 2048], F32)
    grow = nc.alloc_sbuf_tensor("grow", [1, 1032], F32)
    gstat = nc.alloc_sbuf_tensor("gstat", [128, 64], F32)
    rstat = nc.alloc_sbuf_tensor("rstat", [128, 64], F32)
    sstat = nc.alloc_sbuf_tensor("sstat", [128, 16], F32)

    def A(off, p, n, dt=F32):
        esz = mybir.dt.size(dt)
        assert off % 64 == 0 and off + n * esz <= ARENA, (off, n, dt)
        view = arena_t[:, off:off + n * esz].bitcast(dt)
        return view if p == 128 else view[0:p, :]

    ctx = ExitStack()
    sem = {nm: ctx.enter_context(nc.semaphore(nm))
           for nm in ["ld", "wl", "tn", "vs", "sc", "gq", "cc", "wl2", "gq2",
                      "pe"]}
    ld, wl, tn, vs, sc, gq, cc = (sem[n] for n in
                                  ["ld", "wl", "tn", "vs", "sc", "gq", "cc"])
    wl2, gq2, pe = sem["wl2"], sem["gq2"], sem["pe"]
    wlp = [wl, wl2]
    gqp = [gq, gq2]

    ps = [nc.alloc_psum_tensor("ps%d" % i, [128, 512], F32) for i in range(8)]
    ps_s1 = ps[6][0:1, :]
    ps_s2 = ps[7][0:1, :]
    bank_guard = {}

    def _bkey(pt):
        t = getattr(pt, "tensor", None)
        return t.name if t is not None else pt.name

    def bank_wait(rec, pt):
        g = bank_guard.get(_bkey(pt))
        if g is not None:
            rec.wait_ge(g[0], g[1])

    def bank_done(pt, sm):
        bank_guard[_bkey(pt)] = (sm, s.val(sm))

    SY.dma_start(out=R[:].rearrange("p (c t) -> p c t", c=NCH),
                 in_=x_in.rearrange("(c p) t -> p c t", p=128), inc=(ld, 16))
    for nm, src in [("ident_b", ident_b_in), ("ident_f", ident_f_in),
                    ("ones_col", ones_col_in), ("ones_row", ones_row_in),
                    ("eps_row", eps_row_in), ("bffn", bffn_in),
                    ("gate_b", gate_b_in), ("e_expand", e_expand_in),
                    ("ccol", ccol_in), ("arange_col", arange_col_in)]:
        SY.dma_start(out=cst[nm][:], in_=src, inc=(ld, 16))
    setup_ld = s.val(ld)
    for e in Sched.ENGINES:
        Rec(s, e).wait_ge(ld, setup_ld)
    V.memset(gstat[:], 1.0)
    V.memset(rstat[:], 1.0)
    V.memset(sstat[:], 1.0)
    V.memset(lnrow[:], 1.0)
    V.memset(grow[:], 1.0, inc=(vs, 1))
    GQ.wait_ge(vs, s.val(vs))

    def split_hilo(src_ap, hi_ap, lo_ap, inc=None):
        V.tensor_copy(out=hi_ap, in_=src_ap)
        V.tensor_tensor(lo_ap, src_ap, hi_ap, ALU.subtract, inc=inc)

    def mm3(psum_ap, lh, ll, rh, rl, start, stop, inc=None):
        T.matmul(psum_ap, lh, rh, start=start, stop=False)
        T.matmul(psum_ap, lh, rl, start=False, stop=False)
        T.matmul(psum_ap, ll, rh, start=False, stop=stop, inc=inc)

    class WStream:

        def __init__(self, img_ap, cw, off):
            self.img = img_ap
            self.cw = cw
            self.fb = [A(off, 128, cw), A(off + cw * 4, 128, cw)]
            o2 = off + cw * 8
            self.hb = [A(o2, 128, cw, BF16), A(o2 + cw * 2, 128, cw, BF16)]
            self.lb = [A(o2 + cw * 4, 128, cw, BF16), A(o2 + cw * 6, 128, cw, BF16)]
            self.split_done = [0, 0]
            self.freed = [(vs, s.val(vs)), (vs, s.val(vs))]
            self.n = 0

        def load(self, j):
            b = self.n % 2
            self.n += 1
            SY.wait_ge(*self.freed[b])
            SY.dma_start(out=self.fb[b][:, :], in_=self.img[j],
                         inc=(wlp[b], 16))
            V.wait_ge(wlp[b], s.val(wlp[b]))
            split_hilo(self.fb[b][:, :], self.hb[b][:, :], self.lb[b][:, :],
                       inc=(vs, 1))
            self.split_done[b] = s.val(vs)
            return b

        def tiles(self, b, c):
            return (self.hb[b][:, 128 * c:128 * (c + 1)],
                    self.lb[b][:, 128 * c:128 * (c + 1)])

        def wait(self, rec, b):
            rec.wait_ge(vs, self.split_done[b])

        def release(self, b, sm):
            self.freed[b] = (sm, s.val(sm))

    def rope_emit(psum_list, cos_ap, sin_ap, tmp, out_f32, tb):
        p = psum_list
        sl = slice(512 * tb, 512 * (tb + 1))
        V.tensor_scalar_mul(tmp[0:64, 0:512], p[64:128, :], -1.0)
        V.tensor_copy(out=tmp[64:128, 0:512], in_=p[0:64, :])
        V.tensor_tensor(out_f32[:, sl], p[:, :], cos_ap[:, sl], ALU.mult)
        V.tensor_tensor(tmp[:, 512:1024], tmp[:, 0:512], sin_ap[:, sl], ALU.mult)
        V.tensor_tensor(out_f32[:, sl], out_f32[:, sl], tmp[:, 512:1024], ALU.add)

    def layer_norm_tb(y_ap_fn, tb, sq_tiles):
        y_ready = s.val(vs)
        C.wait_ge(vs, y_ready)
        T.wait_ge(vs, y_ready)
        sq_guard = [None, None]
        for c in range(NCH):
            src = y_ap_fn(c)
            sqt = sq_tiles[c % 2]
            if sq_guard[c % 2] is not None:
                C.wait_ge(tn, sq_guard[c % 2])
            C.activation(out=sqt, in_=src, func=AF.Square, inc=(sc, 1))
            sq_done = s.val(sc)
            if c == 0:
                bank_wait(T, ps_s1)
                bank_wait(T, ps_s2)
            T.matmul(ps_s1[:], cst["ones_col"][:, 0:1], src,
                     start=(c == 0), stop=(c == NCH - 1))
            T.wait_ge(sc, sq_done)
            T.matmul(ps_s2[:], cst["ones_col"][:, 0:1], sqt,
                     start=(c == 0), stop=(c == NCH - 1),
                     inc=(tn, 1))
            sq_guard[c % 2] = s.val(tn)
        V.wait_ge(tn, s.val(tn))
        mean, e2 = lnrow[:, 0:512], lnrow[:, 512:1024]
        aro, bro = lnrow[:, 1024:1536], lnrow[:, 1536:2048]
        V.tensor_scalar_mul(mean, ps_s1[:], 1.0 / DM)
        V.tensor_scalar_mul(e2, ps_s2[:], 1.0 / DM)
        V.tensor_tensor(bro, mean, mean, ALU.mult)
        V.tensor_tensor(e2, e2, bro, ALU.subtract)
        V.tensor_tensor(e2, e2, cst["eps_row"][:], ALU.add, inc=(vs, 1))
        bank_done(ps_s1, vs)
        bank_done(ps_s2, vs)
        C.wait_ge(vs, s.val(vs))
        C.activation(out=aro, in_=e2, func=AF.Sqrt, inc=(sc, 1))
        V.wait_ge(sc, s.val(sc))
        V.reciprocal(aro, aro)
        V.scalar_tensor_tensor(bro, mean, -1.0, aro, ALU.mult, ALU.mult,
                               inc=(vs, 1))
        bank_wait(T, ps[4])
        bank_wait(T, ps[5])
        T.wait_ge(vs, s.val(vs))
        T.matmul(ps[4][:], cst["ones_row"][:], aro, start=True, stop=True)
        T.matmul(ps[5][:], cst["ones_row"][:], bro, start=True, stop=True,
                 inc=(tn, 1))
        V.wait_ge(tn, s.val(tn))
        for c in range(NCH):
            src = y_ap_fn(c)
            dst = R[:, SH * c + 512 * tb: SH * c + 512 * (tb + 1)]
            V.tensor_tensor(dst, src, ps[4][:], ALU.mult)
            V.tensor_tensor(dst, dst, ps[5][:], ALU.add,
                            inc=(vs, 1) if c == NCH - 1 else None)
        bank_done(ps[4], vs)
        bank_done(ps[5], vs)
        return s.val(vs)

    XH0, XL0 = 0, 32768
    CO0, SI0 = 65536, 69632
    WS0 = 73728
    QT0 = 106496
    RT0 = 110592

    def proj_head(ws, col, cos_ap, sin_ap, out_hi, out_lo, xh, xl, rope, qtmp,
                  rtmp):
        b = ws.load(col)
        ws.wait(T, b)
        for tb in range(NTB):
            pt = ps[tb]
            bank_wait(T, pt)
            for c in range(NCH):
                lh, ll = ws.tiles(b, c)
                rh = xh[:, SH * c + 512 * tb: SH * c + 512 * (tb + 1)]
                rl = xl[:, SH * c + 512 * tb: SH * c + 512 * (tb + 1)]
                mm3(pt[:], lh, ll, rh, rl, start=(c == 0), stop=(c == NCH - 1),
                    inc=(tn, 1) if c == NCH - 1 else None)
            V.wait_ge(tn, s.val(tn))
            sl = slice(512 * tb, 512 * (tb + 1))
            if rope:
                rope_emit(pt[:], cos_ap, sin_ap, rtmp, qtmp, tb)
                split_hilo(qtmp[:, sl], out_hi[:, sl], out_lo[:, sl],
                           inc=(vs, 1))
            else:
                split_hilo(pt[:], out_hi[:, sl], out_lo[:, sl], inc=(vs, 1))
            bank_done(pt, vs)
        ws.release(b, vs)
        return s.val(vs)

    def emit_l0_proj():
        xh = A(XH0, 128, NCH * SH, BF16)
        xl = A(XL0, 128, NCH * SH, BF16)
        cos_e = A(CO0, 128, SH)
        sin_e = A(SI0, 128, SH)
        qtmp = A(QT0, 128, SH)
        rtmp = A(RT0, 128, SH)
        SY.dma_start(out=cos_e[:, :], in_=cos0_h, inc=(ld, 16))
        SY.dma_start(out=qtmp[:, :], in_=cos0_l, inc=(ld, 16))
        V.wait_ge(ld, s.val(ld))
        V.tensor_tensor(cos_e[:, :], cos_e[:, :], qtmp[:, :], ALU.add, inc=(vs, 1))
        cos_done = s.val(vs)
        SY.wait_ge(vs, cos_done)
        SY.dma_start(out=sin_e[:, :], in_=sin0_h, inc=(ld, 16))
        SY.dma_start(out=qtmp[:, :], in_=sin0_l, inc=(ld, 16))
        V.wait_ge(ld, s.val(ld))
        V.tensor_tensor(sin_e[:, :], sin_e[:, :], qtmp[:, :], ALU.add, inc=(vs, 1))
        for c in range(NCH):
            split_hilo(R[:, SH * c:SH * (c + 1)],
                       xh[:, SH * c:SH * (c + 1)], xl[:, SH * c:SH * (c + 1)],
                       inc=(vs, 1) if c == NCH - 1 else None)
        ws = WStream(q0_img, NCH * 128, WS0)
        oh = A(114688, 128, SH, BF16)
        ol = A(116736, 128, SH, BF16)
        agf = ag0_in.ap()
        q0f = q0_save.ap()
        for hh in range(H):
            done = proj_head(ws, hh, cos_e, sin_e, oh, ol, xh, xl, True,
                             qtmp, rtmp)
            GQ.wait_ge(vs, done)
            GQ.dma_start(out=q0f[128 * hh:128 * (hh + 1), :], in_=oh[:, :],
                         inc=(gq, 16))
            GQ.dma_start(out=q0f[128 * (16 + hh):128 * (17 + hh), :],
                         in_=ol[:, :], inc=(gq, 16))
            V.wait_ge(gq, s.val(gq))
        ws_k = WStream(k0_img, NCH * 128, WS0)
        for hh in range(H):
            done = proj_head(ws_k, hh, cos_e, sin_e, oh, ol, xh, xl, True,
                             qtmp, rtmp)
            GQ.wait_ge(vs, done)
            GQ.dma_start(out=agf[128 * hh:128 * (hh + 1), :], in_=oh[:, :],
                         inc=(gq, 16))
            GQ.dma_start(out=agf[128 * (16 + hh):128 * (17 + hh), :],
                         in_=ol[:, :], inc=(gq, 16))
            V.wait_ge(gq, s.val(gq))
        ws_v = WStream(v0_img, NCH * 128, WS0)
        for hh in range(H):
            done = proj_head(ws_v, hh, None, None, oh, ol, xh, xl, False,
                             qtmp, rtmp)
            GQ.wait_ge(vs, done)
            GQ.dma_start(out=agf[128 * (32 + hh):128 * (33 + hh), :],
                         in_=oh[:, :], inc=(gq, 16))
            GQ.dma_start(out=agf[128 * (48 + hh):128 * (49 + hh), :],
                         in_=ol[:, :], inc=(gq, 16))
            V.wait_ge(gq, s.val(gq))
        GQ.wait_ge(gq, s.val(gq))
        GQ.collective_compute("AllGather", ALU.bypass, replica_groups=PAIRS,
                              ins=[ag0_in.ap().opt()],
                              outs=[ag0_out.ap().opt()], inc=(cc, 1))
        return s.val(cc)

    attn_hi = R[:, 0:8192].bitcast(BF16)
    attn_lo = R[:, 8192:16384].bitcast(BF16)

    KH0 = 0
    KL0 = 4096
    VF0 = 8192
    VT0 = 16384
    PP0 = 24576
    PT0 = 32768
    QH0 = 65536
    QF1 = 66560

    def emit_l0_core(ag_ready):
        kh = A(KH0, 128, 2048, BF16)
        kl = A(KL0, 128, 2048, BF16)
        vfh = A(VF0, 128, 2048, BF16)
        vfl = A(VF0 + 4096, 128, 2048, BF16)
        vth = A(VT0, 128, 2048, BF16)
        vtl = A(VT0 + 4096, 128, 2048, BF16)
        P = A(PP0, 128, 2048)
        pth = A(PT0, 128, 8192, BF16)
        ptl = A(PT0 + 16384, 128, 8192, BF16)
        qh = A(QH0, 128, 128, BF16)
        ql = A(QH0 + 256, 128, 128, BF16)
        pth_r = pth.rearrange("p (kc qt qq) -> p kc qt qq", kc=16, qq=128)
        ptl_r = ptl.rearrange("p (kc qt qq) -> p kc qt qq", kc=16, qq=128)
        psr = [ps[i][:].rearrange("p (i one q) -> p i one q", i=4, one=1)
               for i in range(8)]
        ps_tbv = [ps[i][:, 0:256].bitcast(BF16) for i in range(4)]
        ago = ag0_out.ap()
        q0f = q0_save.ap()
        GQ.wait_ge(cc, ag_ready)
        SY.wait_ge(cc, ag_ready)
        nheads = DEV.get("nheads", H)
        part = DEV.get("part", 4)
        pools = [ps[0:4], ps[4:8]]
        P2 = [A(PP0, 128, 2048), A(67584, 128, 2048)]
        qhb = [qh, A(75776, 128, 128, BF16)]
        qlb = [ql, A(75776 + 256, 128, 128, BF16)]
        wlq = [sem["wl"], sem["wl2"]]
        q_free = [(tn, s.val(tn)), (tn, s.val(tn))]
        p_free = [(tn, s.val(tn)), (tn, s.val(tn))]
        for hh in range(nheads):
            prev_v = s.val(vs)
            SY.wait_ge(vs, prev_v)
            for (dst, base) in ((kh, hh), (kl, 16 + hh)):
                for r in range(2):
                    SY.dma_start(out=dst[:, 1024 * r:1024 * (r + 1)],
                                 in_=ago[8192 * r + 128 * base:
                                         8192 * r + 128 * (base + 1), :],
                                 inc=(ld, 16))
            for (dst, base) in ((vfh, 32 + hh), (vfl, 48 + hh)):
                for r in range(2):
                    SY.dma_start(out=dst[:, 1024 * r:1024 * (r + 1)],
                                 in_=ago[8192 * r + 128 * base:
                                         8192 * r + 128 * (base + 1), :],
                                 inc=(ld, 16))
            kv_ld = s.val(ld)
            T.wait_ge(ld, kv_ld)
            gi = 0
            for (src, dst) in ((vfh, vth), (vfl, vtl)):
                for g in range(4):
                    bk = gi % 4
                    gi += 1
                    bank_wait(T, ps[bk])
                    for i in range(4):
                        c = 4 * g + i
                        T.transpose(ps_tbv[bk][:, 128 * i:128 * (i + 1)],
                                    src[:, 128 * c:128 * (c + 1)],
                                    cst["ident_b"][:],
                                    inc=(tn, 1) if i == 3 else None)
                    V.wait_ge(tn, s.val(tn))
                    V.tensor_copy(out=dst[:, 512 * g:512 * (g + 1)],
                                  in_=ps_tbv[bk][:, :], inc=(vs, 1))
                    bank_done(ps[bk], vs)
            vt_done = s.val(vs)
            if part < 2:
                continue
            pn_done = {}

            def scores_stage(qi):
                par = qi % 2
                pool = pools[par]
                SY.wait_ge(*q_free[par])
                SY.dma_start(out=qhb[par][:, :],
                             in_=q0f[128 * hh:128 * (hh + 1),
                                     128 * qi:128 * (qi + 1)],
                             inc=(wlq[par], 16))
                SY.dma_start(out=qlb[par][:, :],
                             in_=q0f[128 * (16 + hh):128 * (17 + hh),
                                     128 * qi:128 * (qi + 1)],
                             inc=(wlq[par], 16))
                q_ld = s.val(wlq[par])
                T.wait_ge(wlq[par], q_ld)
                for kb in range(4):
                    pt = pool[kb]
                    bank_wait(T, pt)
                    mm3(pt[:], qhb[par][:, :], qlb[par][:, :],
                        kh[:, 512 * kb:512 * (kb + 1)],
                        kl[:, 512 * kb:512 * (kb + 1)],
                        start=True, stop=True,
                        inc=(tn, 1) if kb == 3 else None)
                q_free[par] = (tn, s.val(tn))
                V.wait_ge(tn, s.val(tn))
                for kb in range(4):
                    V.reduce_max(gstat[:, kb:kb + 1], pool[kb][:], AX.X,
                                 inc=(vs, 1) if kb == 3 else None)
                V.wait_ge(vs, s.val(vs))
                V.reduce_max(gstat[:, 4:5], gstat[:, 0:4], AX.X,
                             negate=True, inc=(vs, 1))
                V.wait_ge(vs, s.val(vs))
                V.tensor_scalar_mul(rstat[:, 0:8], gstat[:, 0:8], SCALE0,
                                    inc=(vs, 1))
                C.wait_ge(vs, s.val(vs))
                C.wait_ge(*p_free[par])
                Pd = P2[par]
                for kb in range(4):
                    C.activation(out=Pd[:, 512 * kb:512 * (kb + 1)],
                                 in_=pool[kb][:], func=AF.Exp,
                                 bias=rstat[:, 4:5], scale=SCALE0,
                                 inc=(sc, 1) if kb == 3 else None)
                exp_done = s.val(sc)
                for kb in range(4):
                    bank_done(pool[kb], sc)
                V.wait_ge(sc, exp_done)
                V.reduce_sum(gstat[:, 8:9], Pd[:, :], AX.X, inc=(vs, 1))
                V.wait_ge(vs, s.val(vs))
                V.reciprocal(rstat[:, 8:16], gstat[:, 8:16], inc=(vs, 1))
                V.wait_ge(vs, s.val(vs))
                V.tensor_scalar_mul(Pd[:, :], Pd[:, :], rstat[:, 8:9],
                                    inc=(vs, 1))
                pn_done[qi] = s.val(vs)

            def tail_stage(qi):
                par = qi % 2
                pool = pools[par]
                qb, qt = qi // 4, qi % 4
                Pd = P2[par]
                T.wait_ge(vs, pn_done[qi])
                tg = []
                for g in range(4):
                    bank_wait(T, pool[g])
                    for i in range(4):
                        c = 4 * g + i
                        T.transpose(pool[g][:, 128 * i:128 * (i + 1)],
                                    Pd[:, 128 * c:128 * (c + 1)],
                                    cst["ident_f"][:],
                                    inc=(tn, 1) if i == 3 else None)
                    tg.append(s.val(tn))
                p_free[par] = (tn, tg[3])
                for g in range(4):
                    C.wait_ge(tn, tg[g])
                    oh_ap = pth_r[:, 4 * g:4 * g + 4, qt:qt + 1, :]
                    ol_ap = ptl_r[:, 4 * g:4 * g + 4, qt:qt + 1, :]
                    C.activation(out=oh_ap, in_=psr[4 * par + g],
                                 func=AF.Copy, inc=(sc, 1))
                    V.wait_ge(sc, s.val(sc))
                    V.tensor_tensor(ol_ap, psr[4 * par + g], oh_ap,
                                    ALU.subtract, inc=(pe, 1))
                    bank_done(pool[g], pe)
                if qt != 3:
                    return
                pt5 = ps[5]
                bank_wait(T, pt5)
                T.wait_ge(pe, s.val(pe))
                T.wait_ge(vs, vt_done)
                for c in range(16):
                    mm3(pt5[:], vth[:, 128 * c:128 * (c + 1)],
                        vtl[:, 128 * c:128 * (c + 1)],
                        pth[:, 512 * c:512 * (c + 1)],
                        ptl[:, 512 * c:512 * (c + 1)],
                        start=(c == 0), stop=(c == 15),
                        inc=(tn, 1) if c == 15 else None)
                pv_done = s.val(tn)
                off = SH * hh + 512 * qb
                C.wait_ge(tn, pv_done)
                C.activation(out=attn_hi[:, off:off + 512], in_=pt5[:],
                             func=AF.Copy, inc=(sc, 1))
                V.wait_ge(sc, s.val(sc))
                V.tensor_tensor(attn_lo[:, off:off + 512], pt5[:],
                                attn_hi[:, off:off + 512], ALU.subtract,
                                inc=(vs, 1))
                bank_done(pt5, vs)

            for qi in range(8):
                scores_stage(qi)
                if qi > 0:
                    tail_stage(qi - 1)
            tail_stage(7)
        return s.val(vs)


    def emit_outproj_ln(w_img, res_src, attn_done):
        ws = WStream(w_img, NCH * 128, 0)
        xres = [A(32768, 128, 512), A(32768 + 2048, 128, 512)]
        y_full = A(36864, 128, 16384)
        sq = [A(36864 + 65536, 128, 512), A(36864 + 65536 + 2048, 128, 512)]
        T.wait_ge(vs, attn_done)
        for dc in range(NCH):
            b = ws.load(dc)
            ws.wait(T, b)
            for tb in range(NTB):
                xr = xres[tb]
                src = (x_in if res_src == "x" else r_save.ap())
                GQ.wait_ge(vs, s.val(vs))
                GQ.dma_start(out=xr[:, :],
                             in_=src[128 * dc:128 * (dc + 1),
                                     512 * tb:512 * (tb + 1)], inc=(gq, 16))
                xr_ld = s.val(gq)
                pt = ps[tb]
                bank_wait(T, pt)
                for oc in range(NCH):
                    lh, ll = ws.tiles(b, oc)
                    off = SH * oc + 512 * tb
                    mm3(pt[:], lh, ll, attn_hi[:, off:off + 512],
                        attn_lo[:, off:off + 512],
                        start=(oc == 0), stop=(oc == NCH - 1),
                        inc=(tn, 1) if oc == NCH - 1 else None)
                V.wait_ge(tn, s.val(tn))
                V.wait_ge(gq, xr_ld)
                V.tensor_tensor(y_full[:, SH * dc + 512 * tb:
                                       SH * dc + 512 * (tb + 1)], pt[:],
                                xr[:, :], ALU.add, inc=(vs, 1))
                bank_done(pt, vs)
            ws.release(b, tn)
        for tb in range(NTB):
            layer_norm_tb(
                lambda c, tb=tb: y_full[:, SH * c + 512 * tb:
                                        SH * c + 512 * (tb + 1)], tb, sq)
        return s.val(vs)

    def emit_l0_ffn():
        XTH, XTL = 0, 16384
        WSF = 32768
        BSH, BSL = 65536, 69632
        ATH, ATL = 73728, 77824
        GW0 = 81920
        XA0 = 83968
        SXH, SXL = 86016, 87040
        WGT = 88064
        YTB = 90112
        xh_t = A(XTH, 128, 16 * 512, BF16)
        xl_t = A(XTL, 128, 16 * 512, BF16)
        bsh = A(BSH, 128, 2048, BF16)
        bsl = A(BSL, 128, 2048, BF16)
        ath = A(ATH, 128, 2048, BF16)
        atl = A(ATL, 128, 2048, BF16)
        gw = A(GW0, 128, 128)
        gwh = A(GW0 + 512, 128, 128, BF16)
        gwl = A(GW0 + 768, 128, 128, BF16)
        lg = A(GW0 + 1024, 128, 8)
        eqt = A(GW0 + 1088, 128, 8)
        mkd = A(GW0 + 1152, 128, 8)
        et = A(GW0 + 1216, 128, 8)
        emt = A(GW0 + 1280, 128, 8)
        dent = A(GW0 + 1344, 128, 8)
        rect = A(GW0 + 1408, 128, 8)
        indt = A(GW0 + 1472, 128, 8)
        wgt = A(GW0 + 1536, 128, 8)
        xa_f = A(XA0, 128, 512)
        sxh = A(SXH, 128, 512, BF16)
        sxl = A(SXL, 128, 512, BF16)
        wgT_sb = A(WGT, 8, 512)
        y_tb = A(YTB, 128, 8192)
        sqs = [A(83968, 128, 512), A(86016, 128, 512)]

        SY.wait_ge(vs, s.val(vs))
        SY.wait_ge(tn, s.val(tn))
        SY.wait_ge(sc, s.val(sc))
        SY.dma_start(out=gw[:, :].rearrange("p (c e) -> p c e", c=16),
                     in_=gate_w_in.rearrange("c p e -> p c e"), inc=(ld, 16))
        V.wait_ge(ld, s.val(ld))
        split_hilo(gw[:, :], gwh[:, :], gwl[:, :], inc=(vs, 1))
        bstg = A(YTB, 128, 2048)
        SY.dma_start(out=bstg[:, :].rearrange("p (c m) -> p c m", c=16),
                     in_=bstack_img.rearrange("c p m -> p c m"), inc=(ld, 16))
        V.wait_ge(ld, s.val(ld))
        split_hilo(bstg[:, :], bsh[:, :], bsl[:, :], inc=(vs, 1))
        atg = A(YTB, 128, 2048)
        SY.wait_ge(vs, s.val(vs))
        SY.dma_start(out=atg[:, :], in_=aT_img.rearrange("o p n -> p (o n)"),
                     inc=(ld, 16))
        V.wait_ge(ld, s.val(ld))
        split_hilo(atg[:, :], ath[:, :], atl[:, :], inc=(vs, 1))

        for tb in range(NTB):
            V.wait_ge(tn, s.val(tn))
            for c in range(NCH):
                sl = slice(SH * c + 512 * tb, SH * c + 512 * (tb + 1))
                split_hilo(R[:, sl], xh_t[:, 512 * c:512 * (c + 1)],
                           xl_t[:, 512 * c:512 * (c + 1)],
                           inc=(vs, 1) if c == NCH - 1 else None)
            xsplit_done = s.val(vs)

            pg = ps[2][:, 0:8]
            for tt_ in range(4):
                bank_wait(T, ps[2])
                T.wait_ge(vs, xsplit_done)
                for c in range(NCH):
                    lhh = xh_t[:, 512 * c + 128 * tt_:512 * c + 128 * (tt_ + 1)]
                    lll = xl_t[:, 512 * c + 128 * tt_:512 * c + 128 * (tt_ + 1)]
                    mm3(pg, lhh, lll, gwh[:, 8 * c:8 * (c + 1)],
                        gwl[:, 8 * c:8 * (c + 1)],
                        start=(c == 0), stop=(c == NCH - 1),
                        inc=(tn, 1) if c == NCH - 1 else None)
                V.wait_ge(tn, s.val(tn))
                V.tensor_tensor(lg[:, :], pg, cst["gate_b"][:], ALU.add,
                                inc=(vs, 1))
                V.wait_ge(vs, s.val(vs))
                V.reduce_max(gstat[:, 0:1], lg[:, :], AX.X)
                V.reduce_max(gstat[:, 1:2], lg[:, :], AX.X, negate=True,
                             inc=(vs, 1))
                V.wait_ge(vs, s.val(vs))
                V.tensor_single_scalar(eqt[:, :], lg[:, :], gstat[:, 0:1],
                                       ALU.is_equal, inc=(vs, 1))
                V.wait_ge(vs, s.val(vs))
                V.scalar_tensor_tensor(mkd[:, :], eqt[:, :], FMIN, lg[:, :],
                                       ALU.mult, ALU.add, inc=(vs, 1))
                V.wait_ge(vs, s.val(vs))
                V.reduce_max(gstat[:, 2:3], mkd[:, :], AX.X, inc=(vs, 1))
                V.wait_ge(vs, s.val(vs))
                V.tensor_single_scalar(dent[:, :], gstat[:, 0:8],
                                       gstat[:, 1:2], ALU.add, inc=(vs, 1))
                V.wait_ge(vs, s.val(vs))
                V.tensor_single_scalar(dent[:, :], dent[:, :], 0.0, ALU.min,
                                       inc=(vs, 1))
                lgv = s.val(vs)
                C.wait_ge(vs, lgv)
                C.activation(out=et[:, :], in_=lg[:, :], func=AF.Exp,
                             bias=gstat[:, 1:2], scale=1.0)
                C.activation(out=emt[:, :], in_=dent[:, :], func=AF.Exp,
                             inc=(sc, 1))
                bank_done(ps[2], sc)
                V.wait_ge(sc, s.val(sc))
                V.tensor_scalar_add(dent[:, :], emt[:, :], 1.0, inc=(vs, 1))
                V.wait_ge(vs, s.val(vs))
                V.reciprocal(rect[:, :], dent[:, :], inc=(vs, 1))
                V.wait_ge(vs, s.val(vs))
                V.tensor_single_scalar(indt[:, :], lg[:, :], gstat[:, 2:3],
                                       ALU.is_ge, inc=(vs, 1))
                V.wait_ge(vs, s.val(vs))
                V.tensor_tensor(wgt[:, :], et[:, :], indt[:, :], ALU.mult,
                                inc=(vs, 1))
                V.wait_ge(vs, s.val(vs))
                V.tensor_single_scalar(wgt[:, :], wgt[:, :], rect[:, 2:3],
                                       ALU.mult, inc=(vs, 1))
                wg_done = s.val(vs)
                bank_wait(T, ps[3]) if tt_ == 0 else None
                T.wait_ge(vs, wg_done)
                T.transpose(ps[3][0:8, 128 * tt_:128 * (tt_ + 1)], wgt[:, :],
                            cst["ident_f"][:], inc=(tn, 1))
            V.wait_ge(tn, s.val(tn))
            V.tensor_copy(out=wgT_sb[:, :], in_=ps[3][0:8, :], inc=(vs, 1))
            bank_done(ps[3], vs)
            bank_wait(T, ps[4])
            T.wait_ge(vs, s.val(vs))
            T.matmul(ps[4][:], cst["e_expand"][:], wgT_sb[:, :], start=True,
                     stop=True, inc=(tn, 1))
            bank_wait(T, ps[5])
            for c in range(NCH):
                mm3(ps[5][:], ath[:, 128 * c:128 * (c + 1)],
                    atl[:, 128 * c:128 * (c + 1)],
                    xh_t[:, 512 * c:512 * (c + 1)],
                    xl_t[:, 512 * c:512 * (c + 1)],
                    start=(c == 0), stop=(c == NCH - 1),
                    inc=(tn, 1) if c == NCH - 1 else None)
            V.wait_ge(tn, s.val(tn))
            V.tensor_copy(out=xa_f[:, :], in_=ps[5][:])
            V.tensor_tensor(xa_f[:, :], xa_f[:, :], ps[4][:], ALU.mult)
            split_hilo(xa_f[:, :], sxh[:, :], sxl[:, :], inc=(vs, 1))
            bank_done(ps[4], vs)
            bank_done(ps[5], vs)
            sxa_done = s.val(vs)

            ws = WStream(wffn_img, NCH * 128, WSF)
            for oc in range(NCH):
                b = ws.load(oc)
                ws.wait(T, b)
                pt = ps[oc % 2]
                bank_wait(T, pt)
                T.wait_ge(vs, sxa_done)
                nolora = DEV.get("nolora", False)
                for c in range(NCH):
                    lh, ll = ws.tiles(b, c)
                    mm3(pt[:], lh, ll,
                        xh_t[:, 512 * c:512 * (c + 1)],
                        xl_t[:, 512 * c:512 * (c + 1)],
                        start=(c == 0),
                        stop=(nolora and c == NCH - 1),
                        inc=(tn, 1) if (nolora and c == NCH - 1) else None)
                if not nolora:
                    mm3(pt[:], bsh[:, 128 * oc:128 * (oc + 1)],
                        bsl[:, 128 * oc:128 * (oc + 1)], sxh[:, :], sxl[:, :],
                        start=False, stop=True, inc=(tn, 1))
                ws.release(b, tn)
                mm_done = s.val(tn)
                V.wait_ge(tn, mm_done)
                ysl = y_tb[:, 512 * oc:512 * (oc + 1)]
                V.scalar_tensor_tensor(
                    ysl, pt[:], cst["bffn"][:, oc:oc + 1],
                    R[:, SH * oc + 512 * tb:SH * oc + 512 * (tb + 1)],
                    ALU.add, ALU.add, inc=(vs, 1))
                bank_done(pt, vs)
            layer_norm_tb(
                lambda c, tb=tb: y_tb[:, 512 * c:512 * (c + 1)], tb, sqs)
        return s.val(vs)

    def emit_l1_attn():
        xh = A(XH0, 128, NCH * SH, BF16)
        xl = A(XL0, 128, NCH * SH, BF16)
        cos_e = A(CO0, 128, SH)
        sin_e = A(SI0, 128, SH)
        qtmp = A(QT0, 128, SH)
        rtmp = A(RT0, 128, SH)
        oh = A(114688, 128, SH, BF16)
        ol = A(116736, 128, SH, BF16)
        gth = A(118784, 128, 1024, BF16)
        gtl = A(120832, 128, 1024, BF16)
        kgh = A(122880, 128, 256, BF16)
        kgl = A(123392, 128, 256, BF16)
        vgh = A(123904, 128, 256, BF16)
        vgl = A(124416, 128, 256, BF16)
        cge = A(124928, 128, 64)
        sge = A(125184, 128, 64)
        gr_tmp = A(125440, 128, 64)
        gr_tmp2 = A(125696, 128, 64)

        ent_v = s.val(vs)
        SY.wait_ge(vs, ent_v)
        SY.wait_ge(tn, s.val(tn))
        SY.wait_ge(sc, s.val(sc))
        SY.dma_start(out=cos_e[:, :], in_=cos1_h, inc=(ld, 16))
        SY.dma_start(out=qtmp[:, :], in_=cos1_l, inc=(ld, 16))
        V.wait_ge(ld, s.val(ld))
        V.tensor_tensor(cos_e[:, :], cos_e[:, :], qtmp[:, :], ALU.add, inc=(vs, 1))
        SY.wait_ge(vs, s.val(vs))
        SY.dma_start(out=sin_e[:, :], in_=sin1_h, inc=(ld, 16))
        SY.dma_start(out=qtmp[:, :], in_=sin1_l, inc=(ld, 16))
        V.wait_ge(ld, s.val(ld))
        V.tensor_tensor(sin_e[:, :], sin_e[:, :], qtmp[:, :], ALU.add, inc=(vs, 1))
        SY.wait_ge(vs, s.val(vs))
        SY.dma_start(out=cge[:, :], in_=cos1g, inc=(ld, 16))
        SY.dma_start(out=gr_tmp[:, :], in_=cos1g_l, inc=(ld, 16))
        V.wait_ge(ld, s.val(ld))
        V.tensor_tensor(cge[:, :], cge[:, :], gr_tmp[:, :], ALU.add, inc=(vs, 1))
        SY.wait_ge(vs, s.val(vs))
        SY.dma_start(out=sge[:, :], in_=sin1g, inc=(ld, 16))
        SY.dma_start(out=gr_tmp[:, :], in_=sin1g_l, inc=(ld, 16))
        V.wait_ge(ld, s.val(ld))
        V.tensor_tensor(sge[:, :], sge[:, :], gr_tmp[:, :], ALU.add, inc=(vs, 1))
        SY.wait_ge(vs, s.val(vs))
        SY.dma_start(out=qtmp[:, :].rearrange("p (c g) -> p c g", c=16),
                     in_=gt_img.rearrange("c p g -> p c g"), inc=(ld, 16))
        V.wait_ge(ld, s.val(ld))
        split_hilo(qtmp[:, :], gth[:, :], gtl[:, :], inc=(vs, 1))
        for c in range(NCH):
            split_hilo(R[:, SH * c:SH * (c + 1)],
                       xh[:, SH * c:SH * (c + 1)], xl[:, SH * c:SH * (c + 1)],
                       inc=(vs, 1) if c == NCH - 1 else None)
        xs_done = s.val(vs)
        GQ.wait_ge(vs, xs_done)
        GQ.dma_start(out=r_save.ap().rearrange("(c p) t -> p c t", p=128),
                     in_=R[:].rearrange("p (c t) -> p c t", c=NCH), inc=(gq, 16))
        rspill = s.val(gq)

        q1f = q1_save.ap()
        ag1 = ag1_in.ap()
        ws_q = WStream(wq1_img, NCH * 128, WS0)
        for hh in range(H):
            done = proj_head(ws_q, hh, cos_e, sin_e, oh, ol, xh, xl, True,
                             qtmp, rtmp)
            GQ.wait_ge(vs, done)
            GQ.dma_start(out=q1f[128 * hh:128 * (hh + 1), :], in_=oh[:, :],
                         inc=(gq, 16))
            GQ.dma_start(out=q1f[128 * (16 + hh):128 * (17 + hh), :],
                         in_=ol[:, :], inc=(gq, 16))
            V.wait_ge(gq, s.val(gq))
        ws_k = WStream(wk1_img, NCH * 128, WS0)
        for g in range(NKV):
            b = ws_k.load(g)
            ws_k.wait(T, b)
            bank_wait(T, ps[2])
            for c in range(NCH):
                lh, ll = ws_k.tiles(b, c)
                mm3(ps[2][:, 0:64], lh, ll, gth[:, 64 * c:64 * (c + 1)],
                    gtl[:, 64 * c:64 * (c + 1)], start=(c == 0),
                    stop=(c == NCH - 1), inc=(tn, 1) if c == NCH - 1 else None)
            V.wait_ge(tn, s.val(tn))
            p64 = ps[2][:, 0:64]
            V.tensor_scalar_mul(gr_tmp[0:64, 0:64], p64[64:128, :], -1.0)
            V.tensor_copy(out=gr_tmp[64:128, 0:64], in_=p64[0:64, :],
                          inc=(vs, 1))
            V.wait_ge(vs, s.val(vs))
            V.tensor_tensor(gr_tmp2[:, :], p64, cge[:, :], ALU.mult)
            V.tensor_tensor(gr_tmp[:, 0:64], gr_tmp[:, 0:64], sge[:, :],
                            ALU.mult, inc=(vs, 1))
            V.wait_ge(vs, s.val(vs))
            V.tensor_tensor(gr_tmp2[:, :], gr_tmp2[:, :], gr_tmp[:, 0:64],
                            ALU.add, inc=(vs, 1))
            V.wait_ge(vs, s.val(vs))
            split_hilo(gr_tmp2[:, :], kgh[:, 64 * g:64 * (g + 1)],
                       kgl[:, 64 * g:64 * (g + 1)], inc=(vs, 1))
            bank_done(ps[2], vs)
            for tb in range(NTB):
                pt = ps[tb]
                bank_wait(T, pt)
                for c in range(NCH):
                    lh, ll = ws_k.tiles(b, c)
                    rh = xh[:, SH * c + 512 * tb: SH * c + 512 * (tb + 1)]
                    rl = xl[:, SH * c + 512 * tb: SH * c + 512 * (tb + 1)]
                    mm3(pt[:], lh, ll, rh, rl, start=(c == 0),
                        stop=(c == NCH - 1),
                        inc=(tn, 1) if c == NCH - 1 else None)
                V.wait_ge(tn, s.val(tn))
                sl = slice(512 * tb, 512 * (tb + 1))
                rope_emit(pt[:], cos_e, sin_e, rtmp, qtmp, tb)
                split_hilo(qtmp[:, sl], oh[:, sl], ol[:, sl], inc=(vs, 1))
                bank_done(pt, vs)
            ws_k.release(b, vs)
            GQ.wait_ge(vs, s.val(vs))
            GQ.dma_start(out=ag1[128 * g:128 * (g + 1), :], in_=oh[:, :],
                         inc=(gq, 16))
            GQ.dma_start(out=ag1[128 * (4 + g):128 * (5 + g), :], in_=ol[:, :],
                         inc=(gq, 16))
            V.wait_ge(gq, s.val(gq))
        ws_v = WStream(wv1_img, NCH * 128, WS0)
        for g in range(NKV):
            b = ws_v.load(g)
            ws_v.wait(T, b)
            bank_wait(T, ps[2])
            for c in range(NCH):
                lh, ll = ws_v.tiles(b, c)
                mm3(ps[2][:, 0:64], lh, ll, gth[:, 64 * c:64 * (c + 1)],
                    gtl[:, 64 * c:64 * (c + 1)], start=(c == 0),
                    stop=(c == NCH - 1), inc=(tn, 1) if c == NCH - 1 else None)
            V.wait_ge(tn, s.val(tn))
            split_hilo(ps[2][:, 0:64], vgh[:, 64 * g:64 * (g + 1)],
                       vgl[:, 64 * g:64 * (g + 1)], inc=(vs, 1))
            bank_done(ps[2], vs)
            for tb in range(NTB):
                pt = ps[tb]
                bank_wait(T, pt)
                for c in range(NCH):
                    lh, ll = ws_v.tiles(b, c)
                    rh = xh[:, SH * c + 512 * tb: SH * c + 512 * (tb + 1)]
                    rl = xl[:, SH * c + 512 * tb: SH * c + 512 * (tb + 1)]
                    mm3(pt[:], lh, ll, rh, rl, start=(c == 0),
                        stop=(c == NCH - 1),
                        inc=(tn, 1) if c == NCH - 1 else None)
                V.wait_ge(tn, s.val(tn))
                sl = slice(512 * tb, 512 * (tb + 1))
                split_hilo(pt[:], oh[:, sl], ol[:, sl], inc=(vs, 1))
                bank_done(pt, vs)
            ws_v.release(b, vs)
            done = s.val(vs)
            GQ.wait_ge(vs, done)
            GQ.dma_start(out=ag1[128 * (8 + g):128 * (9 + g), :], in_=oh[:, :],
                         inc=(gq, 16))
            GQ.dma_start(out=ag1[128 * (12 + g):128 * (13 + g), :],
                         in_=ol[:, :], inc=(gq, 16))
            V.wait_ge(gq, s.val(gq))
        if dbg == "kg":
            V.tensor_copy(out=qtmp[:, 0:256], in_=kgh[:, :])
            V.tensor_copy(out=qtmp[:, 256:512], in_=kgl[:, :])
            V.tensor_copy(out=qtmp[:, 512:768], in_=vgh[:, :])
            V.tensor_copy(out=qtmp[:, 768:1024], in_=vgl[:, :], inc=(vs, 1))
            GQ.wait_ge(vs, s.val(vs))
            GQ.dma_start(out=dbg_out[0:128, :], in_=qtmp[:, :], inc=(gq, 16))
        GQ.wait_ge(gq, s.val(gq))
        GQ.collective_compute("AllGather", ALU.bypass, replica_groups=PAIRS,
                              ins=[ag1_in.ap().opt()],
                              outs=[ag1_out.ap().opt()], inc=(cc, 1))
        ag_ready = s.val(cc)

        k1h = A(0, 128, 4 * E1P, BF16)
        k1l = A(17408, 128, 4 * E1P, BF16)
        v1h = A(34816, 128, E1P, BF16)
        v1l = A(39168, 128, E1P, BF16)
        vt1h = A(43520, 128, 4 * E1P, BF16)
        vt1l = A(60928, 128, 4 * E1P, BF16)
        maskb = A(78336, 128, E1P, BF16)
        jr = A(82688, 128, E1P)
        jb = A(91392, 128, E1P)
        P1 = A(100096, 128, E1P)
        pth = A(108800, 128, E1P, BF16)
        ptl = A(113152, 128, E1P, BF16)
        qh = A(117504, 128, 128, BF16)
        ql = A(117504 + 256, 128, 128, BF16)
        tmp2 = A(118016, 128, E1P)
        ag1o = ag1_out.ap()
        SY.wait_ge(cc, ag_ready)
        GQ.wait_ge(cc, ag_ready)
        SY.dma_start(out=jr[:, :], in_=jrow_in, inc=(ld, 16))
        SY.dma_start(out=jb[:, :], in_=jbig_in, inc=(ld, 16))
        jld = s.val(ld)
        V.wait_ge(ld, jld)
        for g in range(NKV):
            for (dst, srow, gsb) in ((k1h, g, kgh), (k1l, 4 + g, kgl)):
                base = E1P * g
                V.tensor_copy(out=dst[:, base:base + 64],
                              in_=gsb[:, 64 * g:64 * (g + 1)])
                V.memset(dst[:, base + 2112:base + 2176], 0.0, inc=(vs, 1))
                SY.wait_ge(vs, s.val(vs))
                for r in range(2):
                    SY.dma_start(
                        out=dst[:, base + 64 + 1024 * r:base + 64 + 1024 * (r + 1)],
                        in_=ag1o[2048 * r + 128 * srow:2048 * r + 128 * (srow + 1), :],
                        inc=(ld, 16))
            prev = s.val(vs)
            SY.wait_ge(vs, prev)
            for (dst, srow, gsb) in ((v1h, 8 + g, vgh), (v1l, 12 + g, vgl)):
                V.tensor_copy(out=dst[:, 0:64], in_=gsb[:, 64 * g:64 * (g + 1)])
                V.memset(dst[:, 2112:2176], 0.0, inc=(vs, 1))
                SY.wait_ge(vs, s.val(vs))
                for r in range(2):
                    SY.dma_start(
                        out=dst[:, 64 + 1024 * r:64 + 1024 * (r + 1)],
                        in_=ag1o[2048 * r + 128 * srow:2048 * r + 128 * (srow + 1), :],
                        inc=(ld, 16))
            v_ld = s.val(ld)
            T.wait_ge(ld, v_ld)
            T.wait_ge(vs, s.val(vs))
            ps_tb = ps[5][:, 0:256].bitcast(BF16)
            for (src_t, dst_t) in ((v1h, vt1h), (v1l, vt1l)):
                for grp in range(5):
                    n_in = 4 if grp < 4 else 1
                    bank_wait(T, ps[5])
                    for i in range(n_in):
                        c = 4 * grp + i
                        T.transpose(ps_tb[:, 128 * i:128 * (i + 1)],
                                    src_t[:, 128 * c:128 * (c + 1)],
                                    cst["ident_b"][:],
                                    inc=(tn, 1) if i == n_in - 1 else None)
                    V.wait_ge(tn, s.val(tn))
                    V.tensor_copy(
                        out=dst_t[:, E1P * g + 512 * grp:
                                  E1P * g + 512 * grp + 128 * n_in],
                        in_=ps_tb[:, 0:128 * n_in], inc=(vs, 1))
                    bank_done(ps[5], vs)
        kv_done = s.val(vs)
        kv_ld = s.val(ld)

        qbase_col = cst["arange_col"]
        for qt in range(8):
            V.wait_ge(ld, jld)
            V.tensor_scalar_add(gstat[:, 32:40], qbase_col[:], float(128 * qt),
                                inc=(vs, 1))
            V.wait_ge(vs, s.val(vs))
            V.tensor_scalar_add(gstat[:, 40:48], gstat[:, 32:40], -512.0,
                                inc=(vs, 1))
            V.wait_ge(vs, s.val(vs))
            V.tensor_single_scalar(P1[:, :], jr[:, :], gstat[:, 32:33],
                                   ALU.is_le)
            V.tensor_single_scalar(tmp2[:, :], jb[:, :], gstat[:, 40:41],
                                   ALU.is_ge)
            V.tensor_tensor(P1[:, :], P1[:, :], tmp2[:, :], ALU.mult)
            V.tensor_scalar(maskb[:, :], P1[:, :], MASKV, -MASKV, ALU.mult,
                            ALU.add, inc=(vs, 1))
            mask_done = s.val(vs)
            for hq in range(H):
                g = hq // 4
                kb_ = E1P * g
                SY.wait_ge(vs, s.val(vs))
                SY.dma_start(out=qh[:, :],
                             in_=q1f[128 * hq:128 * (hq + 1),
                                     128 * qt:128 * (qt + 1)], inc=(ld, 16))
                SY.dma_start(out=ql[:, :],
                             in_=q1f[128 * (16 + hq):128 * (17 + hq),
                                     128 * qt:128 * (qt + 1)], inc=(ld, 16))
                q_ld = s.val(ld)
                T.wait_ge(ld, q_ld)
                T.wait_ge(vs, mask_done)
                for kb in range(5):
                    n = 512 if kb < 4 else 128
                    pt = ps[kb]
                    bank_wait(T, pt)
                    mm3(pt[:, 0:n], qh[:, :], ql[:, :],
                        k1h[:, kb_ + 512 * kb:kb_ + 512 * kb + n],
                        k1l[:, kb_ + 512 * kb:kb_ + 512 * kb + n],
                        start=True, stop=True,
                        inc=(tn, 1) if kb == 4 else None)
                V.wait_ge(tn, s.val(tn))
                for kb in range(5):
                    n = 512 if kb < 4 else 128
                    V.tensor_tensor(P1[:, 512 * kb:512 * kb + n],
                                    ps[kb][:, 0:n],
                                    maskb[:, 512 * kb:512 * kb + n], ALU.add,
                                    inc=(vs, 1) if kb == 4 else None)
                for kb in range(5):
                    bank_done(ps[kb], vs)
                V.wait_ge(vs, s.val(vs))
                V.reduce_max(gstat[:, 4:5], P1[:, :], AX.X, negate=True,
                             inc=(vs, 1))
                V.wait_ge(vs, s.val(vs))
                V.tensor_scalar_mul(rstat[:, 0:8], gstat[:, 0:8], SCALE1,
                                    inc=(vs, 1))
                C.wait_ge(vs, s.val(vs))
                C.activation(out=P1[:, :], in_=P1[:, :], func=AF.Exp,
                             bias=rstat[:, 4:5], scale=SCALE1, inc=(sc, 1))
                V.wait_ge(sc, s.val(sc))
                V.reduce_sum(gstat[:, 8:9], P1[:, :], AX.X, inc=(vs, 1))
                V.wait_ge(vs, s.val(vs))
                V.reciprocal(rstat[:, 8:16], gstat[:, 8:16], inc=(vs, 1))
                V.wait_ge(vs, s.val(vs))
                V.tensor_scalar_mul(P1[:, :], P1[:, :], rstat[:, 8:9],
                                    inc=(vs, 1))
                pn_done = s.val(vs)
                T.wait_ge(vs, pn_done)
                ps4_r = ps[4][:].rearrange("p (i q) -> p i q", i=4)
                for grp in range(5):
                    n_in = 4 if grp < 4 else 1
                    bank_wait(T, ps[4])
                    for i in range(n_in):
                        c = 4 * grp + i
                        T.transpose(ps[4][:, 128 * i:128 * (i + 1)],
                                    P1[:, 128 * c:128 * (c + 1)],
                                    cst["ident_f"][:],
                                    inc=(tn, 1) if i == n_in - 1 else None)
                    tr_done = s.val(tn)
                    V.wait_ge(tn, tr_done)
                    C.wait_ge(tn, tr_done)
                    oh_ap = pth[:, 512 * grp:512 * grp + 128 * n_in]
                    ol_ap = ptl[:, 512 * grp:512 * grp + 128 * n_in]
                    C.activation(out=oh_ap, in_=ps[4][:, 0:128 * n_in],
                                 func=AF.Copy, inc=(sc, 1))
                    V.wait_ge(sc, s.val(sc))
                    V.tensor_tensor(ol_ap, ps[4][:, 0:128 * n_in], oh_ap,
                                    ALU.subtract, inc=(vs, 1))
                    bank_done(ps[4], vs)
                pt0 = ps[0]
                bank_wait(T, pt0)
                T.wait_ge(vs, s.val(vs))
                T.wait_ge(vs, kv_done)
                for c in range(KC1):
                    mm3(pt0[:, 0:128],
                        vt1h[:, kb_ + 128 * c:kb_ + 128 * (c + 1)],
                        vt1l[:, kb_ + 128 * c:kb_ + 128 * (c + 1)],
                        pth[:, 128 * c:128 * (c + 1)],
                        ptl[:, 128 * c:128 * (c + 1)],
                        start=(c == 0), stop=(c == KC1 - 1),
                        inc=(tn, 1) if c == KC1 - 1 else None)
                pv_done = s.val(tn)
                off = SH * hq + 128 * qt
                C.wait_ge(tn, pv_done)
                C.activation(out=attn_hi[:, off:off + 128], in_=pt0[:, 0:128],
                             func=AF.Copy, inc=(sc, 1))
                V.wait_ge(sc, s.val(sc))
                V.tensor_tensor(attn_lo[:, off:off + 128], pt0[:, 0:128],
                                attn_hi[:, off:off + 128], ALU.subtract,
                                inc=(vs, 1))
                bank_done(pt0, vs)
        return s.val(vs)

    def quant_pipe(src_ap, out_ap, sinv_col, w, qa, qb, qt3, qt4, qt5,
                   signed=True):
        if signed:
            V.tensor_single_scalar(qa[:, 0:w], src_ap, sinv_col, ALU.mult)
            V.tensor_scalar_mul(qb[:, 0:w], qa[:, 0:w], -1.0)
            V.tensor_tensor(qb[:, 0:w], qb[:, 0:w], qa[:, 0:w], ALU.max)
        else:
            V.tensor_single_scalar(qb[:, 0:w], src_ap, sinv_col, ALU.mult)
        V.tensor_single_scalar(qt3[:, 0:w], qb[:, 0:w], 2.0 ** -6, ALU.max,
                               inc=(vs, 1))
        C.wait_ge(vs, s.val(vs))
        C.activation(out=qt3[:, 0:w], in_=qt3[:, 0:w], func=AF.Ln, inc=(sc, 1))
        V.wait_ge(sc, s.val(sc))
        V.tensor_scalar(qt3[:, 0:w], qt3[:, 0:w], LN2_INV, 63.5, ALU.mult,
                        ALU.add)
        V.tensor_copy(out=qt4[:, 0:w], in_=qt3[:, 0:w])
        V.tensor_copy(out=qt3[:, 0:w], in_=qt4[:, 0:w])
        V.tensor_scalar(qt3[:, 0:w], qt3[:, 0:w], -1.0, 194.0, ALU.mult,
                        ALU.add)
        V.tensor_copy(out=qt5[:, 0:w], in_=qt3[:, 0:w])
        V.tensor_single_scalar(qt5[:, 0:w], qt5[:, 0:w], 23,
                               ALU.arith_shift_left)
        V.tensor_scalar_add(qt4[:, 0:w], qt4[:, 0:w], 60)
        V.tensor_single_scalar(qt4[:, 0:w], qt4[:, 0:w], 23,
                               ALU.arith_shift_left)
        V.tensor_tensor(qb[:, 0:w], qb[:, 0:w], qt5[:, 0:w].bitcast(F32),
                        ALU.mult)
        V.tensor_copy(out=qt5[:, 0:w], in_=qb[:, 0:w])
        V.tensor_copy(out=qb[:, 0:w], in_=qt5[:, 0:w])
        V.tensor_tensor(qb[:, 0:w], qb[:, 0:w], qt4[:, 0:w].bitcast(F32),
                        ALU.mult)
        if signed:
            V.tensor_scalar(qa[:, 0:w], qa[:, 0:w], 0.0, 2.0, ALU.is_ge,
                            ALU.mult)
            V.tensor_scalar_add(qa[:, 0:w], qa[:, 0:w], -1.0)
            V.tensor_tensor(out_ap, qb[:, 0:w], qa[:, 0:w], ALU.mult,
                            inc=(vs, 1))
        else:
            V.tensor_copy(out=out_ap, in_=qb[:, 0:w], inc=(vs, 1))

    def pmax_gather(col_src_ap, grow_sl, srow_col, tag):
        GQ.wait_ge(vs, s.val(vs))
        GQ.dma_start(out=gcol.ap()[:, 0:1], in_=col_src_ap, inc=(gq, 16))
        GQ.wait_ge(gq, s.val(gq))
        GQ.dma_start(out=grow[0:1, grow_sl],
                     in_=gcol.ap()[:, 0:1].rearrange("p o -> o p"),
                     inc=(gq, 16))
        V.wait_ge(gq, s.val(gq))
        V.reduce_max(grow[0:1, srow_col:srow_col + 1],
                     grow[0:1, grow_sl], AX.X, inc=(vs, 1))
        V.wait_ge(vs, s.val(vs))

    def emit_l1_ffn():
        X80 = 0
        WF0 = 32768
        W80 = 40960
        QA0, QB0, QT3 = 49152, 53248, 57344
        QT4, QT5 = 61440, 65536
        HS0 = 69632
        H20 = 70144
        H80 = 74240
        WP0 = 76288
        x8 = A(X80, 128, NCH * SH, BF16)
        wf = [A(WF0, 128, 1024), A(WF0 + 4096, 128, 1024)]
        w8b = [A(W80, 128, 2048, BF16), A(W80 + 4096, 128, 2048, BF16)]
        qa = A(QA0, 128, 1024)
        qb = A(QB0, 128, 1024)
        qt3 = A(QT3, 128, 1024)
        qt4 = A(QT4, 128, 1024, I32)
        qt5 = A(QT5, 128, 1024, I32)
        hstat = A(HS0, 128, 128)
        h2s = [A(H20, 128, 512), A(H20 + 2048, 128, 512)]
        h8b = [A(H80, 128, 512, BF16), A(H80 + 1024, 128, 512, BF16)]
        wpb = [A(WP0, 128, 512, BF16), A(WP0 + 1024, 128, 512, BF16)]
        agwo = agw_out.ap()
        agpo = agp_out.ap()
        h2sv = h2_save.ap()
        h8sv = h8_save.ap()

        ent_v = s.val(vs)
        SY.wait_ge(vs, ent_v)
        SY.wait_ge(tn, s.val(tn))
        SY.wait_ge(sc, s.val(sc))
        V.memset(grow[:], 0.0)
        V.memset(hstat[:, :], 0.0, inc=(vs, 1))

        for c in range(NCH):
            V.reduce_max(hstat[:, 24 + c:25 + c], R[:, SH * c:SH * (c + 1)],
                         AX.X, apply_absolute_value=True,
                         inc=(vs, 1) if c == NCH - 1 else None)
        V.wait_ge(vs, s.val(vs))
        V.reduce_max(hstat[:, 40:41], hstat[:, 24:40], AX.X, inc=(vs, 1))
        pmax_gather(hstat[:, 40:41], slice(264, 392), 2, "x")
        wfree = [(vs, s.val(vs)), (vs, s.val(vs))]
        for j in range(8):
            for hf in range(2):
                b = (2 * j + hf) % 2
                SY.wait_ge(*wfree[b])
                SY.dma_start(out=wf[b][:, :],
                             in_=wfc_shard[j:j + 1, :, 1024 * hf:1024 * (hf + 1)]
                             .rearrange("o p n -> p (o n)"), inc=(wlp[b], 16))
                V.wait_ge(wlp[b], s.val(wlp[b]))
                V.reduce_max(hstat[:, 2 * j + hf:2 * j + hf + 1], wf[b][:, :],
                             AX.X, apply_absolute_value=True, inc=(vs, 1))
                wfree[b] = (vs, s.val(vs))
        V.wait_ge(vs, s.val(vs))
        V.reduce_max(hstat[:, 41:42], hstat[:, 0:16], AX.X, inc=(vs, 1))
        pmax_gather(hstat[:, 41:42], slice(8, 136), 0, "wfc")
        wfree = [(vs, s.val(vs)), (vs, s.val(vs))]
        for j in range(2):
            for pc in range(8):
                b = (8 * j + pc) % 2
                SY.wait_ge(*wfree[b])
                SY.dma_start(out=wf[b][:, :],
                             in_=wproj_shard[j:j + 1, :,
                                             1024 * pc:1024 * (pc + 1)]
                             .rearrange("o p n -> p (o n)"),
                             inc=(wlp[b], 16))
                V.wait_ge(wlp[b], s.val(wlp[b]))
                V.reduce_max(hstat[:, 48 + 8 * j + pc:49 + 8 * j + pc],
                             wf[b][:, :], AX.X, apply_absolute_value=True,
                             inc=(vs, 1))
                wfree[b] = (vs, s.val(vs))
        V.wait_ge(vs, s.val(vs))
        V.reduce_max(hstat[:, 42:43], hstat[:, 48:64], AX.X, inc=(vs, 1))
        pmax_gather(hstat[:, 42:43], slice(136, 264), 1, "wproj")
        GQ.wait_ge(vs, s.val(vs))
        GQ.dma_start(out=ar_in.ap(), in_=grow[0:1, 0:8], inc=(gq, 16))
        GQ.wait_ge(gq, s.val(gq))
        GQ.collective_compute("AllReduce", ALU.max, replica_groups=ALL8,
                              ins=[ar_in.ap().opt()], outs=[ar_out.ap().opt()],
                              inc=(cc, 1))
        GQ.wait_ge(cc, s.val(cc))
        GQ.dma_start(out=grow[0:1, 392:400], in_=ar_out.ap(), inc=(gq, 16))
        bank_wait(T, ps[4])
        T.wait_ge(gq, s.val(gq))
        T.matmul(ps[4][:, 0:8], cst["ones_row"][:], grow[0:1, 392:400],
                 start=True, stop=True, inc=(tn, 1))
        V.wait_ge(tn, s.val(tn))
        V.tensor_copy(out=sstat[:, 0:8], in_=ps[4][:, 0:8], inc=(vs, 1))
        bank_done(ps[4], vs)
        V.wait_ge(vs, s.val(vs))
        V.tensor_scalar(sstat[:, 8:16], sstat[:, 0:8], 1.0 / FP8_MAX, 1e-6,
                        ALU.mult, ALU.max, inc=(vs, 1))
        V.wait_ge(vs, s.val(vs))
        V.reciprocal(rstat[:, 16:24], sstat[:, 8:16], inc=(vs, 1))
        V.wait_ge(vs, s.val(vs))
        V.tensor_single_scalar(gstat[:, 48:56], sstat[:, 8:16],
                               sstat[:, 8:9], ALU.mult, inc=(vs, 1))
        V.wait_ge(vs, s.val(vs))
        agwi = agw_in.ap()
        wfree = [(vs, s.val(vs)), (vs, s.val(vs))]
        for j in range(8):
            V.wait_ge(gq, s.val(gq))
            for hf in range(2):
                b = (2 * j + hf) % 2
                SY.wait_ge(*wfree[b])
                SY.dma_start(out=wf[b][:, :],
                             in_=wfc_shard[j:j + 1, :, 1024 * hf:1024 * (hf + 1)]
                             .rearrange("o p n -> p (o n)"), inc=(wlp[b], 16))
                V.wait_ge(wlp[b], s.val(wlp[b]))
                quant_pipe(wf[b][:, :], w8b[0][:, 1024 * hf:1024 * (hf + 1)],
                           rstat[:, 16:17], 1024, qa, qb, qt3, qt4, qt5)
                wfree[b] = (vs, s.val(vs))
            GQ.wait_ge(vs, s.val(vs))
            GQ.dma_start(out=agwi[128 * j:128 * (j + 1), :], in_=w8b[0][:, :],
                         inc=(gq, 16))
        agpi = agp_in.ap()
        wfree = [(vs, s.val(vs)), (vs, s.val(vs))]
        w8free = [(gqp[0], s.val(gqp[0])), (gqp[1], s.val(gqp[1]))]
        for j in range(2):
            for pc in range(8):
                b = pc % 2
                SY.wait_ge(*wfree[b])
                SY.dma_start(out=wf[b][:, :],
                             in_=wproj_shard[j:j + 1, :,
                                             1024 * pc:1024 * (pc + 1)]
                             .rearrange("o p n -> p (o n)"),
                             inc=(wlp[b], 16))
                V.wait_ge(wlp[b], s.val(wlp[b]))
                V.wait_ge(*w8free[b])
                quant_pipe(wf[b][:, :], w8b[b][:, 0:1024],
                           rstat[:, 17:18], 1024, qa, qb, qt3, qt4, qt5)
                wfree[b] = (vs, s.val(vs))
                GQ.wait_ge(vs, s.val(vs))
                GQ.dma_start(out=agpi[128 * j:128 * (j + 1),
                                      1024 * pc:1024 * (pc + 1)],
                             in_=w8b[b][:, 0:1024], inc=(gqp[b], 16))
                w8free[b] = (gqp[b], s.val(gqp[b]))
        for c in range(NCH):
            quant_pipe(R[:, SH * c:SH * (c + 1)],
                       x8[:, SH * c:SH * (c + 1)], rstat[:, 18:19], 1024,
                       qa, qb, qt3, qt4, qt5)
        x8_done = s.val(vs)
        GQ.wait_ge(gq, s.val(gq))
        GQ.wait_ge(gq2, s.val(gq2))
        GQ.collective_compute("AllGather", ALU.bypass, replica_groups=ALL8,
                              ins=[agw_in.ap().opt()], outs=[agw_out.ap().opt()],
                              inc=(cc, 1))
        GQ.collective_compute("AllGather", ALU.bypass, replica_groups=ALL8,
                              ins=[agp_in.ap().opt()], outs=[agp_out.ap().opt()],
                              inc=(cc, 1))
        wg_ready = s.val(cc)
        SY.wait_ge(cc, wg_ready)

        w8_loaded = [None, None]
        h2_stored = [(gqp[0], s.val(gqp[0])), (gqp[1], s.val(gqp[1]))]
        w8_freed = [(tn, s.val(tn)), (tn, s.val(tn))]
        for oc in range(64):
            b = oc % 2
            SY.wait_ge(*w8_freed[b])
            r, lj = oc // 8, oc % 8
            SY.dma_start(out=w8b[b][:, :],
                         in_=agwo[1024 * r + 128 * lj:1024 * r + 128 * (lj + 1), :],
                         inc=(wlp[b], 16))
            w8_loaded[b] = (wlp[b], s.val(wlp[b]))
            for tb in range(NTB):
                pt = ps[2 + tb]
                bank_wait(T, pt)
                T.wait_ge(*w8_loaded[b])
                T.wait_ge(vs, x8_done)
                for c in range(NCH):
                    T.matmul(pt[:], w8b[b][:, 128 * c:128 * (c + 1)],
                             x8[:, SH * c + 512 * tb:SH * c + 512 * (tb + 1)],
                             start=(c == 0), stop=(c == NCH - 1),
                             inc=(tn, 1) if c == NCH - 1 else None)
                w8_freed[b] = (tn, s.val(tn))
                hb = h2s[tb]
                C.wait_ge(tn, s.val(tn))
                C.wait_ge(*h2_stored[tb])
                C.activation(out=hb[:, :], in_=pt[:], func=AF.Relu,
                             scale=gstat[:, 50:51])
                C.activation(out=hb[:, :], in_=hb[:, :], func=AF.Square,
                             inc=(sc, 1))
                sq_done = s.val(sc)
                bank_done(pt, sc)
                V.wait_ge(sc, sq_done)
                V.reduce_max(hstat[:, 2 * oc + tb:2 * oc + tb + 1],
                             hb[:, :], AX.X, inc=(vs, 1))
                GQ.wait_ge(vs, s.val(vs))
                GQ.dma_start(out=h2sv[128 * oc:128 * (oc + 1),
                                      512 * tb:512 * (tb + 1)],
                             in_=hb[:, :], inc=(gqp[tb], 16))
                h2_stored[tb] = (gqp[tb], s.val(gqp[tb]))
        V.wait_ge(vs, s.val(vs))
        V.reduce_max(hstat[:, 40:41], hstat[:, 0:128], AX.X, inc=(vs, 1))
        V.wait_ge(vs, s.val(vs))
        V.memset(grow[0:1, 0:8], 0.0, inc=(vs, 1))
        pmax_gather(hstat[:, 40:41], slice(8, 136), 0, "h")
        GQ.wait_ge(vs, s.val(vs))
        GQ.dma_start(out=ar2_in.ap(), in_=grow[0:1, 0:8], inc=(gq, 16))
        GQ.wait_ge(gq, s.val(gq))
        GQ.collective_compute("AllReduce", ALU.max, replica_groups=ALL8,
                              ins=[ar2_in.ap().opt()], outs=[ar2_out.ap().opt()],
                              inc=(cc, 1))
        GQ.wait_ge(cc, s.val(cc))
        GQ.dma_start(out=grow[0:1, 392:400], in_=ar2_out.ap(), inc=(gq, 16))
        bank_wait(T, ps[4])
        T.wait_ge(gq, s.val(gq))
        T.matmul(ps[4][:, 0:8], cst["ones_row"][:], grow[0:1, 392:400],
                 start=True, stop=True, inc=(tn, 1))
        V.wait_ge(tn, s.val(tn))
        V.tensor_copy(out=sstat[:, 0:8], in_=ps[4][:, 0:8], inc=(vs, 1))
        bank_done(ps[4], vs)
        V.wait_ge(vs, s.val(vs))
        V.tensor_scalar(sstat[:, 0:8], sstat[:, 0:8], 1.0 / FP8_MAX, 1e-6,
                        ALU.mult, ALU.max, inc=(vs, 1))
        V.wait_ge(vs, s.val(vs))
        V.reciprocal(rstat[:, 24:32], sstat[:, 0:8], inc=(vs, 1))
        V.wait_ge(vs, s.val(vs))
        V.tensor_single_scalar(gstat[:, 56:64], sstat[:, 8:16],
                               sstat[:, 0:1], ALU.mult, inc=(vs, 1))
        V.wait_ge(vs, s.val(vs))
        h2free = [(vs, s.val(vs)), (vs, s.val(vs))]
        h8free = [(gqp[0], s.val(gqp[0])), (gqp[1], s.val(gqp[1]))]
        for oc in range(64):
            for tb in range(NTB):
                b = (2 * oc + tb) % 2
                SY.wait_ge(*h2free[b])
                SY.dma_start(out=h2s[b][:, :],
                             in_=h2sv[128 * oc:128 * (oc + 1),
                                      512 * tb:512 * (tb + 1)],
                             inc=(wlp[b], 16))
                V.wait_ge(wlp[b], s.val(wlp[b]))
                V.wait_ge(*h8free[b])
                quant_pipe(h2s[b][:, :], h8b[b][:, :], rstat[:, 24:25], 512,
                           qa, qb, qt3, qt4, qt5, signed=False)
                h2free[b] = (vs, s.val(vs))
                GQ.wait_ge(vs, s.val(vs))
                GQ.dma_start(out=h8sv[128 * oc:128 * (oc + 1),
                                      512 * tb:512 * (tb + 1)],
                             in_=h8b[b][:, :], inc=(gqp[b], 16))
                h8free[b] = (gqp[b], s.val(gqp[b]))
        y_full = A(0, 128, 16384)
        SY.wait_ge(gq, s.val(gq))
        SY.wait_ge(gq2, s.val(gq2))
        for tb in range(NTB):
            for dcg in range(4):
                for i in range(4):
                    bank_wait(T, ps[i])
                buf_free = [(tn, s.val(tn)), (tn, s.val(tn))]
                for oc in range(64):
                    b = oc % 2
                    SY.wait_ge(*buf_free[b])
                    SY.dma_start(out=h8b[b][:, :],
                                 in_=h8sv[128 * oc:128 * (oc + 1),
                                          512 * tb:512 * (tb + 1)],
                                 inc=(wlp[b], 16))
                    for i in range(4):
                        dc = 4 * dcg + i
                        SY.dma_start(out=wpb[b][:, 128 * i:128 * (i + 1)],
                                     in_=agpo[128 * dc:128 * (dc + 1),
                                              128 * oc:128 * (oc + 1)],
                                     inc=(wlp[b], 16))
                    T.wait_ge(wlp[b], s.val(wlp[b]))
                    for i in range(4):
                        T.matmul(ps[i][:],
                                 wpb[b][:, 128 * i:128 * (i + 1)],
                                 h8b[b][:, :], start=(oc == 0),
                                 stop=(oc == 63),
                                 inc=(tn, 1) if i == 3 else None)
                    buf_free[b] = (tn, s.val(tn))
                V.wait_ge(tn, s.val(tn))
                for i in range(4):
                    dc = 4 * dcg + i
                    ysl = y_full[:, SH * dc + 512 * tb:SH * dc + 512 * (tb + 1)]
                    V.scalar_tensor_tensor(
                        ysl, ps[i][:], gstat[:, 57:58],
                        R[:, SH * dc + 512 * tb:SH * dc + 512 * (tb + 1)],
                        ALU.mult, ALU.add, inc=(vs, 1))
                    bank_done(ps[i], vs)
        sqf = [h2s[0], h2s[1]]
        for tb in range(NTB):
            layer_norm_tb(
                lambda c, tb=tb: y_full[:, SH * c + 512 * tb:
                                        SH * c + 512 * (tb + 1)], tb, sqf)
        return s.val(vs)

    r_ready = s.val(vs)
    if "l0proj" in phases:
        agr = emit_l0_proj()
        if dbg == "ag0":
            GQ.wait_ge(cc, agr)
            GQ.dma_start(out=dbg_out, in_=ag0_out.ap(), inc=(gq, 16))
    if "l0core" in phases:
        agr = emit_l0_proj()
        attn_done = emit_l0_core(agr)
        if dbg == "attn_raw":
            GQ.wait_ge(vs, attn_done)
            GQ.wait_ge(sc, s.val(sc))
            GQ.dma_start(out=dbg_out.rearrange("(c p) t -> p c t", p=128),
                         in_=R[:].rearrange("p (c t) -> p c t", c=NCH),
                         inc=(gq, 16))
    if "l0attn" in phases:
        agr = emit_l0_proj()
        attn_done = emit_l0_core(agr)
        if dbg == "attn_raw":
            GQ.wait_ge(vs, attn_done)
            GQ.wait_ge(sc, s.val(sc))
            GQ.dma_start(out=dbg_out.rearrange("(c p) t -> p c t", p=128),
                         in_=R[:].rearrange("p (c t) -> p c t", c=NCH),
                         inc=(gq, 16))
        else:
            r_ready = emit_outproj_ln(wout_img, "x", attn_done)

    if "l0ffn" in phases:
        r_ready = emit_l0_ffn()
    if "l1attn" in phases:
        attn1_done = emit_l1_attn()
        r_ready = emit_outproj_ln(wo1_img, "rsave", attn1_done)
    if "l1ffn" in phases:
        r_ready = emit_l1_ffn()
    if dbg == "ffnstats":
        GQ.wait_ge(vs, s.val(vs))
        GQ.wait_ge(sc, s.val(sc))
        GQ.dma_start(out=dbg_out[0:128, 0:16], in_=sstat[:], inc=(gq, 16))
        GQ.dma_start(out=dbg_out[128:256, 0:64], in_=rstat[:], inc=(gq, 16))
        GQ.dma_start(out=dbg_out[256:384, 0:64], in_=gstat[:], inc=(gq, 16))
        GQ.dma_start(out=dbg_out[384:385, 0:1024], in_=grow[0:1, 0:1024],
                     inc=(gq, 16))
        GQ.dma_start(out=dbg_out[512:640, :], in_=h2_save.ap()[0:128, :],
                     inc=(gq, 16))
        GQ.dma_start(out=dbg_out[640:768, :], in_=h2_save.ap()[1024:1152, :],
                     inc=(gq, 16))
    if dbg == "r_l1attn":
        GQ.wait_ge(vs, r_ready)
        GQ.dma_start(out=dbg_out.rearrange("(c p) t -> p c t", p=128),
                     in_=R[:].rearrange("p (c t) -> p c t", c=NCH), inc=(gq, 16))
    if dbg == "r_l0ffn":
        GQ.wait_ge(vs, r_ready)
        GQ.dma_start(out=dbg_out.rearrange("(c p) t -> p c t", p=128),
                     in_=R[:].rearrange("p (c t) -> p c t", c=NCH), inc=(gq, 16))
    if dbg == "r_l0attn":
        GQ.wait_ge(vs, r_ready)
        GQ.dma_start(out=dbg_out.rearrange("(c p) t -> p c t", p=128),
                     in_=R[:].rearrange("p (c t) -> p c t", c=NCH), inc=(gq, 16))

    GQ.wait_ge(vs, s.val(vs))
    GQ.wait_ge(sc, s.val(sc))
    GQ.wait_ge(tn, s.val(tn))
    GQ.dma_start(out=y_out.rearrange("(c p) t -> p c t", p=128),
                 in_=R[:].rearrange("p (c t) -> p c t", c=NCH), inc=(gq, 16))
    GQ.wait_ge(gq, s.val(gq))

    with nc.Block() as block:
        @block.sync
        def _(h):
            for fn in s.q["sync"]:
                fn(h)

        @block.tensor
        def _(h):
            for fn in s.q["tensor"]:
                fn(h)

        @block.vector
        def _(h):
            for fn in s.q["vector"]:
                fn(h)

        @block.scalar
        def _(h):
            for fn in s.q["scalar"]:
                fn(h)

        @block.gpsimd
        def _(h):
            for fn in s.q["gpsimd"]:
                fn(h)

    ctx.close()
    return nc


def make_inputs(core, inputs):
    b, hh = core // 2, core % 2
    tsl = slice(SH * hh, SH * (hh + 1))
    x = np.asarray(inputs["L0_attn_x"])
    d = {}
    d["x_fm"] = np.ascontiguousarray(x[b].T[:, tsl])

    wqkvT = np.ascontiguousarray(np.asarray(inputs["L0_attn_w_qkv"]).T)
    d["q0_img"] = tile_lhsT(wqkvT[:, 0:2048])
    d["k0_img"] = tile_lhsT(wqkvT[:, 2048:4096])
    d["v0_img"] = tile_lhsT(wqkvT[:, 4096:6144])
    d["wout_img"] = tile_lhsT(np.ascontiguousarray(
        np.asarray(inputs["L0_attn_w_out"]).T))

    pos0 = np.arange(SH * hh, SH * (hh + 1))
    (ch, cl), (sh_, sl_) = rope_tables(pos0, HD, 10000.0)
    d["cos0_hi"], d["cos0_lo"] = ch, cl
    d["sin0_hi"], d["sin0_lo"] = sh_, sl_

    d["wffn_img"] = tile_lhsT(np.ascontiguousarray(np.asarray(inputs["L0_ffn_W"]).T))
    A_ = np.asarray(inputs["L0_ffn_lora_A"])
    d["aT_img"] = tile_lhsT(np.ascontiguousarray(
        A_.transpose(2, 0, 1).reshape(2048, 128)))
    B_ = np.asarray(inputs["L0_ffn_lora_B"])
    d["bstack_img"] = tile_lhsT(np.ascontiguousarray(
        B_.transpose(0, 2, 1).reshape(128, 2048)))
    d["gate_w"] = np.ascontiguousarray(
        np.asarray(inputs["L0_ffn_gate_w"]).T).reshape(16, 128, 8)
    d["gate_b"] = np.ascontiguousarray(np.broadcast_to(
        np.asarray(inputs["L0_ffn_gate_b"])[None, :], (128, 8))).astype(np.float32)
    d["bffn"] = np.ascontiguousarray(
        np.asarray(inputs["L0_ffn_b"]).reshape(16, 128).T)
    ee = np.zeros((8, 128), np.float32)
    for e in range(8):
        ee[e, 16 * e:16 * (e + 1)] = 1.0
    d["e_expand"] = ee

    d["wq1_img"] = tile_lhsT(np.asarray(inputs["L1_attn_Wq"]))
    d["wk1_img"] = tile_lhsT(np.asarray(inputs["L1_attn_Wk"]))
    d["wv1_img"] = tile_lhsT(np.asarray(inputs["L1_attn_Wv"]))
    d["wo1_img"] = tile_lhsT(np.asarray(inputs["L1_attn_Wo"]))
    d["gt_img"] = np.ascontiguousarray(
        np.asarray(inputs["L1_attn_global_tokens"]).T).reshape(16, 128, G)
    pos1 = G + SH * hh + np.arange(SH)
    (c1h, c1l), (s1h, s1l) = rope_tables(pos1, HD, 500000.0)
    d["cos1_hi"], d["cos1_lo"] = c1h, c1l
    d["sin1_hi"], d["sin1_lo"] = s1h, s1l
    (cgh, cgl), (sgh, sgl) = rope_tables(np.arange(G), HD, 500000.0)
    d["cos1g"], d["sin1g"] = cgh, sgh
    d["cos1g_lo"], d["sin1g_lo"] = cgl, sgl

    j = np.arange(E1P, dtype=np.float64)
    d["jrow"] = np.ascontiguousarray(
        np.broadcast_to(j[None, :], (128, E1P))).astype(np.float32)
    jb = j + np.where((j % 256) == 0, 1e9, 0.0)
    d["jbig"] = np.ascontiguousarray(
        np.broadcast_to(jb[None, :], (128, E1P))).astype(np.float32)

    if "_wfc_img" not in _SHARED:
        _SHARED["_wfc_img"] = tile_lhsT(np.ascontiguousarray(
            np.asarray(inputs["L1_ffn_w_fc"]).T))
        _SHARED["_wproj_img"] = tile_lhsT(np.ascontiguousarray(
            np.asarray(inputs["L1_ffn_w_proj"]).T))
    d["wfc_shard"] = np.ascontiguousarray(
        _SHARED["_wfc_img"][8 * core:8 * (core + 1)])
    d["wproj_shard"] = np.ascontiguousarray(
        _SHARED["_wproj_img"][2 * core:2 * (core + 1)])

    d["ident_b"] = np.eye(128).astype(np.float32)
    import ml_dtypes
    d["ident_b"] = np.eye(128).astype(ml_dtypes.bfloat16)
    d["ident_f"] = np.eye(128, dtype=np.float32)
    oc = np.zeros((128, 8), np.float32)
    oc[:, 0] = 1.0
    d["ones_col"] = oc
    d["ones_row"] = np.ones((1, 128), np.float32)
    d["eps_row"] = np.full((1, 512), EPS, np.float32)
    d["arange_col"] = np.broadcast_to(
        (G + SH * hh + np.arange(128, dtype=np.float32))[:, None],
        (128, 8)).copy()
    cc_ = np.zeros((128, 8), np.float32)
    cc_[:, 0] = -0.5
    cc_[:, 1] = 124.0
    cc_[:, 2] = 130.0
    d["ccol"] = cc_
    return d


_NC_CACHE = {}
_SHARED = {}


def kernel(**inputs):
    _SHARED.clear()
    key = ("full",)
    if key not in _NC_CACHE:
        _NC_CACHE[key] = build_program(ALL_PHASES)
    nc = _NC_CACHE[key]
    in_maps = [make_inputs(c, inputs) for c in range(8)]
    res = run_bass_kernel_spmd(nc, in_maps, core_ids=list(range(8)))
    out = np.empty((B, DM, S), np.float32)
    for c in range(8):
        b, hh = c // 2, c % 2
        out[b][:, SH * hh:SH * (hh + 1)] = res.results[c]["y_fm"]
    return np.ascontiguousarray(out.transpose(0, 2, 1))


# version 08
# speedup vs baseline: 1.2171x; 1.0552x over previous
import math
from contextlib import ExitStack

import numpy as np
import concourse.bass as bass
import concourse.mybir as mybir
from concourse.bass_utils import run_bass_kernel_spmd

F32 = mybir.dt.float32
BF16 = mybir.dt.bfloat16
I32 = mybir.dt.int32
U8 = mybir.dt.uint8
AF = mybir.ActivationFunctionType
ALU = mybir.AluOpType
AX = mybir.AxisListType

B, S, DM = 4, 2048, 2048
H, HD = 16, 128
SH = 1024
NCH = DM // 128
NTB = 2
G = 64
E1 = 2112
E1P = 2176
KC1 = E1P // 128
NKV = 4
SCALE0 = 1.0 / math.sqrt(HD)
SCALE1 = HD ** (-0.5)
FMIN = float(np.finfo(np.float32).min)
MASKV = float(2.0 ** 126)
LN2_INV = 1.0 / math.log(2.0)
EPS = 1e-5
FP8_MAX = 448.0
ARENA = 124 * 1024

PAIRS = [[0, 1], [2, 3], [4, 5], [6, 7]]
ALL8 = [[0, 1, 2, 3, 4, 5, 6, 7]]

ALL_PHASES = ("l0attn", "l0ffn", "l1attn", "l1ffn")
DEV = {}


class Sched:
    ENGINES = ("sync", "tensor", "vector", "scalar", "gpsimd")

    def __init__(self):
        self.q = {e: [] for e in self.ENGINES}
        self.cnt = {}

    def bump(self, sem, n):
        k = id(sem)
        self.cnt[k] = self.cnt.get(k, 0) + n
        return self.cnt[k]

    def val(self, sem):
        return self.cnt.get(id(sem), 0)


class Rec:
    def __init__(self, sched, key):
        self.__dict__["_s"] = sched
        self.__dict__["_k"] = key

    def __getattr__(self, attr):
        s = self.__dict__["_s"]
        k = self.__dict__["_k"]

        def call(*a, **kw):
            inc = kw.pop("inc", None)
            if inc is not None:
                s.bump(inc[0], inc[1])

            def run(h, a=a, kw=kw, inc=inc, attr=attr):
                inst = getattr(h, attr)(*a, **kw)
                if inc is not None:
                    inst.then_inc(*inc)
            s.q[k].append(run)
        return call


def tile_lhsT(w):
    K, M = w.shape
    kc, mc = K // 128, M // 128
    return np.ascontiguousarray(
        w.reshape(kc, 128, mc, 128).transpose(2, 1, 0, 3).reshape(mc, 128, kc * 128)
    ).astype(np.float32)


def rope_tables(positions, dim, base):
    inv = 1.0 / (float(base) ** (np.arange(0, dim, 2, dtype=np.float64) / dim))
    freqs = np.outer(np.asarray(positions, dtype=np.float64), inv)
    emb = np.concatenate([freqs, freqs], axis=-1)

    def hl(t64):
        t = np.ascontiguousarray(t64.T)
        hi = t.astype(np.float32)
        lo = (t - hi.astype(np.float64)).astype(np.float32)
        return hi, np.ascontiguousarray(lo)
    return hl(np.cos(emb)), hl(np.sin(emb))


def build_program(phases=ALL_PHASES, dbg=None):
    nc = bass.Bass("TRN2", target_bir_lowering=False, detect_race_conditions=False)
    s = Sched()
    SY, T, V, C, GQ = (Rec(s, "sync"), Rec(s, "tensor"), Rec(s, "vector"),
                       Rec(s, "scalar"), Rec(s, "gpsimd"))

    def inp(name, shape, dt=F32):
        return nc.dram_tensor(name, list(shape), dt, kind="ExternalInput").ap()

    def dram(name, shape, dt=F32):
        return nc.dram_tensor(name, list(shape), dt)

    x_in = inp("x_fm", (DM, SH))
    q0_img = inp("q0_img", (16, 128, NCH * 128))
    k0_img = inp("k0_img", (16, 128, NCH * 128))
    v0_img = inp("v0_img", (16, 128, NCH * 128))
    wout_img = inp("wout_img", (16, 128, NCH * 128))
    cos0_h = inp("cos0_hi", (128, SH)); cos0_l = inp("cos0_lo", (128, SH))
    sin0_h = inp("sin0_hi", (128, SH)); sin0_l = inp("sin0_lo", (128, SH))
    wffn_img = inp("wffn_img", (16, 128, NCH * 128))
    aT_img = inp("aT_img", (1, 128, NCH * 128))
    bstack_img = inp("bstack_img", (16, 128, 128))
    gate_w_in = inp("gate_w", (16, 128, 8))
    wq1_img = inp("wq1_img", (16, 128, NCH * 128))
    wk1_img = inp("wk1_img", (4, 128, NCH * 128))
    wv1_img = inp("wv1_img", (4, 128, NCH * 128))
    wo1_img = inp("wo1_img", (16, 128, NCH * 128))
    gt_img = inp("gt_img", (16, 128, G))
    cos1_h = inp("cos1_hi", (128, SH)); cos1_l = inp("cos1_lo", (128, SH))
    sin1_h = inp("sin1_hi", (128, SH)); sin1_l = inp("sin1_lo", (128, SH))
    cos1g = inp("cos1g", (128, G)); sin1g = inp("sin1g", (128, G))
    cos1g_l = inp("cos1g_lo", (128, G)); sin1g_l = inp("sin1g_lo", (128, G))
    jrow_in = inp("jrow", (128, E1P))
    jbig_in = inp("jbig", (128, E1P))
    wfc_shard = inp("wfc_shard", (8, 128, NCH * 128))
    wproj_shard = inp("wproj_shard", (2, 128, 64 * 128))
    ident_b_in = inp("ident_b", (128, 128), BF16)
    ident_f_in = inp("ident_f", (128, 128), F32)
    ones_col_in = inp("ones_col", (128, 8))
    ones_row_in = inp("ones_row", (1, 128))
    eps_row_in = inp("eps_row", (1, 512))
    bffn_in = inp("bffn", (128, 16))
    gate_b_in = inp("gate_b", (128, 8))
    e_expand_in = inp("e_expand", (8, 128))
    ccol_in = inp("ccol", (128, 8))
    arange_col_in = inp("arange_col", (128, 8))

    y_out = nc.dram_tensor("y_fm", [DM, SH], F32, kind="ExternalOutput").ap()
    if dbg == "ag0":
        dbg_out = nc.dram_tensor("dbg", [2 * 64 * 128, SH], BF16,
                                 kind="ExternalOutput").ap()
    elif dbg:
        dbg_out = nc.dram_tensor("dbg", [DM, SH], F32, kind="ExternalOutput").ap()
    else:
        dbg_out = None

    ag0_in = dram("ag0_in", (64 * 128, SH), BF16)
    ag0_out = dram("ag0_out", (2 * 64 * 128, SH), BF16)
    q0_save = dram("q0_save", (32 * 128, SH), BF16)
    ag1_in = dram("ag1_in", (16 * 128, SH), BF16)
    ag1_out = dram("ag1_out", (2 * 16 * 128, SH), BF16)
    q1_save = dram("q1_save", (32 * 128, SH), BF16)
    r_save = dram("r_save", (DM, SH), F32)
    h2_save = dram("h2_save", (64 * 128, SH), F32)
    agw_in = dram("agw_in", (8 * 128, 2048), BF16)
    agw_out = dram("agw_out", (64 * 128, 2048), BF16)
    agp_in = dram("agp_in", (2 * 128, 8192), BF16)
    agp_out = dram("agp_out", (16 * 128, 8192), BF16)
    ar_in = dram("ar_in", (1, 8), F32)
    ar_out = dram("ar_out", (1, 8), F32)
    ar2_in = dram("ar2_in", (1, 8), F32)
    ar2_out = dram("ar2_out", (1, 8), F32)
    h8_save = dram("h8_save", (64 * 128, SH), BF16)
    gcol = dram("gcol", (128, 1), F32)

    R = nc.alloc_sbuf_tensor("R", [128, NCH * SH], F32)
    arena_t = nc.alloc_sbuf_tensor("arena", [128, ARENA], U8)
    cst = {}
    for nm, shape, dt in [
        ("ident_b", (128, 128), BF16), ("ident_f", (128, 128), F32),
        ("ones_col", (128, 8), F32), ("ones_row", (1, 128), F32),
        ("eps_row", (1, 512), F32), ("bffn", (128, 16), F32),
        ("gate_b", (128, 8), F32), ("e_expand", (8, 128), F32),
        ("ccol", (128, 8), F32), ("arange_col", (128, 8), F32),
    ]:
        cst[nm] = nc.alloc_sbuf_tensor("c_" + nm, list(shape), dt)
    lnrow = nc.alloc_sbuf_tensor("lnrow", [1, 2048], F32)
    grow = nc.alloc_sbuf_tensor("grow", [1, 1032], F32)
    gstat = nc.alloc_sbuf_tensor("gstat", [128, 64], F32)
    rstat = nc.alloc_sbuf_tensor("rstat", [128, 64], F32)
    sstat = nc.alloc_sbuf_tensor("sstat", [128, 16], F32)

    def A(off, p, n, dt=F32):
        esz = mybir.dt.size(dt)
        assert off % 64 == 0 and off + n * esz <= ARENA, (off, n, dt)
        view = arena_t[:, off:off + n * esz].bitcast(dt)
        return view if p == 128 else view[0:p, :]

    ctx = ExitStack()
    sem = {nm: ctx.enter_context(nc.semaphore(nm))
           for nm in ["ld", "wl", "tn", "vs", "sc", "gq", "cc", "wl2", "gq2",
                      "pe"]}
    ld, wl, tn, vs, sc, gq, cc = (sem[n] for n in
                                  ["ld", "wl", "tn", "vs", "sc", "gq", "cc"])
    wl2, gq2, pe = sem["wl2"], sem["gq2"], sem["pe"]
    wlp = [wl, wl2]
    gqp = [gq, gq2]

    ps = [nc.alloc_psum_tensor("ps%d" % i, [128, 512], F32) for i in range(8)]
    ps_s1 = ps[6][0:1, :]
    ps_s2 = ps[7][0:1, :]
    bank_guard = {}

    def _bkey(pt):
        t = getattr(pt, "tensor", None)
        return t.name if t is not None else pt.name

    def bank_wait(rec, pt):
        g = bank_guard.get(_bkey(pt))
        if g is not None:
            rec.wait_ge(g[0], g[1])

    def bank_done(pt, sm):
        bank_guard[_bkey(pt)] = (sm, s.val(sm))

    SY.dma_start(out=R[:].rearrange("p (c t) -> p c t", c=NCH),
                 in_=x_in.rearrange("(c p) t -> p c t", p=128), inc=(ld, 16))
    for nm, src in [("ident_b", ident_b_in), ("ident_f", ident_f_in),
                    ("ones_col", ones_col_in), ("ones_row", ones_row_in),
                    ("eps_row", eps_row_in), ("bffn", bffn_in),
                    ("gate_b", gate_b_in), ("e_expand", e_expand_in),
                    ("ccol", ccol_in), ("arange_col", arange_col_in)]:
        SY.dma_start(out=cst[nm][:], in_=src, inc=(ld, 16))
    setup_ld = s.val(ld)
    for e in Sched.ENGINES:
        Rec(s, e).wait_ge(ld, setup_ld)
    V.memset(gstat[:], 1.0)
    V.memset(rstat[:], 1.0)
    V.memset(sstat[:], 1.0)
    V.memset(lnrow[:], 1.0)
    V.memset(grow[:], 1.0, inc=(vs, 1))
    GQ.wait_ge(vs, s.val(vs))

    def split_hilo(src_ap, hi_ap, lo_ap, inc=None):
        V.tensor_copy(out=hi_ap, in_=src_ap)
        V.tensor_tensor(lo_ap, src_ap, hi_ap, ALU.subtract, inc=inc)

    def mm3(psum_ap, lh, ll, rh, rl, start, stop, inc=None):
        T.matmul(psum_ap, lh, rh, start=start, stop=False)
        T.matmul(psum_ap, lh, rl, start=False, stop=False)
        T.matmul(psum_ap, ll, rh, start=False, stop=stop, inc=inc)

    class WStream:

        def __init__(self, img_ap, cw, off):
            self.img = img_ap
            self.cw = cw
            self.fb = [A(off, 128, cw), A(off + cw * 4, 128, cw)]
            o2 = off + cw * 8
            self.hb = [A(o2, 128, cw, BF16), A(o2 + cw * 2, 128, cw, BF16)]
            self.lb = [A(o2 + cw * 4, 128, cw, BF16), A(o2 + cw * 6, 128, cw, BF16)]
            self.split_done = [0, 0]
            self.freed = [(vs, s.val(vs)), (vs, s.val(vs))]
            self.n = 0

        def load(self, j):
            b = self.n % 2
            self.n += 1
            SY.wait_ge(*self.freed[b])
            SY.dma_start(out=self.fb[b][:, :], in_=self.img[j],
                         inc=(wlp[b], 16))
            V.wait_ge(wlp[b], s.val(wlp[b]))
            split_hilo(self.fb[b][:, :], self.hb[b][:, :], self.lb[b][:, :],
                       inc=(vs, 1))
            self.split_done[b] = s.val(vs)
            return b

        def tiles(self, b, c):
            return (self.hb[b][:, 128 * c:128 * (c + 1)],
                    self.lb[b][:, 128 * c:128 * (c + 1)])

        def wait(self, rec, b):
            rec.wait_ge(vs, self.split_done[b])

        def release(self, b, sm):
            self.freed[b] = (sm, s.val(sm))

    def rope_emit(psum_list, cos_ap, sin_ap, tmp, out_f32, tb):
        p = psum_list
        sl = slice(512 * tb, 512 * (tb + 1))
        V.tensor_scalar_mul(tmp[0:64, 0:512], p[64:128, :], -1.0)
        V.tensor_copy(out=tmp[64:128, 0:512], in_=p[0:64, :])
        V.tensor_tensor(out_f32[:, sl], p[:, :], cos_ap[:, sl], ALU.mult)
        V.tensor_tensor(tmp[:, 512:1024], tmp[:, 0:512], sin_ap[:, sl], ALU.mult)
        V.tensor_tensor(out_f32[:, sl], out_f32[:, sl], tmp[:, 512:1024], ALU.add)

    def layer_norm_tb(y_ap_fn, tb, sq_tiles):
        y_ready = s.val(vs)
        C.wait_ge(vs, y_ready)
        T.wait_ge(vs, y_ready)
        sq_guard = [None, None]
        for c in range(NCH):
            src = y_ap_fn(c)
            sqt = sq_tiles[c % 2]
            if sq_guard[c % 2] is not None:
                C.wait_ge(tn, sq_guard[c % 2])
            C.activation(out=sqt, in_=src, func=AF.Square, inc=(sc, 1))
            sq_done = s.val(sc)
            if c == 0:
                bank_wait(T, ps_s1)
                bank_wait(T, ps_s2)
            T.matmul(ps_s1[:], cst["ones_col"][:, 0:1], src,
                     start=(c == 0), stop=(c == NCH - 1))
            T.wait_ge(sc, sq_done)
            T.matmul(ps_s2[:], cst["ones_col"][:, 0:1], sqt,
                     start=(c == 0), stop=(c == NCH - 1),
                     inc=(tn, 1))
            sq_guard[c % 2] = s.val(tn)
        V.wait_ge(tn, s.val(tn))
        mean, e2 = lnrow[:, 0:512], lnrow[:, 512:1024]
        aro, bro = lnrow[:, 1024:1536], lnrow[:, 1536:2048]
        V.tensor_scalar_mul(mean, ps_s1[:], 1.0 / DM)
        V.tensor_scalar_mul(e2, ps_s2[:], 1.0 / DM)
        V.tensor_tensor(bro, mean, mean, ALU.mult)
        V.tensor_tensor(e2, e2, bro, ALU.subtract)
        V.tensor_tensor(e2, e2, cst["eps_row"][:], ALU.add, inc=(vs, 1))
        bank_done(ps_s1, vs)
        bank_done(ps_s2, vs)
        C.wait_ge(vs, s.val(vs))
        C.activation(out=aro, in_=e2, func=AF.Sqrt, inc=(sc, 1))
        V.wait_ge(sc, s.val(sc))
        V.reciprocal(aro, aro)
        V.scalar_tensor_tensor(bro, mean, -1.0, aro, ALU.mult, ALU.mult,
                               inc=(vs, 1))
        bank_wait(T, ps[4])
        bank_wait(T, ps[5])
        T.wait_ge(vs, s.val(vs))
        T.matmul(ps[4][:], cst["ones_row"][:], aro, start=True, stop=True)
        T.matmul(ps[5][:], cst["ones_row"][:], bro, start=True, stop=True,
                 inc=(tn, 1))
        V.wait_ge(tn, s.val(tn))
        for c in range(NCH):
            src = y_ap_fn(c)
            dst = R[:, SH * c + 512 * tb: SH * c + 512 * (tb + 1)]
            V.tensor_tensor(dst, src, ps[4][:], ALU.mult)
            V.tensor_tensor(dst, dst, ps[5][:], ALU.add,
                            inc=(vs, 1) if c == NCH - 1 else None)
        bank_done(ps[4], vs)
        bank_done(ps[5], vs)
        return s.val(vs)

    XH0, XL0 = 0, 32768
    CO0, SI0 = 65536, 69632
    WS0 = 73728
    QT0 = 106496
    RT0 = 110592

    def proj_head(ws, col, cos_ap, sin_ap, out_hi, out_lo, xh, xl, rope, qtmp,
                  rtmp):
        b = ws.load(col)
        ws.wait(T, b)
        for tb in range(NTB):
            pt = ps[tb]
            bank_wait(T, pt)
            for c in range(NCH):
                lh, ll = ws.tiles(b, c)
                rh = xh[:, SH * c + 512 * tb: SH * c + 512 * (tb + 1)]
                rl = xl[:, SH * c + 512 * tb: SH * c + 512 * (tb + 1)]
                mm3(pt[:], lh, ll, rh, rl, start=(c == 0), stop=(c == NCH - 1),
                    inc=(tn, 1) if c == NCH - 1 else None)
            V.wait_ge(tn, s.val(tn))
            sl = slice(512 * tb, 512 * (tb + 1))
            if rope:
                rope_emit(pt[:], cos_ap, sin_ap, rtmp, qtmp, tb)
                split_hilo(qtmp[:, sl], out_hi[:, sl], out_lo[:, sl],
                           inc=(vs, 1))
            else:
                split_hilo(pt[:], out_hi[:, sl], out_lo[:, sl], inc=(vs, 1))
            bank_done(pt, vs)
        ws.release(b, vs)
        return s.val(vs)

    def emit_l0_proj():
        xh = A(XH0, 128, NCH * SH, BF16)
        xl = A(XL0, 128, NCH * SH, BF16)
        cos_e = A(CO0, 128, SH)
        sin_e = A(SI0, 128, SH)
        qtmp = A(QT0, 128, SH)
        rtmp = A(RT0, 128, SH)
        SY.dma_start(out=cos_e[:, :], in_=cos0_h, inc=(ld, 16))
        SY.dma_start(out=qtmp[:, :], in_=cos0_l, inc=(ld, 16))
        V.wait_ge(ld, s.val(ld))
        V.tensor_tensor(cos_e[:, :], cos_e[:, :], qtmp[:, :], ALU.add, inc=(vs, 1))
        cos_done = s.val(vs)
        SY.wait_ge(vs, cos_done)
        SY.dma_start(out=sin_e[:, :], in_=sin0_h, inc=(ld, 16))
        SY.dma_start(out=qtmp[:, :], in_=sin0_l, inc=(ld, 16))
        V.wait_ge(ld, s.val(ld))
        V.tensor_tensor(sin_e[:, :], sin_e[:, :], qtmp[:, :], ALU.add, inc=(vs, 1))
        for c in range(NCH):
            split_hilo(R[:, SH * c:SH * (c + 1)],
                       xh[:, SH * c:SH * (c + 1)], xl[:, SH * c:SH * (c + 1)],
                       inc=(vs, 1) if c == NCH - 1 else None)
        ws = WStream(q0_img, NCH * 128, WS0)
        oh = A(114688, 128, SH, BF16)
        ol = A(116736, 128, SH, BF16)
        agf = ag0_in.ap()
        q0f = q0_save.ap()
        for hh in range(H):
            done = proj_head(ws, hh, cos_e, sin_e, oh, ol, xh, xl, True,
                             qtmp, rtmp)
            GQ.wait_ge(vs, done)
            GQ.dma_start(out=q0f[128 * hh:128 * (hh + 1), :], in_=oh[:, :],
                         inc=(gq, 16))
            GQ.dma_start(out=q0f[128 * (16 + hh):128 * (17 + hh), :],
                         in_=ol[:, :], inc=(gq, 16))
            V.wait_ge(gq, s.val(gq))
        ws_k = WStream(k0_img, NCH * 128, WS0)
        for hh in range(H):
            done = proj_head(ws_k, hh, cos_e, sin_e, oh, ol, xh, xl, True,
                             qtmp, rtmp)
            GQ.wait_ge(vs, done)
            GQ.dma_start(out=agf[128 * hh:128 * (hh + 1), :], in_=oh[:, :],
                         inc=(gq, 16))
            GQ.dma_start(out=agf[128 * (16 + hh):128 * (17 + hh), :],
                         in_=ol[:, :], inc=(gq, 16))
            V.wait_ge(gq, s.val(gq))
        ws_v = WStream(v0_img, NCH * 128, WS0)
        for hh in range(H):
            done = proj_head(ws_v, hh, None, None, oh, ol, xh, xl, False,
                             qtmp, rtmp)
            GQ.wait_ge(vs, done)
            GQ.dma_start(out=agf[128 * (32 + hh):128 * (33 + hh), :],
                         in_=oh[:, :], inc=(gq, 16))
            GQ.dma_start(out=agf[128 * (48 + hh):128 * (49 + hh), :],
                         in_=ol[:, :], inc=(gq, 16))
            V.wait_ge(gq, s.val(gq))
        GQ.wait_ge(gq, s.val(gq))
        GQ.collective_compute("AllGather", ALU.bypass, replica_groups=PAIRS,
                              ins=[ag0_in.ap().opt()],
                              outs=[ag0_out.ap().opt()], inc=(cc, 1))
        return s.val(cc)

    attn_hi = R[:, 0:8192].bitcast(BF16)
    attn_lo = R[:, 8192:16384].bitcast(BF16)

    KH0 = 0
    KL0 = 4096
    VF0 = 8192
    VT0 = 16384
    PP0 = 24576
    PT0 = 32768
    QH0 = 65536
    QF1 = 66560

    def emit_l0_core(ag_ready):
        kh = A(KH0, 128, 2048, BF16)
        kl = A(KL0, 128, 2048, BF16)
        vfh = A(VF0, 128, 2048, BF16)
        vfl = A(VF0 + 4096, 128, 2048, BF16)
        vth = A(VT0, 128, 2048, BF16)
        vtl = A(VT0 + 4096, 128, 2048, BF16)
        P = A(PP0, 128, 2048)
        pth = A(PT0, 128, 8192, BF16)
        ptl = A(PT0 + 16384, 128, 8192, BF16)
        qh = A(QH0, 128, 128, BF16)
        ql = A(QH0 + 256, 128, 128, BF16)
        pth_r = pth.rearrange("p (kc qt qq) -> p kc qt qq", kc=16, qq=128)
        ptl_r = ptl.rearrange("p (kc qt qq) -> p kc qt qq", kc=16, qq=128)
        psr = [ps[i][:].rearrange("p (i one q) -> p i one q", i=4, one=1)
               for i in range(8)]
        ps_tbv = [ps[i][:, 0:256].bitcast(BF16) for i in range(4)]
        ago = ag0_out.ap()
        q0f = q0_save.ap()
        GQ.wait_ge(cc, ag_ready)
        SY.wait_ge(cc, ag_ready)
        nheads = DEV.get("nheads", H)
        part = DEV.get("part", 4)
        pools = [ps[0:4], ps[4:8]]
        P2 = [A(PP0, 128, 2048), A(67584, 128, 2048)]
        qhb = [qh, A(75776, 128, 128, BF16)]
        qlb = [ql, A(75776 + 256, 128, 128, BF16)]
        wlq = [sem["wl"], sem["wl2"]]
        q_free = [(tn, s.val(tn)), (tn, s.val(tn))]
        p_free = [(tn, s.val(tn)), (tn, s.val(tn))]
        for hh in range(nheads):
            prev_v = s.val(vs)
            SY.wait_ge(vs, prev_v)
            for (dst, base) in ((kh, hh), (kl, 16 + hh)):
                for r in range(2):
                    SY.dma_start(out=dst[:, 1024 * r:1024 * (r + 1)],
                                 in_=ago[8192 * r + 128 * base:
                                         8192 * r + 128 * (base + 1), :],
                                 inc=(ld, 16))
            for (dst, base) in ((vfh, 32 + hh), (vfl, 48 + hh)):
                for r in range(2):
                    SY.dma_start(out=dst[:, 1024 * r:1024 * (r + 1)],
                                 in_=ago[8192 * r + 128 * base:
                                         8192 * r + 128 * (base + 1), :],
                                 inc=(ld, 16))
            kv_ld = s.val(ld)
            T.wait_ge(ld, kv_ld)
            gi = 0
            for (src, dst) in ((vfh, vth), (vfl, vtl)):
                for g in range(4):
                    bk = gi % 4
                    gi += 1
                    bank_wait(T, ps[bk])
                    for i in range(4):
                        c = 4 * g + i
                        T.transpose(ps_tbv[bk][:, 128 * i:128 * (i + 1)],
                                    src[:, 128 * c:128 * (c + 1)],
                                    cst["ident_b"][:],
                                    inc=(tn, 1) if i == 3 else None)
                    V.wait_ge(tn, s.val(tn))
                    V.tensor_copy(out=dst[:, 512 * g:512 * (g + 1)],
                                  in_=ps_tbv[bk][:, :], inc=(vs, 1))
                    bank_done(ps[bk], vs)
            vt_done = s.val(vs)
            if part < 2:
                continue
            pn_done = {}

            def scores_stage(qi):
                par = qi % 2
                pool = pools[par]
                SY.wait_ge(*q_free[par])
                SY.dma_start(out=qhb[par][:, :],
                             in_=q0f[128 * hh:128 * (hh + 1),
                                     128 * qi:128 * (qi + 1)],
                             inc=(wlq[par], 16))
                SY.dma_start(out=qlb[par][:, :],
                             in_=q0f[128 * (16 + hh):128 * (17 + hh),
                                     128 * qi:128 * (qi + 1)],
                             inc=(wlq[par], 16))
                q_ld = s.val(wlq[par])
                T.wait_ge(wlq[par], q_ld)
                for kb in range(4):
                    pt = pool[kb]
                    bank_wait(T, pt)
                    mm3(pt[:], qhb[par][:, :], qlb[par][:, :],
                        kh[:, 512 * kb:512 * (kb + 1)],
                        kl[:, 512 * kb:512 * (kb + 1)],
                        start=True, stop=True,
                        inc=(tn, 1) if kb == 3 else None)
                q_free[par] = (tn, s.val(tn))
                V.wait_ge(tn, s.val(tn))
                for kb in range(4):
                    V.reduce_max(gstat[:, kb:kb + 1], pool[kb][:], AX.X,
                                 inc=(vs, 1) if kb == 3 else None)
                V.wait_ge(vs, s.val(vs))
                V.reduce_max(gstat[:, 4:5], gstat[:, 0:4], AX.X,
                             negate=True, inc=(vs, 1))
                V.wait_ge(vs, s.val(vs))
                V.tensor_scalar_mul(rstat[:, 0:8], gstat[:, 0:8], SCALE0,
                                    inc=(vs, 1))
                C.wait_ge(vs, s.val(vs))
                C.wait_ge(*p_free[par])
                Pd = P2[par]
                for kb in range(4):
                    C.activation(out=Pd[:, 512 * kb:512 * (kb + 1)],
                                 in_=pool[kb][:], func=AF.Exp,
                                 bias=rstat[:, 4:5], scale=SCALE0,
                                 inc=(sc, 1) if kb == 3 else None)
                exp_done = s.val(sc)
                for kb in range(4):
                    bank_done(pool[kb], sc)
                V.wait_ge(sc, exp_done)
                V.reduce_sum(gstat[:, 8:9], Pd[:, :], AX.X, inc=(vs, 1))
                V.wait_ge(vs, s.val(vs))
                V.reciprocal(rstat[:, 8:16], gstat[:, 8:16], inc=(vs, 1))
                V.wait_ge(vs, s.val(vs))
                V.tensor_scalar_mul(Pd[:, :], Pd[:, :], rstat[:, 8:9],
                                    inc=(vs, 1))
                pn_done[qi] = s.val(vs)

            def tail_stage(qi):
                par = qi % 2
                pool = pools[par]
                qb, qt = qi // 4, qi % 4
                Pd = P2[par]
                T.wait_ge(vs, pn_done[qi])
                tg = []
                for g in range(4):
                    bank_wait(T, pool[g])
                    for i in range(4):
                        c = 4 * g + i
                        T.transpose(pool[g][:, 128 * i:128 * (i + 1)],
                                    Pd[:, 128 * c:128 * (c + 1)],
                                    cst["ident_f"][:],
                                    inc=(tn, 1) if i == 3 else None)
                    tg.append(s.val(tn))
                p_free[par] = (tn, tg[3])
                for g in range(4):
                    C.wait_ge(tn, tg[g])
                    oh_ap = pth_r[:, 4 * g:4 * g + 4, qt:qt + 1, :]
                    ol_ap = ptl_r[:, 4 * g:4 * g + 4, qt:qt + 1, :]
                    C.activation(out=oh_ap, in_=psr[4 * par + g],
                                 func=AF.Copy, inc=(sc, 1))
                    V.wait_ge(sc, s.val(sc))
                    V.tensor_tensor(ol_ap, psr[4 * par + g], oh_ap,
                                    ALU.subtract, inc=(pe, 1))
                    bank_done(pool[g], pe)
                if qt != 3:
                    return
                pt5 = ps[5]
                bank_wait(T, pt5)
                T.wait_ge(pe, s.val(pe))
                T.wait_ge(vs, vt_done)
                for c in range(16):
                    mm3(pt5[:], vth[:, 128 * c:128 * (c + 1)],
                        vtl[:, 128 * c:128 * (c + 1)],
                        pth[:, 512 * c:512 * (c + 1)],
                        ptl[:, 512 * c:512 * (c + 1)],
                        start=(c == 0), stop=(c == 15),
                        inc=(tn, 1) if c == 15 else None)
                pv_done = s.val(tn)
                off = SH * hh + 512 * qb
                C.wait_ge(tn, pv_done)
                C.activation(out=attn_hi[:, off:off + 512], in_=pt5[:],
                             func=AF.Copy, inc=(sc, 1))
                V.wait_ge(sc, s.val(sc))
                V.tensor_tensor(attn_lo[:, off:off + 512], pt5[:],
                                attn_hi[:, off:off + 512], ALU.subtract,
                                inc=(vs, 1))
                bank_done(pt5, vs)

            for qi in range(8):
                scores_stage(qi)
                if qi > 0:
                    tail_stage(qi - 1)
            tail_stage(7)
        return s.val(vs)


    def emit_outproj_ln(w_img, res_src, attn_done):
        ws = WStream(w_img, NCH * 128, 0)
        xres = [A(32768, 128, 512), A(32768 + 2048, 128, 512)]
        y_full = A(36864, 128, 16384)
        sq = [A(36864 + 65536, 128, 512), A(36864 + 65536 + 2048, 128, 512)]
        T.wait_ge(vs, attn_done)
        for dc in range(NCH):
            b = ws.load(dc)
            ws.wait(T, b)
            for tb in range(NTB):
                xr = xres[tb]
                src = (x_in if res_src == "x" else r_save.ap())
                GQ.wait_ge(vs, s.val(vs))
                GQ.dma_start(out=xr[:, :],
                             in_=src[128 * dc:128 * (dc + 1),
                                     512 * tb:512 * (tb + 1)], inc=(gq, 16))
                xr_ld = s.val(gq)
                pt = ps[tb]
                bank_wait(T, pt)
                for oc in range(NCH):
                    lh, ll = ws.tiles(b, oc)
                    off = SH * oc + 512 * tb
                    mm3(pt[:], lh, ll, attn_hi[:, off:off + 512],
                        attn_lo[:, off:off + 512],
                        start=(oc == 0), stop=(oc == NCH - 1),
                        inc=(tn, 1) if oc == NCH - 1 else None)
                V.wait_ge(tn, s.val(tn))
                V.wait_ge(gq, xr_ld)
                V.tensor_tensor(y_full[:, SH * dc + 512 * tb:
                                       SH * dc + 512 * (tb + 1)], pt[:],
                                xr[:, :], ALU.add, inc=(vs, 1))
                bank_done(pt, vs)
            ws.release(b, tn)
        for tb in range(NTB):
            layer_norm_tb(
                lambda c, tb=tb: y_full[:, SH * c + 512 * tb:
                                        SH * c + 512 * (tb + 1)], tb, sq)
        return s.val(vs)

    def emit_l0_ffn():
        XTH, XTL = 0, 16384
        WSF = 32768
        BSH, BSL = 65536, 69632
        ATH, ATL = 73728, 77824
        GW0 = 81920
        XA0 = 83968
        SXH, SXL = 86016, 87040
        WGT = 88064
        YTB = 90112
        xh_t = A(XTH, 128, 16 * 512, BF16)
        xl_t = A(XTL, 128, 16 * 512, BF16)
        bsh = A(BSH, 128, 2048, BF16)
        bsl = A(BSL, 128, 2048, BF16)
        ath = A(ATH, 128, 2048, BF16)
        atl = A(ATL, 128, 2048, BF16)
        gw = A(GW0, 128, 128)
        gwh = A(GW0 + 512, 128, 128, BF16)
        gwl = A(GW0 + 768, 128, 128, BF16)
        lg = A(GW0 + 1024, 128, 8)
        eqt = A(GW0 + 1088, 128, 8)
        mkd = A(GW0 + 1152, 128, 8)
        et = A(GW0 + 1216, 128, 8)
        emt = A(GW0 + 1280, 128, 8)
        dent = A(GW0 + 1344, 128, 8)
        rect = A(GW0 + 1408, 128, 8)
        indt = A(GW0 + 1472, 128, 8)
        wgt = A(GW0 + 1536, 128, 8)
        xa_f = A(XA0, 128, 512)
        sxh = A(SXH, 128, 512, BF16)
        sxl = A(SXL, 128, 512, BF16)
        wgT_sb = A(WGT, 8, 512)
        y_tb = A(YTB, 128, 8192)
        sqs = [A(83968, 128, 512), A(86016, 128, 512)]

        SY.wait_ge(vs, s.val(vs))
        SY.wait_ge(tn, s.val(tn))
        SY.wait_ge(sc, s.val(sc))
        SY.dma_start(out=gw[:, :].rearrange("p (c e) -> p c e", c=16),
                     in_=gate_w_in.rearrange("c p e -> p c e"), inc=(ld, 16))
        V.wait_ge(ld, s.val(ld))
        split_hilo(gw[:, :], gwh[:, :], gwl[:, :], inc=(vs, 1))
        bstg = A(YTB, 128, 2048)
        SY.dma_start(out=bstg[:, :].rearrange("p (c m) -> p c m", c=16),
                     in_=bstack_img.rearrange("c p m -> p c m"), inc=(ld, 16))
        V.wait_ge(ld, s.val(ld))
        split_hilo(bstg[:, :], bsh[:, :], bsl[:, :], inc=(vs, 1))
        atg = A(YTB, 128, 2048)
        SY.wait_ge(vs, s.val(vs))
        SY.dma_start(out=atg[:, :], in_=aT_img.rearrange("o p n -> p (o n)"),
                     inc=(ld, 16))
        V.wait_ge(ld, s.val(ld))
        split_hilo(atg[:, :], ath[:, :], atl[:, :], inc=(vs, 1))

        for tb in range(NTB):
            V.wait_ge(tn, s.val(tn))
            for c in range(NCH):
                sl = slice(SH * c + 512 * tb, SH * c + 512 * (tb + 1))
                split_hilo(R[:, sl], xh_t[:, 512 * c:512 * (c + 1)],
                           xl_t[:, 512 * c:512 * (c + 1)],
                           inc=(vs, 1) if c == NCH - 1 else None)
            xsplit_done = s.val(vs)

            pg = ps[2][:, 0:8]
            for tt_ in range(4):
                bank_wait(T, ps[2])
                T.wait_ge(vs, xsplit_done)
                for c in range(NCH):
                    lhh = xh_t[:, 512 * c + 128 * tt_:512 * c + 128 * (tt_ + 1)]
                    lll = xl_t[:, 512 * c + 128 * tt_:512 * c + 128 * (tt_ + 1)]
                    mm3(pg, lhh, lll, gwh[:, 8 * c:8 * (c + 1)],
                        gwl[:, 8 * c:8 * (c + 1)],
                        start=(c == 0), stop=(c == NCH - 1),
                        inc=(tn, 1) if c == NCH - 1 else None)
                V.wait_ge(tn, s.val(tn))
                V.tensor_tensor(lg[:, :], pg, cst["gate_b"][:], ALU.add,
                                inc=(vs, 1))
                V.wait_ge(vs, s.val(vs))
                V.reduce_max(gstat[:, 0:1], lg[:, :], AX.X)
                V.reduce_max(gstat[:, 1:2], lg[:, :], AX.X, negate=True,
                             inc=(vs, 1))
                V.wait_ge(vs, s.val(vs))
                V.tensor_single_scalar(eqt[:, :], lg[:, :], gstat[:, 0:1],
                                       ALU.is_equal, inc=(vs, 1))
                V.wait_ge(vs, s.val(vs))
                V.scalar_tensor_tensor(mkd[:, :], eqt[:, :], FMIN, lg[:, :],
                                       ALU.mult, ALU.add, inc=(vs, 1))
                V.wait_ge(vs, s.val(vs))
                V.reduce_max(gstat[:, 2:3], mkd[:, :], AX.X, inc=(vs, 1))
                V.wait_ge(vs, s.val(vs))
                V.tensor_single_scalar(dent[:, :], gstat[:, 0:8],
                                       gstat[:, 1:2], ALU.add, inc=(vs, 1))
                V.wait_ge(vs, s.val(vs))
                V.tensor_single_scalar(dent[:, :], dent[:, :], 0.0, ALU.min,
                                       inc=(vs, 1))
                lgv = s.val(vs)
                C.wait_ge(vs, lgv)
                C.activation(out=et[:, :], in_=lg[:, :], func=AF.Exp,
                             bias=gstat[:, 1:2], scale=1.0)
                C.activation(out=emt[:, :], in_=dent[:, :], func=AF.Exp,
                             inc=(sc, 1))
                bank_done(ps[2], sc)
                V.wait_ge(sc, s.val(sc))
                V.tensor_scalar_add(dent[:, :], emt[:, :], 1.0, inc=(vs, 1))
                V.wait_ge(vs, s.val(vs))
                V.reciprocal(rect[:, :], dent[:, :], inc=(vs, 1))
                V.wait_ge(vs, s.val(vs))
                V.tensor_single_scalar(indt[:, :], lg[:, :], gstat[:, 2:3],
                                       ALU.is_ge, inc=(vs, 1))
                V.wait_ge(vs, s.val(vs))
                V.tensor_tensor(wgt[:, :], et[:, :], indt[:, :], ALU.mult,
                                inc=(vs, 1))
                V.wait_ge(vs, s.val(vs))
                V.tensor_single_scalar(wgt[:, :], wgt[:, :], rect[:, 2:3],
                                       ALU.mult, inc=(vs, 1))
                wg_done = s.val(vs)
                bank_wait(T, ps[3]) if tt_ == 0 else None
                T.wait_ge(vs, wg_done)
                T.transpose(ps[3][0:8, 128 * tt_:128 * (tt_ + 1)], wgt[:, :],
                            cst["ident_f"][:], inc=(tn, 1))
            V.wait_ge(tn, s.val(tn))
            V.tensor_copy(out=wgT_sb[:, :], in_=ps[3][0:8, :], inc=(vs, 1))
            bank_done(ps[3], vs)
            bank_wait(T, ps[4])
            T.wait_ge(vs, s.val(vs))
            T.matmul(ps[4][:], cst["e_expand"][:], wgT_sb[:, :], start=True,
                     stop=True, inc=(tn, 1))
            bank_wait(T, ps[5])
            for c in range(NCH):
                mm3(ps[5][:], ath[:, 128 * c:128 * (c + 1)],
                    atl[:, 128 * c:128 * (c + 1)],
                    xh_t[:, 512 * c:512 * (c + 1)],
                    xl_t[:, 512 * c:512 * (c + 1)],
                    start=(c == 0), stop=(c == NCH - 1),
                    inc=(tn, 1) if c == NCH - 1 else None)
            V.wait_ge(tn, s.val(tn))
            V.tensor_copy(out=xa_f[:, :], in_=ps[5][:])
            V.tensor_tensor(xa_f[:, :], xa_f[:, :], ps[4][:], ALU.mult)
            split_hilo(xa_f[:, :], sxh[:, :], sxl[:, :], inc=(vs, 1))
            bank_done(ps[4], vs)
            bank_done(ps[5], vs)
            sxa_done = s.val(vs)

            ws = WStream(wffn_img, NCH * 128, WSF)
            for oc in range(NCH):
                b = ws.load(oc)
                ws.wait(T, b)
                pt = ps[oc % 2]
                bank_wait(T, pt)
                T.wait_ge(vs, sxa_done)
                nolora = DEV.get("nolora", False)
                for c in range(NCH):
                    lh, ll = ws.tiles(b, c)
                    mm3(pt[:], lh, ll,
                        xh_t[:, 512 * c:512 * (c + 1)],
                        xl_t[:, 512 * c:512 * (c + 1)],
                        start=(c == 0),
                        stop=(nolora and c == NCH - 1),
                        inc=(tn, 1) if (nolora and c == NCH - 1) else None)
                if not nolora:
                    mm3(pt[:], bsh[:, 128 * oc:128 * (oc + 1)],
                        bsl[:, 128 * oc:128 * (oc + 1)], sxh[:, :], sxl[:, :],
                        start=False, stop=True, inc=(tn, 1))
                ws.release(b, tn)
                mm_done = s.val(tn)
                V.wait_ge(tn, mm_done)
                ysl = y_tb[:, 512 * oc:512 * (oc + 1)]
                V.scalar_tensor_tensor(
                    ysl, pt[:], cst["bffn"][:, oc:oc + 1],
                    R[:, SH * oc + 512 * tb:SH * oc + 512 * (tb + 1)],
                    ALU.add, ALU.add, inc=(vs, 1))
                bank_done(pt, vs)
            layer_norm_tb(
                lambda c, tb=tb: y_tb[:, 512 * c:512 * (c + 1)], tb, sqs)
        return s.val(vs)

    def emit_l1_attn():
        xh = A(XH0, 128, NCH * SH, BF16)
        xl = A(XL0, 128, NCH * SH, BF16)
        cos_e = A(CO0, 128, SH)
        sin_e = A(SI0, 128, SH)
        qtmp = A(QT0, 128, SH)
        rtmp = A(RT0, 128, SH)
        oh = A(114688, 128, SH, BF16)
        ol = A(116736, 128, SH, BF16)
        gth = A(118784, 128, 1024, BF16)
        gtl = A(120832, 128, 1024, BF16)
        kgh = A(122880, 128, 256, BF16)
        kgl = A(123392, 128, 256, BF16)
        vgh = A(123904, 128, 256, BF16)
        vgl = A(124416, 128, 256, BF16)
        cge = A(124928, 128, 64)
        sge = A(125184, 128, 64)
        gr_tmp = A(125440, 128, 64)
        gr_tmp2 = A(125696, 128, 64)

        ent_v = s.val(vs)
        SY.wait_ge(vs, ent_v)
        SY.wait_ge(tn, s.val(tn))
        SY.wait_ge(sc, s.val(sc))
        SY.dma_start(out=cos_e[:, :], in_=cos1_h, inc=(ld, 16))
        SY.dma_start(out=qtmp[:, :], in_=cos1_l, inc=(ld, 16))
        V.wait_ge(ld, s.val(ld))
        V.tensor_tensor(cos_e[:, :], cos_e[:, :], qtmp[:, :], ALU.add, inc=(vs, 1))
        SY.wait_ge(vs, s.val(vs))
        SY.dma_start(out=sin_e[:, :], in_=sin1_h, inc=(ld, 16))
        SY.dma_start(out=qtmp[:, :], in_=sin1_l, inc=(ld, 16))
        V.wait_ge(ld, s.val(ld))
        V.tensor_tensor(sin_e[:, :], sin_e[:, :], qtmp[:, :], ALU.add, inc=(vs, 1))
        SY.wait_ge(vs, s.val(vs))
        SY.dma_start(out=cge[:, :], in_=cos1g, inc=(ld, 16))
        SY.dma_start(out=gr_tmp[:, :], in_=cos1g_l, inc=(ld, 16))
        V.wait_ge(ld, s.val(ld))
        V.tensor_tensor(cge[:, :], cge[:, :], gr_tmp[:, :], ALU.add, inc=(vs, 1))
        SY.wait_ge(vs, s.val(vs))
        SY.dma_start(out=sge[:, :], in_=sin1g, inc=(ld, 16))
        SY.dma_start(out=gr_tmp[:, :], in_=sin1g_l, inc=(ld, 16))
        V.wait_ge(ld, s.val(ld))
        V.tensor_tensor(sge[:, :], sge[:, :], gr_tmp[:, :], ALU.add, inc=(vs, 1))
        SY.wait_ge(vs, s.val(vs))
        SY.dma_start(out=qtmp[:, :].rearrange("p (c g) -> p c g", c=16),
                     in_=gt_img.rearrange("c p g -> p c g"), inc=(ld, 16))
        V.wait_ge(ld, s.val(ld))
        split_hilo(qtmp[:, :], gth[:, :], gtl[:, :], inc=(vs, 1))
        for c in range(NCH):
            split_hilo(R[:, SH * c:SH * (c + 1)],
                       xh[:, SH * c:SH * (c + 1)], xl[:, SH * c:SH * (c + 1)],
                       inc=(vs, 1) if c == NCH - 1 else None)
        xs_done = s.val(vs)
        GQ.wait_ge(vs, xs_done)
        GQ.dma_start(out=r_save.ap().rearrange("(c p) t -> p c t", p=128),
                     in_=R[:].rearrange("p (c t) -> p c t", c=NCH), inc=(gq, 16))
        rspill = s.val(gq)

        q1f = q1_save.ap()
        ag1 = ag1_in.ap()
        ws_q = WStream(wq1_img, NCH * 128, WS0)
        for hh in range(H):
            done = proj_head(ws_q, hh, cos_e, sin_e, oh, ol, xh, xl, True,
                             qtmp, rtmp)
            GQ.wait_ge(vs, done)
            GQ.dma_start(out=q1f[128 * hh:128 * (hh + 1), :], in_=oh[:, :],
                         inc=(gq, 16))
            GQ.dma_start(out=q1f[128 * (16 + hh):128 * (17 + hh), :],
                         in_=ol[:, :], inc=(gq, 16))
            V.wait_ge(gq, s.val(gq))
        ws_k = WStream(wk1_img, NCH * 128, WS0)
        for g in range(NKV):
            b = ws_k.load(g)
            ws_k.wait(T, b)
            bank_wait(T, ps[2])
            for c in range(NCH):
                lh, ll = ws_k.tiles(b, c)
                mm3(ps[2][:, 0:64], lh, ll, gth[:, 64 * c:64 * (c + 1)],
                    gtl[:, 64 * c:64 * (c + 1)], start=(c == 0),
                    stop=(c == NCH - 1), inc=(tn, 1) if c == NCH - 1 else None)
            V.wait_ge(tn, s.val(tn))
            p64 = ps[2][:, 0:64]
            V.tensor_scalar_mul(gr_tmp[0:64, 0:64], p64[64:128, :], -1.0)
            V.tensor_copy(out=gr_tmp[64:128, 0:64], in_=p64[0:64, :],
                          inc=(vs, 1))
            V.wait_ge(vs, s.val(vs))
            V.tensor_tensor(gr_tmp2[:, :], p64, cge[:, :], ALU.mult)
            V.tensor_tensor(gr_tmp[:, 0:64], gr_tmp[:, 0:64], sge[:, :],
                            ALU.mult, inc=(vs, 1))
            V.wait_ge(vs, s.val(vs))
            V.tensor_tensor(gr_tmp2[:, :], gr_tmp2[:, :], gr_tmp[:, 0:64],
                            ALU.add, inc=(vs, 1))
            V.wait_ge(vs, s.val(vs))
            split_hilo(gr_tmp2[:, :], kgh[:, 64 * g:64 * (g + 1)],
                       kgl[:, 64 * g:64 * (g + 1)], inc=(vs, 1))
            bank_done(ps[2], vs)
            for tb in range(NTB):
                pt = ps[tb]
                bank_wait(T, pt)
                for c in range(NCH):
                    lh, ll = ws_k.tiles(b, c)
                    rh = xh[:, SH * c + 512 * tb: SH * c + 512 * (tb + 1)]
                    rl = xl[:, SH * c + 512 * tb: SH * c + 512 * (tb + 1)]
                    mm3(pt[:], lh, ll, rh, rl, start=(c == 0),
                        stop=(c == NCH - 1),
                        inc=(tn, 1) if c == NCH - 1 else None)
                V.wait_ge(tn, s.val(tn))
                sl = slice(512 * tb, 512 * (tb + 1))
                rope_emit(pt[:], cos_e, sin_e, rtmp, qtmp, tb)
                split_hilo(qtmp[:, sl], oh[:, sl], ol[:, sl], inc=(vs, 1))
                bank_done(pt, vs)
            ws_k.release(b, vs)
            GQ.wait_ge(vs, s.val(vs))
            GQ.dma_start(out=ag1[128 * g:128 * (g + 1), :], in_=oh[:, :],
                         inc=(gq, 16))
            GQ.dma_start(out=ag1[128 * (4 + g):128 * (5 + g), :], in_=ol[:, :],
                         inc=(gq, 16))
            V.wait_ge(gq, s.val(gq))
        ws_v = WStream(wv1_img, NCH * 128, WS0)
        for g in range(NKV):
            b = ws_v.load(g)
            ws_v.wait(T, b)
            bank_wait(T, ps[2])
            for c in range(NCH):
                lh, ll = ws_v.tiles(b, c)
                mm3(ps[2][:, 0:64], lh, ll, gth[:, 64 * c:64 * (c + 1)],
                    gtl[:, 64 * c:64 * (c + 1)], start=(c == 0),
                    stop=(c == NCH - 1), inc=(tn, 1) if c == NCH - 1 else None)
            V.wait_ge(tn, s.val(tn))
            split_hilo(ps[2][:, 0:64], vgh[:, 64 * g:64 * (g + 1)],
                       vgl[:, 64 * g:64 * (g + 1)], inc=(vs, 1))
            bank_done(ps[2], vs)
            for tb in range(NTB):
                pt = ps[tb]
                bank_wait(T, pt)
                for c in range(NCH):
                    lh, ll = ws_v.tiles(b, c)
                    rh = xh[:, SH * c + 512 * tb: SH * c + 512 * (tb + 1)]
                    rl = xl[:, SH * c + 512 * tb: SH * c + 512 * (tb + 1)]
                    mm3(pt[:], lh, ll, rh, rl, start=(c == 0),
                        stop=(c == NCH - 1),
                        inc=(tn, 1) if c == NCH - 1 else None)
                V.wait_ge(tn, s.val(tn))
                sl = slice(512 * tb, 512 * (tb + 1))
                split_hilo(pt[:], oh[:, sl], ol[:, sl], inc=(vs, 1))
                bank_done(pt, vs)
            ws_v.release(b, vs)
            done = s.val(vs)
            GQ.wait_ge(vs, done)
            GQ.dma_start(out=ag1[128 * (8 + g):128 * (9 + g), :], in_=oh[:, :],
                         inc=(gq, 16))
            GQ.dma_start(out=ag1[128 * (12 + g):128 * (13 + g), :],
                         in_=ol[:, :], inc=(gq, 16))
            V.wait_ge(gq, s.val(gq))
        if dbg == "kg":
            V.tensor_copy(out=qtmp[:, 0:256], in_=kgh[:, :])
            V.tensor_copy(out=qtmp[:, 256:512], in_=kgl[:, :])
            V.tensor_copy(out=qtmp[:, 512:768], in_=vgh[:, :])
            V.tensor_copy(out=qtmp[:, 768:1024], in_=vgl[:, :], inc=(vs, 1))
            GQ.wait_ge(vs, s.val(vs))
            GQ.dma_start(out=dbg_out[0:128, :], in_=qtmp[:, :], inc=(gq, 16))
        GQ.wait_ge(gq, s.val(gq))
        GQ.collective_compute("AllGather", ALU.bypass, replica_groups=PAIRS,
                              ins=[ag1_in.ap().opt()],
                              outs=[ag1_out.ap().opt()], inc=(cc, 1))
        ag_ready = s.val(cc)

        k1h = A(0, 128, 4 * E1P, BF16)
        k1l = A(17408, 128, 4 * E1P, BF16)
        v1h = A(34816, 128, E1P, BF16)
        v1l = A(39168, 128, E1P, BF16)
        vt1h = A(43520, 128, 4 * E1P, BF16)
        vt1l = A(60928, 128, 4 * E1P, BF16)
        maskb = A(78336, 128, E1P, BF16)
        jr = A(82688, 128, E1P)
        jb = A(91392, 128, E1P)
        P1 = A(100096, 128, E1P)
        pth = A(108800, 128, E1P, BF16)
        ptl = A(113152, 128, E1P, BF16)
        qh = A(117504, 128, 128, BF16)
        ql = A(117504 + 256, 128, 128, BF16)
        tmp2 = A(118016, 128, E1P)
        ag1o = ag1_out.ap()
        SY.wait_ge(cc, ag_ready)
        GQ.wait_ge(cc, ag_ready)
        SY.dma_start(out=jr[:, :], in_=jrow_in, inc=(ld, 16))
        SY.dma_start(out=jb[:, :], in_=jbig_in, inc=(ld, 16))
        jld = s.val(ld)
        V.wait_ge(ld, jld)
        for g in range(NKV):
            for (dst, srow, gsb) in ((k1h, g, kgh), (k1l, 4 + g, kgl)):
                base = E1P * g
                V.tensor_copy(out=dst[:, base:base + 64],
                              in_=gsb[:, 64 * g:64 * (g + 1)])
                V.memset(dst[:, base + 2112:base + 2176], 0.0, inc=(vs, 1))
                SY.wait_ge(vs, s.val(vs))
                for r in range(2):
                    SY.dma_start(
                        out=dst[:, base + 64 + 1024 * r:base + 64 + 1024 * (r + 1)],
                        in_=ag1o[2048 * r + 128 * srow:2048 * r + 128 * (srow + 1), :],
                        inc=(ld, 16))
            prev = s.val(vs)
            SY.wait_ge(vs, prev)
            for (dst, srow, gsb) in ((v1h, 8 + g, vgh), (v1l, 12 + g, vgl)):
                V.tensor_copy(out=dst[:, 0:64], in_=gsb[:, 64 * g:64 * (g + 1)])
                V.memset(dst[:, 2112:2176], 0.0, inc=(vs, 1))
                SY.wait_ge(vs, s.val(vs))
                for r in range(2):
                    SY.dma_start(
                        out=dst[:, 64 + 1024 * r:64 + 1024 * (r + 1)],
                        in_=ag1o[2048 * r + 128 * srow:2048 * r + 128 * (srow + 1), :],
                        inc=(ld, 16))
            v_ld = s.val(ld)
            T.wait_ge(ld, v_ld)
            T.wait_ge(vs, s.val(vs))
            ps_tb = ps[5][:, 0:256].bitcast(BF16)
            for (src_t, dst_t) in ((v1h, vt1h), (v1l, vt1l)):
                for grp in range(5):
                    n_in = 4 if grp < 4 else 1
                    bank_wait(T, ps[5])
                    for i in range(n_in):
                        c = 4 * grp + i
                        T.transpose(ps_tb[:, 128 * i:128 * (i + 1)],
                                    src_t[:, 128 * c:128 * (c + 1)],
                                    cst["ident_b"][:],
                                    inc=(tn, 1) if i == n_in - 1 else None)
                    V.wait_ge(tn, s.val(tn))
                    V.tensor_copy(
                        out=dst_t[:, E1P * g + 512 * grp:
                                  E1P * g + 512 * grp + 128 * n_in],
                        in_=ps_tb[:, 0:128 * n_in], inc=(vs, 1))
                    bank_done(ps[5], vs)
        kv_done = s.val(vs)
        kv_ld = s.val(ld)

        qbase_col = cst["arange_col"]
        for qt in range(8):
            V.wait_ge(ld, jld)
            V.tensor_scalar_add(gstat[:, 32:40], qbase_col[:], float(128 * qt),
                                inc=(vs, 1))
            V.wait_ge(vs, s.val(vs))
            V.tensor_scalar_add(gstat[:, 40:48], gstat[:, 32:40], -512.0,
                                inc=(vs, 1))
            V.wait_ge(vs, s.val(vs))
            V.tensor_single_scalar(P1[:, :], jr[:, :], gstat[:, 32:33],
                                   ALU.is_le)
            V.tensor_single_scalar(tmp2[:, :], jb[:, :], gstat[:, 40:41],
                                   ALU.is_ge)
            V.tensor_tensor(P1[:, :], P1[:, :], tmp2[:, :], ALU.mult)
            V.tensor_scalar(maskb[:, :], P1[:, :], MASKV, -MASKV, ALU.mult,
                            ALU.add, inc=(vs, 1))
            mask_done = s.val(vs)
            nblk = [3, 3, 3, 4, 4, 4, 4, 5][qt]
            kca = 10 + qt
            wa = min(512 * nblk, E1P)
            for hq in range(H):
                g = hq // 4
                kb_ = E1P * g
                SY.wait_ge(vs, s.val(vs))
                SY.dma_start(out=qh[:, :],
                             in_=q1f[128 * hq:128 * (hq + 1),
                                     128 * qt:128 * (qt + 1)], inc=(ld, 16))
                SY.dma_start(out=ql[:, :],
                             in_=q1f[128 * (16 + hq):128 * (17 + hq),
                                     128 * qt:128 * (qt + 1)], inc=(ld, 16))
                q_ld = s.val(ld)
                T.wait_ge(ld, q_ld)
                T.wait_ge(vs, mask_done)
                for kb in range(nblk):
                    n = min(512, E1P - 512 * kb)
                    pt = ps[kb]
                    bank_wait(T, pt)
                    mm3(pt[:, 0:n], qh[:, :], ql[:, :],
                        k1h[:, kb_ + 512 * kb:kb_ + 512 * kb + n],
                        k1l[:, kb_ + 512 * kb:kb_ + 512 * kb + n],
                        start=True, stop=True,
                        inc=(tn, 1) if kb == nblk - 1 else None)
                V.wait_ge(tn, s.val(tn))
                for kb in range(nblk):
                    n = min(512, E1P - 512 * kb)
                    V.tensor_tensor(P1[:, 512 * kb:512 * kb + n],
                                    ps[kb][:, 0:n],
                                    maskb[:, 512 * kb:512 * kb + n], ALU.add,
                                    inc=(vs, 1) if kb == nblk - 1 else None)
                for kb in range(nblk):
                    bank_done(ps[kb], vs)
                V.wait_ge(vs, s.val(vs))
                V.reduce_max(gstat[:, 4:5], P1[:, 0:wa], AX.X, negate=True,
                             inc=(vs, 1))
                V.wait_ge(vs, s.val(vs))
                V.tensor_scalar_mul(rstat[:, 0:8], gstat[:, 0:8], SCALE1,
                                    inc=(vs, 1))
                C.wait_ge(vs, s.val(vs))
                C.activation(out=P1[:, 0:wa], in_=P1[:, 0:wa], func=AF.Exp,
                             bias=rstat[:, 4:5], scale=SCALE1, inc=(sc, 1))
                V.wait_ge(sc, s.val(sc))
                V.reduce_sum(gstat[:, 8:9], P1[:, 0:wa], AX.X, inc=(vs, 1))
                V.wait_ge(vs, s.val(vs))
                V.reciprocal(rstat[:, 8:16], gstat[:, 8:16], inc=(vs, 1))
                V.wait_ge(vs, s.val(vs))
                V.tensor_scalar_mul(P1[:, 0:wa], P1[:, 0:wa], rstat[:, 8:9],
                                    inc=(vs, 1))
                pn_done = s.val(vs)
                T.wait_ge(vs, pn_done)
                ps4_r = ps[4][:].rearrange("p (i q) -> p i q", i=4)
                for grp in range((kca + 3) // 4):
                    n_in = min(4, kca - 4 * grp)
                    bank_wait(T, ps[4])
                    for i in range(n_in):
                        c = 4 * grp + i
                        T.transpose(ps[4][:, 128 * i:128 * (i + 1)],
                                    P1[:, 128 * c:128 * (c + 1)],
                                    cst["ident_f"][:],
                                    inc=(tn, 1) if i == n_in - 1 else None)
                    tr_done = s.val(tn)
                    V.wait_ge(tn, tr_done)
                    C.wait_ge(tn, tr_done)
                    oh_ap = pth[:, 512 * grp:512 * grp + 128 * n_in]
                    ol_ap = ptl[:, 512 * grp:512 * grp + 128 * n_in]
                    C.activation(out=oh_ap, in_=ps[4][:, 0:128 * n_in],
                                 func=AF.Copy, inc=(sc, 1))
                    V.wait_ge(sc, s.val(sc))
                    V.tensor_tensor(ol_ap, ps[4][:, 0:128 * n_in], oh_ap,
                                    ALU.subtract, inc=(vs, 1))
                    bank_done(ps[4], vs)
                pt0 = ps[0]
                bank_wait(T, pt0)
                T.wait_ge(vs, s.val(vs))
                T.wait_ge(vs, kv_done)
                for c in range(kca):
                    mm3(pt0[:, 0:128],
                        vt1h[:, kb_ + 128 * c:kb_ + 128 * (c + 1)],
                        vt1l[:, kb_ + 128 * c:kb_ + 128 * (c + 1)],
                        pth[:, 128 * c:128 * (c + 1)],
                        ptl[:, 128 * c:128 * (c + 1)],
                        start=(c == 0), stop=(c == kca - 1),
                        inc=(tn, 1) if c == kca - 1 else None)
                pv_done = s.val(tn)
                off = SH * hq + 128 * qt
                C.wait_ge(tn, pv_done)
                C.activation(out=attn_hi[:, off:off + 128], in_=pt0[:, 0:128],
                             func=AF.Copy, inc=(sc, 1))
                V.wait_ge(sc, s.val(sc))
                V.tensor_tensor(attn_lo[:, off:off + 128], pt0[:, 0:128],
                                attn_hi[:, off:off + 128], ALU.subtract,
                                inc=(vs, 1))
                bank_done(pt0, vs)
        return s.val(vs)

    def quant_pipe(src_ap, out_ap, sinv_col, w, qa, qb, qt3, qt4, qt5,
                   signed=True):
        if signed:
            V.tensor_single_scalar(qa[:, 0:w], src_ap, sinv_col, ALU.mult)
            V.tensor_scalar_mul(qb[:, 0:w], qa[:, 0:w], -1.0)
            V.tensor_tensor(qb[:, 0:w], qb[:, 0:w], qa[:, 0:w], ALU.max)
        else:
            V.tensor_single_scalar(qb[:, 0:w], src_ap, sinv_col, ALU.mult)
        V.tensor_single_scalar(qt3[:, 0:w], qb[:, 0:w], 2.0 ** -6, ALU.max,
                               inc=(vs, 1))
        C.wait_ge(vs, s.val(vs))
        C.activation(out=qt3[:, 0:w], in_=qt3[:, 0:w], func=AF.Ln, inc=(sc, 1))
        V.wait_ge(sc, s.val(sc))
        V.tensor_scalar(qt3[:, 0:w], qt3[:, 0:w], LN2_INV, 63.5, ALU.mult,
                        ALU.add)
        V.tensor_copy(out=qt4[:, 0:w], in_=qt3[:, 0:w])
        V.tensor_copy(out=qt3[:, 0:w], in_=qt4[:, 0:w])
        V.tensor_scalar(qt3[:, 0:w], qt3[:, 0:w], -1.0, 194.0, ALU.mult,
                        ALU.add)
        V.tensor_copy(out=qt5[:, 0:w], in_=qt3[:, 0:w])
        V.tensor_single_scalar(qt5[:, 0:w], qt5[:, 0:w], 23,
                               ALU.arith_shift_left)
        V.tensor_scalar_add(qt4[:, 0:w], qt4[:, 0:w], 60)
        V.tensor_single_scalar(qt4[:, 0:w], qt4[:, 0:w], 23,
                               ALU.arith_shift_left)
        V.tensor_tensor(qb[:, 0:w], qb[:, 0:w], qt5[:, 0:w].bitcast(F32),
                        ALU.mult)
        V.tensor_copy(out=qt5[:, 0:w], in_=qb[:, 0:w])
        V.tensor_copy(out=qb[:, 0:w], in_=qt5[:, 0:w])
        V.tensor_tensor(qb[:, 0:w], qb[:, 0:w], qt4[:, 0:w].bitcast(F32),
                        ALU.mult)
        if signed:
            V.tensor_scalar(qa[:, 0:w], qa[:, 0:w], 0.0, 2.0, ALU.is_ge,
                            ALU.mult)
            V.tensor_scalar_add(qa[:, 0:w], qa[:, 0:w], -1.0)
            V.tensor_tensor(out_ap, qb[:, 0:w], qa[:, 0:w], ALU.mult,
                            inc=(vs, 1))
        else:
            V.tensor_copy(out=out_ap, in_=qb[:, 0:w], inc=(vs, 1))

    def pmax_gather(col_src_ap, grow_sl, srow_col, tag):
        GQ.wait_ge(vs, s.val(vs))
        GQ.dma_start(out=gcol.ap()[:, 0:1], in_=col_src_ap, inc=(gq, 16))
        GQ.wait_ge(gq, s.val(gq))
        GQ.dma_start(out=grow[0:1, grow_sl],
                     in_=gcol.ap()[:, 0:1].rearrange("p o -> o p"),
                     inc=(gq, 16))
        V.wait_ge(gq, s.val(gq))
        V.reduce_max(grow[0:1, srow_col:srow_col + 1],
                     grow[0:1, grow_sl], AX.X, inc=(vs, 1))
        V.wait_ge(vs, s.val(vs))

    def emit_l1_ffn():
        X80 = 0
        WF0 = 32768
        W80 = 40960
        QA0, QB0, QT3 = 49152, 53248, 57344
        QT4, QT5 = 61440, 65536
        HS0 = 69632
        H20 = 70144
        H80 = 74240
        WP0 = 76288
        x8 = A(X80, 128, NCH * SH, BF16)
        wf = [A(WF0, 128, 1024), A(WF0 + 4096, 128, 1024)]
        w8b = [A(W80, 128, 2048, BF16), A(W80 + 4096, 128, 2048, BF16)]
        qa = A(QA0, 128, 1024)
        qb = A(QB0, 128, 1024)
        qt3 = A(QT3, 128, 1024)
        qt4 = A(QT4, 128, 1024, I32)
        qt5 = A(QT5, 128, 1024, I32)
        hstat = A(HS0, 128, 128)
        h2s = [A(H20, 128, 512), A(H20 + 2048, 128, 512)]
        h8b = [A(H80, 128, 512, BF16), A(H80 + 1024, 128, 512, BF16)]
        wpb = [A(WP0, 128, 512, BF16), A(WP0 + 1024, 128, 512, BF16)]
        agwo = agw_out.ap()
        agpo = agp_out.ap()
        h2sv = h2_save.ap()
        h8sv = h8_save.ap()

        ent_v = s.val(vs)
        SY.wait_ge(vs, ent_v)
        SY.wait_ge(tn, s.val(tn))
        SY.wait_ge(sc, s.val(sc))
        V.memset(grow[:], 0.0)
        V.memset(hstat[:, :], 0.0, inc=(vs, 1))

        for c in range(NCH):
            V.reduce_max(hstat[:, 24 + c:25 + c], R[:, SH * c:SH * (c + 1)],
                         AX.X, apply_absolute_value=True,
                         inc=(vs, 1) if c == NCH - 1 else None)
        V.wait_ge(vs, s.val(vs))
        V.reduce_max(hstat[:, 40:41], hstat[:, 24:40], AX.X, inc=(vs, 1))
        pmax_gather(hstat[:, 40:41], slice(264, 392), 2, "x")
        wfree = [(vs, s.val(vs)), (vs, s.val(vs))]
        for j in range(8):
            for hf in range(2):
                b = (2 * j + hf) % 2
                SY.wait_ge(*wfree[b])
                SY.dma_start(out=wf[b][:, :],
                             in_=wfc_shard[j:j + 1, :, 1024 * hf:1024 * (hf + 1)]
                             .rearrange("o p n -> p (o n)"), inc=(wlp[b], 16))
                V.wait_ge(wlp[b], s.val(wlp[b]))
                V.reduce_max(hstat[:, 2 * j + hf:2 * j + hf + 1], wf[b][:, :],
                             AX.X, apply_absolute_value=True, inc=(vs, 1))
                wfree[b] = (vs, s.val(vs))
        V.wait_ge(vs, s.val(vs))
        V.reduce_max(hstat[:, 41:42], hstat[:, 0:16], AX.X, inc=(vs, 1))
        pmax_gather(hstat[:, 41:42], slice(8, 136), 0, "wfc")
        wfree = [(vs, s.val(vs)), (vs, s.val(vs))]
        for j in range(2):
            for pc in range(8):
                b = (8 * j + pc) % 2
                SY.wait_ge(*wfree[b])
                SY.dma_start(out=wf[b][:, :],
                             in_=wproj_shard[j:j + 1, :,
                                             1024 * pc:1024 * (pc + 1)]
                             .rearrange("o p n -> p (o n)"),
                             inc=(wlp[b], 16))
                V.wait_ge(wlp[b], s.val(wlp[b]))
                V.reduce_max(hstat[:, 48 + 8 * j + pc:49 + 8 * j + pc],
                             wf[b][:, :], AX.X, apply_absolute_value=True,
                             inc=(vs, 1))
                wfree[b] = (vs, s.val(vs))
        V.wait_ge(vs, s.val(vs))
        V.reduce_max(hstat[:, 42:43], hstat[:, 48:64], AX.X, inc=(vs, 1))
        pmax_gather(hstat[:, 42:43], slice(136, 264), 1, "wproj")
        GQ.wait_ge(vs, s.val(vs))
        GQ.dma_start(out=ar_in.ap(), in_=grow[0:1, 0:8], inc=(gq, 16))
        GQ.wait_ge(gq, s.val(gq))
        GQ.collective_compute("AllReduce", ALU.max, replica_groups=ALL8,
                              ins=[ar_in.ap().opt()], outs=[ar_out.ap().opt()],
                              inc=(cc, 1))
        GQ.wait_ge(cc, s.val(cc))
        GQ.dma_start(out=grow[0:1, 392:400], in_=ar_out.ap(), inc=(gq, 16))
        bank_wait(T, ps[4])
        T.wait_ge(gq, s.val(gq))
        T.matmul(ps[4][:, 0:8], cst["ones_row"][:], grow[0:1, 392:400],
                 start=True, stop=True, inc=(tn, 1))
        V.wait_ge(tn, s.val(tn))
        V.tensor_copy(out=sstat[:, 0:8], in_=ps[4][:, 0:8], inc=(vs, 1))
        bank_done(ps[4], vs)
        V.wait_ge(vs, s.val(vs))
        V.tensor_scalar(sstat[:, 8:16], sstat[:, 0:8], 1.0 / FP8_MAX, 1e-6,
                        ALU.mult, ALU.max, inc=(vs, 1))
        V.wait_ge(vs, s.val(vs))
        V.reciprocal(rstat[:, 16:24], sstat[:, 8:16], inc=(vs, 1))
        V.wait_ge(vs, s.val(vs))
        V.tensor_single_scalar(gstat[:, 48:56], sstat[:, 8:16],
                               sstat[:, 8:9], ALU.mult, inc=(vs, 1))
        V.wait_ge(vs, s.val(vs))
        agwi = agw_in.ap()
        wfree = [(vs, s.val(vs)), (vs, s.val(vs))]
        for j in range(8):
            V.wait_ge(gq, s.val(gq))
            for hf in range(2):
                b = (2 * j + hf) % 2
                SY.wait_ge(*wfree[b])
                SY.dma_start(out=wf[b][:, :],
                             in_=wfc_shard[j:j + 1, :, 1024 * hf:1024 * (hf + 1)]
                             .rearrange("o p n -> p (o n)"), inc=(wlp[b], 16))
                V.wait_ge(wlp[b], s.val(wlp[b]))
                quant_pipe(wf[b][:, :], w8b[0][:, 1024 * hf:1024 * (hf + 1)],
                           rstat[:, 16:17], 1024, qa, qb, qt3, qt4, qt5)
                wfree[b] = (vs, s.val(vs))
            GQ.wait_ge(vs, s.val(vs))
            GQ.dma_start(out=agwi[128 * j:128 * (j + 1), :], in_=w8b[0][:, :],
                         inc=(gq, 16))
        agpi = agp_in.ap()
        wfree = [(vs, s.val(vs)), (vs, s.val(vs))]
        w8free = [(gqp[0], s.val(gqp[0])), (gqp[1], s.val(gqp[1]))]
        for j in range(2):
            for pc in range(8):
                b = pc % 2
                SY.wait_ge(*wfree[b])
                SY.dma_start(out=wf[b][:, :],
                             in_=wproj_shard[j:j + 1, :,
                                             1024 * pc:1024 * (pc + 1)]
                             .rearrange("o p n -> p (o n)"),
                             inc=(wlp[b], 16))
                V.wait_ge(wlp[b], s.val(wlp[b]))
                V.wait_ge(*w8free[b])
                quant_pipe(wf[b][:, :], w8b[b][:, 0:1024],
                           rstat[:, 17:18], 1024, qa, qb, qt3, qt4, qt5)
                wfree[b] = (vs, s.val(vs))
                GQ.wait_ge(vs, s.val(vs))
                GQ.dma_start(out=agpi[128 * j:128 * (j + 1),
                                      1024 * pc:1024 * (pc + 1)],
                             in_=w8b[b][:, 0:1024], inc=(gqp[b], 16))
                w8free[b] = (gqp[b], s.val(gqp[b]))
        for c in range(NCH):
            quant_pipe(R[:, SH * c:SH * (c + 1)],
                       x8[:, SH * c:SH * (c + 1)], rstat[:, 18:19], 1024,
                       qa, qb, qt3, qt4, qt5)
        x8_done = s.val(vs)
        GQ.wait_ge(gq, s.val(gq))
        GQ.wait_ge(gq2, s.val(gq2))
        GQ.collective_compute("AllGather", ALU.bypass, replica_groups=ALL8,
                              ins=[agw_in.ap().opt()], outs=[agw_out.ap().opt()],
                              inc=(cc, 1))
        GQ.collective_compute("AllGather", ALU.bypass, replica_groups=ALL8,
                              ins=[agp_in.ap().opt()], outs=[agp_out.ap().opt()],
                              inc=(cc, 1))
        wg_ready = s.val(cc)
        SY.wait_ge(cc, wg_ready)

        w8_loaded = [None, None]
        h2_stored = [(gqp[0], s.val(gqp[0])), (gqp[1], s.val(gqp[1]))]
        w8_freed = [(tn, s.val(tn)), (tn, s.val(tn))]
        for oc in range(64):
            b = oc % 2
            SY.wait_ge(*w8_freed[b])
            r, lj = oc // 8, oc % 8
            SY.dma_start(out=w8b[b][:, :],
                         in_=agwo[1024 * r + 128 * lj:1024 * r + 128 * (lj + 1), :],
                         inc=(wlp[b], 16))
            w8_loaded[b] = (wlp[b], s.val(wlp[b]))
            for tb in range(NTB):
                pt = ps[2 + tb]
                bank_wait(T, pt)
                T.wait_ge(*w8_loaded[b])
                T.wait_ge(vs, x8_done)
                for c in range(NCH):
                    T.matmul(pt[:], w8b[b][:, 128 * c:128 * (c + 1)],
                             x8[:, SH * c + 512 * tb:SH * c + 512 * (tb + 1)],
                             start=(c == 0), stop=(c == NCH - 1),
                             inc=(tn, 1) if c == NCH - 1 else None)
                w8_freed[b] = (tn, s.val(tn))
                hb = h2s[tb]
                C.wait_ge(tn, s.val(tn))
                C.wait_ge(*h2_stored[tb])
                C.activation(out=hb[:, :], in_=pt[:], func=AF.Relu,
                             scale=gstat[:, 50:51])
                C.activation(out=hb[:, :], in_=hb[:, :], func=AF.Square,
                             inc=(sc, 1))
                sq_done = s.val(sc)
                bank_done(pt, sc)
                V.wait_ge(sc, sq_done)
                V.reduce_max(hstat[:, 2 * oc + tb:2 * oc + tb + 1],
                             hb[:, :], AX.X, inc=(vs, 1))
                GQ.wait_ge(vs, s.val(vs))
                GQ.dma_start(out=h2sv[128 * oc:128 * (oc + 1),
                                      512 * tb:512 * (tb + 1)],
                             in_=hb[:, :], inc=(gqp[tb], 16))
                h2_stored[tb] = (gqp[tb], s.val(gqp[tb]))
        V.wait_ge(vs, s.val(vs))
        V.reduce_max(hstat[:, 40:41], hstat[:, 0:128], AX.X, inc=(vs, 1))
        V.wait_ge(vs, s.val(vs))
        V.memset(grow[0:1, 0:8], 0.0, inc=(vs, 1))
        pmax_gather(hstat[:, 40:41], slice(8, 136), 0, "h")
        GQ.wait_ge(vs, s.val(vs))
        GQ.dma_start(out=ar2_in.ap(), in_=grow[0:1, 0:8], inc=(gq, 16))
        GQ.wait_ge(gq, s.val(gq))
        GQ.collective_compute("AllReduce", ALU.max, replica_groups=ALL8,
                              ins=[ar2_in.ap().opt()], outs=[ar2_out.ap().opt()],
                              inc=(cc, 1))
        GQ.wait_ge(cc, s.val(cc))
        GQ.dma_start(out=grow[0:1, 392:400], in_=ar2_out.ap(), inc=(gq, 16))
        bank_wait(T, ps[4])
        T.wait_ge(gq, s.val(gq))
        T.matmul(ps[4][:, 0:8], cst["ones_row"][:], grow[0:1, 392:400],
                 start=True, stop=True, inc=(tn, 1))
        V.wait_ge(tn, s.val(tn))
        V.tensor_copy(out=sstat[:, 0:8], in_=ps[4][:, 0:8], inc=(vs, 1))
        bank_done(ps[4], vs)
        V.wait_ge(vs, s.val(vs))
        V.tensor_scalar(sstat[:, 0:8], sstat[:, 0:8], 1.0 / FP8_MAX, 1e-6,
                        ALU.mult, ALU.max, inc=(vs, 1))
        V.wait_ge(vs, s.val(vs))
        V.reciprocal(rstat[:, 24:32], sstat[:, 0:8], inc=(vs, 1))
        V.wait_ge(vs, s.val(vs))
        V.tensor_single_scalar(gstat[:, 56:64], sstat[:, 8:16],
                               sstat[:, 0:1], ALU.mult, inc=(vs, 1))
        V.wait_ge(vs, s.val(vs))
        h2free = [(vs, s.val(vs)), (vs, s.val(vs))]
        h8free = [(gqp[0], s.val(gqp[0])), (gqp[1], s.val(gqp[1]))]
        for oc in range(64):
            for tb in range(NTB):
                b = (2 * oc + tb) % 2
                SY.wait_ge(*h2free[b])
                SY.dma_start(out=h2s[b][:, :],
                             in_=h2sv[128 * oc:128 * (oc + 1),
                                      512 * tb:512 * (tb + 1)],
                             inc=(wlp[b], 16))
                V.wait_ge(wlp[b], s.val(wlp[b]))
                V.wait_ge(*h8free[b])
                quant_pipe(h2s[b][:, :], h8b[b][:, :], rstat[:, 24:25], 512,
                           qa, qb, qt3, qt4, qt5, signed=False)
                h2free[b] = (vs, s.val(vs))
                GQ.wait_ge(vs, s.val(vs))
                GQ.dma_start(out=h8sv[128 * oc:128 * (oc + 1),
                                      512 * tb:512 * (tb + 1)],
                             in_=h8b[b][:, :], inc=(gqp[b], 16))
                h8free[b] = (gqp[b], s.val(gqp[b]))
        y_full = A(0, 128, 16384)
        SY.wait_ge(gq, s.val(gq))
        SY.wait_ge(gq2, s.val(gq2))
        for tb in range(NTB):
            for dcg in range(4):
                for i in range(4):
                    bank_wait(T, ps[i])
                buf_free = [(tn, s.val(tn)), (tn, s.val(tn))]
                for oc in range(64):
                    b = oc % 2
                    SY.wait_ge(*buf_free[b])
                    SY.dma_start(out=h8b[b][:, :],
                                 in_=h8sv[128 * oc:128 * (oc + 1),
                                          512 * tb:512 * (tb + 1)],
                                 inc=(wlp[b], 16))
                    for i in range(4):
                        dc = 4 * dcg + i
                        SY.dma_start(out=wpb[b][:, 128 * i:128 * (i + 1)],
                                     in_=agpo[128 * dc:128 * (dc + 1),
                                              128 * oc:128 * (oc + 1)],
                                     inc=(wlp[b], 16))
                    T.wait_ge(wlp[b], s.val(wlp[b]))
                    for i in range(4):
                        T.matmul(ps[i][:],
                                 wpb[b][:, 128 * i:128 * (i + 1)],
                                 h8b[b][:, :], start=(oc == 0),
                                 stop=(oc == 63),
                                 inc=(tn, 1) if i == 3 else None)
                    buf_free[b] = (tn, s.val(tn))
                V.wait_ge(tn, s.val(tn))
                for i in range(4):
                    dc = 4 * dcg + i
                    ysl = y_full[:, SH * dc + 512 * tb:SH * dc + 512 * (tb + 1)]
                    V.scalar_tensor_tensor(
                        ysl, ps[i][:], gstat[:, 57:58],
                        R[:, SH * dc + 512 * tb:SH * dc + 512 * (tb + 1)],
                        ALU.mult, ALU.add, inc=(vs, 1))
                    bank_done(ps[i], vs)
        sqf = [h2s[0], h2s[1]]
        for tb in range(NTB):
            layer_norm_tb(
                lambda c, tb=tb: y_full[:, SH * c + 512 * tb:
                                        SH * c + 512 * (tb + 1)], tb, sqf)
        return s.val(vs)

    r_ready = s.val(vs)
    if "l0proj" in phases:
        agr = emit_l0_proj()
        if dbg == "ag0":
            GQ.wait_ge(cc, agr)
            GQ.dma_start(out=dbg_out, in_=ag0_out.ap(), inc=(gq, 16))
    if "l0core" in phases:
        agr = emit_l0_proj()
        attn_done = emit_l0_core(agr)
        if dbg == "attn_raw":
            GQ.wait_ge(vs, attn_done)
            GQ.wait_ge(sc, s.val(sc))
            GQ.dma_start(out=dbg_out.rearrange("(c p) t -> p c t", p=128),
                         in_=R[:].rearrange("p (c t) -> p c t", c=NCH),
                         inc=(gq, 16))
    if "l0attn" in phases:
        agr = emit_l0_proj()
        attn_done = emit_l0_core(agr)
        if dbg == "attn_raw":
            GQ.wait_ge(vs, attn_done)
            GQ.wait_ge(sc, s.val(sc))
            GQ.dma_start(out=dbg_out.rearrange("(c p) t -> p c t", p=128),
                         in_=R[:].rearrange("p (c t) -> p c t", c=NCH),
                         inc=(gq, 16))
        else:
            r_ready = emit_outproj_ln(wout_img, "x", attn_done)

    if "l0ffn" in phases:
        r_ready = emit_l0_ffn()
    if "l1attn" in phases:
        attn1_done = emit_l1_attn()
        r_ready = emit_outproj_ln(wo1_img, "rsave", attn1_done)
    if "l1ffn" in phases:
        r_ready = emit_l1_ffn()
    if dbg == "ffnstats":
        GQ.wait_ge(vs, s.val(vs))
        GQ.wait_ge(sc, s.val(sc))
        GQ.dma_start(out=dbg_out[0:128, 0:16], in_=sstat[:], inc=(gq, 16))
        GQ.dma_start(out=dbg_out[128:256, 0:64], in_=rstat[:], inc=(gq, 16))
        GQ.dma_start(out=dbg_out[256:384, 0:64], in_=gstat[:], inc=(gq, 16))
        GQ.dma_start(out=dbg_out[384:385, 0:1024], in_=grow[0:1, 0:1024],
                     inc=(gq, 16))
        GQ.dma_start(out=dbg_out[512:640, :], in_=h2_save.ap()[0:128, :],
                     inc=(gq, 16))
        GQ.dma_start(out=dbg_out[640:768, :], in_=h2_save.ap()[1024:1152, :],
                     inc=(gq, 16))
    if dbg == "r_l1attn":
        GQ.wait_ge(vs, r_ready)
        GQ.dma_start(out=dbg_out.rearrange("(c p) t -> p c t", p=128),
                     in_=R[:].rearrange("p (c t) -> p c t", c=NCH), inc=(gq, 16))
    if dbg == "r_l0ffn":
        GQ.wait_ge(vs, r_ready)
        GQ.dma_start(out=dbg_out.rearrange("(c p) t -> p c t", p=128),
                     in_=R[:].rearrange("p (c t) -> p c t", c=NCH), inc=(gq, 16))
    if dbg == "r_l0attn":
        GQ.wait_ge(vs, r_ready)
        GQ.dma_start(out=dbg_out.rearrange("(c p) t -> p c t", p=128),
                     in_=R[:].rearrange("p (c t) -> p c t", c=NCH), inc=(gq, 16))

    GQ.wait_ge(vs, s.val(vs))
    GQ.wait_ge(sc, s.val(sc))
    GQ.wait_ge(tn, s.val(tn))
    GQ.dma_start(out=y_out.rearrange("(c p) t -> p c t", p=128),
                 in_=R[:].rearrange("p (c t) -> p c t", c=NCH), inc=(gq, 16))
    GQ.wait_ge(gq, s.val(gq))

    with nc.Block() as block:
        @block.sync
        def _(h):
            for fn in s.q["sync"]:
                fn(h)

        @block.tensor
        def _(h):
            for fn in s.q["tensor"]:
                fn(h)

        @block.vector
        def _(h):
            for fn in s.q["vector"]:
                fn(h)

        @block.scalar
        def _(h):
            for fn in s.q["scalar"]:
                fn(h)

        @block.gpsimd
        def _(h):
            for fn in s.q["gpsimd"]:
                fn(h)

    ctx.close()
    return nc


def make_inputs(core, inputs):
    b, hh = core // 2, core % 2
    tsl = slice(SH * hh, SH * (hh + 1))
    x = np.asarray(inputs["L0_attn_x"])
    d = {}
    d["x_fm"] = np.ascontiguousarray(x[b].T[:, tsl])

    wqkvT = np.ascontiguousarray(np.asarray(inputs["L0_attn_w_qkv"]).T)
    d["q0_img"] = tile_lhsT(wqkvT[:, 0:2048])
    d["k0_img"] = tile_lhsT(wqkvT[:, 2048:4096])
    d["v0_img"] = tile_lhsT(wqkvT[:, 4096:6144])
    d["wout_img"] = tile_lhsT(np.ascontiguousarray(
        np.asarray(inputs["L0_attn_w_out"]).T))

    pos0 = np.arange(SH * hh, SH * (hh + 1))
    (ch, cl), (sh_, sl_) = rope_tables(pos0, HD, 10000.0)
    d["cos0_hi"], d["cos0_lo"] = ch, cl
    d["sin0_hi"], d["sin0_lo"] = sh_, sl_

    d["wffn_img"] = tile_lhsT(np.ascontiguousarray(np.asarray(inputs["L0_ffn_W"]).T))
    A_ = np.asarray(inputs["L0_ffn_lora_A"])
    d["aT_img"] = tile_lhsT(np.ascontiguousarray(
        A_.transpose(2, 0, 1).reshape(2048, 128)))
    B_ = np.asarray(inputs["L0_ffn_lora_B"])
    d["bstack_img"] = tile_lhsT(np.ascontiguousarray(
        B_.transpose(0, 2, 1).reshape(128, 2048)))
    d["gate_w"] = np.ascontiguousarray(
        np.asarray(inputs["L0_ffn_gate_w"]).T).reshape(16, 128, 8)
    d["gate_b"] = np.ascontiguousarray(np.broadcast_to(
        np.asarray(inputs["L0_ffn_gate_b"])[None, :], (128, 8))).astype(np.float32)
    d["bffn"] = np.ascontiguousarray(
        np.asarray(inputs["L0_ffn_b"]).reshape(16, 128).T)
    ee = np.zeros((8, 128), np.float32)
    for e in range(8):
        ee[e, 16 * e:16 * (e + 1)] = 1.0
    d["e_expand"] = ee

    d["wq1_img"] = tile_lhsT(np.asarray(inputs["L1_attn_Wq"]))
    d["wk1_img"] = tile_lhsT(np.asarray(inputs["L1_attn_Wk"]))
    d["wv1_img"] = tile_lhsT(np.asarray(inputs["L1_attn_Wv"]))
    d["wo1_img"] = tile_lhsT(np.asarray(inputs["L1_attn_Wo"]))
    d["gt_img"] = np.ascontiguousarray(
        np.asarray(inputs["L1_attn_global_tokens"]).T).reshape(16, 128, G)
    pos1 = G + SH * hh + np.arange(SH)
    (c1h, c1l), (s1h, s1l) = rope_tables(pos1, HD, 500000.0)
    d["cos1_hi"], d["cos1_lo"] = c1h, c1l
    d["sin1_hi"], d["sin1_lo"] = s1h, s1l
    (cgh, cgl), (sgh, sgl) = rope_tables(np.arange(G), HD, 500000.0)
    d["cos1g"], d["sin1g"] = cgh, sgh
    d["cos1g_lo"], d["sin1g_lo"] = cgl, sgl

    j = np.arange(E1P, dtype=np.float64)
    d["jrow"] = np.ascontiguousarray(
        np.broadcast_to(j[None, :], (128, E1P))).astype(np.float32)
    jb = j + np.where((j % 256) == 0, 1e9, 0.0)
    d["jbig"] = np.ascontiguousarray(
        np.broadcast_to(jb[None, :], (128, E1P))).astype(np.float32)

    if "_wfc_img" not in _SHARED:
        _SHARED["_wfc_img"] = tile_lhsT(np.ascontiguousarray(
            np.asarray(inputs["L1_ffn_w_fc"]).T))
        _SHARED["_wproj_img"] = tile_lhsT(np.ascontiguousarray(
            np.asarray(inputs["L1_ffn_w_proj"]).T))
    d["wfc_shard"] = np.ascontiguousarray(
        _SHARED["_wfc_img"][8 * core:8 * (core + 1)])
    d["wproj_shard"] = np.ascontiguousarray(
        _SHARED["_wproj_img"][2 * core:2 * (core + 1)])

    d["ident_b"] = np.eye(128).astype(np.float32)
    import ml_dtypes
    d["ident_b"] = np.eye(128).astype(ml_dtypes.bfloat16)
    d["ident_f"] = np.eye(128, dtype=np.float32)
    oc = np.zeros((128, 8), np.float32)
    oc[:, 0] = 1.0
    d["ones_col"] = oc
    d["ones_row"] = np.ones((1, 128), np.float32)
    d["eps_row"] = np.full((1, 512), EPS, np.float32)
    d["arange_col"] = np.broadcast_to(
        (G + SH * hh + np.arange(128, dtype=np.float32))[:, None],
        (128, 8)).copy()
    cc_ = np.zeros((128, 8), np.float32)
    cc_[:, 0] = -0.5
    cc_[:, 1] = 124.0
    cc_[:, 2] = 130.0
    d["ccol"] = cc_
    return d


_NC_CACHE = {}
_SHARED = {}


def kernel(**inputs):
    _SHARED.clear()
    key = ("full",)
    if key not in _NC_CACHE:
        _NC_CACHE[key] = build_program(ALL_PHASES)
    nc = _NC_CACHE[key]
    in_maps = [make_inputs(c, inputs) for c in range(8)]
    res = run_bass_kernel_spmd(nc, in_maps, core_ids=list(range(8)))
    out = np.empty((B, DM, S), np.float32)
    for c in range(8):
        b, hh = c // 2, c % 2
        out[b][:, SH * hh:SH * (hh + 1)] = res.results[c]["y_fm"]
    return np.ascontiguousarray(out.transpose(0, 2, 1))
